# version 07
# baseline (speedup 1.0000x reference)
import numpy as np
from contextlib import ExitStack
import concourse.bass as bass
import concourse.mybir as mybir
from concourse.bass_core_rust import IndirectOffsetOnAxis
from concourse.bass_utils import run_bass_kernel_spmd

F32 = mybir.dt.float32
BF16 = mybir.dt.bfloat16
I32 = mybir.dt.int32
AF = mybir.ActivationFunctionType
ALU = mybir.AluOpType
AX = mybir.AxisListType

NC = 8
TLOC = 2048
DM = 1024
DI = 2048
NSTATE = 16
NEXP = 8
CAP = 2048
HD = 4096
NH = 16
DK = 64
TSEQ = 4096
LN2 = 6.93147180559945286e-01
GC2 = 2.0 * 0.7978845608028654
GA2 = GC2 * 0.044715

_LAUNCHES = []



class Prog:

    def __init__(self, nc, ctx):
        self.nc = nc
        self.ctx = ctx
        self.sem = {}
        self.count = {}
        self.swdge = set()
        self.touched = set()
        self.phase_names = {}
        for name in ("v", "s", "p", "g"):
            self._mksem(name)

    def _mksem(self, name):
        self._serial = getattr(self, "_serial", 0) + 1
        h = self.nc.alloc_semaphore(name=f"{name}_s{self._serial}")
        self.sem[name] = h
        self.count[name] = 0

    def new_phase(self):
        self.phase_names = {}

    def _canon(self, name):
        if name in ("v", "s", "p", "g") or name in self.swdge or name == "CC":
            return name
        if name not in self.phase_names:
            canon = f"H{len(self.phase_names)}"
            self.phase_names[name] = canon
            if canon not in self.sem:
                self._mksem(canon)
        return self.phase_names[name]

    def stream(self, name):
        if name not in self.sem:
            self._mksem(name)
        return name

    def eng(self, name):
        nc = self.nc
        return {"v": nc.vector, "s": nc.scalar, "p": nc.tensor, "g": nc.gpsimd,
                "sync": nc.sync}[name]

    def _wait(self, eng_obj, deps):
        need = {}
        for d in deps:
            if d is None:
                continue
            (s, v) = d
            if v > need.get(s, -1):
                need[s] = v
        for s, v in need.items():
            if v > 0:
                eng_obj.wait_ge(self.sem[s], v)

    def op(self, ename, method, *args, deps=(), **kw):
        eng_obj = self.eng(ename)
        deps = list(deps)
        if ename in ("v", "s", "g") and self.count[ename] > 0:
            deps.append((ename, self.count[ename]))
        self._wait(eng_obj, deps)
        inst = getattr(eng_obj, method)(*args, **kw)
        self.count[ename] += 1
        inst.then_inc(self.sem[ename], 1)
        return (ename, self.count[ename])

    def dma(self, stream, out, in_, deps=(), queue="sync", **kw):
        if queue == "g":
            self.stream(stream)
            self.swdge.add(stream)
        else:
            stream = self._canon(stream)
            self.stream(stream)
        eng_obj = self.eng(queue)
        self._wait(eng_obj, deps)
        inst = eng_obj.dma_start(out=out, in_=in_, **kw)
        self.touched.add(stream)
        self.count[stream] += 16
        inst.then_inc(self.sem[stream], 16)
        return (stream, self.count[stream])

    def idma(self, stream, out, out_offset, in_, in_offset, bounds, deps=(),
             compute_op=ALU.bypass, rot=8):
        if not hasattr(self, "_rot"):
            self._rot = {}
        i = self._rot.get(stream, 0)
        self._rot[stream] = i + 1
        sub = f"{stream}~{i % rot}"
        self.stream(sub)
        self.swdge.add(sub)
        g = self.eng("g")
        deps = list(deps)
        if self.count[sub] > 0:
            deps.append((sub, self.count[sub]))
        self._wait(g, deps)
        inst = g.indirect_dma_start(out=out, out_offset=out_offset, in_=in_,
                                    in_offset=in_offset, bounds_check=bounds,
                                    oob_is_err=False, compute_op=compute_op)
        self.touched.add(sub)
        self.count[sub] += 16
        inst.then_inc(self.sem[sub], 16)
        return (sub, self.count[sub])

    def rot_toks(self, stream, rot=8):
        return [(f"{stream}~{i}", self.count.get(f"{stream}~{i}", 0))
                for i in range(rot)]

    def coll(self, stream, kind, op, ins, outs, deps=()):
        self.stream(stream)
        self.swdge.add(stream)
        g = self.eng("g")
        self._wait(g, deps)
        inst = g.collective_compute(kind, op, replica_groups=[list(range(NC))],
                                    ins=ins, outs=outs)
        self.touched.add(stream)
        self.count[stream] += 1
        inst.then_inc(self.sem[stream])
        return (stream, self.count[stream])

    def tok(self, name):
        if name not in self.sem or name in self.phase_names:
            name = self._canon(name)
        return (name, self.count.get(name, 0))


class Builder:
    def __init__(self, name):
        self.nc = bass.Bass("TRN2", target_bir_lowering=False, name=name)
        self.ctx = ExitStack()
        self.pr = Prog(self.nc, self.ctx)
        self.block = self.ctx.enter_context(self.nc.Block())
        self._final_deps = []
        self.prefix = ""
        self._scope = None

    def pname(self, name):
        return self.prefix + name

    def sub_side(self):
        return "left" if self._scope is not None else "right"

    def sb(self, name, shape, dt=F32, ctx=None, side=None):
        if ctx is None and self._scope is not None:
            ctx = self._scope
            side = "right"
        return (ctx or self.ctx).enter_context(
            self.nc.sbuf_tensor(self.pname(name), shape, dt, side=side))

    def ps(self, name, shape, dt=F32):
        ctx = self._scope if self._scope is not None else self.ctx
        return ctx.enter_context(self.nc.psum_tensor(self.pname(name), shape, dt))

    def psn(self, name, shape, dt=F32):
        return self.nc.psum_tensor(self.pname(name), shape, dt)

    def din(self, name, shape, dt=F32):
        return self.nc.dram_tensor(name, shape, dt, kind="ExternalInput").ap()

    def dout(self, name, shape, dt=F32):
        return self.nc.dram_tensor(name, shape, dt, kind="ExternalOutput").ap()

    def dint(self, name, shape, dt=F32, **kw):
        return self.nc.dram_tensor(self.pname(name), shape, dt,
                                   kind="Internal", **kw).ap()

    def out_done(self, tok):
        self._final_deps.append(tok)

    def bound_reg(self, value):
        if not hasattr(self, "_bregs"):
            self._bregs = {}
        if value not in self._bregs:
            reg = self.nc.gpsimd.alloc_register(f"bnd{value}")
            self.nc.gpsimd.reg_mov(reg, value)
            self._bregs[value] = reg
        return self._bregs[value]

    def finish(self):
        pr = self.pr

        def _waits():
            need = {}
            for (s, v) in self._final_deps:
                if s in pr.sem:
                    need[s] = max(need.get(s, 0), v)
            for s, v in need.items():
                self.nc.sync.wait_ge(pr.sem[s], v)
            for sname, cnt in list(pr.count.items()):
                if cnt > 0 and sname not in need:
                    self.nc.sync.wait_ge(pr.sem[sname], cnt)

        self.block.sync(lambda _s: _waits())
        self.ctx.close()
        return self.nc


def body(b):
    def deco(f):
        if getattr(b, "_in_body", False):
            f()
            return f
        b._in_body = True
        try:
            b.block.sync(lambda _s: f())
        finally:
            b._in_body = False
        return f
    return deco



def emit_weight_prep(b, pairs, ctx=None):
    pr = b.pr
    side = b.sub_side() if ctx is not None else None
    tf = [b.sb(f"wpf{i}", [128, 4096], F32, ctx=ctx, side=side) for i in range(2)]
    tb = [b.sb(f"wpb{i}", [128, 4096], BF16, ctx=ctx, side=side) for i in range(2)]
    seq = []
    for (src, dst, R, C) in pairs:
        for r0 in range(0, R, 128):
            rr = min(128, R - r0)
            for c0 in range(0, C, 4096):
                cc = min(4096, C - c0)
                seq.append((src, dst, r0, rr, c0, cc))
    toks = []
    cvt_by_buf = [None, None]
    st_by_buf = [None, None]
    for j, (src, dst, r0, rr, c0, cc) in enumerate(seq):
        i = j % 2
        ld = pr.dma(f"wp{i}", tf[i][0:rr, 0:cc], src[r0:r0+rr, c0:c0+cc],
                    deps=[cvt_by_buf[i]])
        cvt_by_buf[i] = pr.op("v", "tensor_copy", tb[i][0:rr, 0:cc],
                              tf[i][0:rr, 0:cc], deps=[ld, st_by_buf[i]])
        st_by_buf[i] = pr.dma(f"wps{i}", dst[r0:r0+rr, c0:c0+cc],
                              tb[i][0:rr, 0:cc], deps=[cvt_by_buf[i]])
        toks.append(st_by_buf[i])
    return [pr.tok("wps0"), pr.tok("wps1")]


def make_ln_stats(b, tag):
    st = {
        "bnout": b.sb(f"{tag}_bnout", [128, 12], F32),
        "bnag": b.sb(f"{tag}_bnag", [128, 2], F32),
        "sd": b.sb(f"{tag}_sd", [128, 1], F32),
        "rstd": b.sb(f"{tag}_rstd", [128, 1], F32),
        "negmur": b.sb(f"{tag}_negmur", [128, 1], F32),
        "eps": b.sb(f"{tag}_eps", [128, 1], F32),
    }
    t = b.pr.op("v", "memset", st["eps"][:], 1e-5)
    st["_init"] = t
    return st


def emit_ln_tile(b, st, x_tile, out_tile, deps=(), out_deps=()):
    pr = b.pr
    pr.op("v", "bn_stats", st["bnout"][:, 0:6], x_tile[:, 0:512],
          deps=list(deps) + [st["_init"]])
    pr.op("v", "bn_stats", st["bnout"][:, 6:12], x_tile[:, 512:1024], deps=deps)
    pr.op("v", "bn_aggr", st["bnag"][:], st["bnout"][:])
    t3 = pr.op("s", "activation", st["sd"][:], st["bnag"][:, 1:2], AF.Sqrt,
               bias=st["eps"][:], deps=[b.pr.tok("v")])
    t4 = pr.op("v", "reciprocal", st["rstd"][:], st["sd"][:], deps=[t3])
    t5 = pr.op("v", "scalar_tensor_tensor", st["negmur"][:],
               st["bnag"][:, 0:1], -1.0, st["rstd"][:], ALU.mult, ALU.mult)
    t6 = pr.op("s", "activation", out_tile, x_tile, AF.Identity,
               bias=st["negmur"][:], scale=st["rstd"][:],
               deps=[t5] + list(out_deps))
    return t6


def pe_fence(b):
    b.pr.eng("p").wait_ge(b.pr.sem["v"], b.pr.count["v"])


def emit_row_broadcast(b, src_row, dst_sb, C, tag, deps=()):
    pr = b.pr
    ones = b.sb(f"{tag}_ones", [1, 128], F32)
    t0 = pr.op("v", "memset", ones[:], 1.0)
    toks = []
    with b.nc.psum_tensor(b.prefix + f"{tag}_ps", [128, 512], F32) as pst:
        for c0 in range(0, C, 512):
            cc = min(512, C - c0)
            tm = pr.op("p", "matmul", pst[:, 0:cc], ones[:], src_row[0:1, c0:c0+cc],
                       start=True, stop=True, deps=list(deps) + [t0, pr.tok("v")])
            toks.append(pr.op("v", "tensor_copy", dst_sb[:, c0:c0+cc],
                              pst[:, 0:cc], deps=[tm]))
    pe_fence(b)
    return toks[-1]


def emit_transposes(b, src_row_bf, dst_col_bf, colbase, ident, psum_bf, n_dchunks,
                    dstride, deps=()):
    pr = b.pr
    last = None
    out_toks = []
    for dc in range(n_dchunks):
        tt = pr.op("p", "transpose", psum_bf[:],
                   src_row_bf[:, dc*128:(dc+1)*128], ident[:],
                   deps=list(deps) + ([last] if last else []))
        last = pr.op("v", "tensor_copy",
                     dst_col_bf[:, dc*dstride + colbase: dc*dstride + colbase + 128],
                     psum_bf[:], deps=[tt])
        out_toks.append(last)
    return out_toks


def emit_gelu_tanh_psum(b, ps_tile, bias_col, w, tmp, out_bf, deps=()):
    pr = b.pr
    t1 = pr.op("s", "activation", tmp["ub"][:, 0:w], ps_tile, AF.Identity,
               bias=bias_col, deps=deps)
    t2 = pr.op("s", "activation", tmp["sq"][:, 0:w], tmp["ub"][:, 0:w], AF.Square)
    t3 = pr.op("v", "tensor_scalar", tmp["p1"][:, 0:w], tmp["sq"][:, 0:w],
               GA2, GC2, ALU.mult, ALU.add, deps=[t2])
    t4 = pr.op("v", "tensor_mul", tmp["warg"][:, 0:w], tmp["p1"][:, 0:w],
               tmp["ub"][:, 0:w], deps=[t1])
    t5 = pr.op("s", "activation", tmp["sq"][:, 0:w], tmp["warg"][:, 0:w],
               AF.Sigmoid, deps=[t4])
    t6 = pr.op("v", "tensor_mul", out_bf, tmp["sq"][:, 0:w], tmp["ub"][:, 0:w],
               deps=[t5])
    return t6, t1



def emit_ffn0_pre(b, io):
    pr = b.pr
    xr = io["xr"]
    gate = io["gate"]
    ws1 = io["ws1"]
    bs1_rs = io["bs1_rs"]
    ws2 = io["ws2"]
    bs2 = io["bs2"]
    ident_in = io["ident_in"]

    o_xn = io["o_xn"]
    o_probs = io["o_probs"]
    o_shared = io["o_shared"]

    ws1b_d = b.dint("ws1b", [DM, HD], BF16)
    ws2b_d = b.dint("ws2b", [HD, DM], BF16)

    NT = TLOC // 128

    @body(b)
    def _main():
        wp = emit_weight_prep(b, [(ws1, ws1b_d, DM, HD), (ws2, ws2b_d, HD, DM)])
        wp_done_all = wp

        ident = b.sb("identt", [128, 128], BF16)
        t_id = pr.dma("Lid", ident[:], ident_in)
        gatetf = b.sb("gatetf", [128, 8 * NEXP], F32)
        gatet = b.sb("gatet", [128, 8 * NEXP], BF16)
        t_gate0 = pr.dma("Lg", gatetf[:].rearrange("p (k e) -> p k e", k=8),
                         gate.rearrange("(k p) e -> p k e", p=128))
        t_gate = pr.op("v", "tensor_copy", gatet[:], gatetf[:], deps=[t_gate0])
        bs1t = b.sb("bs1t", [128, HD // 128], F32)
        t_bs1 = pr.dma("Lb1", bs1t[:], bs1_rs)
        bs2row = b.sb("bs2row", [1, DM], F32)
        t_bs2 = pr.dma("Lb2", bs2row[:], bs2)
        bs2bc = b.sb("bs2bc", [128, DM], F32)
        t_bs2bc = emit_row_broadcast(b, bs2row, bs2bc, DM, "b2b", deps=[t_bs2])

        st = make_ln_stats(b, "ln")
        xt = [b.sb(f"xt{i}", [128, DM], F32) for i in range(2)]
        xnb = [b.sb(f"xnb{i}", [128, DM], BF16) for i in range(2)]
        xnT = b.sb("xnT", [128, 8 * TLOC], BF16)
        pbf = b.ps("pbf", [128, 128], BF16)

        prev_ln = {0: None, 1: None}
        prev_users = {0: (), 1: ()}
        prev_trs = []
        for i in range(NT):
            bi = i % 2
            ld = pr.dma(f"Lx{bi}", xt[bi][:], xr[i*128:(i+1)*128, :],
                        deps=[prev_ln[bi]])
            t_ln = emit_ln_tile(b, st, xt[bi][:], xnb[bi][:], deps=[ld],
                                out_deps=prev_users[bi])
            prev_ln[bi] = t_ln
            t_st = pr.dma(f"Sxn{bi}", o_xn[i*128:(i+1)*128, :], xnb[bi][:],
                          deps=[t_ln])
            trs = emit_transposes(b, xnb[bi], xnT, i*128, ident, pbf, 8, TLOC,
                                  deps=[t_ln, t_id] + prev_trs[-1:])
            prev_trs = trs
            prev_users[bi] = (t_st, trs[-1], pr.tok("p"))
        b.out_done(pr.tok("Sxn0"))
        b.out_done(pr.tok("Sxn1"))
        t_xnT = pr.tok("v")

        pgate = b.ps("pgate", [128, NEXP], F32)
        probs_t = b.sb("probs_t", [128, NEXP], F32)
        negmax = b.sb("negmax", [128, 1], F32)
        ssum = b.sb("ssum", [128, 1], F32)
        sinv = b.sb("sinv", [128, 1], F32)
        prev_gate_rd = None
        prev_sp = None
        for i in range(NT):
            tms = []
            for k in range(8):
                dep0 = [t_xnT, t_gate] + ([prev_gate_rd] if prev_gate_rd else [])
                tms.append(pr.op("p", "matmul", pgate[:],
                                 xnT[:, k*TLOC + i*128: k*TLOC + (i+1)*128],
                                 gatet[:, k*NEXP:(k+1)*NEXP],
                                 start=(k == 0), stop=(k == 7),
                                 deps=dep0 if k == 0 else ()))
            t_mx = pr.op("v", "reduce_max", negmax[:], pgate[:], axis=AX.X,
                         negate=True, deps=[tms[-1]])
            t_e = pr.op("s", "activation", probs_t[:], pgate[:], AF.Exp,
                        bias=negmax[:], accum_out=ssum[:],
                        deps=[t_mx] + ([prev_sp] if prev_sp else []))
            t_r = pr.op("v", "reciprocal", sinv[:], ssum[:], deps=[t_e])
            t_p = pr.op("v", "tensor_scalar", probs_t[:], probs_t[:], sinv[:],
                        None, ALU.mult)
            prev_gate_rd = t_e
            prev_sp = pr.dma("Sp", o_probs[i*128:(i+1)*128, :], probs_t[:], deps=[t_p])
        b.out_done(pr.tok("Sp"))

        w1t = [b.sb(f"w1t{i}", [128, 8 * 128], BF16) for i in range(2)]
        w2t = [b.sb(f"w2t{i}", [128, 512], BF16) for i in range(2)]
        hT = b.sb("hT", [128, 32 * 512], BF16)
        gtmp = {"ub": b.sb("g_ub", [128, 512], F32),
                "sq": b.sb("g_sq", [128, 512], F32),
                "p1": b.sb("g_p1", [128, 512], F32),
                "warg": b.sb("g_wa", [128, 512], F32)}
        outrow = b.sb("outrow", [128, DM], F32)
        prev_ssh = None

        for nch in range(4):
            tn0 = nch * 512
            with b.nc.psum_tensor(b.prefix + f"hps{nch}", [128, 512], F32) as hps:
                t_ld_prev = {0: (pr.tok("v"), pr.tok("s")), 1: ()}
                prev_hps_rd = pr.tok("s")
                prev_ge = None
                for m in range(32):
                    bi = m % 2
                    ldw = pr.dma(f"Lw1{bi}",
                                 w1t[bi][:].rearrange("p (k c) -> p k c", k=8),
                                 ws1b_d[:, m*128:(m+1)*128].rearrange(
                                     "(k p) c -> p k c", p=128),
                                 deps=wp_done_all + list(t_ld_prev.get(bi) or ()))
                    mt = None
                    for k in range(8):
                        mt = pr.op("p", "matmul", hps[:],
                                   w1t[bi][:, k*128:(k+1)*128],
                                   xnT[:, k*TLOC + tn0: k*TLOC + tn0 + 512],
                                   start=(k == 0), stop=(k == 7),
                                   deps=[ldw, t_xnT, prev_hps_rd] if k == 0 else ())
                    ge, prev_hps_rd = emit_gelu_tanh_psum(
                        b, hps[:], bs1t[:, m:m+1], 512,
                        gtmp, hT[:, m*512:(m+1)*512],
                        deps=[mt, t_bs1] + ([prev_ge] if prev_ge else []))
                    prev_ge = ge
                    t_ld_prev[bi] = (ge,)
            t_hT = pr.tok("v")
            with (b.nc.psum_tensor(b.prefix + f"psA0_{nch}", [128, 512], F32) as psA0,
                  b.nc.psum_tensor(b.prefix + f"psA1_{nch}", [128, 512], F32) as psA1,
                  b.nc.psum_tensor(b.prefix + f"psA2_{nch}", [128, 512], F32) as psA2,
                  b.nc.psum_tensor(b.prefix + f"psA3_{nch}", [128, 512], F32) as psA3):
                psA = [psA0, psA1, psA2, psA3]
                for dn in range(2):
                    mts = [None] * 4
                    prev_ld = {0: (pr.tok("v"), pr.tok("s")), 1: ()}
                    for k in range(32):
                        bi = k % 2
                        ldw = pr.dma(f"Lw2{bi}", w2t[bi][:],
                                     ws2b_d[k*128:(k+1)*128, dn*512:(dn+1)*512],
                                     deps=wp_done_all + list(prev_ld.get(bi) or ()))
                        for tt in range(4):
                            dps = [ldw] if tt == 0 else []
                            if k == 0:
                                dps += [t_hT]
                            mts[tt] = pr.op("p", "matmul", psA[tt][:],
                                            hT[:, k*512 + tt*128: k*512 + (tt+1)*128],
                                            w2t[bi][:],
                                            start=(k == 0), stop=(k == 31),
                                            deps=dps)
                        prev_ld[bi] = (mts[3],)
                    for tt in range(4):
                        t_o = pr.op("v", "tensor_add", outrow[:, dn*512:(dn+1)*512],
                                    psA[tt][:], bs2bc[:, dn*512:(dn+1)*512],
                                    deps=[mts[tt], t_bs2bc] +
                                         ([prev_ssh] if prev_ssh else []))
                        trow = tn0 + tt*128
                        prev_ssh = pr.dma("Ssh", o_shared[trow:trow+128, dn*512:(dn+1)*512],
                                          outrow[:, dn*512:(dn+1)*512], deps=[t_o])
        b.out_done(pr.tok("Ssh"))


def build_ffn0_pre():
    b = Builder("ffn0_pre")
    io = {}
    io["xr"] = b.din("xr", [TLOC, DM], F32)
    io["gate"] = b.din("gate", [DM, NEXP], F32)
    io["ws1"] = b.din("ws1", [DM, HD], F32)
    io["bs1_rs"] = b.din("bs1_rs", [128, HD // 128], F32)
    io["ws2"] = b.din("ws2", [HD, DM], F32)
    io["bs2"] = b.din("bs2", [1, DM], F32)
    io["ident_in"] = b.din("ident", [128, 128], BF16)
    io["o_xn"] = b.dout("o_xn", [TLOC, DM], BF16)
    io["o_probs"] = b.dout("o_probs", [TLOC, NEXP], F32)
    io["o_shared"] = b.dout("o_shared", [TLOC, DM], F32)
    emit_ffn0_pre(b, io)
    return b.finish()



def _f32(a):
    return np.ascontiguousarray(a, dtype=np.float32)


def run_ffn0_pre(x_full, I):
    import ml_dtypes
    nc = build_ffn0_pre()
    xf = x_full.reshape(NC, TLOC, DM)
    ident = np.eye(128, dtype=ml_dtypes.bfloat16)
    bs1_rs = _f32(I["L0_ffn_bs1"]).reshape(HD // 128, 128).T.copy()
    common = dict(gate=_f32(I["L0_ffn_gate"]), ws1=_f32(I["L0_ffn_Ws1"]),
                  bs1_rs=bs1_rs, ws2=_f32(I["L0_ffn_Ws2"]),
                  bs2=_f32(I["L0_ffn_bs2"]).reshape(1, DM), ident=ident)
    in_maps = [dict(xr=_f32(xf[i]), **common) for i in range(NC)]
    res = run_bass_kernel_spmd(nc, in_maps, core_ids=list(range(NC)))
    xn = np.concatenate([res.results[i]["o_xn"] for i in range(NC)], 0)
    probs = np.concatenate([res.results[i]["o_probs"] for i in range(NC)], 0)
    shared = np.concatenate([res.results[i]["o_shared"] for i in range(NC)], 0)
    return xn, probs, shared


def kernel(**inputs):
    raise NotImplementedError("under construction")



def emit_ffn0_expert(b, io):
    pr = b.pr
    pe_in = io["pe_in"]
    xn_all = io["xn_all"]
    w1 = io["w1"]
    b1_rs = io["b1_rs"]
    w2 = io["w2"]
    b2 = io["b2"]
    ident_in = io["ident_in"]

    o_eout = io["o_eout"]
    o_tau = io["o_tau"]
    o_idx = io["o_idx"]
    o_slots = io["o_slots"]

    w1b_d = b.dint("w1b", [DM, HD], BF16)
    w2b_d = b.dint("w2b", [HD, DM], BF16)
    idx_d = b.dint("idx_d", [CAP, 1], I32)
    scr_d = b.dint("scr_d", [1, 16], F32)

    NITER = 34
    BIG = float(1 << 20)
    R_NEXT = 7

    @body(b)
    def _main():
        pe_flat = pe_in.rearrange("r (t u) -> (r t) u", u=1)
        moe_ctx = ExitStack()
        wp = emit_weight_prep(b, [(w1, w1b_d, DM, HD), (w2, w2b_d, HD, DM)],
                              ctx=moe_ctx)

        ident = b.sb("identt", [128, 128], BF16)
        t_id = pr.dma("Lid", ident[:], ident_in)
        pe = b.sb("pe_t", [NEXP, TLOC], F32, ctx=moe_ctx, side=b.sub_side())
        t_pe = pr.dma("Lpe", pe[:], pe_in)

        ones88 = b.sb("ones88", [NEXP, NEXP], F32, ctx=moe_ctx, side=b.sub_side())
        onesrow = b.sb("onesrow", [NEXP, TLOC], F32, ctx=moe_ctx, side=b.sub_side())
        ident8 = b.sb("ident8", [NEXP, NEXP], F32, ctx=moe_ctx, side=b.sub_side())
        iota88 = b.sb("iota88", [NEXP, NEXP], F32, ctx=moe_ctx, side=b.sub_side())
        iota128 = b.sb("iota128", [128, 1], F32)
        pr.op("v", "memset", ones88[:], 1.0)
        pr.op("v", "memset", onesrow[:], 1.0)
        pr.op("g", "iota", iota88[:], pattern=[[-1, NEXP]], base=0,
              channel_multiplier=1, allow_small_or_imprecise_dtypes=True)
        pr.op("g", "iota", iota128[:], pattern=[[1, 1]], base=0,
              channel_multiplier=1, allow_small_or_imprecise_dtypes=True)
        t_i8 = pr.op("v", "tensor_scalar", ident8[:], iota88[:], 0.0, None,
                     ALU.is_equal, deps=[pr.tok("g")])

        lo8 = b.sb("lo8", [NEXP, 1], F32)
        hi8 = b.sb("hi8", [NEXP, 1], F32)
        mid8 = b.sb("mid8", [NEXP, 1], F32)
        sel8 = b.sb("sel8", [NEXP, 1], F32)
        nsel8 = b.sb("nsel8", [NEXP, 1], F32)
        tmp8 = b.sb("tmp8", [NEXP, 1], F32)
        cnt8 = b.sb("cnt8", [NEXP, 1], F32)
        part8 = b.sb("part8", [NEXP, 1], F32)
        cmp8 = b.sb("cmp8", [NEXP, TLOC], F32, ctx=moe_ctx, side=b.sub_side())
        pcnt8 = b.ps("pcnt8", [NEXP, 1], F32)
        pr.op("v", "memset", lo8[:], 0.0)
        pr.op("v", "memset", hi8[:], 1.0)
        pr.op("v", "memset", mid8[:], 0.5)

        def count_gt(thr8, extra=()):
            t_c = pr.op("v", "tensor_scalar", cmp8[:], pe[:], thr8[:], None,
                        ALU.is_gt, ALU.add, accum_out=part8[:],
                        deps=[t_pe] + list(extra))
            t_m = pr.op("p", "matmul", pcnt8[:], ones88[:], part8[:],
                        start=True, stop=True, deps=[t_c, pr.tok("v")])
            return pr.op("v", "tensor_copy", cnt8[:], pcnt8[:], deps=[t_m])

        for it in range(NITER):
            t_ct = count_gt(mid8)
            pr.op("v", "tensor_scalar", sel8[:], cnt8[:], float(CAP) - 0.5,
                  None, ALU.is_gt, deps=[t_ct])
            pr.op("v", "tensor_scalar", nsel8[:], sel8[:], -1.0, 1.0,
                  ALU.mult, ALU.add)
            pr.op("v", "tensor_mul", tmp8[:], sel8[:], mid8[:])
            pr.op("v", "tensor_mul", lo8[:], nsel8[:], lo8[:])
            pr.op("v", "tensor_add", lo8[:], lo8[:], tmp8[:])
            pr.op("v", "tensor_mul", tmp8[:], nsel8[:], mid8[:])
            pr.op("v", "tensor_mul", hi8[:], sel8[:], hi8[:])
            pr.op("v", "tensor_add", hi8[:], hi8[:], tmp8[:])
            pr.op("v", "tensor_add", mid8[:], lo8[:], hi8[:])
            pr.op("v", "tensor_scalar", mid8[:], mid8[:], 0.5, None, ALU.mult)

        t_clo = count_gt(lo8)
        c1 = b.sb("c1", [1, 1], F32)
        pr.op("v", "tensor_copy", c1[:], cnt8[0:1, :], deps=[t_clo])
        mv = b.sb("mv", [NEXP, TLOC], F32, ctx=moe_ctx, side=b.sub_side())
        rowmin = b.sb("rowmin", [NEXP, 1], F32, ctx=moe_ctx, side=b.sub_side())
        mrow = b.sb("mrow", [1, 16], F32)
        mj1 = b.sb("mj1", [1, 1], F32)
        mlist = b.sb("mlist", [1, R_NEXT], F32)
        thr8 = b.sb("thr8", [NEXP, 1], F32)
        pmid = b.ps("pmid", [NEXP, 1], F32)
        pr.op("v", "tensor_copy", thr8[:], lo8[:])
        for j in range(R_NEXT):
            t_c = pr.op("v", "tensor_scalar", cmp8[:], pe[:], thr8[:], None,
                        ALU.is_gt)
            pr.op("v", "tensor_mul", mv[:], cmp8[:], pe[:])
            pr.op("v", "tensor_scalar", cmp8[:], cmp8[:], -2.0, 2.0,
                  ALU.mult, ALU.add)
            t_mv = pr.op("v", "tensor_add", mv[:], mv[:], cmp8[:])
            t_rm = pr.op("v", "tensor_reduce", rowmin[:], mv[:], AX.X, ALU.min)
            t_st = pr.dma("EGs", scr_d[0:1, 0:NEXP].rearrange("u r -> r u"),
                          rowmin[:], deps=[t_rm])
            t_ld = pr.dma("EGl", mrow[0:1, 0:NEXP], scr_d[0:1, 0:NEXP], deps=[t_st])
            t_mn = pr.op("v", "tensor_reduce", mj1[:], mrow[0:1, 0:NEXP], AX.X,
                         ALU.min, deps=[t_ld])
            t_cp = pr.op("v", "tensor_copy", mlist[0:1, j:j+1], mj1[:])
            t_bb = pr.op("p", "matmul", pmid[:], ones88[0:1, :], mj1[:],
                         start=True, stop=True, deps=[t_cp, pr.tok("v")])
            pr.op("v", "tensor_copy", thr8[:], pmid[:], deps=[t_bb])
        vk = b.sb("vk", [1, 1], F32)
        ej = b.sb("ej", [1, 1], F32)
        tt_ = b.sb("tt_", [1, 1], F32)
        diff = b.sb("diff", [1, 1], F32)
        pr.op("v", "tensor_scalar", diff[:], c1[:], -float(CAP), None, ALU.add)
        pr.op("v", "memset", vk[:], 0.0)
        for j in range(R_NEXT):
            if j < R_NEXT - 1:
                pr.op("v", "tensor_scalar", ej[:], diff[:], float(j), None,
                      ALU.is_equal)
            else:
                pr.op("v", "tensor_scalar", ej[:], diff[:], float(j) - 0.5, None,
                      ALU.is_gt)
            pr.op("v", "tensor_mul", tt_[:], ej[:], mlist[0:1, j:j+1])
            pr.op("v", "tensor_add", vk[:], vk[:], tt_[:])
        vk8 = b.sb("vk8", [NEXP, 1], F32)
        t_vb = pr.op("p", "matmul", pmid[:], ones88[0:1, :], vk[:],
                     start=True, stop=True, deps=[pr.tok("v")])
        pr.op("v", "tensor_copy", vk8[:], pmid[:], deps=[t_vb])
        t_cgt = count_gt(vk8)
        cgt1 = b.sb("cgt1", [1, 1], F32)
        pr.op("v", "tensor_copy", cgt1[:], cnt8[0:1, :], deps=[t_cgt])
        tau_sb = b.sb("tau_sb", [1, 2], F32)
        pr.op("v", "tensor_copy", tau_sb[0:1, 0:1], vk[:])
        t_tau = pr.op("v", "tensor_copy", tau_sb[0:1, 1:2], cgt1[:])
        b.out_done(pr.dma("Stau", o_tau, tau_sb[:], deps=[t_tau]))

        gt = b.sb("gt", [NEXP, TLOC], F32, ctx=moe_ctx, side=b.sub_side())
        eq = b.sb("eq", [NEXP, TLOC], F32, ctx=moe_ctx, side=b.sub_side())
        pfx = b.sb("pfx", [NEXP, TLOC], F32, ctx=moe_ctx, side=b.sub_side())
        base8 = b.sb("base8", [NEXP, 1], F32)
        rowtot = b.sb("rowtot", [NEXP, 1], F32)
        brow = b.sb("brow", [1, 16], F32)
        brow2 = b.sb("brow2", [1, 16], F32)
        kq8 = b.sb("kq8", [NEXP, 1], F32)
        mask = b.sb("mask", [NEXP, TLOC], F32, ctx=moe_ctx, side=b.sub_side())
        slotp = b.sb("slotp", [NEXP, TLOC], F32, ctx=moe_ctx, side=b.sub_side())

        eg_prev = [None]
        def row_excl_bases(src_col, dst_col, extra=()):
            t1 = pr.dma("EGs", scr_d[0:1, 0:NEXP].rearrange("u r -> r u"),
                        src_col,
                        deps=list(extra) + ([eg_prev[0]] if eg_prev[0] else []))
            t2 = pr.dma("EGl", brow[0:1, 0:NEXP], scr_d[0:1, 0:NEXP], deps=[t1])
            t3 = pr.op("v", "tensor_tensor_scan", brow2[0:1, 0:NEXP],
                       onesrow[0:1, 0:NEXP], brow[0:1, 0:NEXP], 0.0,
                       ALU.mult, ALU.add, deps=[t2])
            t4 = pr.op("v", "tensor_sub", brow2[0:1, 0:NEXP], brow2[0:1, 0:NEXP],
                       brow[0:1, 0:NEXP])
            t5 = pr.dma("EGs", scr_d[0:1, 0:NEXP], brow2[0:1, 0:NEXP], deps=[t4])
            t6 = pr.dma("EGl", dst_col,
                        scr_d[0:1, 0:NEXP].rearrange("u r -> r u"), deps=[t5])
            eg_prev[0] = t6
            return t6

        t_gt = pr.op("v", "tensor_scalar", gt[:], pe[:], vk8[:], None, ALU.is_gt)
        t_eq = pr.op("v", "tensor_scalar", eq[:], pe[:], vk8[:], None, ALU.is_equal)
        pr.op("v", "tensor_tensor_scan", pfx[:], onesrow[:], eq[:], 0.0,
              ALU.mult, ALU.add)
        pr.op("v", "tensor_copy", rowtot[:], pfx[:, TLOC-1:TLOC])
        t_b = row_excl_bases(rowtot[:], base8[:], extra=[pr.tok("v")])
        pr.op("v", "tensor_scalar", pfx[:], pfx[:], base8[:], None, ALU.add,
              deps=[t_b])
        t_kq = pr.op("p", "matmul", pmid[:], ones88[0:1, :], cgt1[:],
                     start=True, stop=True, deps=[pr.tok("v")])
        pr.op("v", "tensor_copy", kq8[:], pmid[:], deps=[t_kq])
        pr.op("v", "tensor_scalar", kq8[:], kq8[:], -1.0, float(CAP),
              ALU.mult, ALU.add)
        seleq = b.sb("seleq", [NEXP, TLOC], F32, ctx=moe_ctx, side=b.sub_side())
        pr.op("v", "tensor_scalar", seleq[:], pfx[:], kq8[:], None, ALU.is_le)
        pr.op("v", "tensor_mul", seleq[:], seleq[:], eq[:])
        pr.op("v", "tensor_add", mask[:], gt[:], seleq[:])
        pr.op("v", "tensor_tensor_scan", slotp[:], onesrow[:], mask[:], 0.0,
              ALU.mult, ALU.add)
        pr.op("v", "tensor_copy", rowtot[:], slotp[:, TLOC-1:TLOC])
        t_b2 = row_excl_bases(rowtot[:], base8[:], extra=[pr.tok("v")])
        pr.op("v", "tensor_scalar", slotp[:], slotp[:], base8[:], None, ALU.add,
              deps=[t_b2])
        slotm = b.sb("slotm", [NEXP, TLOC], F32, ctx=moe_ctx, side=b.sub_side())
        pr.op("v", "tensor_scalar", slotm[:], slotp[:], -BIG, None, ALU.add)
        pr.op("v", "tensor_mul", slotm[:], slotm[:], mask[:])
        t_sm = pr.op("v", "tensor_scalar", slotm[:], slotm[:], BIG, None, ALU.add)

        b.out_done(pr.dma("Ssl", o_slots, slotm[:], deps=[t_sm]))

        ptr8 = b.ps("ptr8", [128, NEXP], F32)
        ptcol = b.sb("ptcol", [128, NEXP], F32, ctx=moe_ctx, side=b.sub_side())
        slotcol = b.sb("slotcol", [128, 1], F32)
        slot_i = b.sb("slot_i", [128, 1], I32)
        gidf = b.sb("gidf", [128, 1], F32)
        gid_i = b.sb("gid_i", [128, 1], I32)
        prev_sc = None
        for s in range(16):
            t_tr = pr.op("p", "transpose", ptr8[:], slotm[:, s*128:(s+1)*128],
                         ident8[:], deps=[t_sm, t_i8, pr.tok("v")])
            t_pc = pr.op("v", "tensor_copy", ptcol[:], ptr8[:], deps=[t_tr])
            for r in range(NEXP):
                pr.op("v", "tensor_scalar", slotcol[:], ptcol[:, r:r+1], -1.0,
                      None, ALU.add,
                      deps=[prev_sc] if prev_sc else [])
                t_si = pr.op("v", "tensor_copy", slot_i[:], slotcol[:])
                pr.op("v", "tensor_scalar", gidf[:], iota128[:],
                      float(r * TLOC + s * 128), None, ALU.add)
                t_gi = pr.op("v", "tensor_copy", gid_i[:], gidf[:])
                prev_sc = pr.idma("SCAT", idx_d,
                                  IndirectOffsetOnAxis(ap=slot_i[:], axis=0),
                                  gid_i[:], None, b.bound_reg(CAP - 1),
                                  deps=[t_si, t_gi])
        t_scat_l = pr.rot_toks("SCAT")
        moe_ctx.close()
        drain = [pr.tok("Ssl")]
        drain_t = b.sb("drain_t", [1, 1], F32)
        t_dv = pr.op("v", "memset", drain_t[:], 0.0, deps=drain)
        pr.op("s", "activation", drain_t[:], drain_t[:], AF.Identity,
              deps=[t_dv])

        idx_sb = b.sb("idx_sb", [128, 16], I32)
        t_idx = pr.dma("Lidx", idx_sb[:],
                       idx_d.rearrange("(s p) one -> p (s one)", p=128),
                       deps=t_scat_l + drain, queue="g",
                       allow_slow_non_contiguous=True)
        b.out_done(pr.dma("Sidx", o_idx, idx_sb[:], deps=[t_idx] + drain))
        tok_sb = b.sb("tok_sb", [128, 16 * DM], BF16)
        scores = b.sb("scores", [128, 16], F32)
        gt_toks = []
        for s in range(16):
            gt_toks.append(pr.idma("GATH", tok_sb[:, s*DM:(s+1)*DM], None,
                                   xn_all, IndirectOffsetOnAxis(ap=idx_sb[:, s:s+1], axis=0),
                                   b.bound_reg(NC * TLOC - 1), deps=[t_idx]))
            gt_toks.append(pr.idma("GATH", scores[:, s:s+1], None,
                                   pe_flat,
                                   IndirectOffsetOnAxis(ap=idx_sb[:, s:s+1], axis=0),
                                   b.bound_reg(NC * TLOC - 1), deps=[t_idx]))
        t_gath_l = pr.rot_toks("GATH")

        tokT = b.sb("tokT", [128, 8 * CAP], BF16)
        pbf2 = b.ps("pbf2", [128, 128], BF16)
        prev_tr = []
        for s in range(16):
            prev_tr = emit_transposes(b, tok_sb[:, s*DM:(s+1)*DM], tokT, s*128,
                                      ident, pbf2, 8, CAP,
                                      deps=t_gath_l + [t_id] + prev_tr[-1:])
        t_tokT = pr.tok("v")

        b1t = b.sb("b1t", [128, HD // 128], F32)
        t_b1 = pr.dma("Lb1", b1t[:], b1_rs)
        b2row = b.sb("b2row", [1, DM], F32)
        t_b2r = pr.dma("Lb2", b2row[:], b2)
        b2bc = b.sb("b2bc", [128, DM], F32)
        t_b2bc = emit_row_broadcast(b, b2row, b2bc, DM, "b2be", deps=[t_b2r])

        w1t = [b.sb(f"w1t{i}", [128, 8 * 128], BF16) for i in range(2)]
        w2t = [b.sb(f"w2t{i}", [128, 512], BF16) for i in range(2)]
        hT = b.sb("hT", [128, 32 * 512], BF16)
        gtmp = {"ub": b.sb("g_ub", [128, 512], F32),
                "sq": b.sb("g_sq", [128, 512], F32),
                "p1": b.sb("g_p1", [128, 512], F32),
                "warg": b.sb("g_wa", [128, 512], F32)}
        outrow = b.sb("outrow", [128, DM], F32)
        outsc = b.sb("outsc", [128, DM], F32)
        prev_ssh = None

        for nch in range(4):
            tn0 = nch * 512
            with b.nc.psum_tensor(b.prefix + f"ehps{nch}", [128, 512], F32) as hps:
                t_ld_prev = {0: (pr.tok("v"), pr.tok("s")), 1: ()}
                prev_hps_rd = pr.tok("s")
                prev_ge = None
                for m in range(32):
                    bi = m % 2
                    ldw = pr.dma(f"Lw1{bi}",
                                 w1t[bi][:].rearrange("p (k c) -> p k c", k=8),
                                 w1b_d[:, m*128:(m+1)*128].rearrange(
                                     "(k p) c -> p k c", p=128),
                                 deps=wp + list(t_ld_prev.get(bi) or ()))
                    mt = None
                    for k in range(8):
                        mt = pr.op("p", "matmul", hps[:],
                                   w1t[bi][:, k*128:(k+1)*128],
                                   tokT[:, k*CAP + tn0: k*CAP + tn0 + 512],
                                   start=(k == 0), stop=(k == 7),
                                   deps=[ldw, t_tokT, prev_hps_rd] if k == 0 else ())
                    ge, prev_hps_rd = emit_gelu_tanh_psum(
                        b, hps[:], b1t[:, m:m+1], 512,
                        gtmp, hT[:, m*512:(m+1)*512],
                        deps=[mt, t_b1] + ([prev_ge] if prev_ge else []))
                    prev_ge = ge
                    t_ld_prev[bi] = (ge,)
            t_hT = pr.tok("v")
            with (b.nc.psum_tensor(b.prefix + f"epsA0_{nch}", [128, 512], F32) as psA0,
                  b.nc.psum_tensor(b.prefix + f"epsA1_{nch}", [128, 512], F32) as psA1,
                  b.nc.psum_tensor(b.prefix + f"epsA2_{nch}", [128, 512], F32) as psA2,
                  b.nc.psum_tensor(b.prefix + f"epsA3_{nch}", [128, 512], F32) as psA3):
                psA = [psA0, psA1, psA2, psA3]
                for dn in range(2):
                    mts = [None] * 4
                    prev_ld = {0: (pr.tok("v"), pr.tok("s")), 1: ()}
                    for k in range(32):
                        bi = k % 2
                        ldw = pr.dma(f"Lw2{bi}", w2t[bi][:],
                                     w2b_d[k*128:(k+1)*128, dn*512:(dn+1)*512],
                                     deps=wp + list(prev_ld.get(bi) or ()))
                        for tt in range(4):
                            dps = [ldw] if tt == 0 else []
                            if k == 0:
                                dps += [t_hT]
                            mts[tt] = pr.op("p", "matmul", psA[tt][:],
                                            hT[:, k*512 + tt*128: k*512 + (tt+1)*128],
                                            w2t[bi][:],
                                            start=(k == 0), stop=(k == 31),
                                            deps=dps)
                        prev_ld[bi] = (mts[3],)
                    for tt in range(4):
                        stile = nch * 4 + tt
                        t_o = pr.op("v", "tensor_add", outrow[:, dn*512:(dn+1)*512],
                                    psA[tt][:], b2bc[:, dn*512:(dn+1)*512],
                                    deps=[mts[tt], t_b2bc])
                        t_s = pr.op("v", "tensor_scalar", outsc[:, dn*512:(dn+1)*512],
                                    outrow[:, dn*512:(dn+1)*512],
                                    scores[:, stile:stile+1], None, ALU.mult,
                                    deps=t_gath_l + ([prev_ssh] if prev_ssh else []))
                        srow = stile * 128
                        prev_ssh = pr.dma("Seo", o_eout[srow:srow+128, dn*512:(dn+1)*512],
                                          outsc[:, dn*512:(dn+1)*512], deps=[t_s])
        b.out_done(pr.tok("Seo"))


def build_ffn0_expert():
    b = Builder("ffn0_expert")
    io = {}
    io["pe_in"] = b.din("pe", [NEXP, TLOC], F32)
    io["xn_all"] = b.din("xn_all", [NC * TLOC, DM], BF16)
    io["w1"] = b.din("w1", [DM, HD], F32)
    io["b1_rs"] = b.din("b1_rs", [128, HD // 128], F32)
    io["w2"] = b.din("w2", [HD, DM], F32)
    io["b2"] = b.din("b2", [1, DM], F32)
    io["ident_in"] = b.din("ident", [128, 128], BF16)
    io["o_eout"] = b.dout("o_eout", [CAP, DM], F32)
    io["o_tau"] = b.dout("o_tau", [1, 2], F32)
    io["o_idx"] = b.dout("o_idx", [128, 16], I32)
    io["o_slots"] = b.dout("o_slots", [NEXP, TLOC], F32)
    emit_ffn0_expert(b, io)
    return b.finish()


def emit_ffn0_close(b, io):
    pr = b.pr
    xr = io["xr"]
    shared = io["shared"]
    slots_win = io["slots_win"]
    eout_all = io.get("eout_all")
    eouts = None if eout_all is not None else [io[f"eout{e}"] for e in range(NEXP)]

    o_x = io["o_x"]

    @body(b)
    def _main():
        sw = b.sb("sw", [NEXP, TLOC], F32)
        t_sw = pr.dma("Lsw", sw[:], slots_win)
        ident8 = b.sb("ident8", [NEXP, NEXP], F32)
        iota88 = b.sb("iota88", [NEXP, NEXP], F32)
        pr.op("g", "iota", iota88[:], pattern=[[-1, NEXP]], base=0,
              channel_multiplier=1, allow_small_or_imprecise_dtypes=True)
        t_i8 = pr.op("v", "tensor_scalar", ident8[:], iota88[:], 0.0, None,
                     ALU.is_equal, deps=[pr.tok("g")])
        ptr8 = b.ps("ptr8", [128, NEXP], F32)
        ptcol = b.sb("ptcol", [128, NEXP], F32)
        slotcol = b.sb("slotcol", [128, 1], F32)
        slot_i = b.sb("slot_i", [128, 1], I32)
        xt = b.sb("xt", [128, DM], F32)
        acc = b.sb("acc", [128, DM], F32)
        outt = b.sb("outt", [128, DM], F32)

        prev_users = []
        for i in range(TLOC // 128):
            t_tr = pr.op("p", "transpose", ptr8[:], sw[:, i*128:(i+1)*128],
                         ident8[:], deps=[t_sw, t_i8] + prev_users)
            t_pc = pr.op("v", "tensor_copy", ptcol[:], ptr8[:], deps=[t_tr])
            t_ld = pr.dma(f"Lx{i%2}", xt[:], xr[i*128:(i+1)*128, :],
                          deps=prev_users)
            t_sh = pr.dma(f"Lsh{i%2}", acc[:], shared[i*128:(i+1)*128, :],
                          deps=prev_users)
            gath_toks = []
            for e in range(NEXP):
                if eout_all is not None:
                    t_sc = pr.op("v", "tensor_scalar", slotcol[:], ptcol[:, e:e+1],
                                 float(e * CAP - 1), None, ALU.add,
                                 deps=gath_toks[-1:])
                    t_si = pr.op("v", "tensor_copy", slot_i[:], slotcol[:])
                    gath_toks.append(pr.idma("GA", acc[:], None, eout_all,
                                             IndirectOffsetOnAxis(ap=slot_i[:], axis=0),
                                             b.bound_reg((e + 1) * CAP - 1),
                                             deps=[t_si, t_sh],
                                             compute_op=ALU.add))
                    continue
                t_sc = pr.op("v", "tensor_scalar", slotcol[:], ptcol[:, e:e+1],
                             -1.0, None, ALU.add,
                             deps=gath_toks[-1:])
                t_si = pr.op("v", "tensor_copy", slot_i[:], slotcol[:])
                gath_toks.append(pr.idma("GA", acc[:], None, eouts[e],
                                         IndirectOffsetOnAxis(ap=slot_i[:], axis=0),
                                         b.bound_reg(CAP - 1),
                                         deps=[t_si, t_sh],
                                         compute_op=ALU.add))
            t_o = pr.op("v", "scalar_tensor_tensor", outt[:], acc[:], 0.5,
                        xt[:], ALU.mult, ALU.add,
                        deps=[t_ld] + gath_toks)
            t_st = pr.dma("Sx", o_x[i*128:(i+1)*128, :], outt[:], deps=[t_o])
            prev_users = [t_st, t_o]
        b.out_done(pr.tok("Sx"))


def build_ffn0_close():
    b = Builder("ffn0_close")
    io = {}
    io["xr"] = b.din("xr", [TLOC, DM], F32)
    io["shared"] = b.din("shared", [TLOC, DM], F32)
    io["slots_win"] = b.din("slots_win", [NEXP, TLOC], F32)
    io["o_x"] = b.dout("o_x", [TLOC, DM], F32)
    for e in range(NEXP):
        io[f"eout{e}"] = b.din(f"eout{e}", [CAP, DM], F32)
    emit_ffn0_close(b, io)
    return b.finish()

def run_moe_layer(x_full, I, pref):
    import ml_dtypes
    xf = x_full.reshape(NC, TLOC, DM)
    ident = np.eye(128, dtype=ml_dtypes.bfloat16)

    ncA = build_ffn0_pre()
    bs1_rs = _f32(I[pref + "bs1"]).reshape(HD // 128, 128).T.copy()
    common = dict(gate=_f32(I[pref + "gate"]), ws1=_f32(I[pref + "Ws1"]),
                  bs1_rs=bs1_rs, ws2=_f32(I[pref + "Ws2"]),
                  bs2=_f32(I[pref + "bs2"]).reshape(1, DM), ident=ident)
    in_maps = [dict(xr=_f32(xf[i]), **common) for i in range(NC)]
    resA = run_bass_kernel_spmd(ncA, in_maps, core_ids=list(range(NC)))
    xn = np.concatenate([resA.results[i]["o_xn"] for i in range(NC)], 0)
    probs = np.concatenate([resA.results[i]["o_probs"] for i in range(NC)], 0)
    sharedf = np.concatenate([resA.results[i]["o_shared"] for i in range(NC)], 0)

    ncB = build_ffn0_expert()
    in_maps = []
    for e in range(NC):
        b1_rs = np.ascontiguousarray(_f32(I[pref + "b1"])[e].reshape(HD // 128, 128).T)
        in_maps.append(dict(
            pe=np.ascontiguousarray(probs[:, e].reshape(NC, TLOC)),
            xn_all=np.ascontiguousarray(xn),
            w1=np.ascontiguousarray(_f32(I[pref + "W1"])[e]), b1_rs=b1_rs,
            w2=np.ascontiguousarray(_f32(I[pref + "W2"])[e]),
            b2=_f32(I[pref + "b2"])[e].reshape(1, DM), ident=ident))
    resB = run_bass_kernel_spmd(ncB, in_maps, core_ids=list(range(NC)))
    eouts = [np.ascontiguousarray(resB.results[e]["o_eout"]) for e in range(NC)]
    slots = np.stack([resB.results[e]["o_slots"].reshape(-1) for e in range(NC)])

    ncC = build_ffn0_close()
    in_maps = []
    for i in range(NC):
        im = dict(xr=_f32(xf[i]),
                  shared=np.ascontiguousarray(sharedf[i*TLOC:(i+1)*TLOC]),
                  slots_win=np.ascontiguousarray(slots[:, i*TLOC:(i+1)*TLOC]))
        for e in range(NC):
            im[f"eout{e}"] = eouts[e]
        in_maps.append(im)
    resC = run_bass_kernel_spmd(ncC, in_maps, core_ids=list(range(NC)))
    x1 = np.concatenate([resC.results[i]["o_x"] for i in range(NC)], 0)
    return x1.reshape(x_full.shape)


def emit_mamba_pre(b, io):
    pr = b.pr
    xr = io["xr"]
    xprev = io["xprev"]
    w_inT = io["w_inT"]
    normw_rs = io["normw_rs"]
    normb_rs = io["normb_rs"]
    convw = io["convw"]
    convb_rs = io["convb_rs"]
    w_dtT = io["w_dtT"]
    bdt_rs = io["bdt_rs"]
    w_BT = io["w_BT"]
    w_CT = io["w_CT"]
    ident_in = io["ident_in"]

    o_dt = io["o_dt"]
    o_u = io["o_u"]
    o_xc = io["o_xc"]
    o_z = io["o_z"]
    o_B = io["o_B"]
    o_C = io["o_C"]

    w_inb_d = b.dint("w_inb", [DM, 2 * DI], BF16)
    w_dtb_d = b.dint("w_dtb", [DI, DI], BF16)
    NT = TLOC // 128

    @body(b)
    def _main():
        xnT_ctx = ExitStack()
        W = 128 * (NT + 1)
        xnT = b.sb("xnT", [128, 8 * W], BF16, ctx=xnT_ctx, side=b.sub_side())
        prep_ctx = ExitStack()
        normw = b.sb("normw", [128, DM // 128], F32)
        t_nw = pr.dma("Lnw", normw[:], normw_rs)
        wpf = b.sb("wpf", [128, 4096], F32, ctx=prep_ctx, side=b.sub_side())
        wpb = b.sb("wpb", [128, 4096], BF16, ctx=prep_ctx, side=b.sub_side())
        wp_toks = []
        prev = None
        for r in range(DM // 128):
            ld = pr.dma("WP0", wpf[:], w_inT[r*128:(r+1)*128, 0:4096],
                        deps=[prev])
            t_s = pr.op("v", "tensor_scalar", wpb[:], wpf[:],
                        normw[:, r:r+1], None, ALU.mult, deps=[ld, t_nw])
            prev = pr.dma("WP1", w_inb_d[r*128:(r+1)*128, 0:4096], wpb[:],
                          deps=[t_s])
            wp_toks.append(prev)
        wp1 = [pr.tok("WP0"), pr.tok("WP1")]
        wp2 = emit_weight_prep(b, [(w_dtT, w_dtb_d, DI, DI)], ctx=prep_ctx)
        wp_all = wp1 + wp2
        prep_ctx.close()

        ident = b.sb("identt", [128, 128], BF16)
        t_id = pr.dma("Lid", ident[:], ident_in)

        normb = b.sb("normb", [128, DM // 128], F32)
        t_nb = pr.dma("Lnb", normb[:], normb_rs)
        normb_bf = b.sb("normb_bf", [128, DM // 128], BF16)
        nwinv = b.sb("nwinv", [128, DM // 128], F32)
        pr.op("v", "reciprocal", nwinv[:], normw[:], deps=[t_nw])
        pr.op("v", "tensor_mul", nwinv[:], nwinv[:], normb[:], deps=[t_nb])
        t_nbb = pr.op("v", "tensor_copy", normb_bf[:], nwinv[:])
        bias_e = b.sb("bias_e", [128, 2 * DI // 128], F32)
        wtile = [b.sb(f"wtile{i}", [128, 128], BF16) for i in range(2)]
        pbias = b.ps("pbias", [128, 1], F32)
        prev_ev = None
        mts = []
        for ec in range(2 * DI // 128):
            for k in range(DM // 128):
                bi = k % 2
                ldw = pr.dma(f"LB{bi}", wtile[bi][:],
                             w_inb_d[k*128:(k+1)*128, ec*128:(ec+1)*128],
                             deps=wp_all + mts[-2:-1] +
                                  ([prev_ev] if prev_ev else []))
                mts.append(pr.op("p", "matmul", pbias[:], wtile[bi][:],
                                 normb_bf[:, k:k+1],
                                 start=(k == 0), stop=(k == DM // 128 - 1),
                                 deps=[ldw, t_nbb] + ([prev_ev] if prev_ev and k == 0 else [])))
            prev_ev = pr.op("v", "tensor_copy", bias_e[:, ec:ec+1], pbias[:],
                            deps=[mts[-1]])
        t_bias = prev_ev

        st = make_ln_stats(b, "ln")
        xt = [b.sb(f"xt{i}", [128, DM], F32) for i in range(2)]
        xnb = [b.sb(f"xnb{i}", [128, DM], BF16) for i in range(2)]
        pbf = b.ps("pbf", [128, 128], BF16)
        prev_ln = {0: None, 1: None}
        prev_users = {0: (), 1: ()}
        prev_trs = []
        for i in range(NT + 1):
            bi = i % 2
            src = xprev if i == 0 else xr[(i-1)*128:i*128, :]
            ld = pr.dma(f"Lx{bi}", xt[bi][:], src, deps=[prev_ln[bi]])
            t_ln = emit_ln_tile(b, st, xt[bi][:], xnb[bi][:], deps=[ld],
                                out_deps=prev_users[bi])
            prev_ln[bi] = t_ln
            trs = emit_transposes(b, xnb[bi], xnT, i*128, ident, pbf, 8, W,
                                  deps=[t_ln, t_id] + prev_trs[-1:])
            prev_trs = trs
            prev_users[bi] = (trs[-1], pr.tok("p"))
        t_xnT = pr.tok("v")

        convw_t = b.sb("convw_t", [128, DI // 128 * 4], F32)
        t_cw = pr.dma("Lcw", convw_t[:], convw)
        wm = b.sb("wm", [128, 1], F32)
        t_wm = pr.dma("Lwm", wm[:], io["wmask"])
        convb_t = b.sb("convb_t", [128, DI // 128], F32)
        t_cb = pr.dma("Lcb", convb_t[:], convb_rs)
        w1t = [b.sb(f"w1t{i}", [128, 8 * 128], BF16) for i in range(2)]
        xpc = b.sb("xpc", [128, W], F32)
        cacc = b.sb("cacc", [128, TLOC], F32)
        xcT_bf = b.sb("xcT_bf", [128, 16 * TLOC], BF16)
        zbf = b.sb("zbf", [128, TLOC], BF16)

        t_ld_prev = {0: (), 1: ()}
        prev_xpc = None
        prev_z = None
        prev_t_z = None
        for m in range(2 * DI // 128):
            bi = m % 2
            ldw = pr.dma(f"Lw1{bi}",
                         w1t[bi][:].rearrange("p (k c) -> p k c", k=8),
                         w_inb_d[:, m*128:(m+1)*128].rearrange(
                             "(k p) c -> p k c", p=128),
                         deps=wp_all + list(t_ld_prev.get(bi) or ()))
            is_x = m < 16
            with b.nc.psum_tensor(b.prefix + f"dps{m}", [128, 512], F32) as dps:
                for nch in range(W // 512 if not is_x else 0):
                    pass
                n_chunks = W // 512 if is_x else TLOC // 512
                for nch in range(n_chunks + (1 if is_x else 0)):
                    if is_x:
                        c0 = nch * 512
                        cw = min(512, W - c0)
                    else:
                        c0 = 128 + nch * 512
                        cw = 512
                    if cw <= 0:
                        continue
                    mt = None
                    for k in range(8):
                        mt = pr.op("p", "matmul", dps[:, 0:cw],
                                   w1t[bi][:, k*128:(k+1)*128],
                                   xnT[:, k*W + c0: k*W + c0 + cw],
                                   start=(k == 0), stop=(k == 7),
                                   deps=([ldw, t_xnT, t_bias] +
                                         ([prev_xpc] if prev_xpc else []) +
                                         ([prev_z] if prev_z else []))
                                        if k == 0 else ())
                    if is_x:
                        prev_xpc = pr.op("s", "activation", xpc[:, c0:c0+cw],
                                         dps[:, 0:cw], AF.Identity,
                                         bias=bias_e[:, m:m+1], deps=[mt])
                    else:
                        prev_z = pr.op("s", "activation", zbf[:, c0-128:c0-128+cw],
                                       dps[:, 0:cw], AF.Identity,
                                       bias=bias_e[:, m:m+1],
                                       deps=[mt] + ([prev_t_z] if prev_t_z else []))
            if is_x:
                t_msk = pr.op("v", "tensor_scalar", xpc[:, 125:128],
                              xpc[:, 125:128], wm[:], None, ALU.mult,
                              deps=[prev_xpc, t_wm])
                taps = []
                for k in range(4):
                    o0 = 128 - 3 + k
                    wk = convw_t[:, m*4 + k: m*4 + k + 1]
                    if k == 0:
                        taps.append(pr.op("v", "tensor_scalar", cacc[:],
                                          xpc[:, o0:o0+TLOC], wk, None,
                                          ALU.mult, deps=[t_msk, t_cw]))
                    else:
                        taps.append(pr.op("v", "scalar_tensor_tensor", cacc[:],
                                          xpc[:, o0:o0+TLOC], wk, cacc[:],
                                          ALU.mult, ALU.add))
                t_sl = pr.op("s", "activation", xcT_bf[:, m*TLOC:(m+1)*TLOC],
                             cacc[:], AF.Silu, bias=convb_t[:, m:m+1],
                             deps=[taps[-1], t_cb])
                t_xc = pr.dma("Sxc", o_xc[m*128:(m+1)*128, :],
                              xcT_bf[:, m*TLOC:(m+1)*TLOC], deps=[t_sl])
                t_ld_prev[bi] = (t_sl,)
            else:
                t_z = pr.dma("Sz", o_z[(m-16)*128:(m-15)*128, :], zbf[:],
                             deps=[prev_z])
                prev_t_z = t_z
                t_ld_prev[bi] = (prev_z, t_z)
        b.out_done(pr.tok("Sxc"))
        b.out_done(pr.tok("Sz"))
        t_xc_all = pr.tok("s")
        xnT_ctx.close()

        bdt = b.sb("bdt", [128, DI // 128], F32)
        t_bdt = pr.dma("Lbdt", bdt[:], bdt_rs)
        onec = b.sb("onec", [128, 1], F32)
        pr.op("v", "memset", onec[:], 1.0)
        w2t = [b.sb(f"w2t{i}", [128, 16 * 128], BF16) for i in range(2)]
        ub = b.sb("ub", [128, 512], F32)
        ab = b.sb("ab", [128, 512], F32)
        dtv = b.sb("dtv", [128, 512], F32)
        dtb16 = b.sb("dtb16", [128, TLOC], BF16)
        ub16 = b.sb("ub16", [128, TLOC], BF16)
        xcf = b.sb("xcf", [128, 512], F32)
        t_ld_prev = {0: (), 1: ()}
        for m in range(DI // 128):
            bi = m % 2
            ldw = pr.dma(f"Lw2{bi}",
                         w2t[bi][:].rearrange("p (k c) -> p k c", k=16),
                         w_dtb_d[:, m*128:(m+1)*128].rearrange(
                             "(k p) c -> p k c", p=128),
                         deps=wp2 + list(t_ld_prev.get(bi) or ()))
            with b.nc.psum_tensor(b.prefix + f"tps{m}", [128, 512], F32) as tps:
                last_u = None
                for nch in range(TLOC // 512):
                    c0 = nch * 512
                    mt = None
                    for k in range(16):
                        mt = pr.op("p", "matmul", tps[:],
                                   w2t[bi][:, k*128:(k+1)*128],
                                   xcT_bf[:, k*TLOC + c0: k*TLOC + c0 + 512],
                                   start=(k == 0), stop=(k == 15),
                                   deps=[ldw, t_xc_all] +
                                        ([last_u] if last_u else [])
                                        if k == 0 else ())
                    t1 = pr.op("s", "activation", ub[:], tps[:], AF.Identity,
                               bias=bdt[:, m:m+1], deps=[mt, t_bdt])
                    t2 = pr.op("s", "activation", ab[:], ub[:], AF.Abs)
                    t3 = pr.op("s", "activation", ab[:], ab[:], AF.Exp, scale=-1.0)
                    t4 = pr.op("s", "activation", ab[:], ab[:], AF.Ln,
                               bias=onec[:], deps=[pr.tok("v")])
                    t5 = pr.op("v", "scalar_tensor_tensor", dtv[:], ub[:], 0.0,
                               ab[:], ALU.max, ALU.add, deps=[t4])
                    t6 = pr.op("v", "tensor_copy", dtb16[:, c0:c0+512], dtv[:],
                               deps=list(t_ld_prev.get("st") or ()))
                    t7 = pr.op("v", "tensor_copy", xcf[:],
                               xcT_bf[:, m*TLOC + c0: m*TLOC + c0 + 512])
                    t8 = pr.op("v", "tensor_mul", ub16[:, c0:c0+512], dtv[:],
                               xcf[:])
                    last_u = t8
                t_sd = pr.dma("Sdt", o_dt[m*128:(m+1)*128, :], dtb16[:],
                              deps=[pr.tok("v")])
                t_su = pr.dma("Su", o_u[m*128:(m+1)*128, :], ub16[:],
                              deps=[pr.tok("v")])
            pe_fence(b)
            t_ld_prev[bi] = (t_sd, t_su)
            t_ld_prev["st"] = (t_sd, t_su)
        b.out_done(pr.tok("Sdt"))
        b.out_done(pr.tok("Su"))

        wBt = b.sb("wBt", [128, 16 * NSTATE], BF16)
        wCt = b.sb("wCt", [128, 16 * NSTATE], BF16)
        wBf = b.sb("wBf", [128, 16 * NSTATE], F32)
        t_wb0 = pr.dma("LwB", wBf[:].rearrange("p (k c) -> p k c", k=16),
                       w_BT.rearrange("(k p) c -> p k c", p=128))
        t_wb = pr.op("v", "tensor_copy", wBt[:], wBf[:], deps=[t_wb0])
        t_wc0 = pr.dma("LwC", wBf[:].rearrange("p (k c) -> p k c", k=16),
                       w_CT.rearrange("(k p) c -> p k c", p=128), deps=[t_wb])
        t_wc = pr.op("v", "tensor_copy", wCt[:], wBf[:], deps=[t_wc0])
        bct = b.sb("bct", [NSTATE, TLOC], F32)
        with b.nc.psum_tensor(b.prefix + "bps", [NSTATE, 512], F32) as bps:
            prev_e = None
            prev_store = None
            for (wt, wtok, odst, sname) in ((wBt, t_wb, o_B, "SB"), (wCt, t_wc, o_C, "SC")):
                for nch in range(TLOC // 512):
                    c0 = nch * 512
                    mt = None
                    for k in range(16):
                        mt = pr.op("p", "matmul", bps[:],
                                   wt[:, k*NSTATE:(k+1)*NSTATE],
                                   xcT_bf[:, k*TLOC + c0: k*TLOC + c0 + 512],
                                   start=(k == 0), stop=(k == 15),
                                   deps=[wtok, t_xc_all] +
                                        ([prev_e] if prev_e else [])
                                        if k == 0 else ())
                    prev_e = pr.op("v", "tensor_copy", bct[:, c0:c0+512], bps[:],
                                   deps=[mt] + ([prev_store] if prev_store and nch == 0 else []))
                prev_store = pr.dma(sname, odst, bct[:], deps=[prev_e])
        b.out_done(pr.tok("SB"))
        b.out_done(pr.tok("SC"))


def build_mamba_pre():
    b = Builder("mamba_pre")
    io = {}
    io["xr"] = b.din("xr", [TLOC, DM], F32)
    io["xprev"] = b.din("xprev", [128, DM], F32)
    io["wmask"] = b.din("wmask", [128, 1], F32)
    io["w_inT"] = b.din("w_inT", [DM, 2 * DI], F32)
    io["normw_rs"] = b.din("normw_rs", [128, DM // 128], F32)
    io["normb_rs"] = b.din("normb_rs", [128, DM // 128], F32)
    io["convw"] = b.din("convw", [128, DI // 128 * 4], F32)
    io["convb_rs"] = b.din("convb_rs", [128, DI // 128], F32)
    io["w_dtT"] = b.din("w_dtT", [DI, DI], F32)
    io["bdt_rs"] = b.din("bdt_rs", [128, DI // 128], F32)
    io["w_BT"] = b.din("w_BT", [DI, NSTATE], F32)
    io["w_CT"] = b.din("w_CT", [DI, NSTATE], F32)
    io["ident_in"] = b.din("ident", [128, 128], BF16)
    io["o_dt"] = b.dout("o_dt", [DI, TLOC], BF16)
    io["o_u"] = b.dout("o_u", [DI, TLOC], BF16)
    io["o_xc"] = b.dout("o_xc", [DI, TLOC], BF16)
    io["o_z"] = b.dout("o_z", [DI, TLOC], BF16)
    io["o_B"] = b.dout("o_B", [NSTATE, TLOC], F32)
    io["o_C"] = b.dout("o_C", [NSTATE, TLOC], F32)
    emit_mamba_pre(b, io)
    return b.finish()


def emit_mamba_scan(b, io):
    pr = b.pr
    TG = 4 * TSEQ
    dts = io["dts"]
    us = io["us"]
    Bs = io["Bs"]
    Cs = io["Cs"]
    alog = io["alog"]
    o_y = io["o_y"]
    HALF = TSEQ // 2

    @body(b)
    def _main():
        sel = b.sb("sel", [NSTATE, NSTATE * 128], F32)
        t_sel = pr.dma("Lsel", sel[:], io["sel16"])
        Ball = [b.sb(f"Ball{i}", [NSTATE, HALF], F32) for i in range(2)]
        Call = [b.sb(f"Call{i}", [NSTATE, HALF], F32) for i in range(2)]
        alog_sb = b.sb("alog_sb", [128, NSTATE], F32)
        Aneg = b.sb("Aneg", [128, NSTATE], F32)
        dt_sb = b.sb("dt_sb", [128, TG], BF16)
        u_sb = b.sb("u_sb", [128, TG], BF16)
        ufh = b.sb("ufh", [128, HALF], F32)
        af = b.sb("af", [128, HALF], F32)
        un = b.sb("un", [128, HALF], F32)
        hh = b.sb("hh", [128, HALF], F32)
        hct = b.sb("hct", [128, HALF], F32)
        yacc = b.sb("yacc", [128, HALF], F32)
        ybf = b.sb("ybf", [128, HALF], BF16)
        hcarr = b.sb("hcarr", [128, NSTATE], F32)
        pB = b.ps("pB", [128, HALF], F32)
        pC = b.ps("pC", [128, HALF], F32)

        for di in range(2):
            t_dt = pr.dma("Ldt", dt_sb[:].rearrange("p (r t) -> p r t", r=NC),
                          dts[di*128:(di+1)*128],
                          deps=[pr.tok("s"), pr.tok("v")])
            t_u = pr.dma("Lu", u_sb[:].rearrange("p (r t) -> p r t", r=NC),
                         us[di*128:(di+1)*128],
                         deps=[pr.tok("v")])
            t_al = pr.dma("Lal", alog_sb[:], alog[di*128:(di+1)*128, :],
                          deps=[pr.tok("v")])
            t_a1 = pr.op("s", "activation", Aneg[:], alog_sb[:], AF.Exp,
                         deps=[t_al])
            t_a2 = pr.op("v", "tensor_scalar", Aneg[:], Aneg[:], -1.0, None,
                         ALU.mult, deps=[t_a1])
            lastBC = {0: None, 1: None}
            for bb in range(4):
                for half in range(2):
                    c0 = bb * TSEQ + half * HALF
                    blk = 2 * bb + half
                    bi = blk % 2
                    t_lB = pr.dma(f"LBb{bi}", Ball[bi][:],
                                  Bs[blk*NSTATE:(blk+1)*NSTATE, 0:HALF],
                                  deps=[lastBC[bi]])
                    t_lC = pr.dma(f"LCb{bi}", Call[bi][:],
                                  Cs[blk*NSTATE:(blk+1)*NSTATE, 0:HALF],
                                  deps=[lastBC[bi]])
                    t_uf = pr.op("v", "tensor_copy", ufh[:],
                                 u_sb[:, c0:c0+HALF], deps=[t_u])
                    prev_un = None
                    prev_hc = None
                    for n in range(NSTATE):
                        bts = []
                        for c in range(HALF // 512):
                            bts.append(pr.op(
                                "p", "matmul", pB[:, c*512:(c+1)*512],
                                sel[:, n*128:(n+1)*128],
                                Ball[bi][:, c*512:(c+1)*512],
                                start=True, stop=True,
                                deps=[t_lB, t_sel, pr.tok("v")] +
                                     ([prev_un] if prev_un else [])))
                            bts.append(pr.op(
                                "p", "matmul", pC[:, c*512:(c+1)*512],
                                sel[:, n*128:(n+1)*128],
                                Call[bi][:, c*512:(c+1)*512],
                                start=True, stop=True,
                                deps=[t_lC] + ([prev_hc] if prev_hc
                                               else [pr.tok("v")])))
                        t_ae = pr.op("s", "activation", af[:],
                                     dt_sb[:, c0:c0+HALF], AF.Exp,
                                     scale=Aneg[:, n:n+1],
                                     deps=[t_dt, t_a2, pr.tok("v")])
                        prev_un = pr.op("v", "tensor_mul", un[:], ufh[:],
                                        pB[:], deps=[t_uf, bts[-1]])
                        init = 0.0 if half == 0 else hcarr[:, n:n+1]
                        t_sc = pr.op("v", "tensor_tensor_scan", hh[:], af[:],
                                     un[:], init, ALU.mult, ALU.add,
                                     deps=[t_ae])
                        if half == 0:
                            pr.op("v", "tensor_copy", hcarr[:, n:n+1],
                                  hh[:, HALF-1:HALF])
                        prev_hc = pr.op("v", "tensor_mul", hct[:], hh[:], pC[:])
                        if n == 0:
                            pr.op("v", "tensor_copy", yacc[:], hct[:])
                        else:
                            pr.op("v", "tensor_add", yacc[:], yacc[:], hct[:])
                    lastBC[bi] = prev_hc
                    t_yb = pr.op("v", "tensor_copy", ybf[:], yacc[:],
                                 deps=[pr.tok("Sy")])
                    w = 2 * bb + half
                    pr.dma("Sy", o_y[di*128:(di+1)*128, w:w+1, :],
                           ybf[:].rearrange("p (u t) -> p u t", u=1),
                           deps=[t_yb])
        b.out_done(pr.tok("Sy"))


def build_mamba_scan():
    b = Builder("mamba_scan")
    TG = 4 * TSEQ
    io = {}
    io["dts"] = b.din("dts", [256, TG], BF16)
    io["us"] = b.din("us", [256, TG], BF16)
    io["Bs"] = b.din("Bs", [128, TLOC], F32)
    io["Cs"] = b.din("Cs", [128, TLOC], F32)
    io["alog"] = b.din("alog", [256, NSTATE], F32)
    io["sel16"] = b.din("sel16", [NSTATE, NSTATE * 128], F32)
    io["o_y"] = b.dout("o_y", [256, TG], BF16).rearrange("c (w t) -> c w t", w=NC)
    io["dts"] = io["dts"].rearrange("c (r t) -> c r t", r=NC)
    io["us"] = io["us"].rearrange("c (r t) -> c r t", r=NC)
    emit_mamba_scan(b, io)
    return b.finish()



def emit_mamba_close(b, io):
    pr = b.pr
    yT = io["yT"]
    xcT = io["xcT"]
    zT = io["zT"]
    D_rs = io["D_rs"]
    woT = io["woT"]
    xr = io["xr"]
    o_x = io["o_x"]
    o_yg = io["o_yg"]
    wob_d = b.dint("wob", [DI, DM], BF16)
    NK = DI // 128

    @body(b)
    def _main():
        prep_ctx = ExitStack()
        wp = emit_weight_prep(b, [(woT, wob_d, DI, DM)], ctx=prep_ctx)
        prep_ctx.close()
        Dv = b.sb("Dv", [128, DI // 128], F32)
        t_D = pr.dma("LD", Dv[:], D_rs)
        ygT = b.sb("ygT", [128, NK * TLOC], BF16)
        ych = b.sb("ych", [128, TLOC], BF16)
        xch = b.sb("xch", [128, TLOC], BF16)
        zch = b.sb("zch", [128, TLOC], BF16)
        yff = b.sb("yff", [128, TLOC], F32)
        szf = b.sb("szf", [128, TLOC], F32)
        for m in range(NK):
            t_y = pr.dma(f"Ly{m%2}", ych[:], yT[m*128:(m+1)*128, :],
                         deps=[pr.tok("v")])
            t_xc = pr.dma(f"Lxc{m%2}", xch[:], xcT[m*128:(m+1)*128, :],
                          deps=[pr.tok("v")])
            t_z = pr.dma(f"Lz{m%2}", zch[:], zT[m*128:(m+1)*128, :],
                         deps=[pr.tok("s")])
            t_yf = pr.op("v", "scalar_tensor_tensor", yff[:], xch[:],
                         Dv[:, m:m+1], ych[:], ALU.mult, ALU.add,
                         deps=[t_y, t_xc, t_D])
            t_sz = pr.op("s", "activation", szf[:], zch[:], AF.Silu,
                         deps=[t_z, pr.tok("v")])
            t_yg = pr.op("v", "tensor_mul", ygT[:, m*TLOC:(m+1)*TLOC],
                         yff[:], szf[:], deps=[t_sz])
        t_yg_all = pr.tok("v")
        for m in range(NK):
            pr.dma("Syg", o_yg[m*128:(m+1)*128, :], ygT[:, m*TLOC:(m+1)*TLOC],
                   deps=[t_yg_all])
        b.out_done(pr.tok("Syg"))

        wo_sb = b.sb("wo_sb", [128, NK * DM], BF16)
        t_wo = pr.dma("Lwo", wo_sb[:].rearrange("p (k c) -> p k c", k=NK),
                      wob_d.rearrange("(k p) c -> p k c", p=128), deps=wp)
        xt = b.sb("xt", [128, DM], F32)
        xnf = b.sb("xnf", [128, DM], F32)
        outt = b.sb("outt", [128, DM], F32)
        st = make_ln_stats(b, "lnf")
        prev_users = []
        with b.nc.psum_tensor(b.prefix + "fps", [128, 512], F32) as fps:
            last_ev = None
            for tt in range(TLOC // 128):
                t_ld = pr.dma(f"Lx{tt%2}", xt[:], xr[tt*128:(tt+1)*128, :],
                              deps=prev_users)
                t_ln = emit_ln_tile(b, st, xt[:], xnf[:], deps=[t_ld],
                                    out_deps=prev_users)
                for dn in range(2):
                    mt = None
                    for k in range(NK):
                        mt = pr.op("p", "matmul", fps[:],
                                   ygT[:, k*TLOC + tt*128: k*TLOC + (tt+1)*128],
                                   wo_sb[:, k*DM + dn*512: k*DM + dn*512 + 512],
                                   start=(k == 0), stop=(k == NK - 1),
                                   deps=[t_yg_all, t_wo] + prev_users +
                                        ([last_ev] if last_ev else [])
                                        if k == 0 else ())
                    t_o = pr.op("v", "tensor_add", outt[:, dn*512:(dn+1)*512],
                                fps[:], xt[:, dn*512:(dn+1)*512],
                                deps=[mt, t_ld])
                    t_o = pr.op("v", "tensor_add", outt[:, dn*512:(dn+1)*512],
                                outt[:, dn*512:(dn+1)*512],
                                xnf[:, dn*512:(dn+1)*512], deps=[t_ln])
                    last_ev = t_o
                t_st = pr.dma("Sx", o_x[tt*128:(tt+1)*128, :], outt[:],
                              deps=[pr.tok("v")])
                prev_users = [t_st, pr.tok("v"), pr.tok("s")]
        b.out_done(pr.tok("Sx"))


def build_mamba_close():
    b = Builder("mamba_close")
    io = {}
    io["yT"] = b.din("yT", [DI, TLOC], BF16)
    io["xcT"] = b.din("xcT", [DI, TLOC], BF16)
    io["zT"] = b.din("zT", [DI, TLOC], BF16)
    io["D_rs"] = b.din("D_rs", [128, DI // 128], F32)
    io["woT"] = b.din("woT", [DI, DM], F32)
    io["xr"] = b.din("xr", [TLOC, DM], F32)
    io["o_x"] = b.dout("o_x", [TLOC, DM], F32)
    io["o_yg"] = b.dout("o_yg", [DI, TLOC], BF16)
    emit_mamba_close(b, io)
    return b.finish()


def run_mamba_layer(x_full, I):
    import ml_dtypes
    xf = x_full.reshape(NC, TLOC, DM)
    ident = np.eye(128, dtype=ml_dtypes.bfloat16)

    ncD = build_mamba_pre()
    w_inT = np.ascontiguousarray(_f32(I["L0_attn_W_in"]).T)
    normw_rs = np.ascontiguousarray(_f32(I["L0_attn_norm_w"]).reshape(DM // 128, 128).T)
    normb_rs = np.ascontiguousarray(_f32(I["L0_attn_norm_b"]).reshape(DM // 128, 128).T)
    cw = _f32(I["L0_attn_conv_w"])
    convw = np.ascontiguousarray(cw.reshape(DI // 128, 128, 4).transpose(1, 0, 2).reshape(128, -1))
    convb_rs = np.ascontiguousarray(_f32(I["L0_attn_conv_b"]).reshape(DI // 128, 128).T)
    w_dtT = np.ascontiguousarray(_f32(I["L0_attn_W_dt"]).T)
    bdt_rs = np.ascontiguousarray(_f32(I["L0_attn_b_dt"]).reshape(DI // 128, 128).T)
    w_BT = np.ascontiguousarray(_f32(I["L0_attn_W_B"]).T)
    w_CT = np.ascontiguousarray(_f32(I["L0_attn_W_C"]).T)
    common = dict(w_inT=w_inT, normw_rs=normw_rs, normb_rs=normb_rs,
                  convw=convw, convb_rs=convb_rs, w_dtT=w_dtT, bdt_rs=bdt_rs,
                  w_BT=w_BT, w_CT=w_CT, ident=ident)
    in_maps = []
    for i in range(NC):
        half = i % 2
        if half == 0:
            xprev = np.zeros((128, DM), np.float32)
        else:
            xprev = _f32(xf[i - 1][TLOC - 128:])
        in_maps.append(dict(xr=_f32(xf[i]), xprev=xprev,
                            wmask=np.full((128, 1), 1.0 if i % 2 else 0.0,
                                          np.float32), **common))
    resD = run_bass_kernel_spmd(ncD, in_maps, core_ids=list(range(NC)))

    def glue(key, dtype):
        return np.concatenate([np.asarray(resD.results[i][key]) for i in range(NC)], 1)
    dtT = glue("o_dt", None)
    uT = glue("o_u", None)
    BT = np.concatenate([np.asarray(resD.results[i]["o_B"]) for i in range(NC)], 1)
    CT = np.concatenate([np.asarray(resD.results[i]["o_C"]) for i in range(NC)], 1)

    ncE = build_mamba_scan()
    alog = _f32(I["L0_attn_A_log"])
    Bs = np.ascontiguousarray(BT.T.reshape(NC, TLOC, NSTATE).transpose(0, 2, 1)
                              .reshape(NC * NSTATE, TLOC))
    Cs = np.ascontiguousarray(CT.T.reshape(NC, TLOC, NSTATE).transpose(0, 2, 1)
                              .reshape(NC * NSTATE, TLOC))
    in_maps = []
    for e in range(NC):
        rows = slice(e * 256, (e + 1) * 256)
        in_maps.append(dict(dts=np.ascontiguousarray(dtT[rows]),
                            us=np.ascontiguousarray(uT[rows]),
                            Bs=Bs, Cs=Cs,
                            alog=np.ascontiguousarray(alog[rows])))
    resE = run_bass_kernel_spmd(ncE, in_maps, core_ids=list(range(NC)))
    yT = np.concatenate([np.asarray(resE.results[e]["o_y"]) for e in range(NC)], 0)

    ncF = build_mamba_close()
    D_rs = np.ascontiguousarray(_f32(I["L0_attn_D"]).reshape(DI // 128, 128).T)
    woT = np.ascontiguousarray(_f32(I["L0_attn_W_out"]).T)
    in_maps = []
    for i in range(NC):
        in_maps.append(dict(
            yT=np.ascontiguousarray(yT[:, i*TLOC:(i+1)*TLOC]),
            xcT=np.asarray(resD.results[i]["o_xc"]),
            zT=np.asarray(resD.results[i]["o_z"]),
            D_rs=D_rs, woT=woT, xr=_f32(xf[i])))
    resF = run_bass_kernel_spmd(ncF, in_maps, core_ids=list(range(NC)))
    x2 = np.concatenate([resF.results[i]["o_x"] for i in range(NC)], 0)
    return x2.reshape(x_full.shape)


def emit_round_grid(b, v_tile, out_tile, w, tmp, mb, emin=None, deps=()):
    pr = b.pr
    t1 = pr.op("s", "activation", tmp["a"][:, 0:w], v_tile, AF.Abs, deps=deps)
    t2 = pr.op("v", "tensor_scalar", tmp["a"][:, 0:w], tmp["a"][:, 0:w],
               float(2.0 ** -30), None, ALU.max, deps=[t1])
    t3 = pr.op("s", "activation", tmp["l"][:, 0:w], tmp["a"][:, 0:w], AF.Ln,
               deps=[t2])
    t4 = pr.op("v", "tensor_scalar", tmp["l"][:, 0:w], tmp["l"][:, 0:w],
               1.4426950408889634, 64.0, ALU.mult, ALU.add, deps=[t3])
    t5 = pr.op("v", "tensor_copy", tmp["ri"][:, 0:w], tmp["l"][:, 0:w])
    t6 = pr.op("v", "tensor_copy", tmp["l"][:, 0:w], tmp["ri"][:, 0:w])
    lo = (emin + 64.0) if emin is not None else -1e9
    t7 = pr.op("v", "tensor_scalar", tmp["l"][:, 0:w], tmp["l"][:, 0:w],
               float(lo), float(-64.0 - mb), ALU.max, ALU.add)
    t8 = pr.op("s", "activation", tmp["d"][:, 0:w], tmp["l"][:, 0:w], AF.Exp,
               scale=-0.6931471805599453, deps=[t7])
    t9 = pr.op("s", "activation", tmp["l"][:, 0:w], tmp["l"][:, 0:w], AF.Exp,
               scale=0.6931471805599453)
    ta = pr.op("v", "tensor_mul", tmp["r0"][:, 0:w], v_tile, tmp["d"][:, 0:w],
               deps=[t8])
    tb = pr.op("v", "tensor_copy", tmp["ri"][:, 0:w], tmp["r0"][:, 0:w])
    tc = pr.op("v", "tensor_copy", tmp["d"][:, 0:w], tmp["ri"][:, 0:w])
    td = pr.op("v", "tensor_sub", tmp["a"][:, 0:w], tmp["r0"][:, 0:w],
               tmp["d"][:, 0:w])
    te = pr.op("v", "tensor_scalar", tmp["aj"][:, 0:w], tmp["a"][:, 0:w],
               0.5, None, ALU.is_gt)
    tf = pr.op("v", "tensor_add", tmp["d"][:, 0:w], tmp["d"][:, 0:w],
               tmp["aj"][:, 0:w])
    tg = pr.op("v", "tensor_scalar", tmp["aj"][:, 0:w], tmp["a"][:, 0:w],
               -0.5, None, ALU.is_lt)
    th = pr.op("v", "tensor_sub", tmp["d"][:, 0:w], tmp["d"][:, 0:w],
               tmp["aj"][:, 0:w])
    ti = pr.op("v", "tensor_tensor", out_tile, tmp["d"][:, 0:w],
               tmp["l"][:, 0:w], ALU.mult, deps=[t9])
    return ti


def mk_grid_tmp(b, tag, w):
    return {"a": b.sb(f"{tag}_a", [128, w], F32),
            "l": b.sb(f"{tag}_l", [128, w], F32),
            "r0": b.sb(f"{tag}_r0", [128, w], F32),
            "d": b.sb(f"{tag}_d", [128, w], F32),
            "aj": b.sb(f"{tag}_aj", [128, w], F32),
            "ri": b.sb(f"{tag}_ri", [128, w], I32)}


def emit_part_reduce(b, col, out11, scr_d, op, tag, deps=()):
    pr = b.pr
    row = b.sb(f"{tag}_row", [1, 128], F32)
    t1 = pr.dma(f"{tag}S", scr_d[0:1, 0:128].rearrange("u r -> r u"), col,
                deps=list(deps))
    t2 = pr.dma(f"{tag}L", row[:], scr_d[0:1, 0:128], deps=[t1])
    t3 = pr.op("v", "tensor_reduce", out11, row[:], AX.X, op, deps=[t2])
    return t3


def emit_ffn1_ln(b, io):
    pr = b.pr
    xr = io["xr"]
    o_xn = io["o_xn"]
    o_amax = io["o_amax"]
    scr_d = b.dint("scr", [1, 128], F32)

    @body(b)
    def _main():
        st = make_ln_stats(b, "ln")
        xt = [b.sb(f"xt{i}", [128, DM], F32) for i in range(2)]
        xnb = [b.sb(f"xnb{i}", [128, DM], F32) for i in range(2)]
        mcol = b.sb("mcol", [128, 1], F32)
        macc = b.sb("macc", [128, 1], F32)
        am = b.sb("am", [1, 1], F32)
        pr.op("v", "memset", macc[:], 0.0)
        prev_ln = {0: None, 1: None}
        prev_st = {0: (), 1: ()}
        for i in range(TLOC // 128):
            bi = i % 2
            ld = pr.dma(f"Lx{bi}", xt[bi][:], xr[i*128:(i+1)*128, :],
                        deps=[prev_ln[bi]])
            t_ln = emit_ln_tile(b, st, xt[bi][:], xnb[bi][:], deps=[ld],
                                out_deps=prev_st[bi])
            prev_ln[bi] = t_ln
            t_st = pr.dma(f"Sxn{bi}", o_xn[i*128:(i+1)*128, :], xnb[bi][:],
                          deps=[t_ln])
            t_m = pr.op("v", "tensor_reduce", mcol[:], xnb[bi][:], AX.X,
                        ALU.max, apply_absolute_value=True, deps=[t_ln])
            t_m2 = pr.op("v", "tensor_max", macc[:], macc[:], mcol[:])
            prev_st[bi] = (t_st, t_m)
        t_am = emit_part_reduce(b, macc[:], am[:], scr_d, ALU.max, "am",
                                deps=[pr.tok("v")])
        b.out_done(pr.dma("Sam", o_amax, am[:], deps=[t_am]))
        b.out_done(pr.tok("Sxn0"))
        b.out_done(pr.tok("Sxn1"))


def build_ffn1_ln():
    b = Builder("ffn1_ln")
    io = {}
    io["xr"] = b.din("xr", [TLOC, DM], F32)
    io["o_xn"] = b.dout("o_xn", [TLOC, DM], F32)
    io["o_amax"] = b.dout("o_amax", [1, 1], F32)
    emit_ffn1_ln(b, io)
    return b.finish()


def emit_ffn1_close(b, io, scale_half):
    pr = b.pr
    xn_in = io["xn_in"]
    amax8 = io["amax8"]
    wT = io["wT"]
    bias = io["bias"]
    xr = io["xr"]
    ident_in = io["ident_in"]
    o_x = io["o_x"]
    scr_d = b.dint("scr", [1, 128], F32)
    NT = TLOC // 128

    @body(b)
    def _main():
        ident = b.sb("identt", [128, 128], BF16)
        t_id = pr.dma("Lid", ident[:], ident_in)
        am8 = b.sb("am8", [1, NC], F32)
        t_a8 = pr.dma("La8", am8[:], amax8)
        amax = b.sb("amax", [1, 1], F32)
        t_am = pr.op("v", "tensor_reduce", amax[:], am8[:], AX.X, ALU.max,
                     deps=[t_a8])
        sx = b.sb("sx", [1, 1], F32)
        pr.op("v", "tensor_scalar", sx[:], amax[:], 1e-12, None, ALU.max)
        pr.op("v", "reciprocal", sx[:], sx[:])
        pr.op("v", "tensor_scalar", sx[:], sx[:], 448.0, 448.0, ALU.mult, ALU.min)
        wtl = b.sb("wtl", [128, DM], F32)
        wcol = b.sb("wcol", [128, 1], F32)
        wacc = b.sb("wacc", [128, 1], F32)
        pr.op("v", "memset", wacc[:], 0.0)
        prev = None
        for i in range(DM // 128):
            ld = pr.dma(f"Lw{i%2}", wtl[:], wT[i*128:(i+1)*128, :], deps=[prev])
            t_m = pr.op("v", "tensor_reduce", wcol[:], wtl[:], AX.X, ALU.max,
                        apply_absolute_value=True, deps=[ld])
            prev = pr.op("v", "tensor_max", wacc[:], wacc[:], wcol[:])
        wam = b.sb("wam", [1, 1], F32)
        t_wam = emit_part_reduce(b, wacc[:], wam[:], scr_d, ALU.max, "wm",
                                 deps=[prev])
        sw = b.sb("sw", [1, 1], F32)
        pr.op("v", "tensor_scalar", sw[:], wam[:], 1e-12, None, ALU.max,
              deps=[t_wam])
        pr.op("v", "reciprocal", sw[:], sw[:])
        pr.op("v", "tensor_scalar", sw[:], sw[:], 448.0, 448.0, ALU.mult, ALU.min)
        dq = b.sb("dq", [1, 1], F32)
        pr.op("v", "tensor_mul", dq[:], sx[:], sw[:])
        t_dq = pr.op("v", "reciprocal", dq[:], dq[:])
        onesf = b.sb("onesf", [1, 128], F32)
        pr.op("v", "memset", onesf[:], 1.0)
        sxc = b.sb("sxc", [128, 1], F32)
        swc = b.sb("swc", [128, 1], F32)
        dqc = b.sb("dqc", [128, 1], F32)
        with b.nc.psum_tensor(b.prefix + "pbc1", [128, 1], F32) as pbc:
            for (src, dst) in ((sx, sxc), (sw, swc), (dq, dqc)):
                t_b = pr.op("p", "matmul", pbc[:], onesf[:], src[:],
                            start=True, stop=True, deps=[t_dq, pr.tok("v")])
                pr.op("v", "tensor_copy", dst[:], pbc[:], deps=[t_b])
        pe_fence(b)
        t_bc = pr.tok("v")

        gtmp = mk_grid_tmp(b, "gq", DM)
        wq = b.sb("wq", [128, (DM // 128) * DM], BF16)
        vsc = b.sb("vsc", [128, DM], F32)
        prev = None
        for i in range(DM // 128):
            ld = pr.dma(f"Lw{i%2}", wtl[:], wT[i*128:(i+1)*128, :],
                        deps=[prev] if prev else [pr.tok("v")])
            t_s = pr.op("v", "tensor_scalar", vsc[:], wtl[:], swc[:], None,
                        ALU.mult, deps=[ld, t_bc])
            t_cl = pr.op("v", "tensor_scalar", vsc[:], vsc[:], 448.0, -448.0,
                         ALU.min, ALU.max)
            prev = emit_round_grid(b, vsc[:], wq[:, i*DM:(i+1)*DM], DM, gtmp,
                                   3.0, emin=-6.0, deps=[t_cl])
        t_wq = prev

        xnt = b.sb("xnt", [128, DM], F32)
        xq_row = b.sb("xq_row", [128, DM], BF16)
        xqT = b.sb("xqT", [128, DM], BF16)
        pbf = b.ps("pbf", [128, 128], BF16)
        xt2 = b.sb("xt2", [128, DM], F32)
        outt = b.sb("outt", [128, DM], F32)
        biasr = b.sb("biasr", [1, DM], F32)
        t_bi = pr.dma("Lbi", biasr[:], bias)
        biasb = b.sb("biasb", [128, DM], F32)
        t_bb = emit_row_broadcast(b, biasr, biasb, DM, "bb", deps=[t_bi])
        gtmp2 = mk_grid_tmp(b, "gx", DM)
        prev_users = []
        with b.nc.psum_tensor(b.prefix + "qps", [128, 512], F32) as qps:
            last_ev = None
            for tt in range(NT):
                t_ld = pr.dma(f"LxT{tt%2}", xnt[:], xn_in[tt*128:(tt+1)*128, :],
                              deps=prev_users)
                t_s = pr.op("v", "tensor_scalar", xt2[:], xnt[:], sxc[:], None,
                            ALU.mult, deps=[t_ld, t_bc] + prev_users)
                t_cl = pr.op("v", "tensor_scalar", xt2[:], xt2[:], 448.0, -448.0,
                             ALU.min, ALU.max)
                t_q = emit_round_grid(b, xt2[:], xq_row[:], DM, gtmp2, 3.0,
                                      emin=-6.0, deps=[t_cl])
                trs = emit_transposes(b, xq_row, xqT, 0, ident, pbf, 8, 128,
                                      deps=[t_q, t_id] + prev_users)
                t_ldr = pr.dma(f"LxR{tt%2}", xt2[:], xr[tt*128:(tt+1)*128, :],
                               deps=[t_q, pr.tok("s")] + prev_users)
                for dn in range(2):
                    mt = None
                    for k in range(DM // 128):
                        mt = pr.op("p", "matmul", qps[:],
                                   xqT[:, k*128:(k+1)*128],
                                   wq[:, k*DM + dn*512: k*DM + dn*512 + 512],
                                   start=(k == 0), stop=(k == DM // 128 - 1),
                                   deps=[trs[-1], t_wq] +
                                        ([last_ev] if last_ev else [])
                                        if k == 0 else ())
                    t_e = pr.op("v", "scalar_tensor_tensor",
                                outt[:, dn*512:(dn+1)*512], qps[:], dqc[:],
                                biasb[:, dn*512:(dn+1)*512],
                                ALU.mult, ALU.add, deps=[mt, t_bb])
                    last_ev = t_e
                t_r = emit_round_grid(b, outt[:], outt[:], DM, gtmp2, 7.0,
                                      deps=[last_ev])
                t_f = pr.op("v", "scalar_tensor_tensor", outt[:], outt[:],
                            scale_half, xt2[:], ALU.mult, ALU.add,
                            deps=[t_r, t_ldr])
                t_st = pr.dma("Sx", o_x[tt*128:(tt+1)*128, :], outt[:],
                              deps=[t_f])
                prev_users = [t_st, t_f]
        b.out_done(pr.tok("Sx"))


def build_ffn1_close(scale_half):
    b = Builder("ffn1_close")
    io = {}
    io["xn_in"] = b.din("xn", [TLOC, DM], F32)
    io["amax8"] = b.din("amax8", [1, NC], F32)
    io["wT"] = b.din("wT", [DM, DM], F32)
    io["bias"] = b.din("bias", [1, DM], F32)
    io["xr"] = b.din("xr", [TLOC, DM], F32)
    io["ident_in"] = b.din("ident", [128, 128], BF16)
    io["o_x"] = b.dout("o_x", [TLOC, DM], F32)
    emit_ffn1_close(b, io, scale_half)
    return b.finish()


def run_ffn1_layer(x_full, I):
    xf = x_full.reshape(NC, TLOC, DM)
    import ml_dtypes
    ncH = build_ffn1_ln()
    in_maps = [dict(xr=_f32(xf[i])) for i in range(NC)]
    resH = run_bass_kernel_spmd(ncH, in_maps, core_ids=list(range(NC)))
    xn = [np.asarray(resH.results[i]["o_xn"]) for i in range(NC)]
    amax8 = np.concatenate([np.asarray(resH.results[i]["o_amax"]) for i in range(NC)],
                           1).reshape(1, NC)
    ncI = build_ffn1_close(0.5)
    ident = np.eye(128, dtype=ml_dtypes.bfloat16)
    wT = np.ascontiguousarray(_f32(I["L1_ffn_weight"]).T)
    bias = _f32(I["L1_ffn_bias"]).reshape(1, DM)
    in_maps = [dict(xn=xn[i], amax8=amax8, wT=wT, bias=bias, xr=_f32(xf[i]),
                    ident=ident) for i in range(NC)]
    resI = run_bass_kernel_spmd(ncI, in_maps, core_ids=list(range(NC)))
    xo = np.concatenate([resI.results[i]["o_x"] for i in range(NC)], 0)
    return xo.reshape(x_full.shape)


def emit_attn_qkv(b, io):
    pr = b.pr
    xr = io["xr"]
    wqT = io["wqT"]
    wkT = io["wkT"]
    wvT = io["wvT"]
    bq_rs = io["bq_rs"]
    bk_rs = io["bk_rs"]
    bv = io["bv"]
    pos = io["pos"]
    ident_in = io["ident_in"]
    o_q = io["o_q"]
    o_k = io["o_k"]
    o_v = io["o_v"]
    wqb_d = b.dint("wqb", [DM, DM], BF16)
    wkb_d = b.dint("wkb", [DM, DM], BF16)
    wvb_d = b.dint("wvb", [DM, DM], BF16)
    NT = TLOC // 128
    TWO_PI = 6.283185307179586
    INV_2PI = 0.15915494309189535

    @body(b)
    def _main():
        prep_ctx = ExitStack()
        wp = emit_weight_prep(b, [(wqT, wqb_d, DM, DM), (wkT, wkb_d, DM, DM),
                                  (wvT, wvb_d, DM, DM)], ctx=prep_ctx)
        prep_ctx.close()
        ident = b.sb("identt", [128, 128], BF16)
        t_id = pr.dma("Lid", ident[:], ident_in)

        st = make_ln_stats(b, "ln")
        xt = [b.sb(f"xt{i}", [128, DM], F32) for i in range(2)]
        xnb = [b.sb(f"xnb{i}", [128, DM], BF16) for i in range(2)]
        xnT = b.sb("xnT", [128, 8 * TLOC], BF16)
        pbf = b.ps("pbf", [128, 128], BF16)
        prev_ln = {0: None, 1: None}
        prev_users = {0: (), 1: ()}
        prev_trs = []
        for i in range(NT):
            bi = i % 2
            ld = pr.dma(f"Lx{bi}", xt[bi][:], xr[i*128:(i+1)*128, :],
                        deps=[prev_ln[bi]])
            t_ln = emit_ln_tile(b, st, xt[bi][:], xnb[bi][:], deps=[ld],
                                out_deps=prev_users[bi])
            prev_ln[bi] = t_ln
            trs = emit_transposes(b, xnb[bi], xnT, i*128, ident, pbf, 8, TLOC,
                                  deps=[t_ln, t_id] + prev_trs[-1:])
            prev_trs = trs
            prev_users[bi] = (trs[-1], pr.tok("p"))
        t_xnT = pr.tok("v")

        iot = b.sb("iot", [128, 1], F32)
        pr.op("g", "iota", iot[:], pattern=[[1, 1]], base=0,
              channel_multiplier=1, allow_small_or_imprecise_dtypes=True)
        pm = b.sb("pm", [128, 1], F32)
        pmi = b.sb("pmi", [128, 1], I32)
        freqc = b.sb("freqc", [128, 1], F32)
        t0 = pr.op("v", "tensor_scalar", pm[:], iot[:], 1.0 / 32.0, None,
                   ALU.mult, deps=[pr.tok("g")])
        pr.op("v", "tensor_copy", pmi[:], pm[:])
        pr.op("v", "tensor_copy", pm[:], pmi[:])
        pr.op("v", "tensor_scalar", pm[:], pm[:], -32.0, None, ALU.mult)
        pr.op("v", "tensor_add", pm[:], pm[:], iot[:])
        t_fr = pr.op("s", "activation", freqc[:], pm[:], AF.Exp,
                     scale=-0.2878231366242557, deps=[pr.tok("v")])
        posr = b.sb("posr", [1, TLOC], F32)
        t_pos = pr.dma("Lpos", posr[:], pos)
        posb = b.sb("posb", [128, TLOC], F32)
        t_pb = emit_row_broadcast(b, posr, posb, TLOC, "pb", deps=[t_pos])
        ang = b.sb("ang", [128, TLOC], F32)
        red = b.sb("red", [128, TLOC], F32)
        kf = b.sb("kf", [128, TLOC], F32)
        kfi = b.sb("kfi", [128, TLOC], I32)
        msk = b.sb("msk", [128, TLOC], F32)
        cosT = b.sb("cosT", [128, TLOC], F32)
        sinT = b.sb("sinT", [128, TLOC], F32)

        def build_trig(dst, shift, dep):
            t1 = pr.op("v", "tensor_scalar", ang[:], posb[:], freqc[:], None,
                       ALU.mult, deps=[t_pb, t_fr, dep])
            if shift:
                t1 = pr.op("v", "tensor_scalar", ang[:], ang[:], shift, None,
                           ALU.add)
            pr.op("v", "tensor_scalar", red[:], ang[:], INV_2PI, None, ALU.mult)
            pr.op("v", "tensor_copy", kfi[:], red[:])
            pr.op("v", "tensor_copy", kf[:], kfi[:])
            pr.op("v", "tensor_sub", msk[:], red[:], kf[:])
            pr.op("v", "tensor_scalar", msk[:], msk[:], 0.0, None, ALU.is_lt)
            pr.op("v", "tensor_sub", kf[:], kf[:], msk[:])
            pr.op("v", "scalar_tensor_tensor", red[:], kf[:], -TWO_PI, ang[:],
                  ALU.mult, ALU.add)
            pr.op("v", "tensor_scalar", msk[:], red[:], 3.14159265, None,
                  ALU.is_gt)
            t2 = pr.op("v", "scalar_tensor_tensor", red[:], msk[:], -TWO_PI,
                       red[:], ALU.mult, ALU.add)
            return pr.op("s", "activation", dst[:], red[:], AF.Sin, deps=[t2])

        t_cos = build_trig(cosT, 1.5707963267948966, None)
        t_sin = build_trig(sinT, None, t_cos)

        bqc = b.sb("bqc", [128, DM // 128], F32)
        t_bq = pr.dma("Lbq", bqc[:], bq_rs)
        bkc = b.sb("bkc", [128, DM // 128], F32)
        t_bk = pr.dma("Lbk", bkc[:], bk_rs)
        bvc = b.sb("bvc", [128, DM // 128], F32)
        t_bvc = pr.dma("Lbv", bvc[:], bv)

        wtl = [b.sb(f"wtl{i}", [128, 8 * 128], BF16) for i in range(2)]
        qcol = b.sb("qcol", [128, TLOC], F32)
        qrot = b.sb("qrot", [128, TLOC], F32)
        tA = b.sb("tA", [64, TLOC], F32)
        tB = b.sb("tB", [64, TLOC], F32)
        qb16 = b.sb("qb16", [128, TLOC], BF16)

        prev_stq = ()
        for (wsrc, bcol, btok, odst, sname) in (
                (wqb_d, bqc, t_bq, o_q, "Sq"), (wkb_d, bkc, t_bk, o_k, "Sk")):
            t_ld_prev = {0: (), 1: ()}
            for m in range(DM // 128):
                bi = m % 2
                ldw = pr.dma(f"LW{sname}{bi}",
                             wtl[bi][:].rearrange("p (k c) -> p k c", k=8),
                             wsrc[:, m*128:(m+1)*128].rearrange(
                                 "(k p) c -> p k c", p=128),
                             deps=wp + list(t_ld_prev.get(bi) or ()))
                with b.nc.psum_tensor(b.prefix + f"qp{sname}{m}", [128, 512], F32) as qp:
                    prev_ev = None
                    for nch in range(TLOC // 512):
                        c0 = nch * 512
                        mt = None
                        for k in range(8):
                            mt = pr.op("p", "matmul", qp[:],
                                       wtl[bi][:, k*128:(k+1)*128],
                                       xnT[:, k*TLOC + c0: k*TLOC + c0 + 512],
                                       start=(k == 0), stop=(k == 7),
                                       deps=[ldw, t_xnT] +
                                            ([prev_ev] if prev_ev else [])
                                            if k == 0 else ())
                        prev_ev = pr.op("s", "activation", qcol[:, c0:c0+512],
                                        qp[:], AF.Identity, bias=bcol[:, m:m+1],
                                        deps=[mt, btok, pr.tok("v")])
                pr.op("v", "tensor_mul", tA[:], qcol[0:64, :], cosT[0:64, :],
                      deps=[prev_ev, t_cos, t_sin])
                pr.op("v", "tensor_mul", tB[:], qcol[64:128, :], sinT[64:128, :])
                pr.op("v", "tensor_sub", qrot[0:64, :], tA[:], tB[:])
                pr.op("v", "tensor_mul", tA[:], qcol[0:64, :], sinT[0:64, :])
                pr.op("v", "tensor_mul", tB[:], qcol[64:128, :], cosT[64:128, :])
                pr.op("v", "tensor_add", qrot[64:128, :], tA[:], tB[:])
                t_qb = pr.op("v", "tensor_copy", qb16[:], qrot[:],
                             deps=list(prev_stq))
                t_st = pr.dma(sname, odst[m*128:(m+1)*128, :], qb16[:],
                              deps=[t_qb])
                prev_stq = (t_st,)
                t_ld_prev[bi] = (t_st, t_qb)
            b.out_done(pr.tok(sname))

        vb16 = b.sb("vb16", [128, TLOC], BF16)
        for m in range(DM // 128):
            bi = m % 2
            ldw = pr.dma(f"LWv{bi}",
                         wtl[bi][:].rearrange("p (k c) -> p k c", k=8),
                         wvb_d[:, m*128:(m+1)*128].rearrange(
                             "(k p) c -> p k c", p=128),
                         deps=wp + list(t_ld_prev.get(bi) or ()))
            with b.nc.psum_tensor(b.prefix + f"vqp{m}", [128, 512], F32) as qp:
                prev_ev = None
                for nch in range(TLOC // 512):
                    c0 = nch * 512
                    mt = None
                    for k in range(8):
                        mt = pr.op("p", "matmul", qp[:],
                                   wtl[bi][:, k*128:(k+1)*128],
                                   xnT[:, k*TLOC + c0: k*TLOC + c0 + 512],
                                   start=(k == 0), stop=(k == 7),
                                   deps=[ldw, t_xnT] +
                                        ([prev_ev] if prev_ev else [])
                                        if k == 0 else ())
                    prev_ev = pr.op("s", "activation", vb16[:, c0:c0+512],
                                    qp[:], AF.Identity, bias=bvc[:, m:m+1],
                                    deps=[mt, t_bvc, pr.tok("v"),
                                          pr.tok("Sv")])
            t_st = pr.dma("Sv", o_v[m*128:(m+1)*128, :], vb16[:],
                          deps=[prev_ev])
            t_ld_prev[bi] = (t_st,)
        b.out_done(pr.tok("Sv"))


def build_attn_qkv():
    b = Builder("attn_qkv")
    io = {}
    io["xr"] = b.din("xr", [TLOC, DM], F32)
    io["wqT"] = b.din("wqT", [DM, DM], F32)
    io["wkT"] = b.din("wkT", [DM, DM], F32)
    io["wvT"] = b.din("wvT", [DM, DM], F32)
    io["bq_rs"] = b.din("bq_rs", [128, DM // 128], F32)
    io["bk_rs"] = b.din("bk_rs", [128, DM // 128], F32)
    io["bv"] = b.din("bv", [128, DM // 128], F32)
    io["pos"] = b.din("pos", [1, TLOC], F32)
    io["ident_in"] = b.din("ident", [128, 128], BF16)
    io["o_q"] = b.dout("o_q", [DM, TLOC], BF16)
    io["o_k"] = b.dout("o_k", [DM, TLOC], BF16)
    io["o_v"] = b.dout("o_v", [DM, TLOC], BF16)
    emit_attn_qkv(b, io)
    return b.finish()


def emit_attn_core(b, io):
    pr = b.pr
    ag_q = io["ag_q"]
    ag_k = io["ag_k"]
    ag_v = io["ag_v"]
    qkidx_in = io["qkidx"]
    vidx_in = io["vidx"]
    ident_in = io["ident_in"]
    o_att = io["o_att"]
    NQ = TSEQ // 512
    NROW = NC * DM

    @body(b)
    def _main():
        onesb = b.sb("onesb", [128, 1], BF16)
        pr.op("v", "memset", onesb[:], 1.0)
        ones64 = b.sb("ones64", [1, 64], F32)
        pr.op("v", "memset", ones64[:], 1.0)
        ident = b.sb("identt", [128, 128], BF16)
        t_id = pr.dma("Lid", ident[:], ident_in)
        qkidx = b.sb("qkidx_t", [64, 16], I32)
        t_qi = pr.dma("Lqi", qkidx[:], qkidx_in)
        vidx = b.sb("vidx_t", [64, 16], I32)
        t_vi = pr.dma("Lvi", vidx[:], vidx_in)
        mks = []
        mkf = b.sb("mkf", [128, 512], F32)
        for off in range(4):
            mk = b.sb(f"mk{off}", [128, 512], BF16)
            pr.op("g", "iota", mkf[:], pattern=[[1, 512]], base=-off * 128,
                  channel_multiplier=-1, allow_small_or_imprecise_dtypes=True,
                  deps=[pr.tok("v")])
            pr.op("v", "tensor_scalar", mkf[:], mkf[:], -0.5, None, ALU.is_gt,
                  deps=[pr.tok("g")])
            pr.op("v", "tensor_copy", mk[:], mkf[:])
            mks.append(mk)

        q_sb = b.sb("q_sb", [64, TSEQ], BF16)
        k_sb = b.sb("k_sb", [64, TSEQ], BF16)
        vc_sb = b.sb("vc_sb", [64, TSEQ], BF16)
        v_sb = b.sb("v_sb", [128, (TSEQ // 128) * DK], BF16)
        probs = b.sb("probs", [128, 512], BF16)
        accsb = b.sb("accsb", [64, 512], F32)
        nrm = b.sb("nrm", [1, 512], F32)
        ob16 = b.sb("ob16", [64, 512], BF16)
        Sps = b.ps("Sps", [128, 512], F32)
        acc = b.ps("acc", [64, 512], F32)
        nps = b.ps("nps", [1, 512], F32)
        rbc = b.ps("rbc", [64, 512], F32)
        pvt = b.ps("pvt", [128, 64], BF16)

        bnd = b.bound_reg(NROW - 1)
        for pi in range(8):
            pdeps = [pr.tok("p"), pr.tok("v"), pr.tok("s")]
            g_toks = []
            for (dst, srcag, idxt, it) in ((q_sb, ag_q, qkidx, t_qi),
                                           (k_sb, ag_k, qkidx, t_qi),
                                           (vc_sb, ag_v, vidx, t_vi)):
                for rh in range(2):
                    g_toks.append(pr.idma(
                        "GQK", dst[:, rh*TLOC:(rh+1)*TLOC], None, srcag,
                        IndirectOffsetOnAxis(ap=idxt[:, pi*2+rh:pi*2+rh+1], axis=0),
                        bnd, deps=[it] + pdeps))
            t_g = pr.rot_toks("GQK")
            prev_ev = None
            for kb in range(TSEQ // 128):
                tt = pr.op("p", "transpose", pvt[0:128, 0:64],
                           vc_sb[:, kb*128:(kb+1)*128], ident[0:64, 0:64],
                           deps=t_g + [t_id] + ([prev_ev] if prev_ev else []))
                prev_ev = pr.op("v", "tensor_copy", v_sb[:, kb*DK:(kb+1)*DK],
                                pvt[0:128, 0:64], deps=[tt])
            t_v = prev_ev
            for qb in range(NQ):
                c0 = qb * 512
                nkb = (qb + 1) * 4
                prev_probs = None
                for kb in range(nkb):
                    t_s = pr.op("p", "matmul", Sps[:], k_sb[:, kb*128:(kb+1)*128],
                                q_sb[:, c0:c0+512], start=True, stop=True,
                                deps=t_g + ([prev_probs] if prev_probs else []))
                    t_e = pr.op("s", "activation", probs[:], Sps[:], AF.Exp,
                                scale=0.125, deps=[t_s, pr.tok("v")])
                    tok_probs = t_e
                    off = kb - 4 * qb
                    if off >= 0:
                        tok_probs = pr.op("v", "tensor_mul", probs[:], probs[:],
                                          mks[off][:], deps=[t_e])
                    t_av = pr.op("p", "matmul", acc[:],
                                 v_sb[:, kb*DK:(kb+1)*DK], probs[:],
                                 start=(kb == 0), stop=(kb == nkb - 1),
                                 deps=[tok_probs, t_v])
                    t_nm = pr.op("p", "matmul", nps[:], onesb[:], probs[:],
                                 start=(kb == 0), stop=(kb == nkb - 1),
                                 deps=[tok_probs])
                    prev_probs = t_av
                t_cp = pr.op("v", "tensor_copy", accsb[:], acc[:],
                             deps=[prev_probs, pr.tok("p")])
                t_n = pr.op("v", "tensor_copy", nrm[:], nps[:])
                t_r = pr.op("v", "reciprocal", nrm[:], nrm[:])
                t_bc = pr.op("p", "matmul", rbc[:], ones64[:], nrm[:],
                             start=True, stop=True, deps=[t_r, pr.tok("v")])
                t_o = pr.op("v", "tensor_mul", ob16[:], accsb[:], rbc[:],
                            deps=[t_bc, pr.tok("So")])
                pr.dma("So", o_att[pi*64:(pi+1)*64, c0:c0+512], ob16[:],
                       deps=[t_o])
        b.out_done(pr.tok("So"))


def build_attn_core():
    b = Builder("attn_core")
    io = {}
    io["ag_q"] = b.din("ag_q", [NC * DM, TLOC], BF16)
    io["ag_k"] = b.din("ag_k", [NC * DM, TLOC], BF16)
    io["ag_v"] = b.din("ag_v", [NC * DM, TLOC], BF16)
    io["qkidx"] = b.din("qkidx", [64, 16], I32)
    io["vidx"] = b.din("vidx", [64, 16], I32)
    io["ident_in"] = b.din("ident", [128, 128], BF16)
    io["o_att"] = b.dout("o_att", [8 * DK, TSEQ], BF16)
    emit_attn_core(b, io)
    return b.finish()


def emit_attn_close(b, io):
    pr = b.pr
    aT = io["aT"]
    woT = io["woT"]
    bo = io["bo"]
    xr = io["xr"]
    o_x = io["o_x"]
    wob_d = b.dint("wob2", [DM, DM], BF16)
    NT = TLOC // 128

    @body(b)
    def _main():
        prep_ctx = ExitStack()
        wp = emit_weight_prep(b, [(woT, wob_d, DM, DM)], ctx=prep_ctx)
        prep_ctx.close()
        aT_sb = b.sb("aT_sb", [128, 8 * TLOC], BF16)
        t_a = pr.dma("La", aT_sb[:].rearrange("p (k c) -> p k c", k=8),
                     aT.rearrange("(k p) c -> p k c", p=128))
        wo_sb = b.sb("wo_sb", [128, 8 * DM], BF16)
        t_wo = pr.dma("Lwo", wo_sb[:].rearrange("p (k c) -> p k c", k=8),
                      wob_d.rearrange("(k p) c -> p k c", p=128), deps=wp)
        bor = b.sb("bor", [1, DM], F32)
        t_b0 = pr.dma("Lbo", bor[:], bo)
        bob = b.sb("bob", [128, DM], F32)
        t_bb = emit_row_broadcast(b, bor, bob, DM, "bo", deps=[t_b0])
        xt = b.sb("xt", [128, DM], F32)
        outt = b.sb("outt", [128, DM], F32)
        prev_users = []
        with b.nc.psum_tensor(b.prefix + "cps", [128, 512], F32) as cps:
            last_ev = None
            for tt in range(NT):
                t_ld = pr.dma(f"Lx{tt%2}", xt[:], xr[tt*128:(tt+1)*128, :],
                              deps=prev_users)
                for dn in range(2):
                    mt = None
                    for k in range(8):
                        mt = pr.op("p", "matmul", cps[:],
                                   aT_sb[:, k*TLOC + tt*128: k*TLOC + (tt+1)*128],
                                   wo_sb[:, k*DM + dn*512: k*DM + dn*512 + 512],
                                   start=(k == 0), stop=(k == 7),
                                   deps=[t_a, t_wo] +
                                        ([last_ev] if last_ev else [])
                                        if k == 0 else ())
                    t_o = pr.op("v", "tensor_add", outt[:, dn*512:(dn+1)*512],
                                cps[:], xt[:, dn*512:(dn+1)*512],
                                deps=[mt, t_ld])
                    t_o = pr.op("v", "tensor_add", outt[:, dn*512:(dn+1)*512],
                                outt[:, dn*512:(dn+1)*512],
                                bob[:, dn*512:(dn+1)*512], deps=[t_bb])
                    last_ev = t_o
                t_st = pr.dma("Sx", o_x[tt*128:(tt+1)*128, :], outt[:],
                              deps=[pr.tok("v")])
                prev_users = [t_st, pr.tok("v")]
        b.out_done(pr.tok("Sx"))


def build_attn_close():
    b = Builder("attn_close")
    io = {}
    io["aT"] = b.din("aT", [DM, TLOC], BF16)
    io["woT"] = b.din("woT", [DM, DM], F32)
    io["bo"] = b.din("bo", [1, DM], F32)
    io["xr"] = b.din("xr", [TLOC, DM], F32)
    io["o_x"] = b.dout("o_x", [TLOC, DM], F32)
    emit_attn_close(b, io)
    return b.finish()

def _perm_cols():
    perm = []
    for i in range(DM // 128):
        h0, h1 = 2 * i, 2 * i + 1
        for (h, dr) in ((h0, range(0, 32)), (h1, range(0, 32)),
                        (h0, range(32, 64)), (h1, range(32, 64))):
            perm.extend(h * DK + d for d in dr)
    return np.array(perm)


def attn_idx_tables(core):
    pairs = [(bb, h) for bb in range(4) for h in range(NH)][core*8:(core+1)*8]
    qk = np.zeros((64, 16), np.int32)
    vv = np.zeros((64, 16), np.int32)
    for pi, (bb, h) in enumerate(pairs):
        i = h // 2
        if h % 2 == 0:
            rows = list(range(i*128, i*128+32)) + list(range(i*128+64, i*128+96))
        else:
            rows = list(range(i*128+32, i*128+64)) + list(range(i*128+96, i*128+128))
        for rh in range(2):
            rank = 2 * bb + rh
            qk[:, pi*2+rh] = rank * DM + np.array(rows)
            vv[:, pi*2+rh] = rank * DM + h * DK + np.arange(64)
    return qk, vv


def run_attn_layer(x_full, I):
    import ml_dtypes
    xf = x_full.reshape(NC, TLOC, DM)
    ident = np.eye(128, dtype=ml_dtypes.bfloat16)
    perm = _perm_cols()

    ncJ = build_attn_qkv()
    wqT = np.ascontiguousarray(_f32(I["L1_attn_WQ_w"]).T[:, perm])
    wkT = np.ascontiguousarray(_f32(I["L1_attn_WK_w"]).T[:, perm])
    wvT = np.ascontiguousarray(_f32(I["L1_attn_WV_w"]).T)
    bq_rs = np.ascontiguousarray(_f32(I["L1_attn_WQ_b"])[perm].reshape(DM // 128, 128).T)
    bk_rs = np.ascontiguousarray(_f32(I["L1_attn_WK_b"])[perm].reshape(DM // 128, 128).T)
    bv_rs = np.ascontiguousarray(_f32(I["L1_attn_WV_b"]).reshape(DM // 128, 128).T)
    common = dict(wqT=wqT, wkT=wkT, wvT=wvT, bq_rs=bq_rs, bk_rs=bk_rs,
                  bv=bv_rs, ident=ident)
    in_maps = []
    for i in range(NC):
        t0 = (i % 2) * TLOC
        posr = np.arange(t0, t0 + TLOC, dtype=np.float32).reshape(1, TLOC)
        in_maps.append(dict(xr=_f32(xf[i]), pos=posr, **common))
    resJ = run_bass_kernel_spmd(ncJ, in_maps, core_ids=list(range(NC)))
    ag_q = np.concatenate([np.asarray(resJ.results[i]["o_q"]) for i in range(NC)], 0)
    ag_k = np.concatenate([np.asarray(resJ.results[i]["o_k"]) for i in range(NC)], 0)
    ag_v = np.concatenate([np.asarray(resJ.results[i]["o_v"]) for i in range(NC)], 0)

    ncK = build_attn_core()
    in_maps = []
    for c in range(NC):
        qk, vv = attn_idx_tables(c)
        in_maps.append(dict(ag_q=ag_q, ag_k=ag_k, ag_v=ag_v,
                            qkidx=qk, vidx=vv, ident=ident))
    resK = run_bass_kernel_spmd(ncK, in_maps, core_ids=list(range(NC)))
    ag_att = np.concatenate([np.asarray(resK.results[c]["o_att"]) for c in range(NC)], 0)

    ncL = build_attn_close()
    woT = np.ascontiguousarray(_f32(I["L1_attn_Wo_w"]).T)
    bo = _f32(I["L1_attn_Wo_b"]).reshape(1, DM)
    in_maps = []
    for i in range(NC):
        bb, hf = i // 2, i % 2
        aT = np.ascontiguousarray(ag_att[bb*1024:(bb+1)*1024, hf*TLOC:(hf+1)*TLOC])
        in_maps.append(dict(aT=aT, woT=woT, bo=bo, xr=_f32(xf[i])))
    resL = run_bass_kernel_spmd(ncL, in_maps, core_ids=list(range(NC)))
    x5 = np.concatenate([resL.results[i]["o_x"] for i in range(NC)], 0)
    return x5.reshape(x_full.shape)



def kernel_unfused(**inputs):
    I = inputs
    x = _f32(I["L0_attn_x"]).reshape(-1, DM)
    x = run_moe_layer(x, I, "L0_ffn_").reshape(-1, DM)
    x = run_mamba_layer(x, I).reshape(-1, DM)
    x = run_moe_layer(x, I, "L0_ffn_").reshape(-1, DM)
    x = run_ffn1_layer(x, I).reshape(-1, DM)
    x = run_attn_layer(x, I).reshape(-1, DM)
    x = run_ffn1_layer(x, I).reshape(-1, DM)
    return np.ascontiguousarray(x.reshape(4, TSEQ, DM), dtype=np.float32)


def phase_sync(b):
    pr = b.pr
    names = ["v", "s", "p", "g"] + sorted(pr.touched - {"v", "s", "p", "g"})
    for ename in ("v", "s", "p", "g", "sync"):
        eng = pr.eng(ename)
        for sname in names:
            cnt = pr.count.get(sname, 0)
            if cnt > 0:
                eng.wait_ge(pr.sem[sname], cnt)
    pr.touched.clear()


class PhaseCtx:
    def __init__(self, b, prefix):
        self.b = b
        self.prefix = prefix

    def __enter__(self):
        phase_sync(self.b)
        self.b.pr.new_phase()
        self._sid, _ = self.b.nc.enter_named_scope(self.prefix, False)
        self.b.prefix = self.prefix
        self.b._scope = ExitStack()
        return self

    def __exit__(self, *a):
        phase_sync(self.b)
        self.b.nc.leave_named_scope(self.prefix, self._sid, False)
        self.b._scope.close()
        self.b._scope = None
        self.b.prefix = ""
        return False


def emit_coll(b, kind, op, in_ap, out_ap, tag):
    pr = b.pr
    phase_sync(b)
    return pr.coll("CC", kind, op, [in_ap.opt()], [out_ap.opt()])


def emit_dram_copy(b, dst, src, R, C, dt, tag):
    pr = b.pr
    t = b.sb(f"{tag}_cp", [128, C], dt)
    prev = None
    toks = []
    for r0 in range(0, R, 128):
        rr = min(128, R - r0)
        ld = pr.dma("GCL", t[0:rr, :], src[r0:r0+rr, :], deps=[prev])
        prev = pr.dma("GCS", dst[r0:r0+rr, :], t[0:rr, :], deps=[ld])
        toks.append(prev)
    return toks[-1]


def emit_idx_gather_dram(b, dst_dram, src_dram, idx_in, ntiles, C, dt, bnd, tag,
                         zero_fill=False, rows=128):
    pr = b.pr
    t = b.sb(f"{tag}_g", [rows, C], dt)
    idxt = b.sb(f"{tag}_ix", [rows, ntiles], I32)
    t_ix = pr.dma("GIL", idxt[:], idx_in)
    prev = None
    for j in range(ntiles):
        dep = [t_ix] + ([prev] if prev else [])
        if zero_fill:
            z = pr.op("v", "memset", t[:], 0.0, deps=dep)
            dep = dep + [z]
        g = pr.idma("GIG", t[:], None, src_dram,
                    IndirectOffsetOnAxis(ap=idxt[:, j:j+1], axis=0),
                    bnd, deps=dep)
        prev = pr.dma("GIS", dst_dram[j*rows:(j+1)*rows, :], t[:], deps=[g])
    return prev


def emit_probsT_glue(b, probs_dram, out_dram, tag):
    pr = b.pr
    idf = b.sb(f"{tag}_idf", [128, 128], F32)
    iof = b.sb(f"{tag}_iof", [128, 128], F32)
    pr.op("g", "iota", iof[:], pattern=[[-1, 128]], base=0,
          channel_multiplier=1, allow_small_or_imprecise_dtypes=True)
    t_id = pr.op("v", "tensor_scalar", idf[:], iof[:], 0.0, None, ALU.is_equal,
                 deps=[pr.tok("g")])
    pt = b.sb(f"{tag}_pt", [128, NEXP], F32)
    ot = b.sb(f"{tag}_ot", [NEXP, TLOC], F32)
    pps = b.ps(f"{tag}_ps", [NEXP, 128], F32)
    prev = None
    for i in range(TLOC // 128):
        ld = pr.dma("GPL", pt[:], probs_dram[i*128:(i+1)*128, :],
                    deps=[prev])
        tt = pr.op("p", "transpose", pps[:], pt[:], idf[:],
                   deps=[ld, t_id, pr.tok("v")])
        prev = pr.op("v", "tensor_copy", ot[:, i*128:(i+1)*128], pps[:],
                     deps=[tt])
    return pr.dma("GPS", out_dram, ot[:], deps=[prev])


def build_fused(n_stages=6, debug=False):
    b = Builder("fused")
    dbg_bufs = {}
    RG = [list(range(NC))]

    xin = b.din("xin", [TLOC, DM], F32)
    e_gate = b.din("gate", [DM, NEXP], F32)
    e_ws1 = b.din("ws1", [DM, HD], F32)
    e_bs1 = b.din("bs1_rs", [128, HD // 128], F32)
    e_ws2 = b.din("ws2", [HD, DM], F32)
    e_bs2 = b.din("bs2", [1, DM], F32)
    e_w1 = b.din("w1", [DM, HD], F32)
    e_b1 = b.din("b1_rs", [128, HD // 128], F32)
    e_w2 = b.din("w2", [HD, DM], F32)
    e_b2 = b.din("b2", [1, DM], F32)
    ident_in = b.din("ident", [128, 128], BF16)
    m_winT = b.din("w_inT", [DM, 2 * DI], F32)
    m_nw = b.din("normw_rs", [128, DM // 128], F32)
    m_nb = b.din("normb_rs", [128, DM // 128], F32)
    m_cw = b.din("convw", [128, DI // 128 * 4], F32)
    m_cb = b.din("convb_rs", [128, DI // 128], F32)
    m_wdt = b.din("w_dtT", [DI, DI], F32)
    m_bdt = b.din("bdt_rs", [128, DI // 128], F32)
    m_wB = b.din("w_BT", [DI, NSTATE], F32)
    m_wC = b.din("w_CT", [DI, NSTATE], F32)
    m_alog = b.din("alog", [256, NSTATE], F32)
    m_D = b.din("D_rs", [128, DI // 128], F32)
    m_woT = b.din("woT", [DI, DM], F32)
    xprev_idx = b.din("xprev_idx", [128, 1], I32)
    m_wmask = b.din("wmask", [128, 1], F32)
    m_sel16 = b.din("sel16", [16, 16 * 128], F32)
    f_wT = b.din("wT", [DM, DM], F32)
    f_bias = b.din("bias", [1, DM], F32)
    a_wqT = b.din("wqT", [DM, DM], F32)
    a_wkT = b.din("wkT", [DM, DM], F32)
    a_wvT = b.din("wvT", [DM, DM], F32)
    a_bq = b.din("bq_rs", [128, DM // 128], F32)
    a_bk = b.din("bk_rs", [128, DM // 128], F32)
    a_bv = b.din("bv", [128, DM // 128], F32)
    a_pos = b.din("pos", [1, TLOC], F32)
    a_qkidx = b.din("qkidx", [64, 16], I32)
    a_vidx = b.din("vidx", [64, 16], I32)
    a_woT = b.din("a_woT", [DM, DM], F32)
    a_bo = b.din("bo", [1, DM], F32)
    peidx = b.din("peidx", [NEXP, 1], I32)
    slwidx = b.din("slwidx", [NEXP, 1], I32)
    aidx = b.din("aidx", [128, 8], I32)
    chidx = b.din("chidx", [128, 16], I32)

    o_x = b.dout("o_x", [TLOC, DM], F32)

    xb = [b.dint(f"xb{i}", [TLOC, DM], F32) for i in range(6)]
    pr = b.pr

    def moe_phase(tag, xr_ap, xout_ap):
        c_xn = b.dint(f"{tag}_cxn", [TLOC, DM], BF16)
        c_pr = b.dint(f"{tag}_cpr", [TLOC, NEXP], F32)
        c_sh = b.dint(f"{tag}_csh", [TLOC, DM], F32)
        c_pT = b.dint(f"{tag}_cpT", [NEXP, TLOC], F32)
        ag_xn = b.dint(f"{tag}_agxn", [NC * TLOC, DM], BF16)
        ag_pT = b.dint(f"{tag}_agpT", [NC * NEXP, TLOC], F32)
        c_pe = b.dint(f"{tag}_cpe", [NEXP, TLOC], F32)
        c_eo = b.dint(f"{tag}_ceo", [CAP, DM], F32)
        c_tau = b.dint(f"{tag}_ctau", [1, 2], F32)
        c_sl = b.dint(f"{tag}_csl", [NEXP, TLOC], F32)
        ag_eo = b.dint(f"{tag}_ageo", [NC * CAP, DM], F32)
        ag_sl = b.dint(f"{tag}_agsl", [NC * NEXP, TLOC], F32)
        c_slw = b.dint(f"{tag}_cslw", [NEXP, TLOC], F32)
        scr_idx = b.dint(f"{tag}_scri", [128, 16], I32)

        with PhaseCtx(b, f"{tag}a_"):
            emit_ffn0_pre(b, dict(xr=xr_ap, gate=e_gate, ws1=e_ws1,
                                  bs1_rs=e_bs1, ws2=e_ws2, bs2=e_bs2,
                                  ident_in=ident_in, o_xn=c_xn, o_probs=c_pr,
                                  o_shared=c_sh))
        with PhaseCtx(b, f"{tag}g1_"):
            emit_probsT_glue(b, c_pr, c_pT, f"{tag}g1")
        with PhaseCtx(b, f"{tag}c1_"):
            emit_coll(b, "AllGather", ALU.bypass, c_xn, ag_xn, "agxn")
            emit_coll(b, "AllGather", ALU.bypass, c_pT, ag_pT, "agpT")
        with PhaseCtx(b, f"{tag}g2_"):
            emit_idx_gather_dram(b, c_pe, ag_pT, peidx, 1, TLOC, F32,
                                 b.bound_reg(NC * NEXP - 1), f"{tag}g2",
                                 rows=NEXP)
        with PhaseCtx(b, f"{tag}b_"):
            emit_ffn0_expert(b, dict(pe_in=c_pe, xn_all=ag_xn, w1=e_w1,
                                     b1_rs=e_b1, w2=e_w2, b2=e_b2,
                                     ident_in=ident_in, o_eout=c_eo,
                                     o_tau=c_tau, o_idx=scr_idx,
                                     o_slots=c_sl))
        with PhaseCtx(b, f"{tag}c2_"):
            emit_coll(b, "AllGather", ALU.bypass, c_eo, ag_eo, "ageo")
            emit_coll(b, "AllGather", ALU.bypass, c_sl, ag_sl, "agsl")
        with PhaseCtx(b, f"{tag}g3_"):
            emit_idx_gather_dram(b, c_slw, ag_sl, slwidx, 1, TLOC, F32,
                                 b.bound_reg(NC * NEXP - 1), f"{tag}g3",
                                 rows=NEXP)
        with PhaseCtx(b, f"{tag}c_"):
            emit_ffn0_close(b, dict(xr=xr_ap, shared=c_sh, slots_win=c_slw,
                                    o_x=xout_ap, eout_all=ag_eo))
        if debug and tag == "m0":
            dbg_bufs.update(d_xn=(c_xn, BF16), d_pr=(c_pr, F32),
                            d_pe=(c_pe, F32), d_sl=(c_slw, F32),
                            d_eo=(c_eo, F32), d_sh=(c_sh, F32),
                            d_idx=(scr_idx, I32))

    def mamba_phase(tag, xr_ap, xout_ap):
        c_dt = b.dint(f"{tag}_cdt", [DI, TLOC], BF16)
        c_u = b.dint(f"{tag}_cu", [DI, TLOC], BF16)
        c_xc = b.dint(f"{tag}_cxc", [DI, TLOC], BF16)
        c_z = b.dint(f"{tag}_cz", [DI, TLOC], BF16)
        c_B = b.dint(f"{tag}_cB", [NSTATE, TLOC], F32)
        c_C = b.dint(f"{tag}_cC", [NSTATE, TLOC], F32)
        c_tail = b.dint(f"{tag}_ctail", [128, DM], F32)
        ag_tail = b.dint(f"{tag}_agtail", [NC * 128, DM], F32)
        c_xprev = b.dint(f"{tag}_cxprev", [128, DM], F32)
        ag_dt = b.dint(f"{tag}_agdt", [NC * DI, TLOC], BF16)
        ag_u = b.dint(f"{tag}_agu", [NC * DI, TLOC], BF16)
        g_dt = b.dint(f"{tag}_gdt", [DI, TLOC], BF16)
        g_u = b.dint(f"{tag}_gu", [DI, TLOC], BF16)
        ag_B = b.dint(f"{tag}_agB", [NC * NSTATE, TLOC], F32)
        ag_C = b.dint(f"{tag}_agC", [NC * NSTATE, TLOC], F32)
        yin = b.dint(f"{tag}_yin", [DI, TLOC], BF16)
        ag_y = b.dint(f"{tag}_agy", [NC * DI, TLOC], BF16)
        g_y = b.dint(f"{tag}_gy", [DI, TLOC], BF16)
        c_yg = b.dint(f"{tag}_cyg", [DI, TLOC], BF16)

        with PhaseCtx(b, f"{tag}t_"):
            emit_dram_copy(b, c_tail, xr_ap[TLOC-128:TLOC, :], 128, DM, F32,
                           f"{tag}t")
        with PhaseCtx(b, f"{tag}c0_"):
            emit_coll(b, "AllGather", ALU.bypass, c_tail, ag_tail, "agt")
        with PhaseCtx(b, f"{tag}g0_"):
            emit_idx_gather_dram(b, c_xprev, ag_tail, xprev_idx, 1, DM, F32,
                                 b.bound_reg(NC * 128 - 1), f"{tag}g0",
                                 zero_fill=True)
        with PhaseCtx(b, f"{tag}d_"):
            emit_mamba_pre(b, dict(xr=xr_ap, xprev=c_xprev, wmask=m_wmask,
                                   w_inT=m_winT,
                                   normw_rs=m_nw, normb_rs=m_nb, convw=m_cw,
                                   convb_rs=m_cb, w_dtT=m_wdt, bdt_rs=m_bdt,
                                   w_BT=m_wB, w_CT=m_wC, ident_in=ident_in,
                                   o_dt=c_dt, o_u=c_u, o_xc=c_xc, o_z=c_z,
                                   o_B=c_B, o_C=c_C))
        with PhaseCtx(b, f"{tag}c1_"):
            emit_coll(b, "AllGather", ALU.bypass, c_dt, ag_dt, "agdt")
            emit_coll(b, "AllGather", ALU.bypass, c_u, ag_u, "agu")
            emit_coll(b, "AllGather", ALU.bypass, c_B, ag_B, "agB")
            emit_coll(b, "AllGather", ALU.bypass, c_C, ag_C, "agC")
        with PhaseCtx(b, f"{tag}g1_"):
            emit_idx_gather_dram(b, g_dt, ag_dt, chidx, 16, TLOC, BF16,
                                 b.bound_reg(NC * DI - 1), f"{tag}gdt")
            emit_idx_gather_dram(b, g_u, ag_u, chidx, 16, TLOC, BF16,
                                 b.bound_reg(NC * DI - 1), f"{tag}gu")
        with PhaseCtx(b, f"{tag}e_"):
            emit_mamba_scan(b, dict(
                dts=g_dt.rearrange("(r c) t -> c r t", c=256),
                us=g_u.rearrange("(r c) t -> c r t", c=256),
                Bs=ag_B, Cs=ag_C, alog=m_alog, sel16=m_sel16,
                o_y=yin.rearrange("(w c) t -> c w t", c=256)))
        with PhaseCtx(b, f"{tag}c2_"):
            emit_coll(b, "AllGather", ALU.bypass, yin, ag_y, "agy")
        with PhaseCtx(b, f"{tag}g2_"):
            emit_idx_gather_dram(b, g_y, ag_y, chidx, 16, TLOC, BF16,
                                 b.bound_reg(NC * DI - 1), f"{tag}gy")
        with PhaseCtx(b, f"{tag}f_"):
            emit_mamba_close(b, dict(yT=g_y, xcT=c_xc, zT=c_z, D_rs=m_D,
                                     woT=m_woT, xr=xr_ap, o_x=xout_ap,
                                     o_yg=c_yg))
        if debug:
            dbg_bufs.update(d_dt=(c_dt, BF16), d_xc=(c_xc, BF16),
                            d_z=(c_z, BF16), d_B=(c_B, F32), d_C=(c_C, F32),
                            d_gy=(g_y, BF16), d_yg=(c_yg, BF16),
                            d_xprev=(c_xprev, F32))

    def ffn1_phase(tag, xr_ap, xout_ap):
        c_xn = b.dint(f"{tag}_cxn", [TLOC, DM], F32)
        c_am = b.dint(f"{tag}_cam", [1, 1], F32)
        ag_am = b.dint(f"{tag}_agam", [NC, 1], F32)
        with PhaseCtx(b, f"{tag}h_"):
            emit_ffn1_ln(b, dict(xr=xr_ap, o_xn=c_xn, o_amax=c_am))
        with PhaseCtx(b, f"{tag}c_"):
            emit_coll(b, "AllGather", ALU.bypass, c_am, ag_am, "agam")
        with PhaseCtx(b, f"{tag}i_"):
            emit_ffn1_close(b, dict(xn_in=c_xn,
                                    amax8=ag_am.rearrange("r u -> u r"),
                                    wT=f_wT, bias=f_bias, xr=xr_ap,
                                    ident_in=ident_in, o_x=xout_ap), 0.5)

    def attn_phase(tag, xr_ap, xout_ap):
        c_q = b.dint(f"{tag}_cq", [DM, TLOC], BF16)
        c_k = b.dint(f"{tag}_ck", [DM, TLOC], BF16)
        c_v = b.dint(f"{tag}_cv", [DM, TLOC], BF16)
        ag_q = b.dint(f"{tag}_agq", [NC * DM, TLOC], BF16)
        ag_k = b.dint(f"{tag}_agk", [NC * DM, TLOC], BF16)
        ag_v = b.dint(f"{tag}_agv", [NC * DM, TLOC], BF16)
        c_att = b.dint(f"{tag}_catt", [8 * DK, TSEQ], BF16)
        ag_att = b.dint(f"{tag}_agatt", [NC * 8 * DK, TSEQ], BF16)
        c_aT = b.dint(f"{tag}_caT", [DM, TLOC], BF16)
        with PhaseCtx(b, f"{tag}j_"):
            emit_attn_qkv(b, dict(xr=xr_ap, wqT=a_wqT, wkT=a_wkT, wvT=a_wvT,
                                  bq_rs=a_bq, bk_rs=a_bk, bv=a_bv, pos=a_pos,
                                  ident_in=ident_in, o_q=c_q, o_k=c_k,
                                  o_v=c_v))
        with PhaseCtx(b, f"{tag}c1_"):
            emit_coll(b, "AllGather", ALU.bypass, c_q, ag_q, "agq")
            emit_coll(b, "AllGather", ALU.bypass, c_k, ag_k, "agk")
            emit_coll(b, "AllGather", ALU.bypass, c_v, ag_v, "agv")
        with PhaseCtx(b, f"{tag}k_"):
            emit_attn_core(b, dict(ag_q=ag_q, ag_k=ag_k, ag_v=ag_v,
                                   qkidx=a_qkidx, vidx=a_vidx,
                                   ident_in=ident_in, o_att=c_att))
        with PhaseCtx(b, f"{tag}c2_"):
            emit_coll(b, "AllGather", ALU.bypass, c_att, ag_att, "agat")
        with PhaseCtx(b, f"{tag}g1_"):
            emit_idx_gather_dram(b, c_aT, ag_att, aidx, 8, TLOC, BF16,
                                 b.bound_reg(NC * 8 * DK - 1), f"{tag}g1")
        with PhaseCtx(b, f"{tag}l_"):
            emit_attn_close(b, dict(aT=c_aT, woT=a_woT, bo=a_bo, xr=xr_ap,
                                    o_x=xout_ap))

    stages = [
        lambda d: moe_phase("m0", xin, d),
        lambda d: mamba_phase("mb", xb[0], d),
        lambda d: moe_phase("m1", xb[1], d),
        lambda d: ffn1_phase("f0", xb[2], d),
        lambda d: attn_phase("at", xb[3], d),
        lambda d: ffn1_phase("f1", xb[4], d),
    ]

    @body(b)
    def _():
        for si in range(n_stages):
            dst = o_x if si == n_stages - 1 else xb[si]
            stages[si](dst)
        if debug:
            with PhaseCtx(b, "dbg_"):
                for name, (buf, dt) in dbg_bufs.items():
                    o = b.dout(f"o_{name}", list(buf.tensor.shape), dt)
                    R, C = buf.tensor.shape
                    b.out_done(emit_dram_copy(b, o, buf, R, C, dt, f"dbg{name}"))
    return b.finish()


_FUSED_CACHE = {}


def _chidx(core):
    idx = np.zeros((128, 16), np.int32)
    for r in range(NC):
        for ch in range(2):
            idx[:, r*2 + ch] = r * DI + core * 256 + ch * 128 + np.arange(128)
    return idx


def fused_in_maps(inputs):
    import ml_dtypes
    I = inputs
    xf = _f32(I["L0_attn_x"]).reshape(NC, TLOC, DM)
    ident = np.eye(128, dtype=ml_dtypes.bfloat16)
    perm = _perm_cols()
    cw = _f32(I["L0_attn_conv_w"])
    common = dict(
        gate=_f32(I["L0_ffn_gate"]), ws1=_f32(I["L0_ffn_Ws1"]),
        bs1_rs=np.ascontiguousarray(_f32(I["L0_ffn_bs1"]).reshape(HD // 128, 128).T),
        ws2=_f32(I["L0_ffn_Ws2"]), bs2=_f32(I["L0_ffn_bs2"]).reshape(1, DM),
        ident=ident,
        sel16=np.repeat(np.eye(16, dtype=np.float32), 128, axis=1),
        w_inT=np.ascontiguousarray(_f32(I["L0_attn_W_in"]).T),
        normw_rs=np.ascontiguousarray(_f32(I["L0_attn_norm_w"]).reshape(DM // 128, 128).T),
        normb_rs=np.ascontiguousarray(_f32(I["L0_attn_norm_b"]).reshape(DM // 128, 128).T),
        convw=np.ascontiguousarray(cw.reshape(DI // 128, 128, 4).transpose(1, 0, 2).reshape(128, -1)),
        convb_rs=np.ascontiguousarray(_f32(I["L0_attn_conv_b"]).reshape(DI // 128, 128).T),
        w_dtT=np.ascontiguousarray(_f32(I["L0_attn_W_dt"]).T),
        bdt_rs=np.ascontiguousarray(_f32(I["L0_attn_b_dt"]).reshape(DI // 128, 128).T),
        w_BT=np.ascontiguousarray(_f32(I["L0_attn_W_B"]).T),
        w_CT=np.ascontiguousarray(_f32(I["L0_attn_W_C"]).T),
        D_rs=np.ascontiguousarray(_f32(I["L0_attn_D"]).reshape(DI // 128, 128).T),
        woT=np.ascontiguousarray(_f32(I["L0_attn_W_out"]).T),
        wT=np.ascontiguousarray(_f32(I["L1_ffn_weight"]).T),
        bias=_f32(I["L1_ffn_bias"]).reshape(1, DM),
        wqT=np.ascontiguousarray(_f32(I["L1_attn_WQ_w"]).T[:, perm]),
        wkT=np.ascontiguousarray(_f32(I["L1_attn_WK_w"]).T[:, perm]),
        wvT=np.ascontiguousarray(_f32(I["L1_attn_WV_w"]).T),
        bq_rs=np.ascontiguousarray(_f32(I["L1_attn_WQ_b"])[perm].reshape(DM // 128, 128).T),
        bk_rs=np.ascontiguousarray(_f32(I["L1_attn_WK_b"])[perm].reshape(DM // 128, 128).T),
        bv=np.ascontiguousarray(_f32(I["L1_attn_WV_b"]).reshape(DM // 128, 128).T),
        a_woT=np.ascontiguousarray(_f32(I["L1_attn_Wo_w"]).T),
        bo=_f32(I["L1_attn_Wo_b"]).reshape(1, DM),
    )
    in_maps = []
    for i in range(NC):
        t0 = (i % 2) * TLOC
        qk, vv = attn_idx_tables(i)
        if i % 2 == 1:
            xpidx = ((i - 1) * 128 + np.arange(128, dtype=np.int32)).reshape(128, 1)
        else:
            xpidx = np.full((128, 1), 1 << 28, np.int32)
        bb, hf = i // 2, i % 2
        aidx = np.zeros((128, 8), np.int32)
        for j in range(8):
            aidx[:, j] = bb * 1024 + j * 128 + np.arange(128)
        in_maps.append(dict(
            xin=np.ascontiguousarray(xf[i]),
            w1=np.ascontiguousarray(_f32(I["L0_ffn_W1"])[i]),
            b1_rs=np.ascontiguousarray(_f32(I["L0_ffn_b1"])[i].reshape(HD // 128, 128).T),
            w2=np.ascontiguousarray(_f32(I["L0_ffn_W2"])[i]),
            b2=_f32(I["L0_ffn_b2"])[i].reshape(1, DM),
            alog=np.ascontiguousarray(_f32(I["L0_attn_A_log"])[i*256:(i+1)*256]),
            xprev_idx=xpidx,
            wmask=np.full((128, 1), 1.0 if i % 2 else 0.0, np.float32),
            pos=np.arange(t0, t0 + TLOC, dtype=np.float32).reshape(1, TLOC),
            qkidx=qk, vidx=vv,
            peidx=(np.arange(NC, dtype=np.int32) * NEXP + i).reshape(NEXP, 1),
            slwidx=(np.arange(NEXP, dtype=np.int32) * NEXP + i).reshape(NEXP, 1),
            aidx=aidx,
            chidx=np.ascontiguousarray(
                (np.arange(NC, dtype=np.int32)[None, :, None] * DI + i * 256
                 + np.arange(2, dtype=np.int32)[None, None, :] * 128
                 + np.arange(128, dtype=np.int32)[:, None, None])
                .transpose(0, 1, 2).reshape(128, 16, order="F").astype(np.int32)
                if False else _chidx(i)),
            **common))
    return in_maps


def kernel(**inputs):
    if "nc" not in _FUSED_CACHE:
        _FUSED_CACHE["nc"] = build_fused()
    nc = _FUSED_CACHE["nc"]
    in_maps = fused_in_maps(inputs)
    res = run_bass_kernel_spmd(nc, in_maps, core_ids=list(range(NC)))
    x = np.concatenate([res.results[i]["o_x"] for i in range(NC)], 0)
    return np.ascontiguousarray(x.reshape(4, TSEQ, DM), dtype=np.float32)


# version 08
# speedup vs baseline: 1.0456x; 1.0456x over previous
import numpy as np
from contextlib import ExitStack
import concourse.bass as bass
import concourse.mybir as mybir
from concourse.bass_core_rust import IndirectOffsetOnAxis
from concourse.bass_utils import run_bass_kernel_spmd

F32 = mybir.dt.float32
BF16 = mybir.dt.bfloat16
I32 = mybir.dt.int32
AF = mybir.ActivationFunctionType
ALU = mybir.AluOpType
AX = mybir.AxisListType

NC = 8
TLOC = 2048
DM = 1024
DI = 2048
NSTATE = 16
NEXP = 8
CAP = 2048
HD = 4096
NH = 16
DK = 64
TSEQ = 4096
LN2 = 6.93147180559945286e-01
GC2 = 2.0 * 0.7978845608028654
GA2 = GC2 * 0.044715

_LAUNCHES = []



class Prog:

    def __init__(self, nc, ctx):
        self.nc = nc
        self.ctx = ctx
        self.sem = {}
        self.count = {}
        self.swdge = set()
        self.touched = set()
        self.phase_names = {}
        for name in ("v", "s", "p", "g"):
            self._mksem(name)

    def _mksem(self, name):
        self._serial = getattr(self, "_serial", 0) + 1
        h = self.nc.alloc_semaphore(name=f"{name}_s{self._serial}")
        self.sem[name] = h
        self.count[name] = 0

    def new_phase(self):
        self.phase_names = {}

    def _canon(self, name):
        if name in ("v", "s", "p", "g") or name in self.swdge or name == "CC":
            return name
        if name not in self.phase_names:
            canon = f"H{len(self.phase_names)}"
            self.phase_names[name] = canon
            if canon not in self.sem:
                self._mksem(canon)
        return self.phase_names[name]

    def stream(self, name):
        if name not in self.sem:
            self._mksem(name)
        return name

    def eng(self, name):
        nc = self.nc
        return {"v": nc.vector, "s": nc.scalar, "p": nc.tensor, "g": nc.gpsimd,
                "sync": nc.sync}[name]

    def _wait(self, eng_obj, deps):
        need = {}
        for d in deps:
            if d is None:
                continue
            (s, v) = d
            if v > need.get(s, -1):
                need[s] = v
        for s, v in need.items():
            if v > 0:
                eng_obj.wait_ge(self.sem[s], v)

    def op(self, ename, method, *args, deps=(), **kw):
        eng_obj = self.eng(ename)
        deps = list(deps)
        if ename in ("v", "s", "g") and self.count[ename] > 0:
            deps.append((ename, self.count[ename]))
        self._wait(eng_obj, deps)
        inst = getattr(eng_obj, method)(*args, **kw)
        self.count[ename] += 1
        inst.then_inc(self.sem[ename], 1)
        return (ename, self.count[ename])

    def dma(self, stream, out, in_, deps=(), queue="sync", **kw):
        if queue == "g":
            self.stream(stream)
            self.swdge.add(stream)
        else:
            stream = self._canon(stream)
            self.stream(stream)
        eng_obj = self.eng(queue)
        self._wait(eng_obj, deps)
        inst = eng_obj.dma_start(out=out, in_=in_, **kw)
        self.touched.add(stream)
        self.count[stream] += 16
        inst.then_inc(self.sem[stream], 16)
        return (stream, self.count[stream])

    def idma(self, stream, out, out_offset, in_, in_offset, bounds, deps=(),
             compute_op=ALU.bypass, rot=8):
        if not hasattr(self, "_rot"):
            self._rot = {}
        i = self._rot.get(stream, 0)
        self._rot[stream] = i + 1
        sub = f"{stream}~{i % rot}"
        self.stream(sub)
        self.swdge.add(sub)
        g = self.eng("g")
        deps = list(deps)
        if self.count[sub] > 0:
            deps.append((sub, self.count[sub]))
        self._wait(g, deps)
        inst = g.indirect_dma_start(out=out, out_offset=out_offset, in_=in_,
                                    in_offset=in_offset, bounds_check=bounds,
                                    oob_is_err=False, compute_op=compute_op)
        self.touched.add(sub)
        self.count[sub] += 16
        inst.then_inc(self.sem[sub], 16)
        return (sub, self.count[sub])

    def rot_toks(self, stream, rot=8):
        return [(f"{stream}~{i}", self.count.get(f"{stream}~{i}", 0))
                for i in range(rot)]

    def coll(self, stream, kind, op, ins, outs, deps=()):
        self.stream(stream)
        self.swdge.add(stream)
        g = self.eng("g")
        self._wait(g, deps)
        inst = g.collective_compute(kind, op, replica_groups=[list(range(NC))],
                                    ins=ins, outs=outs)
        self.touched.add(stream)
        self.count[stream] += 1
        inst.then_inc(self.sem[stream])
        return (stream, self.count[stream])

    def tok(self, name):
        if name not in self.sem or name in self.phase_names:
            name = self._canon(name)
        return (name, self.count.get(name, 0))


class Builder:
    def __init__(self, name):
        self.nc = bass.Bass("TRN2", target_bir_lowering=False, name=name)
        self.ctx = ExitStack()
        self.pr = Prog(self.nc, self.ctx)
        self.block = self.ctx.enter_context(self.nc.Block())
        self._final_deps = []
        self.prefix = ""
        self._scope = None

    def pname(self, name):
        return self.prefix + name

    def sub_side(self):
        return "left" if self._scope is not None else "right"

    def sb(self, name, shape, dt=F32, ctx=None, side=None):
        if ctx is None and self._scope is not None:
            ctx = self._scope
            side = "right"
        return (ctx or self.ctx).enter_context(
            self.nc.sbuf_tensor(self.pname(name), shape, dt, side=side))

    def ps(self, name, shape, dt=F32):
        ctx = self._scope if self._scope is not None else self.ctx
        return ctx.enter_context(self.nc.psum_tensor(self.pname(name), shape, dt))

    def psn(self, name, shape, dt=F32):
        return self.nc.psum_tensor(self.pname(name), shape, dt)

    def din(self, name, shape, dt=F32):
        return self.nc.dram_tensor(name, shape, dt, kind="ExternalInput").ap()

    def dout(self, name, shape, dt=F32):
        return self.nc.dram_tensor(name, shape, dt, kind="ExternalOutput").ap()

    def dint(self, name, shape, dt=F32, **kw):
        return self.nc.dram_tensor(self.pname(name), shape, dt,
                                   kind="Internal", **kw).ap()

    def out_done(self, tok):
        self._final_deps.append(tok)

    def bound_reg(self, value):
        if not hasattr(self, "_bregs"):
            self._bregs = {}
        if value not in self._bregs:
            reg = self.nc.gpsimd.alloc_register(f"bnd{value}")
            self.nc.gpsimd.reg_mov(reg, value)
            self._bregs[value] = reg
        return self._bregs[value]

    def finish(self):
        pr = self.pr

        def _waits():
            need = {}
            for (s, v) in self._final_deps:
                if s in pr.sem:
                    need[s] = max(need.get(s, 0), v)
            for s, v in need.items():
                self.nc.sync.wait_ge(pr.sem[s], v)
            for sname, cnt in list(pr.count.items()):
                if cnt > 0 and sname not in need:
                    self.nc.sync.wait_ge(pr.sem[sname], cnt)

        self.block.sync(lambda _s: _waits())
        self.ctx.close()
        return self.nc


def body(b):
    def deco(f):
        if getattr(b, "_in_body", False):
            f()
            return f
        b._in_body = True
        try:
            b.block.sync(lambda _s: f())
        finally:
            b._in_body = False
        return f
    return deco



def emit_weight_prep(b, pairs, ctx=None):
    pr = b.pr
    side = b.sub_side() if ctx is not None else None
    tf = [b.sb(f"wpf{i}", [128, 4096], F32, ctx=ctx, side=side) for i in range(2)]
    tb = [b.sb(f"wpb{i}", [128, 4096], BF16, ctx=ctx, side=side) for i in range(2)]
    seq = []
    for (src, dst, R, C) in pairs:
        for r0 in range(0, R, 128):
            rr = min(128, R - r0)
            for c0 in range(0, C, 4096):
                cc = min(4096, C - c0)
                seq.append((src, dst, r0, rr, c0, cc))
    toks = []
    cvt_by_buf = [None, None]
    st_by_buf = [None, None]
    for j, (src, dst, r0, rr, c0, cc) in enumerate(seq):
        i = j % 2
        ld = pr.dma(f"wp{i}", tf[i][0:rr, 0:cc], src[r0:r0+rr, c0:c0+cc],
                    deps=[cvt_by_buf[i]])
        cvt_by_buf[i] = pr.op("v", "tensor_copy", tb[i][0:rr, 0:cc],
                              tf[i][0:rr, 0:cc], deps=[ld, st_by_buf[i]])
        st_by_buf[i] = pr.dma(f"wps{i}", dst[r0:r0+rr, c0:c0+cc],
                              tb[i][0:rr, 0:cc], deps=[cvt_by_buf[i]])
        toks.append(st_by_buf[i])
    return [pr.tok("wps0"), pr.tok("wps1")]


def make_ln_stats(b, tag):
    st = {
        "bnout": b.sb(f"{tag}_bnout", [128, 12], F32),
        "bnag": b.sb(f"{tag}_bnag", [128, 2], F32),
        "sd": b.sb(f"{tag}_sd", [128, 1], F32),
        "rstd": b.sb(f"{tag}_rstd", [128, 1], F32),
        "negmur": b.sb(f"{tag}_negmur", [128, 1], F32),
        "eps": b.sb(f"{tag}_eps", [128, 1], F32),
    }
    t = b.pr.op("v", "memset", st["eps"][:], 1e-5)
    st["_init"] = t
    return st


def emit_ln_tile(b, st, x_tile, out_tile, deps=(), out_deps=()):
    pr = b.pr
    pr.op("v", "bn_stats", st["bnout"][:, 0:6], x_tile[:, 0:512],
          deps=list(deps) + [st["_init"]])
    pr.op("v", "bn_stats", st["bnout"][:, 6:12], x_tile[:, 512:1024], deps=deps)
    pr.op("v", "bn_aggr", st["bnag"][:], st["bnout"][:])
    t3 = pr.op("s", "activation", st["sd"][:], st["bnag"][:, 1:2], AF.Sqrt,
               bias=st["eps"][:], deps=[b.pr.tok("v")])
    t4 = pr.op("v", "reciprocal", st["rstd"][:], st["sd"][:], deps=[t3])
    t5 = pr.op("v", "scalar_tensor_tensor", st["negmur"][:],
               st["bnag"][:, 0:1], -1.0, st["rstd"][:], ALU.mult, ALU.mult)
    t6 = pr.op("s", "activation", out_tile, x_tile, AF.Identity,
               bias=st["negmur"][:], scale=st["rstd"][:],
               deps=[t5] + list(out_deps))
    return t6


def pe_fence(b):
    b.pr.eng("p").wait_ge(b.pr.sem["v"], b.pr.count["v"])


def emit_row_broadcast(b, src_row, dst_sb, C, tag, deps=()):
    pr = b.pr
    ones = b.sb(f"{tag}_ones", [1, 128], F32)
    t0 = pr.op("v", "memset", ones[:], 1.0)
    toks = []
    with b.nc.psum_tensor(b.prefix + f"{tag}_ps", [128, 512], F32) as pst:
        for c0 in range(0, C, 512):
            cc = min(512, C - c0)
            tm = pr.op("p", "matmul", pst[:, 0:cc], ones[:], src_row[0:1, c0:c0+cc],
                       start=True, stop=True, deps=list(deps) + [t0, pr.tok("v")])
            toks.append(pr.op("v", "tensor_copy", dst_sb[:, c0:c0+cc],
                              pst[:, 0:cc], deps=[tm]))
    pe_fence(b)
    return toks[-1]


def emit_transposes(b, src_row_bf, dst_col_bf, colbase, ident, psum_bf, n_dchunks,
                    dstride, deps=()):
    pr = b.pr
    last = None
    out_toks = []
    for dc in range(n_dchunks):
        tt = pr.op("p", "transpose", psum_bf[:],
                   src_row_bf[:, dc*128:(dc+1)*128], ident[:],
                   deps=list(deps) + ([last] if last else []))
        last = pr.op("v", "tensor_copy",
                     dst_col_bf[:, dc*dstride + colbase: dc*dstride + colbase + 128],
                     psum_bf[:], deps=[tt])
        out_toks.append(last)
    return out_toks


def emit_gelu_tanh_psum(b, ps_tile, bias_col, w, tmp, out_bf, deps=()):
    pr = b.pr
    t1 = pr.op("s", "activation", tmp["ub"][:, 0:w], ps_tile, AF.Identity,
               bias=bias_col, deps=deps)
    t2 = pr.op("s", "activation", tmp["sq"][:, 0:w], tmp["ub"][:, 0:w], AF.Square)
    t3 = pr.op("v", "tensor_scalar", tmp["p1"][:, 0:w], tmp["sq"][:, 0:w],
               GA2, GC2, ALU.mult, ALU.add, deps=[t2])
    t4 = pr.op("v", "tensor_mul", tmp["warg"][:, 0:w], tmp["p1"][:, 0:w],
               tmp["ub"][:, 0:w], deps=[t1])
    t5 = pr.op("s", "activation", tmp["sq"][:, 0:w], tmp["warg"][:, 0:w],
               AF.Sigmoid, deps=[t4])
    t6 = pr.op("v", "tensor_mul", out_bf, tmp["sq"][:, 0:w], tmp["ub"][:, 0:w],
               deps=[t5])
    return t6, t1



def emit_ffn0_pre(b, io):
    pr = b.pr
    xr = io["xr"]
    gate = io["gate"]
    ws1 = io["ws1"]
    bs1_rs = io["bs1_rs"]
    ws2 = io["ws2"]
    bs2 = io["bs2"]
    ident_in = io["ident_in"]

    o_xn = io["o_xn"]
    o_probs = io["o_probs"]
    o_shared = io["o_shared"]

    ws1b_d = b.dint("ws1b", [DM, HD], BF16)
    ws2b_d = b.dint("ws2b", [HD, DM], BF16)

    NT = TLOC // 128

    @body(b)
    def _main():
        wp = emit_weight_prep(b, [(ws1, ws1b_d, DM, HD), (ws2, ws2b_d, HD, DM)])
        wp_done_all = wp

        ident = b.sb("identt", [128, 128], BF16)
        t_id = pr.dma("Lid", ident[:], ident_in)
        gatetf = b.sb("gatetf", [128, 8 * NEXP], F32)
        gatet = b.sb("gatet", [128, 8 * NEXP], BF16)
        t_gate0 = pr.dma("Lg", gatetf[:].rearrange("p (k e) -> p k e", k=8),
                         gate.rearrange("(k p) e -> p k e", p=128))
        t_gate = pr.op("v", "tensor_copy", gatet[:], gatetf[:], deps=[t_gate0])
        bs1t = b.sb("bs1t", [128, HD // 128], F32)
        t_bs1 = pr.dma("Lb1", bs1t[:], bs1_rs)
        bs2row = b.sb("bs2row", [1, DM], F32)
        t_bs2 = pr.dma("Lb2", bs2row[:], bs2)
        bs2bc = b.sb("bs2bc", [128, DM], F32)
        t_bs2bc = emit_row_broadcast(b, bs2row, bs2bc, DM, "b2b", deps=[t_bs2])

        st = make_ln_stats(b, "ln")
        xt = [b.sb(f"xt{i}", [128, DM], F32) for i in range(2)]
        xnb = [b.sb(f"xnb{i}", [128, DM], BF16) for i in range(2)]
        xnT = b.sb("xnT", [128, 8 * TLOC], BF16)
        pbf = b.ps("pbf", [128, 128], BF16)

        prev_ln = {0: None, 1: None}
        prev_users = {0: (), 1: ()}
        prev_trs = []
        for i in range(NT):
            bi = i % 2
            ld = pr.dma(f"Lx{bi}", xt[bi][:], xr[i*128:(i+1)*128, :],
                        deps=[prev_ln[bi]])
            t_ln = emit_ln_tile(b, st, xt[bi][:], xnb[bi][:], deps=[ld],
                                out_deps=prev_users[bi])
            prev_ln[bi] = t_ln
            t_st = pr.dma(f"Sxn{bi}", o_xn[i*128:(i+1)*128, :], xnb[bi][:],
                          deps=[t_ln])
            trs = emit_transposes(b, xnb[bi], xnT, i*128, ident, pbf, 8, TLOC,
                                  deps=[t_ln, t_id] + prev_trs[-1:])
            prev_trs = trs
            prev_users[bi] = (t_st, trs[-1], pr.tok("p"))
        b.out_done(pr.tok("Sxn0"))
        b.out_done(pr.tok("Sxn1"))
        t_xnT = pr.tok("v")

        pgate = b.ps("pgate", [128, NEXP], F32)
        probs_t = b.sb("probs_t", [128, NEXP], F32)
        negmax = b.sb("negmax", [128, 1], F32)
        ssum = b.sb("ssum", [128, 1], F32)
        sinv = b.sb("sinv", [128, 1], F32)
        prev_gate_rd = None
        prev_sp = None
        for i in range(NT):
            tms = []
            for k in range(8):
                dep0 = [t_xnT, t_gate] + ([prev_gate_rd] if prev_gate_rd else [])
                tms.append(pr.op("p", "matmul", pgate[:],
                                 xnT[:, k*TLOC + i*128: k*TLOC + (i+1)*128],
                                 gatet[:, k*NEXP:(k+1)*NEXP],
                                 start=(k == 0), stop=(k == 7),
                                 deps=dep0 if k == 0 else ()))
            t_mx = pr.op("v", "reduce_max", negmax[:], pgate[:], axis=AX.X,
                         negate=True, deps=[tms[-1]])
            t_e = pr.op("s", "activation", probs_t[:], pgate[:], AF.Exp,
                        bias=negmax[:], accum_out=ssum[:],
                        deps=[t_mx] + ([prev_sp] if prev_sp else []))
            t_r = pr.op("v", "reciprocal", sinv[:], ssum[:], deps=[t_e])
            t_p = pr.op("v", "tensor_scalar", probs_t[:], probs_t[:], sinv[:],
                        None, ALU.mult)
            prev_gate_rd = t_e
            prev_sp = pr.dma("Sp", o_probs[i*128:(i+1)*128, :], probs_t[:], deps=[t_p])
        b.out_done(pr.tok("Sp"))

        w1t = [b.sb(f"w1t{i}", [128, 8 * 128], BF16) for i in range(2)]
        w2t = [b.sb(f"w2t{i}", [128, 512], BF16) for i in range(2)]
        hT = b.sb("hT", [128, 32 * 512], BF16)
        gtmp = {"ub": b.sb("g_ub", [128, 512], F32),
                "sq": b.sb("g_sq", [128, 512], F32),
                "p1": b.sb("g_p1", [128, 512], F32),
                "warg": b.sb("g_wa", [128, 512], F32)}
        outrow = b.sb("outrow", [128, DM], F32)
        prev_ssh = None

        for nch in range(4):
            tn0 = nch * 512
            with b.nc.psum_tensor(b.prefix + f"hps{nch}", [128, 512], F32) as hps:
                t_ld_prev = {0: (pr.tok("v"), pr.tok("s")), 1: ()}
                prev_hps_rd = pr.tok("s")
                prev_ge = None
                for m in range(32):
                    bi = m % 2
                    ldw = pr.dma(f"Lw1{bi}",
                                 w1t[bi][:].rearrange("p (k c) -> p k c", k=8),
                                 ws1b_d[:, m*128:(m+1)*128].rearrange(
                                     "(k p) c -> p k c", p=128),
                                 deps=wp_done_all + list(t_ld_prev.get(bi) or ()))
                    mt = None
                    for k in range(8):
                        mt = pr.op("p", "matmul", hps[:],
                                   w1t[bi][:, k*128:(k+1)*128],
                                   xnT[:, k*TLOC + tn0: k*TLOC + tn0 + 512],
                                   start=(k == 0), stop=(k == 7),
                                   deps=[ldw, t_xnT, prev_hps_rd] if k == 0 else ())
                    ge, prev_hps_rd = emit_gelu_tanh_psum(
                        b, hps[:], bs1t[:, m:m+1], 512,
                        gtmp, hT[:, m*512:(m+1)*512],
                        deps=[mt, t_bs1] + ([prev_ge] if prev_ge else []))
                    prev_ge = ge
                    t_ld_prev[bi] = (ge,)
            t_hT = pr.tok("v")
            with (b.nc.psum_tensor(b.prefix + f"psA0_{nch}", [128, 512], F32) as psA0,
                  b.nc.psum_tensor(b.prefix + f"psA1_{nch}", [128, 512], F32) as psA1,
                  b.nc.psum_tensor(b.prefix + f"psA2_{nch}", [128, 512], F32) as psA2,
                  b.nc.psum_tensor(b.prefix + f"psA3_{nch}", [128, 512], F32) as psA3):
                psA = [psA0, psA1, psA2, psA3]
                for dn in range(2):
                    mts = [None] * 4
                    prev_ld = {0: (pr.tok("v"), pr.tok("s")), 1: ()}
                    for k in range(32):
                        bi = k % 2
                        ldw = pr.dma(f"Lw2{bi}", w2t[bi][:],
                                     ws2b_d[k*128:(k+1)*128, dn*512:(dn+1)*512],
                                     deps=wp_done_all + list(prev_ld.get(bi) or ()))
                        for tt in range(4):
                            dps = [ldw] if tt == 0 else []
                            if k == 0:
                                dps += [t_hT]
                            mts[tt] = pr.op("p", "matmul", psA[tt][:],
                                            hT[:, k*512 + tt*128: k*512 + (tt+1)*128],
                                            w2t[bi][:],
                                            start=(k == 0), stop=(k == 31),
                                            deps=dps)
                        prev_ld[bi] = (mts[3],)
                    for tt in range(4):
                        t_o = pr.op("v", "tensor_add", outrow[:, dn*512:(dn+1)*512],
                                    psA[tt][:], bs2bc[:, dn*512:(dn+1)*512],
                                    deps=[mts[tt], t_bs2bc] +
                                         ([prev_ssh] if prev_ssh else []))
                        trow = tn0 + tt*128
                        prev_ssh = pr.dma("Ssh", o_shared[trow:trow+128, dn*512:(dn+1)*512],
                                          outrow[:, dn*512:(dn+1)*512], deps=[t_o])
        b.out_done(pr.tok("Ssh"))


def build_ffn0_pre():
    b = Builder("ffn0_pre")
    io = {}
    io["xr"] = b.din("xr", [TLOC, DM], F32)
    io["gate"] = b.din("gate", [DM, NEXP], F32)
    io["ws1"] = b.din("ws1", [DM, HD], F32)
    io["bs1_rs"] = b.din("bs1_rs", [128, HD // 128], F32)
    io["ws2"] = b.din("ws2", [HD, DM], F32)
    io["bs2"] = b.din("bs2", [1, DM], F32)
    io["ident_in"] = b.din("ident", [128, 128], BF16)
    io["o_xn"] = b.dout("o_xn", [TLOC, DM], BF16)
    io["o_probs"] = b.dout("o_probs", [TLOC, NEXP], F32)
    io["o_shared"] = b.dout("o_shared", [TLOC, DM], F32)
    emit_ffn0_pre(b, io)
    return b.finish()



def _f32(a):
    return np.ascontiguousarray(a, dtype=np.float32)


def run_ffn0_pre(x_full, I):
    import ml_dtypes
    nc = build_ffn0_pre()
    xf = x_full.reshape(NC, TLOC, DM)
    ident = np.eye(128, dtype=ml_dtypes.bfloat16)
    bs1_rs = _f32(I["L0_ffn_bs1"]).reshape(HD // 128, 128).T.copy()
    common = dict(gate=_f32(I["L0_ffn_gate"]), ws1=_f32(I["L0_ffn_Ws1"]),
                  bs1_rs=bs1_rs, ws2=_f32(I["L0_ffn_Ws2"]),
                  bs2=_f32(I["L0_ffn_bs2"]).reshape(1, DM), ident=ident)
    in_maps = [dict(xr=_f32(xf[i]), **common) for i in range(NC)]
    res = run_bass_kernel_spmd(nc, in_maps, core_ids=list(range(NC)))
    xn = np.concatenate([res.results[i]["o_xn"] for i in range(NC)], 0)
    probs = np.concatenate([res.results[i]["o_probs"] for i in range(NC)], 0)
    shared = np.concatenate([res.results[i]["o_shared"] for i in range(NC)], 0)
    return xn, probs, shared


def kernel(**inputs):
    raise NotImplementedError("under construction")



def emit_ffn0_expert(b, io):
    pr = b.pr
    pe_in = io["pe_in"]
    xn_all = io["xn_all"]
    w1 = io["w1"]
    b1_rs = io["b1_rs"]
    w2 = io["w2"]
    b2 = io["b2"]
    ident_in = io["ident_in"]

    o_eout = io["o_eout"]
    o_tau = io["o_tau"]
    o_idx = io["o_idx"]
    o_slots = io["o_slots"]

    w1b_d = b.dint("w1b", [DM, HD], BF16)
    w2b_d = b.dint("w2b", [HD, DM], BF16)
    idx_d = b.dint("idx_d", [CAP, 1], I32)
    scr_d = b.dint("scr_d", [1, 16], F32)

    NITER = 34
    BIG = float(1 << 20)
    R_NEXT = 7

    @body(b)
    def _main():
        pe_flat = pe_in.rearrange("r (t u) -> (r t) u", u=1)
        moe_ctx = ExitStack()
        wp = emit_weight_prep(b, [(w1, w1b_d, DM, HD), (w2, w2b_d, HD, DM)],
                              ctx=moe_ctx)

        ident = b.sb("identt", [128, 128], BF16)
        t_id = pr.dma("Lid", ident[:], ident_in)
        pe = b.sb("pe_t", [NEXP, TLOC], F32, ctx=moe_ctx, side=b.sub_side())
        t_pe = pr.dma("Lpe", pe[:], pe_in)

        ones88 = b.sb("ones88", [NEXP, NEXP], F32, ctx=moe_ctx, side=b.sub_side())
        onesrow = b.sb("onesrow", [NEXP, TLOC], F32, ctx=moe_ctx, side=b.sub_side())
        ident8 = b.sb("ident8", [NEXP, NEXP], F32, ctx=moe_ctx, side=b.sub_side())
        iota88 = b.sb("iota88", [NEXP, NEXP], F32, ctx=moe_ctx, side=b.sub_side())
        iota128 = b.sb("iota128", [128, 1], F32)
        pr.op("v", "memset", ones88[:], 1.0)
        pr.op("v", "memset", onesrow[:], 1.0)
        pr.op("g", "iota", iota88[:], pattern=[[-1, NEXP]], base=0,
              channel_multiplier=1, allow_small_or_imprecise_dtypes=True)
        pr.op("g", "iota", iota128[:], pattern=[[1, 1]], base=0,
              channel_multiplier=1, allow_small_or_imprecise_dtypes=True)
        t_i8 = pr.op("v", "tensor_scalar", ident8[:], iota88[:], 0.0, None,
                     ALU.is_equal, deps=[pr.tok("g")])

        lo8 = b.sb("lo8", [NEXP, 1], F32)
        hi8 = b.sb("hi8", [NEXP, 1], F32)
        mid8 = b.sb("mid8", [NEXP, 1], F32)
        sel8 = b.sb("sel8", [NEXP, 1], F32)
        nsel8 = b.sb("nsel8", [NEXP, 1], F32)
        tmp8 = b.sb("tmp8", [NEXP, 1], F32)
        cnt8 = b.sb("cnt8", [NEXP, 1], F32)
        part8 = b.sb("part8", [NEXP, 1], F32)
        cmp8 = b.sb("cmp8", [NEXP, TLOC], F32, ctx=moe_ctx, side=b.sub_side())
        pcnt8 = b.ps("pcnt8", [NEXP, 1], F32)
        pr.op("v", "memset", lo8[:], 0.0)
        pr.op("v", "memset", hi8[:], 1.0)
        pr.op("v", "memset", mid8[:], 0.5)

        def count_gt(thr8, extra=()):
            t_c = pr.op("v", "tensor_scalar", cmp8[:], pe[:], thr8[:], None,
                        ALU.is_gt, ALU.add, accum_out=part8[:],
                        deps=[t_pe] + list(extra))
            t_m = pr.op("p", "matmul", pcnt8[:], ones88[:], part8[:],
                        start=True, stop=True, deps=[t_c, pr.tok("v")])
            return pr.op("v", "tensor_copy", cnt8[:], pcnt8[:], deps=[t_m])

        for it in range(NITER):
            t_ct = count_gt(mid8)
            pr.op("v", "tensor_scalar", sel8[:], cnt8[:], float(CAP) - 0.5,
                  None, ALU.is_gt, deps=[t_ct])
            pr.op("v", "tensor_scalar", nsel8[:], sel8[:], -1.0, 1.0,
                  ALU.mult, ALU.add)
            pr.op("v", "tensor_mul", tmp8[:], sel8[:], mid8[:])
            pr.op("v", "tensor_mul", lo8[:], nsel8[:], lo8[:])
            pr.op("v", "tensor_add", lo8[:], lo8[:], tmp8[:])
            pr.op("v", "tensor_mul", tmp8[:], nsel8[:], mid8[:])
            pr.op("v", "tensor_mul", hi8[:], sel8[:], hi8[:])
            pr.op("v", "tensor_add", hi8[:], hi8[:], tmp8[:])
            pr.op("v", "tensor_add", mid8[:], lo8[:], hi8[:])
            pr.op("v", "tensor_scalar", mid8[:], mid8[:], 0.5, None, ALU.mult)

        t_clo = count_gt(lo8)
        c1 = b.sb("c1", [1, 1], F32)
        pr.op("v", "tensor_copy", c1[:], cnt8[0:1, :], deps=[t_clo])
        mv = b.sb("mv", [NEXP, TLOC], F32, ctx=moe_ctx, side=b.sub_side())
        rowmin = b.sb("rowmin", [NEXP, 1], F32, ctx=moe_ctx, side=b.sub_side())
        mrow = b.sb("mrow", [1, 16], F32)
        mj1 = b.sb("mj1", [1, 1], F32)
        mlist = b.sb("mlist", [1, R_NEXT], F32)
        thr8 = b.sb("thr8", [NEXP, 1], F32)
        pmid = b.ps("pmid", [NEXP, 1], F32)
        pr.op("v", "tensor_copy", thr8[:], lo8[:])
        for j in range(R_NEXT):
            t_c = pr.op("v", "tensor_scalar", cmp8[:], pe[:], thr8[:], None,
                        ALU.is_gt)
            pr.op("v", "tensor_mul", mv[:], cmp8[:], pe[:])
            pr.op("v", "tensor_scalar", cmp8[:], cmp8[:], -2.0, 2.0,
                  ALU.mult, ALU.add)
            t_mv = pr.op("v", "tensor_add", mv[:], mv[:], cmp8[:])
            t_rm = pr.op("v", "tensor_reduce", rowmin[:], mv[:], AX.X, ALU.min)
            t_st = pr.dma("EGs", scr_d[0:1, 0:NEXP].rearrange("u r -> r u"),
                          rowmin[:], deps=[t_rm])
            t_ld = pr.dma("EGl", mrow[0:1, 0:NEXP], scr_d[0:1, 0:NEXP], deps=[t_st])
            t_mn = pr.op("v", "tensor_reduce", mj1[:], mrow[0:1, 0:NEXP], AX.X,
                         ALU.min, deps=[t_ld])
            t_cp = pr.op("v", "tensor_copy", mlist[0:1, j:j+1], mj1[:])
            t_bb = pr.op("p", "matmul", pmid[:], ones88[0:1, :], mj1[:],
                         start=True, stop=True, deps=[t_cp, pr.tok("v")])
            pr.op("v", "tensor_copy", thr8[:], pmid[:], deps=[t_bb])
        vk = b.sb("vk", [1, 1], F32)
        ej = b.sb("ej", [1, 1], F32)
        tt_ = b.sb("tt_", [1, 1], F32)
        diff = b.sb("diff", [1, 1], F32)
        pr.op("v", "tensor_scalar", diff[:], c1[:], -float(CAP), None, ALU.add)
        pr.op("v", "memset", vk[:], 0.0)
        for j in range(R_NEXT):
            if j < R_NEXT - 1:
                pr.op("v", "tensor_scalar", ej[:], diff[:], float(j), None,
                      ALU.is_equal)
            else:
                pr.op("v", "tensor_scalar", ej[:], diff[:], float(j) - 0.5, None,
                      ALU.is_gt)
            pr.op("v", "tensor_mul", tt_[:], ej[:], mlist[0:1, j:j+1])
            pr.op("v", "tensor_add", vk[:], vk[:], tt_[:])
        vk8 = b.sb("vk8", [NEXP, 1], F32)
        t_vb = pr.op("p", "matmul", pmid[:], ones88[0:1, :], vk[:],
                     start=True, stop=True, deps=[pr.tok("v")])
        pr.op("v", "tensor_copy", vk8[:], pmid[:], deps=[t_vb])
        t_cgt = count_gt(vk8)
        cgt1 = b.sb("cgt1", [1, 1], F32)
        pr.op("v", "tensor_copy", cgt1[:], cnt8[0:1, :], deps=[t_cgt])
        tau_sb = b.sb("tau_sb", [1, 2], F32)
        pr.op("v", "tensor_copy", tau_sb[0:1, 0:1], vk[:])
        t_tau = pr.op("v", "tensor_copy", tau_sb[0:1, 1:2], cgt1[:])
        b.out_done(pr.dma("Stau", o_tau, tau_sb[:], deps=[t_tau]))

        gt = b.sb("gt", [NEXP, TLOC], F32, ctx=moe_ctx, side=b.sub_side())
        eq = b.sb("eq", [NEXP, TLOC], F32, ctx=moe_ctx, side=b.sub_side())
        pfx = b.sb("pfx", [NEXP, TLOC], F32, ctx=moe_ctx, side=b.sub_side())
        base8 = b.sb("base8", [NEXP, 1], F32)
        rowtot = b.sb("rowtot", [NEXP, 1], F32)
        brow = b.sb("brow", [1, 16], F32)
        brow2 = b.sb("brow2", [1, 16], F32)
        kq8 = b.sb("kq8", [NEXP, 1], F32)
        mask = b.sb("mask", [NEXP, TLOC], F32, ctx=moe_ctx, side=b.sub_side())
        slotp = b.sb("slotp", [NEXP, TLOC], F32, ctx=moe_ctx, side=b.sub_side())

        eg_prev = [None]
        def row_excl_bases(src_col, dst_col, extra=()):
            t1 = pr.dma("EGs", scr_d[0:1, 0:NEXP].rearrange("u r -> r u"),
                        src_col,
                        deps=list(extra) + ([eg_prev[0]] if eg_prev[0] else []))
            t2 = pr.dma("EGl", brow[0:1, 0:NEXP], scr_d[0:1, 0:NEXP], deps=[t1])
            t3 = pr.op("v", "tensor_tensor_scan", brow2[0:1, 0:NEXP],
                       onesrow[0:1, 0:NEXP], brow[0:1, 0:NEXP], 0.0,
                       ALU.mult, ALU.add, deps=[t2])
            t4 = pr.op("v", "tensor_sub", brow2[0:1, 0:NEXP], brow2[0:1, 0:NEXP],
                       brow[0:1, 0:NEXP])
            t5 = pr.dma("EGs", scr_d[0:1, 0:NEXP], brow2[0:1, 0:NEXP], deps=[t4])
            t6 = pr.dma("EGl", dst_col,
                        scr_d[0:1, 0:NEXP].rearrange("u r -> r u"), deps=[t5])
            eg_prev[0] = t6
            return t6

        t_gt = pr.op("v", "tensor_scalar", gt[:], pe[:], vk8[:], None, ALU.is_gt)
        t_eq = pr.op("v", "tensor_scalar", eq[:], pe[:], vk8[:], None, ALU.is_equal)
        pr.op("v", "tensor_tensor_scan", pfx[:], onesrow[:], eq[:], 0.0,
              ALU.mult, ALU.add)
        pr.op("v", "tensor_copy", rowtot[:], pfx[:, TLOC-1:TLOC])
        t_b = row_excl_bases(rowtot[:], base8[:], extra=[pr.tok("v")])
        pr.op("v", "tensor_scalar", pfx[:], pfx[:], base8[:], None, ALU.add,
              deps=[t_b])
        t_kq = pr.op("p", "matmul", pmid[:], ones88[0:1, :], cgt1[:],
                     start=True, stop=True, deps=[pr.tok("v")])
        pr.op("v", "tensor_copy", kq8[:], pmid[:], deps=[t_kq])
        pr.op("v", "tensor_scalar", kq8[:], kq8[:], -1.0, float(CAP),
              ALU.mult, ALU.add)
        seleq = b.sb("seleq", [NEXP, TLOC], F32, ctx=moe_ctx, side=b.sub_side())
        pr.op("v", "tensor_scalar", seleq[:], pfx[:], kq8[:], None, ALU.is_le)
        pr.op("v", "tensor_mul", seleq[:], seleq[:], eq[:])
        pr.op("v", "tensor_add", mask[:], gt[:], seleq[:])
        pr.op("v", "tensor_tensor_scan", slotp[:], onesrow[:], mask[:], 0.0,
              ALU.mult, ALU.add)
        pr.op("v", "tensor_copy", rowtot[:], slotp[:, TLOC-1:TLOC])
        t_b2 = row_excl_bases(rowtot[:], base8[:], extra=[pr.tok("v")])
        pr.op("v", "tensor_scalar", slotp[:], slotp[:], base8[:], None, ALU.add,
              deps=[t_b2])
        slotm = b.sb("slotm", [NEXP, TLOC], F32, ctx=moe_ctx, side=b.sub_side())
        pr.op("v", "tensor_scalar", slotm[:], slotp[:], -BIG, None, ALU.add)
        pr.op("v", "tensor_mul", slotm[:], slotm[:], mask[:])
        t_sm = pr.op("v", "tensor_scalar", slotm[:], slotm[:], BIG, None, ALU.add)

        b.out_done(pr.dma("Ssl", o_slots, slotm[:], deps=[t_sm]))

        ptr8 = b.ps("ptr8", [128, NEXP], F32)
        ptcol = b.sb("ptcol", [128, NEXP], F32, ctx=moe_ctx, side=b.sub_side())
        slotcol = b.sb("slotcol", [128, 1], F32)
        slot_i = b.sb("slot_i", [128, 1], I32)
        gidf = b.sb("gidf", [128, 1], F32)
        gid_i = b.sb("gid_i", [128, 1], I32)
        prev_sc = None
        for s in range(16):
            t_tr = pr.op("p", "transpose", ptr8[:], slotm[:, s*128:(s+1)*128],
                         ident8[:], deps=[t_sm, t_i8, pr.tok("v")])
            t_pc = pr.op("v", "tensor_copy", ptcol[:], ptr8[:], deps=[t_tr])
            for r in range(NEXP):
                pr.op("v", "tensor_scalar", slotcol[:], ptcol[:, r:r+1], -1.0,
                      None, ALU.add,
                      deps=[prev_sc] if prev_sc else [])
                t_si = pr.op("v", "tensor_copy", slot_i[:], slotcol[:])
                pr.op("v", "tensor_scalar", gidf[:], iota128[:],
                      float(r * TLOC + s * 128), None, ALU.add)
                t_gi = pr.op("v", "tensor_copy", gid_i[:], gidf[:])
                prev_sc = pr.idma("SCAT", idx_d,
                                  IndirectOffsetOnAxis(ap=slot_i[:], axis=0),
                                  gid_i[:], None, b.bound_reg(CAP - 1),
                                  deps=[t_si, t_gi])
        t_scat_l = pr.rot_toks("SCAT")
        moe_ctx.close()
        drain = [pr.tok("Ssl")]
        drain_t = b.sb("drain_t", [1, 1], F32)
        t_dv = pr.op("v", "memset", drain_t[:], 0.0, deps=drain)
        pr.op("s", "activation", drain_t[:], drain_t[:], AF.Identity,
              deps=[t_dv])

        idx_sb = b.sb("idx_sb", [128, 16], I32)
        t_idx = pr.dma("Lidx", idx_sb[:],
                       idx_d.rearrange("(s p) one -> p (s one)", p=128),
                       deps=t_scat_l + drain, queue="g",
                       allow_slow_non_contiguous=True)
        b.out_done(pr.dma("Sidx", o_idx, idx_sb[:], deps=[t_idx] + drain))
        tok_sb = b.sb("tok_sb", [128, 16 * DM], BF16)
        scores = b.sb("scores", [128, 16], F32)
        gt_toks = []
        for s in range(16):
            gt_toks.append(pr.idma("GATH", tok_sb[:, s*DM:(s+1)*DM], None,
                                   xn_all, IndirectOffsetOnAxis(ap=idx_sb[:, s:s+1], axis=0),
                                   b.bound_reg(NC * TLOC - 1), deps=[t_idx]))
            gt_toks.append(pr.idma("GATH", scores[:, s:s+1], None,
                                   pe_flat,
                                   IndirectOffsetOnAxis(ap=idx_sb[:, s:s+1], axis=0),
                                   b.bound_reg(NC * TLOC - 1), deps=[t_idx]))
        t_gath_l = pr.rot_toks("GATH")

        tokT = b.sb("tokT", [128, 8 * CAP], BF16)
        pbf2 = b.ps("pbf2", [128, 128], BF16)
        prev_tr = []
        for s in range(16):
            prev_tr = emit_transposes(b, tok_sb[:, s*DM:(s+1)*DM], tokT, s*128,
                                      ident, pbf2, 8, CAP,
                                      deps=t_gath_l + [t_id] + prev_tr[-1:])
        t_tokT = pr.tok("v")

        b1t = b.sb("b1t", [128, HD // 128], F32)
        t_b1 = pr.dma("Lb1", b1t[:], b1_rs)
        b2row = b.sb("b2row", [1, DM], F32)
        t_b2r = pr.dma("Lb2", b2row[:], b2)
        b2bc = b.sb("b2bc", [128, DM], F32)
        t_b2bc = emit_row_broadcast(b, b2row, b2bc, DM, "b2be", deps=[t_b2r])

        w1t = [b.sb(f"w1t{i}", [128, 8 * 128], BF16) for i in range(2)]
        w2t = [b.sb(f"w2t{i}", [128, 512], BF16) for i in range(2)]
        hT = b.sb("hT", [128, 32 * 512], BF16)
        gtmp = {"ub": b.sb("g_ub", [128, 512], F32),
                "sq": b.sb("g_sq", [128, 512], F32),
                "p1": b.sb("g_p1", [128, 512], F32),
                "warg": b.sb("g_wa", [128, 512], F32)}
        outrow = b.sb("outrow", [128, DM], F32)
        outsc = b.sb("outsc", [128, DM], F32)
        prev_ssh = None

        for nch in range(4):
            tn0 = nch * 512
            with b.nc.psum_tensor(b.prefix + f"ehps{nch}", [128, 512], F32) as hps:
                t_ld_prev = {0: (pr.tok("v"), pr.tok("s")), 1: ()}
                prev_hps_rd = pr.tok("s")
                prev_ge = None
                for m in range(32):
                    bi = m % 2
                    ldw = pr.dma(f"Lw1{bi}",
                                 w1t[bi][:].rearrange("p (k c) -> p k c", k=8),
                                 w1b_d[:, m*128:(m+1)*128].rearrange(
                                     "(k p) c -> p k c", p=128),
                                 deps=wp + list(t_ld_prev.get(bi) or ()))
                    mt = None
                    for k in range(8):
                        mt = pr.op("p", "matmul", hps[:],
                                   w1t[bi][:, k*128:(k+1)*128],
                                   tokT[:, k*CAP + tn0: k*CAP + tn0 + 512],
                                   start=(k == 0), stop=(k == 7),
                                   deps=[ldw, t_tokT, prev_hps_rd] if k == 0 else ())
                    ge, prev_hps_rd = emit_gelu_tanh_psum(
                        b, hps[:], b1t[:, m:m+1], 512,
                        gtmp, hT[:, m*512:(m+1)*512],
                        deps=[mt, t_b1] + ([prev_ge] if prev_ge else []))
                    prev_ge = ge
                    t_ld_prev[bi] = (ge,)
            t_hT = pr.tok("v")
            with (b.nc.psum_tensor(b.prefix + f"epsA0_{nch}", [128, 512], F32) as psA0,
                  b.nc.psum_tensor(b.prefix + f"epsA1_{nch}", [128, 512], F32) as psA1,
                  b.nc.psum_tensor(b.prefix + f"epsA2_{nch}", [128, 512], F32) as psA2,
                  b.nc.psum_tensor(b.prefix + f"epsA3_{nch}", [128, 512], F32) as psA3):
                psA = [psA0, psA1, psA2, psA3]
                for dn in range(2):
                    mts = [None] * 4
                    prev_ld = {0: (pr.tok("v"), pr.tok("s")), 1: ()}
                    for k in range(32):
                        bi = k % 2
                        ldw = pr.dma(f"Lw2{bi}", w2t[bi][:],
                                     w2b_d[k*128:(k+1)*128, dn*512:(dn+1)*512],
                                     deps=wp + list(prev_ld.get(bi) or ()))
                        for tt in range(4):
                            dps = [ldw] if tt == 0 else []
                            if k == 0:
                                dps += [t_hT]
                            mts[tt] = pr.op("p", "matmul", psA[tt][:],
                                            hT[:, k*512 + tt*128: k*512 + (tt+1)*128],
                                            w2t[bi][:],
                                            start=(k == 0), stop=(k == 31),
                                            deps=dps)
                        prev_ld[bi] = (mts[3],)
                    for tt in range(4):
                        stile = nch * 4 + tt
                        t_o = pr.op("v", "tensor_add", outrow[:, dn*512:(dn+1)*512],
                                    psA[tt][:], b2bc[:, dn*512:(dn+1)*512],
                                    deps=[mts[tt], t_b2bc])
                        t_s = pr.op("v", "tensor_scalar", outsc[:, dn*512:(dn+1)*512],
                                    outrow[:, dn*512:(dn+1)*512],
                                    scores[:, stile:stile+1], None, ALU.mult,
                                    deps=t_gath_l + ([prev_ssh] if prev_ssh else []))
                        srow = stile * 128
                        prev_ssh = pr.dma("Seo", o_eout[srow:srow+128, dn*512:(dn+1)*512],
                                          outsc[:, dn*512:(dn+1)*512], deps=[t_s])
        b.out_done(pr.tok("Seo"))


def build_ffn0_expert():
    b = Builder("ffn0_expert")
    io = {}
    io["pe_in"] = b.din("pe", [NEXP, TLOC], F32)
    io["xn_all"] = b.din("xn_all", [NC * TLOC, DM], BF16)
    io["w1"] = b.din("w1", [DM, HD], F32)
    io["b1_rs"] = b.din("b1_rs", [128, HD // 128], F32)
    io["w2"] = b.din("w2", [HD, DM], F32)
    io["b2"] = b.din("b2", [1, DM], F32)
    io["ident_in"] = b.din("ident", [128, 128], BF16)
    io["o_eout"] = b.dout("o_eout", [CAP, DM], F32)
    io["o_tau"] = b.dout("o_tau", [1, 2], F32)
    io["o_idx"] = b.dout("o_idx", [128, 16], I32)
    io["o_slots"] = b.dout("o_slots", [NEXP, TLOC], F32)
    emit_ffn0_expert(b, io)
    return b.finish()


def emit_ffn0_close(b, io):
    pr = b.pr
    xr = io["xr"]
    shared = io["shared"]
    slots_win = io["slots_win"]
    eout_all = io.get("eout_all")
    eouts = None if eout_all is not None else [io[f"eout{e}"] for e in range(NEXP)]

    o_x = io["o_x"]

    @body(b)
    def _main():
        sw = b.sb("sw", [NEXP, TLOC], F32)
        t_sw = pr.dma("Lsw", sw[:], slots_win)
        ident8 = b.sb("ident8", [NEXP, NEXP], F32)
        iota88 = b.sb("iota88", [NEXP, NEXP], F32)
        pr.op("g", "iota", iota88[:], pattern=[[-1, NEXP]], base=0,
              channel_multiplier=1, allow_small_or_imprecise_dtypes=True)
        t_i8 = pr.op("v", "tensor_scalar", ident8[:], iota88[:], 0.0, None,
                     ALU.is_equal, deps=[pr.tok("g")])
        ptr8 = b.ps("ptr8", [128, NEXP], F32)
        ptcol = b.sb("ptcol", [128, NEXP], F32)
        slotcol = b.sb("slotcol", [128, 1], F32)
        slot_i = b.sb("slot_i", [128, 1], I32)
        xt = b.sb("xt", [128, DM], F32)
        acc = b.sb("acc", [128, DM], F32)
        outt = b.sb("outt", [128, DM], F32)

        prev_users = []
        for i in range(TLOC // 128):
            t_tr = pr.op("p", "transpose", ptr8[:], sw[:, i*128:(i+1)*128],
                         ident8[:], deps=[t_sw, t_i8] + prev_users)
            t_pc = pr.op("v", "tensor_copy", ptcol[:], ptr8[:], deps=[t_tr])
            t_ld = pr.dma(f"Lx{i%2}", xt[:], xr[i*128:(i+1)*128, :],
                          deps=prev_users)
            t_sh = pr.dma(f"Lsh{i%2}", acc[:], shared[i*128:(i+1)*128, :],
                          deps=prev_users)
            gath_toks = []
            for e in range(NEXP):
                if eout_all is not None:
                    t_sc = pr.op("v", "tensor_scalar", slotcol[:], ptcol[:, e:e+1],
                                 float(e * CAP - 1), None, ALU.add,
                                 deps=gath_toks[-1:])
                    t_si = pr.op("v", "tensor_copy", slot_i[:], slotcol[:])
                    gath_toks.append(pr.idma("GA", acc[:], None, eout_all,
                                             IndirectOffsetOnAxis(ap=slot_i[:], axis=0),
                                             b.bound_reg((e + 1) * CAP - 1),
                                             deps=[t_si, t_sh],
                                             compute_op=ALU.add))
                    continue
                t_sc = pr.op("v", "tensor_scalar", slotcol[:], ptcol[:, e:e+1],
                             -1.0, None, ALU.add,
                             deps=gath_toks[-1:])
                t_si = pr.op("v", "tensor_copy", slot_i[:], slotcol[:])
                gath_toks.append(pr.idma("GA", acc[:], None, eouts[e],
                                         IndirectOffsetOnAxis(ap=slot_i[:], axis=0),
                                         b.bound_reg(CAP - 1),
                                         deps=[t_si, t_sh],
                                         compute_op=ALU.add))
            t_o = pr.op("v", "scalar_tensor_tensor", outt[:], acc[:], 0.5,
                        xt[:], ALU.mult, ALU.add,
                        deps=[t_ld] + gath_toks)
            t_st = pr.dma("Sx", o_x[i*128:(i+1)*128, :], outt[:], deps=[t_o])
            prev_users = [t_st, t_o]
        b.out_done(pr.tok("Sx"))


def build_ffn0_close():
    b = Builder("ffn0_close")
    io = {}
    io["xr"] = b.din("xr", [TLOC, DM], F32)
    io["shared"] = b.din("shared", [TLOC, DM], F32)
    io["slots_win"] = b.din("slots_win", [NEXP, TLOC], F32)
    io["o_x"] = b.dout("o_x", [TLOC, DM], F32)
    for e in range(NEXP):
        io[f"eout{e}"] = b.din(f"eout{e}", [CAP, DM], F32)
    emit_ffn0_close(b, io)
    return b.finish()

def run_moe_layer(x_full, I, pref):
    import ml_dtypes
    xf = x_full.reshape(NC, TLOC, DM)
    ident = np.eye(128, dtype=ml_dtypes.bfloat16)

    ncA = build_ffn0_pre()
    bs1_rs = _f32(I[pref + "bs1"]).reshape(HD // 128, 128).T.copy()
    common = dict(gate=_f32(I[pref + "gate"]), ws1=_f32(I[pref + "Ws1"]),
                  bs1_rs=bs1_rs, ws2=_f32(I[pref + "Ws2"]),
                  bs2=_f32(I[pref + "bs2"]).reshape(1, DM), ident=ident)
    in_maps = [dict(xr=_f32(xf[i]), **common) for i in range(NC)]
    resA = run_bass_kernel_spmd(ncA, in_maps, core_ids=list(range(NC)))
    xn = np.concatenate([resA.results[i]["o_xn"] for i in range(NC)], 0)
    probs = np.concatenate([resA.results[i]["o_probs"] for i in range(NC)], 0)
    sharedf = np.concatenate([resA.results[i]["o_shared"] for i in range(NC)], 0)

    ncB = build_ffn0_expert()
    in_maps = []
    for e in range(NC):
        b1_rs = np.ascontiguousarray(_f32(I[pref + "b1"])[e].reshape(HD // 128, 128).T)
        in_maps.append(dict(
            pe=np.ascontiguousarray(probs[:, e].reshape(NC, TLOC)),
            xn_all=np.ascontiguousarray(xn),
            w1=np.ascontiguousarray(_f32(I[pref + "W1"])[e]), b1_rs=b1_rs,
            w2=np.ascontiguousarray(_f32(I[pref + "W2"])[e]),
            b2=_f32(I[pref + "b2"])[e].reshape(1, DM), ident=ident))
    resB = run_bass_kernel_spmd(ncB, in_maps, core_ids=list(range(NC)))
    eouts = [np.ascontiguousarray(resB.results[e]["o_eout"]) for e in range(NC)]
    slots = np.stack([resB.results[e]["o_slots"].reshape(-1) for e in range(NC)])

    ncC = build_ffn0_close()
    in_maps = []
    for i in range(NC):
        im = dict(xr=_f32(xf[i]),
                  shared=np.ascontiguousarray(sharedf[i*TLOC:(i+1)*TLOC]),
                  slots_win=np.ascontiguousarray(slots[:, i*TLOC:(i+1)*TLOC]))
        for e in range(NC):
            im[f"eout{e}"] = eouts[e]
        in_maps.append(im)
    resC = run_bass_kernel_spmd(ncC, in_maps, core_ids=list(range(NC)))
    x1 = np.concatenate([resC.results[i]["o_x"] for i in range(NC)], 0)
    return x1.reshape(x_full.shape)


def emit_mamba_pre(b, io):
    pr = b.pr
    xr = io["xr"]
    xprev = io["xprev"]
    w_inT = io["w_inT"]
    normw_rs = io["normw_rs"]
    normb_rs = io["normb_rs"]
    convw = io["convw"]
    convb_rs = io["convb_rs"]
    w_dtT = io["w_dtT"]
    bdt_rs = io["bdt_rs"]
    w_BT = io["w_BT"]
    w_CT = io["w_CT"]
    ident_in = io["ident_in"]

    o_dt = io["o_dt"]
    o_u = io["o_u"]
    o_xc = io["o_xc"]
    o_z = io["o_z"]
    o_B = io["o_B"]
    o_C = io["o_C"]

    w_inb_d = b.dint("w_inb", [DM, 2 * DI], BF16)
    w_dtb_d = b.dint("w_dtb", [DI, DI], BF16)
    NT = TLOC // 128

    @body(b)
    def _main():
        xnT_ctx = ExitStack()
        W = 128 * (NT + 1)
        xnT = b.sb("xnT", [128, 8 * W], BF16, ctx=xnT_ctx, side=b.sub_side())
        prep_ctx = ExitStack()
        normw = b.sb("normw", [128, DM // 128], F32)
        t_nw = pr.dma("Lnw", normw[:], normw_rs)
        wpf = b.sb("wpf", [128, 4096], F32, ctx=prep_ctx, side=b.sub_side())
        wpb = b.sb("wpb", [128, 4096], BF16, ctx=prep_ctx, side=b.sub_side())
        wp_toks = []
        prev = None
        for r in range(DM // 128):
            ld = pr.dma("WP0", wpf[:], w_inT[r*128:(r+1)*128, 0:4096],
                        deps=[prev])
            t_s = pr.op("v", "tensor_scalar", wpb[:], wpf[:],
                        normw[:, r:r+1], None, ALU.mult, deps=[ld, t_nw])
            prev = pr.dma("WP1", w_inb_d[r*128:(r+1)*128, 0:4096], wpb[:],
                          deps=[t_s])
            wp_toks.append(prev)
        wp1 = [pr.tok("WP0"), pr.tok("WP1")]
        wp2 = emit_weight_prep(b, [(w_dtT, w_dtb_d, DI, DI)], ctx=prep_ctx)
        wp_all = wp1 + wp2
        prep_ctx.close()

        ident = b.sb("identt", [128, 128], BF16)
        t_id = pr.dma("Lid", ident[:], ident_in)

        normb = b.sb("normb", [128, DM // 128], F32)
        t_nb = pr.dma("Lnb", normb[:], normb_rs)
        normb_bf = b.sb("normb_bf", [128, DM // 128], BF16)
        nwinv = b.sb("nwinv", [128, DM // 128], F32)
        pr.op("v", "reciprocal", nwinv[:], normw[:], deps=[t_nw])
        pr.op("v", "tensor_mul", nwinv[:], nwinv[:], normb[:], deps=[t_nb])
        t_nbb = pr.op("v", "tensor_copy", normb_bf[:], nwinv[:])
        bias_e = b.sb("bias_e", [128, 2 * DI // 128], F32)
        wtile = [b.sb(f"wtile{i}", [128, 128], BF16) for i in range(2)]
        pbias = b.ps("pbias", [128, 1], F32)
        prev_ev = None
        mts = []
        for ec in range(2 * DI // 128):
            for k in range(DM // 128):
                bi = k % 2
                ldw = pr.dma(f"LB{bi}", wtile[bi][:],
                             w_inb_d[k*128:(k+1)*128, ec*128:(ec+1)*128],
                             deps=wp_all + mts[-2:-1] +
                                  ([prev_ev] if prev_ev else []))
                mts.append(pr.op("p", "matmul", pbias[:], wtile[bi][:],
                                 normb_bf[:, k:k+1],
                                 start=(k == 0), stop=(k == DM // 128 - 1),
                                 deps=[ldw, t_nbb] + ([prev_ev] if prev_ev and k == 0 else [])))
            prev_ev = pr.op("v", "tensor_copy", bias_e[:, ec:ec+1], pbias[:],
                            deps=[mts[-1]])
        t_bias = prev_ev

        st = make_ln_stats(b, "ln")
        xt = [b.sb(f"xt{i}", [128, DM], F32) for i in range(2)]
        xnb = [b.sb(f"xnb{i}", [128, DM], BF16) for i in range(2)]
        pbf = b.ps("pbf", [128, 128], BF16)
        prev_ln = {0: None, 1: None}
        prev_users = {0: (), 1: ()}
        prev_trs = []
        for i in range(NT + 1):
            bi = i % 2
            src = xprev if i == 0 else xr[(i-1)*128:i*128, :]
            ld = pr.dma(f"Lx{bi}", xt[bi][:], src, deps=[prev_ln[bi]])
            t_ln = emit_ln_tile(b, st, xt[bi][:], xnb[bi][:], deps=[ld],
                                out_deps=prev_users[bi])
            prev_ln[bi] = t_ln
            trs = emit_transposes(b, xnb[bi], xnT, i*128, ident, pbf, 8, W,
                                  deps=[t_ln, t_id] + prev_trs[-1:])
            prev_trs = trs
            prev_users[bi] = (trs[-1], pr.tok("p"))
        t_xnT = pr.tok("v")

        convw_t = b.sb("convw_t", [128, DI // 128 * 4], F32)
        t_cw = pr.dma("Lcw", convw_t[:], convw)
        wm = b.sb("wm", [128, 1], F32)
        t_wm = pr.dma("Lwm", wm[:], io["wmask"])
        convb_t = b.sb("convb_t", [128, DI // 128], F32)
        t_cb = pr.dma("Lcb", convb_t[:], convb_rs)
        w1t = [b.sb(f"w1t{i}", [128, 8 * 128], BF16) for i in range(2)]
        xpc = b.sb("xpc", [128, W], F32)
        cacc = b.sb("cacc", [128, TLOC], F32)
        xcT_bf = b.sb("xcT_bf", [128, 16 * TLOC], BF16)
        zbf = b.sb("zbf", [128, TLOC], BF16)

        t_ld_prev = {0: (), 1: ()}
        prev_xpc = None
        prev_z = None
        prev_t_z = None
        for m in range(2 * DI // 128):
            bi = m % 2
            ldw = pr.dma(f"Lw1{bi}",
                         w1t[bi][:].rearrange("p (k c) -> p k c", k=8),
                         w_inb_d[:, m*128:(m+1)*128].rearrange(
                             "(k p) c -> p k c", p=128),
                         deps=wp_all + list(t_ld_prev.get(bi) or ()))
            is_x = m < 16
            with b.nc.psum_tensor(b.prefix + f"dps{m}", [128, 512], F32) as dps:
                for nch in range(W // 512 if not is_x else 0):
                    pass
                n_chunks = W // 512 if is_x else TLOC // 512
                for nch in range(n_chunks + (1 if is_x else 0)):
                    if is_x:
                        c0 = nch * 512
                        cw = min(512, W - c0)
                    else:
                        c0 = 128 + nch * 512
                        cw = 512
                    if cw <= 0:
                        continue
                    mt = None
                    for k in range(8):
                        mt = pr.op("p", "matmul", dps[:, 0:cw],
                                   w1t[bi][:, k*128:(k+1)*128],
                                   xnT[:, k*W + c0: k*W + c0 + cw],
                                   start=(k == 0), stop=(k == 7),
                                   deps=([ldw, t_xnT, t_bias] +
                                         ([prev_xpc] if prev_xpc else []) +
                                         ([prev_z] if prev_z else []))
                                        if k == 0 else ())
                    if is_x:
                        prev_xpc = pr.op("s", "activation", xpc[:, c0:c0+cw],
                                         dps[:, 0:cw], AF.Identity,
                                         bias=bias_e[:, m:m+1], deps=[mt])
                    else:
                        prev_z = pr.op("s", "activation", zbf[:, c0-128:c0-128+cw],
                                       dps[:, 0:cw], AF.Identity,
                                       bias=bias_e[:, m:m+1],
                                       deps=[mt] + ([prev_t_z] if prev_t_z else []))
            if is_x:
                t_msk = pr.op("v", "tensor_scalar", xpc[:, 125:128],
                              xpc[:, 125:128], wm[:], None, ALU.mult,
                              deps=[prev_xpc, t_wm])
                taps = []
                for k in range(4):
                    o0 = 128 - 3 + k
                    wk = convw_t[:, m*4 + k: m*4 + k + 1]
                    if k == 0:
                        taps.append(pr.op("v", "tensor_scalar", cacc[:],
                                          xpc[:, o0:o0+TLOC], wk, None,
                                          ALU.mult, deps=[t_msk, t_cw]))
                    else:
                        taps.append(pr.op("v", "scalar_tensor_tensor", cacc[:],
                                          xpc[:, o0:o0+TLOC], wk, cacc[:],
                                          ALU.mult, ALU.add))
                t_sl = pr.op("s", "activation", xcT_bf[:, m*TLOC:(m+1)*TLOC],
                             cacc[:], AF.Silu, bias=convb_t[:, m:m+1],
                             deps=[taps[-1], t_cb])
                t_xc = pr.dma("Sxc", o_xc[m*128:(m+1)*128, :],
                              xcT_bf[:, m*TLOC:(m+1)*TLOC], deps=[t_sl])
                t_ld_prev[bi] = (t_sl,)
            else:
                t_z = pr.dma("Sz", o_z[(m-16)*128:(m-15)*128, :], zbf[:],
                             deps=[prev_z])
                prev_t_z = t_z
                t_ld_prev[bi] = (prev_z, t_z)
        b.out_done(pr.tok("Sxc"))
        b.out_done(pr.tok("Sz"))
        t_xc_all = pr.tok("s")
        xnT_ctx.close()

        bdt = b.sb("bdt", [128, DI // 128], F32)
        t_bdt = pr.dma("Lbdt", bdt[:], bdt_rs)
        onec = b.sb("onec", [128, 1], F32)
        pr.op("v", "memset", onec[:], 1.0)
        w2t = [b.sb(f"w2t{i}", [128, 16 * 128], BF16) for i in range(2)]
        ub = b.sb("ub", [128, 512], F32)
        ab = b.sb("ab", [128, 512], F32)
        dtv = b.sb("dtv", [128, 512], F32)
        dtb16 = b.sb("dtb16", [128, TLOC], BF16)
        ub16 = b.sb("ub16", [128, TLOC], BF16)
        xcf = b.sb("xcf", [128, 512], F32)
        t_ld_prev = {0: (), 1: ()}
        for m in range(DI // 128):
            bi = m % 2
            ldw = pr.dma(f"Lw2{bi}",
                         w2t[bi][:].rearrange("p (k c) -> p k c", k=16),
                         w_dtb_d[:, m*128:(m+1)*128].rearrange(
                             "(k p) c -> p k c", p=128),
                         deps=wp2 + list(t_ld_prev.get(bi) or ()))
            with b.nc.psum_tensor(b.prefix + f"tps{m}", [128, 512], F32) as tps:
                last_u = None
                for nch in range(TLOC // 512):
                    c0 = nch * 512
                    mt = None
                    for k in range(16):
                        mt = pr.op("p", "matmul", tps[:],
                                   w2t[bi][:, k*128:(k+1)*128],
                                   xcT_bf[:, k*TLOC + c0: k*TLOC + c0 + 512],
                                   start=(k == 0), stop=(k == 15),
                                   deps=[ldw, t_xc_all] +
                                        ([last_u] if last_u else [])
                                        if k == 0 else ())
                    t1 = pr.op("s", "activation", ub[:], tps[:], AF.Identity,
                               bias=bdt[:, m:m+1], deps=[mt, t_bdt])
                    t2 = pr.op("s", "activation", ab[:], ub[:], AF.Abs)
                    t3 = pr.op("s", "activation", ab[:], ab[:], AF.Exp, scale=-1.0)
                    t4 = pr.op("s", "activation", ab[:], ab[:], AF.Ln,
                               bias=onec[:], deps=[pr.tok("v")])
                    t5 = pr.op("v", "scalar_tensor_tensor", dtv[:], ub[:], 0.0,
                               ab[:], ALU.max, ALU.add, deps=[t4])
                    t6 = pr.op("v", "tensor_copy", dtb16[:, c0:c0+512], dtv[:],
                               deps=list(t_ld_prev.get("st") or ()))
                    t7 = pr.op("v", "tensor_copy", xcf[:],
                               xcT_bf[:, m*TLOC + c0: m*TLOC + c0 + 512])
                    t8 = pr.op("v", "tensor_mul", ub16[:, c0:c0+512], dtv[:],
                               xcf[:])
                    last_u = t8
                t_sd = pr.dma("Sdt", o_dt[m*128:(m+1)*128, :], dtb16[:],
                              deps=[pr.tok("v")])
                t_su = pr.dma("Su", o_u[m*128:(m+1)*128, :], ub16[:],
                              deps=[pr.tok("v")])
            pe_fence(b)
            t_ld_prev[bi] = (t_sd, t_su)
            t_ld_prev["st"] = (t_sd, t_su)
        b.out_done(pr.tok("Sdt"))
        b.out_done(pr.tok("Su"))

        wBt = b.sb("wBt", [128, 16 * NSTATE], BF16)
        wCt = b.sb("wCt", [128, 16 * NSTATE], BF16)
        wBf = b.sb("wBf", [128, 16 * NSTATE], F32)
        t_wb0 = pr.dma("LwB", wBf[:].rearrange("p (k c) -> p k c", k=16),
                       w_BT.rearrange("(k p) c -> p k c", p=128))
        t_wb = pr.op("v", "tensor_copy", wBt[:], wBf[:], deps=[t_wb0])
        t_wc0 = pr.dma("LwC", wBf[:].rearrange("p (k c) -> p k c", k=16),
                       w_CT.rearrange("(k p) c -> p k c", p=128), deps=[t_wb])
        t_wc = pr.op("v", "tensor_copy", wCt[:], wBf[:], deps=[t_wc0])
        bct = b.sb("bct", [NSTATE, TLOC], F32)
        with b.nc.psum_tensor(b.prefix + "bps", [NSTATE, 512], F32) as bps:
            prev_e = None
            prev_store = None
            for (wt, wtok, odst, sname) in ((wBt, t_wb, o_B, "SB"), (wCt, t_wc, o_C, "SC")):
                for nch in range(TLOC // 512):
                    c0 = nch * 512
                    mt = None
                    for k in range(16):
                        mt = pr.op("p", "matmul", bps[:],
                                   wt[:, k*NSTATE:(k+1)*NSTATE],
                                   xcT_bf[:, k*TLOC + c0: k*TLOC + c0 + 512],
                                   start=(k == 0), stop=(k == 15),
                                   deps=[wtok, t_xc_all] +
                                        ([prev_e] if prev_e else [])
                                        if k == 0 else ())
                    prev_e = pr.op("v", "tensor_copy", bct[:, c0:c0+512], bps[:],
                                   deps=[mt] + ([prev_store] if prev_store and nch == 0 else []))
                prev_store = pr.dma(sname, odst, bct[:], deps=[prev_e])
        b.out_done(pr.tok("SB"))
        b.out_done(pr.tok("SC"))


def build_mamba_pre():
    b = Builder("mamba_pre")
    io = {}
    io["xr"] = b.din("xr", [TLOC, DM], F32)
    io["xprev"] = b.din("xprev", [128, DM], F32)
    io["wmask"] = b.din("wmask", [128, 1], F32)
    io["w_inT"] = b.din("w_inT", [DM, 2 * DI], F32)
    io["normw_rs"] = b.din("normw_rs", [128, DM // 128], F32)
    io["normb_rs"] = b.din("normb_rs", [128, DM // 128], F32)
    io["convw"] = b.din("convw", [128, DI // 128 * 4], F32)
    io["convb_rs"] = b.din("convb_rs", [128, DI // 128], F32)
    io["w_dtT"] = b.din("w_dtT", [DI, DI], F32)
    io["bdt_rs"] = b.din("bdt_rs", [128, DI // 128], F32)
    io["w_BT"] = b.din("w_BT", [DI, NSTATE], F32)
    io["w_CT"] = b.din("w_CT", [DI, NSTATE], F32)
    io["ident_in"] = b.din("ident", [128, 128], BF16)
    io["o_dt"] = b.dout("o_dt", [DI, TLOC], BF16)
    io["o_u"] = b.dout("o_u", [DI, TLOC], BF16)
    io["o_xc"] = b.dout("o_xc", [DI, TLOC], BF16)
    io["o_z"] = b.dout("o_z", [DI, TLOC], BF16)
    io["o_B"] = b.dout("o_B", [NSTATE, TLOC], F32)
    io["o_C"] = b.dout("o_C", [NSTATE, TLOC], F32)
    emit_mamba_pre(b, io)
    return b.finish()


def emit_mamba_scan(b, io):
    pr = b.pr
    TG = 4 * TSEQ
    dts = io["dts"]
    us = io["us"]
    Bs = io["Bs"]
    Cs = io["Cs"]
    alog = io["alog"]
    o_y = io["o_y"]
    HALF = TSEQ // 2

    @body(b)
    def _main():
        sel = b.sb("sel", [NSTATE, NSTATE * 128], F32)
        t_sel = pr.dma("Lsel", sel[:], io["sel16"])
        Ball = [b.sb(f"Ball{i}", [NSTATE, HALF], F32) for i in range(2)]
        Call = [b.sb(f"Call{i}", [NSTATE, HALF], F32) for i in range(2)]
        alog_sb = b.sb("alog_sb", [128, NSTATE], F32)
        Aneg = b.sb("Aneg", [128, NSTATE], F32)
        dt_sb = b.sb("dt_sb", [128, TG], BF16)
        u_sb = b.sb("u_sb", [128, TG], BF16)
        ufh = b.sb("ufh", [128, HALF], F32)
        af = [b.sb(f"af{i}", [128, HALF], F32) for i in range(2)]
        un = b.sb("un", [128, HALF], F32)
        hh = b.sb("hh", [128, HALF], F32)
        hct = b.sb("hct", [128, HALF], F32)
        yacc = b.sb("yacc", [128, HALF], F32)
        ybf = b.sb("ybf", [128, HALF], BF16)
        hcarr = b.sb("hcarr", [128, NSTATE], F32)
        pB = b.ps("pB", [128, HALF], F32)
        pC = b.ps("pC", [128, HALF], F32)

        for di in range(2):
            t_dt = pr.dma("Ldt", dt_sb[:].rearrange("p (r t) -> p r t", r=NC),
                          dts[di*128:(di+1)*128],
                          deps=[pr.tok("s"), pr.tok("v")])
            t_u = pr.dma("Lu", u_sb[:].rearrange("p (r t) -> p r t", r=NC),
                         us[di*128:(di+1)*128],
                         deps=[pr.tok("v")])
            t_al = pr.dma("Lal", alog_sb[:], alog[di*128:(di+1)*128, :],
                          deps=[pr.tok("v")])
            t_a1 = pr.op("s", "activation", Aneg[:], alog_sb[:], AF.Exp,
                         deps=[t_al])
            t_a2 = pr.op("v", "tensor_scalar", Aneg[:], Aneg[:], -1.0, None,
                         ALU.mult, deps=[t_a1])
            lastBC = {0: None, 1: None}
            for bb in range(4):
                for half in range(2):
                    c0 = bb * TSEQ + half * HALF
                    blk = 2 * bb + half
                    bi = blk % 2
                    t_lB = pr.dma(f"LBb{bi}", Ball[bi][:],
                                  Bs[blk*NSTATE:(blk+1)*NSTATE, 0:HALF],
                                  deps=[lastBC[bi]])
                    t_lC = pr.dma(f"LCb{bi}", Call[bi][:],
                                  Cs[blk*NSTATE:(blk+1)*NSTATE, 0:HALF],
                                  deps=[lastBC[bi]])
                    t_uf = pr.op("v", "tensor_copy", ufh[:],
                                 u_sb[:, c0:c0+HALF], deps=[t_u])
                    prev_un = None
                    prev_hc = None
                    prev_scan = {0: None, 1: None}
                    for n in range(NSTATE):
                        abi = n % 2
                        btsB, btsC = [], []
                        for c in range(HALF // 512):
                            btsB.append(pr.op(
                                "p", "matmul", pB[:, c*512:(c+1)*512],
                                sel[:, n*128:(n+1)*128],
                                Ball[bi][:, c*512:(c+1)*512],
                                start=True, stop=True,
                                deps=[t_lB, t_sel] +
                                     ([prev_un] if prev_un
                                      else [pr.tok("v")])))
                            btsC.append(pr.op(
                                "p", "matmul", pC[:, c*512:(c+1)*512],
                                sel[:, n*128:(n+1)*128],
                                Call[bi][:, c*512:(c+1)*512],
                                start=True, stop=True,
                                deps=[t_lC] + ([prev_hc] if prev_hc
                                               else [pr.tok("v")])))
                        t_ae = pr.op("s", "activation", af[abi][:],
                                     dt_sb[:, c0:c0+HALF], AF.Exp,
                                     scale=Aneg[:, n:n+1],
                                     deps=[t_dt, t_a2] +
                                          ([prev_scan[abi]] if prev_scan[abi]
                                           else [pr.tok("v")]))
                        prev_un = pr.op("v", "tensor_mul", un[:], ufh[:],
                                        pB[:], deps=[t_uf, btsB[-1]])
                        init = 0.0 if half == 0 else hcarr[:, n:n+1]
                        t_sc = pr.op("v", "tensor_tensor_scan", hh[:],
                                     af[abi][:], un[:], init, ALU.mult,
                                     ALU.add, deps=[t_ae])
                        prev_scan[abi] = t_sc
                        if half == 0:
                            pr.op("v", "tensor_copy", hcarr[:, n:n+1],
                                  hh[:, HALF-1:HALF])
                        prev_hc = pr.op("v", "tensor_mul", hct[:], hh[:],
                                        pC[:], deps=[btsC[-1]])
                        if n == 0:
                            pr.op("v", "tensor_copy", yacc[:], hct[:])
                        else:
                            pr.op("v", "tensor_add", yacc[:], yacc[:], hct[:])
                    lastBC[bi] = prev_hc
                    t_yb = pr.op("v", "tensor_copy", ybf[:], yacc[:],
                                 deps=[pr.tok("Sy")])
                    w = 2 * bb + half
                    pr.dma("Sy", o_y[di*128:(di+1)*128, w:w+1, :],
                           ybf[:].rearrange("p (u t) -> p u t", u=1),
                           deps=[t_yb])
        b.out_done(pr.tok("Sy"))


def build_mamba_scan():
    b = Builder("mamba_scan")
    TG = 4 * TSEQ
    io = {}
    io["dts"] = b.din("dts", [256, TG], BF16)
    io["us"] = b.din("us", [256, TG], BF16)
    io["Bs"] = b.din("Bs", [128, TLOC], F32)
    io["Cs"] = b.din("Cs", [128, TLOC], F32)
    io["alog"] = b.din("alog", [256, NSTATE], F32)
    io["sel16"] = b.din("sel16", [NSTATE, NSTATE * 128], F32)
    io["o_y"] = b.dout("o_y", [256, TG], BF16).rearrange("c (w t) -> c w t", w=NC)
    io["dts"] = io["dts"].rearrange("c (r t) -> c r t", r=NC)
    io["us"] = io["us"].rearrange("c (r t) -> c r t", r=NC)
    emit_mamba_scan(b, io)
    return b.finish()



def emit_mamba_close(b, io):
    pr = b.pr
    yT = io["yT"]
    xcT = io["xcT"]
    zT = io["zT"]
    D_rs = io["D_rs"]
    woT = io["woT"]
    xr = io["xr"]
    o_x = io["o_x"]
    o_yg = io["o_yg"]
    wob_d = b.dint("wob", [DI, DM], BF16)
    NK = DI // 128

    @body(b)
    def _main():
        prep_ctx = ExitStack()
        wp = emit_weight_prep(b, [(woT, wob_d, DI, DM)], ctx=prep_ctx)
        prep_ctx.close()
        Dv = b.sb("Dv", [128, DI // 128], F32)
        t_D = pr.dma("LD", Dv[:], D_rs)
        ygT = b.sb("ygT", [128, NK * TLOC], BF16)
        ych = b.sb("ych", [128, TLOC], BF16)
        xch = b.sb("xch", [128, TLOC], BF16)
        zch = b.sb("zch", [128, TLOC], BF16)
        yff = b.sb("yff", [128, TLOC], F32)
        szf = b.sb("szf", [128, TLOC], F32)
        for m in range(NK):
            t_y = pr.dma(f"Ly{m%2}", ych[:], yT[m*128:(m+1)*128, :],
                         deps=[pr.tok("v")])
            t_xc = pr.dma(f"Lxc{m%2}", xch[:], xcT[m*128:(m+1)*128, :],
                          deps=[pr.tok("v")])
            t_z = pr.dma(f"Lz{m%2}", zch[:], zT[m*128:(m+1)*128, :],
                         deps=[pr.tok("s")])
            t_yf = pr.op("v", "scalar_tensor_tensor", yff[:], xch[:],
                         Dv[:, m:m+1], ych[:], ALU.mult, ALU.add,
                         deps=[t_y, t_xc, t_D])
            t_sz = pr.op("s", "activation", szf[:], zch[:], AF.Silu,
                         deps=[t_z, pr.tok("v")])
            t_yg = pr.op("v", "tensor_mul", ygT[:, m*TLOC:(m+1)*TLOC],
                         yff[:], szf[:], deps=[t_sz])
        t_yg_all = pr.tok("v")
        for m in range(NK):
            pr.dma("Syg", o_yg[m*128:(m+1)*128, :], ygT[:, m*TLOC:(m+1)*TLOC],
                   deps=[t_yg_all])
        b.out_done(pr.tok("Syg"))

        wo_sb = b.sb("wo_sb", [128, NK * DM], BF16)
        t_wo = pr.dma("Lwo", wo_sb[:].rearrange("p (k c) -> p k c", k=NK),
                      wob_d.rearrange("(k p) c -> p k c", p=128), deps=wp)
        xt = b.sb("xt", [128, DM], F32)
        xnf = b.sb("xnf", [128, DM], F32)
        outt = b.sb("outt", [128, DM], F32)
        st = make_ln_stats(b, "lnf")
        prev_users = []
        with b.nc.psum_tensor(b.prefix + "fps", [128, 512], F32) as fps:
            last_ev = None
            for tt in range(TLOC // 128):
                t_ld = pr.dma(f"Lx{tt%2}", xt[:], xr[tt*128:(tt+1)*128, :],
                              deps=prev_users)
                t_ln = emit_ln_tile(b, st, xt[:], xnf[:], deps=[t_ld],
                                    out_deps=prev_users)
                for dn in range(2):
                    mt = None
                    for k in range(NK):
                        mt = pr.op("p", "matmul", fps[:],
                                   ygT[:, k*TLOC + tt*128: k*TLOC + (tt+1)*128],
                                   wo_sb[:, k*DM + dn*512: k*DM + dn*512 + 512],
                                   start=(k == 0), stop=(k == NK - 1),
                                   deps=[t_yg_all, t_wo] + prev_users +
                                        ([last_ev] if last_ev else [])
                                        if k == 0 else ())
                    t_o = pr.op("v", "tensor_add", outt[:, dn*512:(dn+1)*512],
                                fps[:], xt[:, dn*512:(dn+1)*512],
                                deps=[mt, t_ld])
                    t_o = pr.op("v", "tensor_add", outt[:, dn*512:(dn+1)*512],
                                outt[:, dn*512:(dn+1)*512],
                                xnf[:, dn*512:(dn+1)*512], deps=[t_ln])
                    last_ev = t_o
                t_st = pr.dma("Sx", o_x[tt*128:(tt+1)*128, :], outt[:],
                              deps=[pr.tok("v")])
                prev_users = [t_st, pr.tok("v"), pr.tok("s")]
        b.out_done(pr.tok("Sx"))


def build_mamba_close():
    b = Builder("mamba_close")
    io = {}
    io["yT"] = b.din("yT", [DI, TLOC], BF16)
    io["xcT"] = b.din("xcT", [DI, TLOC], BF16)
    io["zT"] = b.din("zT", [DI, TLOC], BF16)
    io["D_rs"] = b.din("D_rs", [128, DI // 128], F32)
    io["woT"] = b.din("woT", [DI, DM], F32)
    io["xr"] = b.din("xr", [TLOC, DM], F32)
    io["o_x"] = b.dout("o_x", [TLOC, DM], F32)
    io["o_yg"] = b.dout("o_yg", [DI, TLOC], BF16)
    emit_mamba_close(b, io)
    return b.finish()


def run_mamba_layer(x_full, I):
    import ml_dtypes
    xf = x_full.reshape(NC, TLOC, DM)
    ident = np.eye(128, dtype=ml_dtypes.bfloat16)

    ncD = build_mamba_pre()
    w_inT = np.ascontiguousarray(_f32(I["L0_attn_W_in"]).T)
    normw_rs = np.ascontiguousarray(_f32(I["L0_attn_norm_w"]).reshape(DM // 128, 128).T)
    normb_rs = np.ascontiguousarray(_f32(I["L0_attn_norm_b"]).reshape(DM // 128, 128).T)
    cw = _f32(I["L0_attn_conv_w"])
    convw = np.ascontiguousarray(cw.reshape(DI // 128, 128, 4).transpose(1, 0, 2).reshape(128, -1))
    convb_rs = np.ascontiguousarray(_f32(I["L0_attn_conv_b"]).reshape(DI // 128, 128).T)
    w_dtT = np.ascontiguousarray(_f32(I["L0_attn_W_dt"]).T)
    bdt_rs = np.ascontiguousarray(_f32(I["L0_attn_b_dt"]).reshape(DI // 128, 128).T)
    w_BT = np.ascontiguousarray(_f32(I["L0_attn_W_B"]).T)
    w_CT = np.ascontiguousarray(_f32(I["L0_attn_W_C"]).T)
    common = dict(w_inT=w_inT, normw_rs=normw_rs, normb_rs=normb_rs,
                  convw=convw, convb_rs=convb_rs, w_dtT=w_dtT, bdt_rs=bdt_rs,
                  w_BT=w_BT, w_CT=w_CT, ident=ident)
    in_maps = []
    for i in range(NC):
        half = i % 2
        if half == 0:
            xprev = np.zeros((128, DM), np.float32)
        else:
            xprev = _f32(xf[i - 1][TLOC - 128:])
        in_maps.append(dict(xr=_f32(xf[i]), xprev=xprev,
                            wmask=np.full((128, 1), 1.0 if i % 2 else 0.0,
                                          np.float32), **common))
    resD = run_bass_kernel_spmd(ncD, in_maps, core_ids=list(range(NC)))

    def glue(key, dtype):
        return np.concatenate([np.asarray(resD.results[i][key]) for i in range(NC)], 1)
    dtT = glue("o_dt", None)
    uT = glue("o_u", None)
    BT = np.concatenate([np.asarray(resD.results[i]["o_B"]) for i in range(NC)], 1)
    CT = np.concatenate([np.asarray(resD.results[i]["o_C"]) for i in range(NC)], 1)

    ncE = build_mamba_scan()
    alog = _f32(I["L0_attn_A_log"])
    Bs = np.ascontiguousarray(BT.T.reshape(NC, TLOC, NSTATE).transpose(0, 2, 1)
                              .reshape(NC * NSTATE, TLOC))
    Cs = np.ascontiguousarray(CT.T.reshape(NC, TLOC, NSTATE).transpose(0, 2, 1)
                              .reshape(NC * NSTATE, TLOC))
    in_maps = []
    for e in range(NC):
        rows = slice(e * 256, (e + 1) * 256)
        in_maps.append(dict(dts=np.ascontiguousarray(dtT[rows]),
                            us=np.ascontiguousarray(uT[rows]),
                            Bs=Bs, Cs=Cs,
                            alog=np.ascontiguousarray(alog[rows])))
    resE = run_bass_kernel_spmd(ncE, in_maps, core_ids=list(range(NC)))
    yT = np.concatenate([np.asarray(resE.results[e]["o_y"]) for e in range(NC)], 0)

    ncF = build_mamba_close()
    D_rs = np.ascontiguousarray(_f32(I["L0_attn_D"]).reshape(DI // 128, 128).T)
    woT = np.ascontiguousarray(_f32(I["L0_attn_W_out"]).T)
    in_maps = []
    for i in range(NC):
        in_maps.append(dict(
            yT=np.ascontiguousarray(yT[:, i*TLOC:(i+1)*TLOC]),
            xcT=np.asarray(resD.results[i]["o_xc"]),
            zT=np.asarray(resD.results[i]["o_z"]),
            D_rs=D_rs, woT=woT, xr=_f32(xf[i])))
    resF = run_bass_kernel_spmd(ncF, in_maps, core_ids=list(range(NC)))
    x2 = np.concatenate([resF.results[i]["o_x"] for i in range(NC)], 0)
    return x2.reshape(x_full.shape)


def emit_round_grid(b, v_tile, out_tile, w, tmp, mb, emin=None, deps=()):
    pr = b.pr
    t1 = pr.op("s", "activation", tmp["a"][:, 0:w], v_tile, AF.Abs, deps=deps)
    t2 = pr.op("v", "tensor_scalar", tmp["a"][:, 0:w], tmp["a"][:, 0:w],
               float(2.0 ** -30), None, ALU.max, deps=[t1])
    t3 = pr.op("s", "activation", tmp["l"][:, 0:w], tmp["a"][:, 0:w], AF.Ln,
               deps=[t2])
    t4 = pr.op("v", "tensor_scalar", tmp["l"][:, 0:w], tmp["l"][:, 0:w],
               1.4426950408889634, 64.0, ALU.mult, ALU.add, deps=[t3])
    t5 = pr.op("v", "tensor_copy", tmp["ri"][:, 0:w], tmp["l"][:, 0:w])
    t6 = pr.op("v", "tensor_copy", tmp["l"][:, 0:w], tmp["ri"][:, 0:w])
    lo = (emin + 64.0) if emin is not None else -1e9
    t7 = pr.op("v", "tensor_scalar", tmp["l"][:, 0:w], tmp["l"][:, 0:w],
               float(lo), float(-64.0 - mb), ALU.max, ALU.add)
    t8 = pr.op("s", "activation", tmp["d"][:, 0:w], tmp["l"][:, 0:w], AF.Exp,
               scale=-0.6931471805599453, deps=[t7])
    t9 = pr.op("s", "activation", tmp["l"][:, 0:w], tmp["l"][:, 0:w], AF.Exp,
               scale=0.6931471805599453)
    ta = pr.op("v", "tensor_mul", tmp["r0"][:, 0:w], v_tile, tmp["d"][:, 0:w],
               deps=[t8])
    tb = pr.op("v", "tensor_copy", tmp["ri"][:, 0:w], tmp["r0"][:, 0:w])
    tc = pr.op("v", "tensor_copy", tmp["d"][:, 0:w], tmp["ri"][:, 0:w])
    td = pr.op("v", "tensor_sub", tmp["a"][:, 0:w], tmp["r0"][:, 0:w],
               tmp["d"][:, 0:w])
    te = pr.op("v", "tensor_scalar", tmp["aj"][:, 0:w], tmp["a"][:, 0:w],
               0.5, None, ALU.is_gt)
    tf = pr.op("v", "tensor_add", tmp["d"][:, 0:w], tmp["d"][:, 0:w],
               tmp["aj"][:, 0:w])
    tg = pr.op("v", "tensor_scalar", tmp["aj"][:, 0:w], tmp["a"][:, 0:w],
               -0.5, None, ALU.is_lt)
    th = pr.op("v", "tensor_sub", tmp["d"][:, 0:w], tmp["d"][:, 0:w],
               tmp["aj"][:, 0:w])
    ti = pr.op("v", "tensor_tensor", out_tile, tmp["d"][:, 0:w],
               tmp["l"][:, 0:w], ALU.mult, deps=[t9])
    return ti


def mk_grid_tmp(b, tag, w):
    return {"a": b.sb(f"{tag}_a", [128, w], F32),
            "l": b.sb(f"{tag}_l", [128, w], F32),
            "r0": b.sb(f"{tag}_r0", [128, w], F32),
            "d": b.sb(f"{tag}_d", [128, w], F32),
            "aj": b.sb(f"{tag}_aj", [128, w], F32),
            "ri": b.sb(f"{tag}_ri", [128, w], I32)}


def emit_part_reduce(b, col, out11, scr_d, op, tag, deps=()):
    pr = b.pr
    row = b.sb(f"{tag}_row", [1, 128], F32)
    t1 = pr.dma(f"{tag}S", scr_d[0:1, 0:128].rearrange("u r -> r u"), col,
                deps=list(deps))
    t2 = pr.dma(f"{tag}L", row[:], scr_d[0:1, 0:128], deps=[t1])
    t3 = pr.op("v", "tensor_reduce", out11, row[:], AX.X, op, deps=[t2])
    return t3


def emit_ffn1_ln(b, io):
    pr = b.pr
    xr = io["xr"]
    o_xn = io["o_xn"]
    o_amax = io["o_amax"]
    scr_d = b.dint("scr", [1, 128], F32)

    @body(b)
    def _main():
        st = make_ln_stats(b, "ln")
        xt = [b.sb(f"xt{i}", [128, DM], F32) for i in range(2)]
        xnb = [b.sb(f"xnb{i}", [128, DM], F32) for i in range(2)]
        mcol = b.sb("mcol", [128, 1], F32)
        macc = b.sb("macc", [128, 1], F32)
        am = b.sb("am", [1, 1], F32)
        pr.op("v", "memset", macc[:], 0.0)
        prev_ln = {0: None, 1: None}
        prev_st = {0: (), 1: ()}
        for i in range(TLOC // 128):
            bi = i % 2
            ld = pr.dma(f"Lx{bi}", xt[bi][:], xr[i*128:(i+1)*128, :],
                        deps=[prev_ln[bi]])
            t_ln = emit_ln_tile(b, st, xt[bi][:], xnb[bi][:], deps=[ld],
                                out_deps=prev_st[bi])
            prev_ln[bi] = t_ln
            t_st = pr.dma(f"Sxn{bi}", o_xn[i*128:(i+1)*128, :], xnb[bi][:],
                          deps=[t_ln])
            t_m = pr.op("v", "tensor_reduce", mcol[:], xnb[bi][:], AX.X,
                        ALU.max, apply_absolute_value=True, deps=[t_ln])
            t_m2 = pr.op("v", "tensor_max", macc[:], macc[:], mcol[:])
            prev_st[bi] = (t_st, t_m)
        t_am = emit_part_reduce(b, macc[:], am[:], scr_d, ALU.max, "am",
                                deps=[pr.tok("v")])
        b.out_done(pr.dma("Sam", o_amax, am[:], deps=[t_am]))
        b.out_done(pr.tok("Sxn0"))
        b.out_done(pr.tok("Sxn1"))


def build_ffn1_ln():
    b = Builder("ffn1_ln")
    io = {}
    io["xr"] = b.din("xr", [TLOC, DM], F32)
    io["o_xn"] = b.dout("o_xn", [TLOC, DM], F32)
    io["o_amax"] = b.dout("o_amax", [1, 1], F32)
    emit_ffn1_ln(b, io)
    return b.finish()


def emit_ffn1_close(b, io, scale_half):
    pr = b.pr
    xn_in = io["xn_in"]
    amax8 = io["amax8"]
    wT = io["wT"]
    bias = io["bias"]
    xr = io["xr"]
    ident_in = io["ident_in"]
    o_x = io["o_x"]
    scr_d = b.dint("scr", [1, 128], F32)
    NT = TLOC // 128

    @body(b)
    def _main():
        ident = b.sb("identt", [128, 128], BF16)
        t_id = pr.dma("Lid", ident[:], ident_in)
        am8 = b.sb("am8", [1, NC], F32)
        t_a8 = pr.dma("La8", am8[:], amax8)
        amax = b.sb("amax", [1, 1], F32)
        t_am = pr.op("v", "tensor_reduce", amax[:], am8[:], AX.X, ALU.max,
                     deps=[t_a8])
        sx = b.sb("sx", [1, 1], F32)
        pr.op("v", "tensor_scalar", sx[:], amax[:], 1e-12, None, ALU.max)
        pr.op("v", "reciprocal", sx[:], sx[:])
        pr.op("v", "tensor_scalar", sx[:], sx[:], 448.0, 448.0, ALU.mult, ALU.min)
        wtl = b.sb("wtl", [128, DM], F32)
        wcol = b.sb("wcol", [128, 1], F32)
        wacc = b.sb("wacc", [128, 1], F32)
        pr.op("v", "memset", wacc[:], 0.0)
        prev = None
        for i in range(DM // 128):
            ld = pr.dma(f"Lw{i%2}", wtl[:], wT[i*128:(i+1)*128, :], deps=[prev])
            t_m = pr.op("v", "tensor_reduce", wcol[:], wtl[:], AX.X, ALU.max,
                        apply_absolute_value=True, deps=[ld])
            prev = pr.op("v", "tensor_max", wacc[:], wacc[:], wcol[:])
        wam = b.sb("wam", [1, 1], F32)
        t_wam = emit_part_reduce(b, wacc[:], wam[:], scr_d, ALU.max, "wm",
                                 deps=[prev])
        sw = b.sb("sw", [1, 1], F32)
        pr.op("v", "tensor_scalar", sw[:], wam[:], 1e-12, None, ALU.max,
              deps=[t_wam])
        pr.op("v", "reciprocal", sw[:], sw[:])
        pr.op("v", "tensor_scalar", sw[:], sw[:], 448.0, 448.0, ALU.mult, ALU.min)
        dq = b.sb("dq", [1, 1], F32)
        pr.op("v", "tensor_mul", dq[:], sx[:], sw[:])
        t_dq = pr.op("v", "reciprocal", dq[:], dq[:])
        onesf = b.sb("onesf", [1, 128], F32)
        pr.op("v", "memset", onesf[:], 1.0)
        sxc = b.sb("sxc", [128, 1], F32)
        swc = b.sb("swc", [128, 1], F32)
        dqc = b.sb("dqc", [128, 1], F32)
        with b.nc.psum_tensor(b.prefix + "pbc1", [128, 1], F32) as pbc:
            for (src, dst) in ((sx, sxc), (sw, swc), (dq, dqc)):
                t_b = pr.op("p", "matmul", pbc[:], onesf[:], src[:],
                            start=True, stop=True, deps=[t_dq, pr.tok("v")])
                pr.op("v", "tensor_copy", dst[:], pbc[:], deps=[t_b])
        pe_fence(b)
        t_bc = pr.tok("v")

        gtmp = mk_grid_tmp(b, "gq", DM)
        wq = b.sb("wq", [128, (DM // 128) * DM], BF16)
        vsc = b.sb("vsc", [128, DM], F32)
        prev = None
        for i in range(DM // 128):
            ld = pr.dma(f"Lw{i%2}", wtl[:], wT[i*128:(i+1)*128, :],
                        deps=[prev] if prev else [pr.tok("v")])
            t_s = pr.op("v", "tensor_scalar", vsc[:], wtl[:], swc[:], None,
                        ALU.mult, deps=[ld, t_bc])
            t_cl = pr.op("v", "tensor_scalar", vsc[:], vsc[:], 448.0, -448.0,
                         ALU.min, ALU.max)
            prev = emit_round_grid(b, vsc[:], wq[:, i*DM:(i+1)*DM], DM, gtmp,
                                   3.0, emin=-6.0, deps=[t_cl])
        t_wq = prev

        xnt = b.sb("xnt", [128, DM], F32)
        xq_row = b.sb("xq_row", [128, DM], BF16)
        xqT = b.sb("xqT", [128, DM], BF16)
        pbf = b.ps("pbf", [128, 128], BF16)
        xt2 = b.sb("xt2", [128, DM], F32)
        outt = b.sb("outt", [128, DM], F32)
        biasr = b.sb("biasr", [1, DM], F32)
        t_bi = pr.dma("Lbi", biasr[:], bias)
        biasb = b.sb("biasb", [128, DM], F32)
        t_bb = emit_row_broadcast(b, biasr, biasb, DM, "bb", deps=[t_bi])
        gtmp2 = mk_grid_tmp(b, "gx", DM)
        prev_users = []
        with b.nc.psum_tensor(b.prefix + "qps", [128, 512], F32) as qps:
            last_ev = None
            for tt in range(NT):
                t_ld = pr.dma(f"LxT{tt%2}", xnt[:], xn_in[tt*128:(tt+1)*128, :],
                              deps=prev_users)
                t_s = pr.op("v", "tensor_scalar", xt2[:], xnt[:], sxc[:], None,
                            ALU.mult, deps=[t_ld, t_bc] + prev_users)
                t_cl = pr.op("v", "tensor_scalar", xt2[:], xt2[:], 448.0, -448.0,
                             ALU.min, ALU.max)
                t_q = emit_round_grid(b, xt2[:], xq_row[:], DM, gtmp2, 3.0,
                                      emin=-6.0, deps=[t_cl])
                trs = emit_transposes(b, xq_row, xqT, 0, ident, pbf, 8, 128,
                                      deps=[t_q, t_id] + prev_users)
                t_ldr = pr.dma(f"LxR{tt%2}", xt2[:], xr[tt*128:(tt+1)*128, :],
                               deps=[t_q, pr.tok("s")] + prev_users)
                for dn in range(2):
                    mt = None
                    for k in range(DM // 128):
                        mt = pr.op("p", "matmul", qps[:],
                                   xqT[:, k*128:(k+1)*128],
                                   wq[:, k*DM + dn*512: k*DM + dn*512 + 512],
                                   start=(k == 0), stop=(k == DM // 128 - 1),
                                   deps=[trs[-1], t_wq] +
                                        ([last_ev] if last_ev else [])
                                        if k == 0 else ())
                    t_e = pr.op("v", "scalar_tensor_tensor",
                                outt[:, dn*512:(dn+1)*512], qps[:], dqc[:],
                                biasb[:, dn*512:(dn+1)*512],
                                ALU.mult, ALU.add, deps=[mt, t_bb])
                    last_ev = t_e
                t_r = emit_round_grid(b, outt[:], outt[:], DM, gtmp2, 7.0,
                                      deps=[last_ev])
                t_f = pr.op("v", "scalar_tensor_tensor", outt[:], outt[:],
                            scale_half, xt2[:], ALU.mult, ALU.add,
                            deps=[t_r, t_ldr])
                t_st = pr.dma("Sx", o_x[tt*128:(tt+1)*128, :], outt[:],
                              deps=[t_f])
                prev_users = [t_st, t_f]
        b.out_done(pr.tok("Sx"))


def build_ffn1_close(scale_half):
    b = Builder("ffn1_close")
    io = {}
    io["xn_in"] = b.din("xn", [TLOC, DM], F32)
    io["amax8"] = b.din("amax8", [1, NC], F32)
    io["wT"] = b.din("wT", [DM, DM], F32)
    io["bias"] = b.din("bias", [1, DM], F32)
    io["xr"] = b.din("xr", [TLOC, DM], F32)
    io["ident_in"] = b.din("ident", [128, 128], BF16)
    io["o_x"] = b.dout("o_x", [TLOC, DM], F32)
    emit_ffn1_close(b, io, scale_half)
    return b.finish()


def run_ffn1_layer(x_full, I):
    xf = x_full.reshape(NC, TLOC, DM)
    import ml_dtypes
    ncH = build_ffn1_ln()
    in_maps = [dict(xr=_f32(xf[i])) for i in range(NC)]
    resH = run_bass_kernel_spmd(ncH, in_maps, core_ids=list(range(NC)))
    xn = [np.asarray(resH.results[i]["o_xn"]) for i in range(NC)]
    amax8 = np.concatenate([np.asarray(resH.results[i]["o_amax"]) for i in range(NC)],
                           1).reshape(1, NC)
    ncI = build_ffn1_close(0.5)
    ident = np.eye(128, dtype=ml_dtypes.bfloat16)
    wT = np.ascontiguousarray(_f32(I["L1_ffn_weight"]).T)
    bias = _f32(I["L1_ffn_bias"]).reshape(1, DM)
    in_maps = [dict(xn=xn[i], amax8=amax8, wT=wT, bias=bias, xr=_f32(xf[i]),
                    ident=ident) for i in range(NC)]
    resI = run_bass_kernel_spmd(ncI, in_maps, core_ids=list(range(NC)))
    xo = np.concatenate([resI.results[i]["o_x"] for i in range(NC)], 0)
    return xo.reshape(x_full.shape)


def emit_attn_qkv(b, io):
    pr = b.pr
    xr = io["xr"]
    wqT = io["wqT"]
    wkT = io["wkT"]
    wvT = io["wvT"]
    bq_rs = io["bq_rs"]
    bk_rs = io["bk_rs"]
    bv = io["bv"]
    pos = io["pos"]
    ident_in = io["ident_in"]
    o_q = io["o_q"]
    o_k = io["o_k"]
    o_v = io["o_v"]
    wqb_d = b.dint("wqb", [DM, DM], BF16)
    wkb_d = b.dint("wkb", [DM, DM], BF16)
    wvb_d = b.dint("wvb", [DM, DM], BF16)
    NT = TLOC // 128
    TWO_PI = 6.283185307179586
    INV_2PI = 0.15915494309189535

    @body(b)
    def _main():
        prep_ctx = ExitStack()
        wp = emit_weight_prep(b, [(wqT, wqb_d, DM, DM), (wkT, wkb_d, DM, DM),
                                  (wvT, wvb_d, DM, DM)], ctx=prep_ctx)
        prep_ctx.close()
        ident = b.sb("identt", [128, 128], BF16)
        t_id = pr.dma("Lid", ident[:], ident_in)

        st = make_ln_stats(b, "ln")
        xt = [b.sb(f"xt{i}", [128, DM], F32) for i in range(2)]
        xnb = [b.sb(f"xnb{i}", [128, DM], BF16) for i in range(2)]
        xnT = b.sb("xnT", [128, 8 * TLOC], BF16)
        pbf = b.ps("pbf", [128, 128], BF16)
        prev_ln = {0: None, 1: None}
        prev_users = {0: (), 1: ()}
        prev_trs = []
        for i in range(NT):
            bi = i % 2
            ld = pr.dma(f"Lx{bi}", xt[bi][:], xr[i*128:(i+1)*128, :],
                        deps=[prev_ln[bi]])
            t_ln = emit_ln_tile(b, st, xt[bi][:], xnb[bi][:], deps=[ld],
                                out_deps=prev_users[bi])
            prev_ln[bi] = t_ln
            trs = emit_transposes(b, xnb[bi], xnT, i*128, ident, pbf, 8, TLOC,
                                  deps=[t_ln, t_id] + prev_trs[-1:])
            prev_trs = trs
            prev_users[bi] = (trs[-1], pr.tok("p"))
        t_xnT = pr.tok("v")

        iot = b.sb("iot", [128, 1], F32)
        pr.op("g", "iota", iot[:], pattern=[[1, 1]], base=0,
              channel_multiplier=1, allow_small_or_imprecise_dtypes=True)
        pm = b.sb("pm", [128, 1], F32)
        pmi = b.sb("pmi", [128, 1], I32)
        freqc = b.sb("freqc", [128, 1], F32)
        t0 = pr.op("v", "tensor_scalar", pm[:], iot[:], 1.0 / 32.0, None,
                   ALU.mult, deps=[pr.tok("g")])
        pr.op("v", "tensor_copy", pmi[:], pm[:])
        pr.op("v", "tensor_copy", pm[:], pmi[:])
        pr.op("v", "tensor_scalar", pm[:], pm[:], -32.0, None, ALU.mult)
        pr.op("v", "tensor_add", pm[:], pm[:], iot[:])
        t_fr = pr.op("s", "activation", freqc[:], pm[:], AF.Exp,
                     scale=-0.2878231366242557, deps=[pr.tok("v")])
        posr = b.sb("posr", [1, TLOC], F32)
        t_pos = pr.dma("Lpos", posr[:], pos)
        posb = b.sb("posb", [128, TLOC], F32)
        t_pb = emit_row_broadcast(b, posr, posb, TLOC, "pb", deps=[t_pos])
        ang = b.sb("ang", [128, TLOC], F32)
        red = b.sb("red", [128, TLOC], F32)
        kf = b.sb("kf", [128, TLOC], F32)
        kfi = b.sb("kfi", [128, TLOC], I32)
        msk = b.sb("msk", [128, TLOC], F32)
        cosT = b.sb("cosT", [128, TLOC], F32)
        sinT = b.sb("sinT", [128, TLOC], F32)

        def build_trig(dst, shift, dep):
            t1 = pr.op("v", "tensor_scalar", ang[:], posb[:], freqc[:], None,
                       ALU.mult, deps=[t_pb, t_fr, dep])
            if shift:
                t1 = pr.op("v", "tensor_scalar", ang[:], ang[:], shift, None,
                           ALU.add)
            pr.op("v", "tensor_scalar", red[:], ang[:], INV_2PI, None, ALU.mult)
            pr.op("v", "tensor_copy", kfi[:], red[:])
            pr.op("v", "tensor_copy", kf[:], kfi[:])
            pr.op("v", "tensor_sub", msk[:], red[:], kf[:])
            pr.op("v", "tensor_scalar", msk[:], msk[:], 0.0, None, ALU.is_lt)
            pr.op("v", "tensor_sub", kf[:], kf[:], msk[:])
            pr.op("v", "scalar_tensor_tensor", red[:], kf[:], -TWO_PI, ang[:],
                  ALU.mult, ALU.add)
            pr.op("v", "tensor_scalar", msk[:], red[:], 3.14159265, None,
                  ALU.is_gt)
            t2 = pr.op("v", "scalar_tensor_tensor", red[:], msk[:], -TWO_PI,
                       red[:], ALU.mult, ALU.add)
            return pr.op("s", "activation", dst[:], red[:], AF.Sin, deps=[t2])

        t_cos = build_trig(cosT, 1.5707963267948966, None)
        t_sin = build_trig(sinT, None, t_cos)

        bqc = b.sb("bqc", [128, DM // 128], F32)
        t_bq = pr.dma("Lbq", bqc[:], bq_rs)
        bkc = b.sb("bkc", [128, DM // 128], F32)
        t_bk = pr.dma("Lbk", bkc[:], bk_rs)
        bvc = b.sb("bvc", [128, DM // 128], F32)
        t_bvc = pr.dma("Lbv", bvc[:], bv)

        wtl = [b.sb(f"wtl{i}", [128, 8 * 128], BF16) for i in range(2)]
        qcol = b.sb("qcol", [128, TLOC], F32)
        qrot = b.sb("qrot", [128, TLOC], F32)
        tA = b.sb("tA", [64, TLOC], F32)
        tB = b.sb("tB", [64, TLOC], F32)
        qb16 = b.sb("qb16", [128, TLOC], BF16)

        prev_stq = ()
        for (wsrc, bcol, btok, odst, sname) in (
                (wqb_d, bqc, t_bq, o_q, "Sq"), (wkb_d, bkc, t_bk, o_k, "Sk")):
            t_ld_prev = {0: (), 1: ()}
            for m in range(DM // 128):
                bi = m % 2
                ldw = pr.dma(f"LW{sname}{bi}",
                             wtl[bi][:].rearrange("p (k c) -> p k c", k=8),
                             wsrc[:, m*128:(m+1)*128].rearrange(
                                 "(k p) c -> p k c", p=128),
                             deps=wp + list(t_ld_prev.get(bi) or ()))
                with b.nc.psum_tensor(b.prefix + f"qp{sname}{m}", [128, 512], F32) as qp:
                    prev_ev = None
                    for nch in range(TLOC // 512):
                        c0 = nch * 512
                        mt = None
                        for k in range(8):
                            mt = pr.op("p", "matmul", qp[:],
                                       wtl[bi][:, k*128:(k+1)*128],
                                       xnT[:, k*TLOC + c0: k*TLOC + c0 + 512],
                                       start=(k == 0), stop=(k == 7),
                                       deps=[ldw, t_xnT] +
                                            ([prev_ev] if prev_ev else [])
                                            if k == 0 else ())
                        prev_ev = pr.op("s", "activation", qcol[:, c0:c0+512],
                                        qp[:], AF.Identity, bias=bcol[:, m:m+1],
                                        deps=[mt, btok, pr.tok("v")])
                pr.op("v", "tensor_mul", tA[:], qcol[0:64, :], cosT[0:64, :],
                      deps=[prev_ev, t_cos, t_sin])
                pr.op("v", "tensor_mul", tB[:], qcol[64:128, :], sinT[64:128, :])
                pr.op("v", "tensor_sub", qrot[0:64, :], tA[:], tB[:])
                pr.op("v", "tensor_mul", tA[:], qcol[0:64, :], sinT[0:64, :])
                pr.op("v", "tensor_mul", tB[:], qcol[64:128, :], cosT[64:128, :])
                pr.op("v", "tensor_add", qrot[64:128, :], tA[:], tB[:])
                t_qb = pr.op("v", "tensor_copy", qb16[:], qrot[:],
                             deps=list(prev_stq))
                t_st = pr.dma(sname, odst[m*128:(m+1)*128, :], qb16[:],
                              deps=[t_qb])
                prev_stq = (t_st,)
                t_ld_prev[bi] = (t_st, t_qb)
            b.out_done(pr.tok(sname))

        vb16 = b.sb("vb16", [128, TLOC], BF16)
        for m in range(DM // 128):
            bi = m % 2
            ldw = pr.dma(f"LWv{bi}",
                         wtl[bi][:].rearrange("p (k c) -> p k c", k=8),
                         wvb_d[:, m*128:(m+1)*128].rearrange(
                             "(k p) c -> p k c", p=128),
                         deps=wp + list(t_ld_prev.get(bi) or ()))
            with b.nc.psum_tensor(b.prefix + f"vqp{m}", [128, 512], F32) as qp:
                prev_ev = None
                for nch in range(TLOC // 512):
                    c0 = nch * 512
                    mt = None
                    for k in range(8):
                        mt = pr.op("p", "matmul", qp[:],
                                   wtl[bi][:, k*128:(k+1)*128],
                                   xnT[:, k*TLOC + c0: k*TLOC + c0 + 512],
                                   start=(k == 0), stop=(k == 7),
                                   deps=[ldw, t_xnT] +
                                        ([prev_ev] if prev_ev else [])
                                        if k == 0 else ())
                    prev_ev = pr.op("s", "activation", vb16[:, c0:c0+512],
                                    qp[:], AF.Identity, bias=bvc[:, m:m+1],
                                    deps=[mt, t_bvc, pr.tok("v"),
                                          pr.tok("Sv")])
            t_st = pr.dma("Sv", o_v[m*128:(m+1)*128, :], vb16[:],
                          deps=[prev_ev])
            t_ld_prev[bi] = (t_st,)
        b.out_done(pr.tok("Sv"))


def build_attn_qkv():
    b = Builder("attn_qkv")
    io = {}
    io["xr"] = b.din("xr", [TLOC, DM], F32)
    io["wqT"] = b.din("wqT", [DM, DM], F32)
    io["wkT"] = b.din("wkT", [DM, DM], F32)
    io["wvT"] = b.din("wvT", [DM, DM], F32)
    io["bq_rs"] = b.din("bq_rs", [128, DM // 128], F32)
    io["bk_rs"] = b.din("bk_rs", [128, DM // 128], F32)
    io["bv"] = b.din("bv", [128, DM // 128], F32)
    io["pos"] = b.din("pos", [1, TLOC], F32)
    io["ident_in"] = b.din("ident", [128, 128], BF16)
    io["o_q"] = b.dout("o_q", [DM, TLOC], BF16)
    io["o_k"] = b.dout("o_k", [DM, TLOC], BF16)
    io["o_v"] = b.dout("o_v", [DM, TLOC], BF16)
    emit_attn_qkv(b, io)
    return b.finish()


def emit_attn_core(b, io):
    pr = b.pr
    ag_q = io["ag_q"]
    ag_k = io["ag_k"]
    ag_v = io["ag_v"]
    qkidx_in = io["qkidx"]
    vidx_in = io["vidx"]
    ident_in = io["ident_in"]
    o_att = io["o_att"]
    NQ = TSEQ // 512
    NROW = NC * DM

    @body(b)
    def _main():
        onesb = b.sb("onesb", [128, 1], BF16)
        pr.op("v", "memset", onesb[:], 1.0)
        ones64 = b.sb("ones64", [1, 64], F32)
        pr.op("v", "memset", ones64[:], 1.0)
        ident = b.sb("identt", [128, 128], BF16)
        t_id = pr.dma("Lid", ident[:], ident_in)
        qkidx = b.sb("qkidx_t", [64, 16], I32)
        t_qi = pr.dma("Lqi", qkidx[:], qkidx_in)
        vidx = b.sb("vidx_t", [64, 16], I32)
        t_vi = pr.dma("Lvi", vidx[:], vidx_in)
        mks = []
        mkf = b.sb("mkf", [128, 512], F32)
        for off in range(4):
            mk = b.sb(f"mk{off}", [128, 512], BF16)
            pr.op("g", "iota", mkf[:], pattern=[[1, 512]], base=-off * 128,
                  channel_multiplier=-1, allow_small_or_imprecise_dtypes=True,
                  deps=[pr.tok("v")])
            pr.op("v", "tensor_scalar", mkf[:], mkf[:], -0.5, None, ALU.is_gt,
                  deps=[pr.tok("g")])
            pr.op("v", "tensor_copy", mk[:], mkf[:])
            mks.append(mk)

        q_sb = b.sb("q_sb", [64, TSEQ], BF16)
        k_sb = b.sb("k_sb", [64, TSEQ], BF16)
        vc_sb = b.sb("vc_sb", [64, TSEQ], BF16)
        v_sb = b.sb("v_sb", [128, (TSEQ // 128) * DK], BF16)
        probs = b.sb("probs", [128, 512], BF16)
        accsb = b.sb("accsb", [64, 512], F32)
        nrm = b.sb("nrm", [1, 512], F32)
        ob16 = b.sb("ob16", [64, 512], BF16)
        Sps = b.ps("Sps", [128, 512], F32)
        acc = b.ps("acc", [64, 512], F32)
        nps = b.ps("nps", [1, 512], F32)
        rbc = b.ps("rbc", [64, 512], F32)
        pvt = b.ps("pvt", [128, 64], BF16)

        bnd = b.bound_reg(NROW - 1)
        for pi in range(8):
            pdeps = [pr.tok("p"), pr.tok("v"), pr.tok("s")]
            g_toks = []
            for (dst, srcag, idxt, it) in ((q_sb, ag_q, qkidx, t_qi),
                                           (k_sb, ag_k, qkidx, t_qi),
                                           (vc_sb, ag_v, vidx, t_vi)):
                for rh in range(2):
                    g_toks.append(pr.idma(
                        "GQK", dst[:, rh*TLOC:(rh+1)*TLOC], None, srcag,
                        IndirectOffsetOnAxis(ap=idxt[:, pi*2+rh:pi*2+rh+1], axis=0),
                        bnd, deps=[it] + pdeps))
            t_g = pr.rot_toks("GQK")
            prev_ev = None
            for kb in range(TSEQ // 128):
                tt = pr.op("p", "transpose", pvt[0:128, 0:64],
                           vc_sb[:, kb*128:(kb+1)*128], ident[0:64, 0:64],
                           deps=t_g + [t_id] + ([prev_ev] if prev_ev else []))
                prev_ev = pr.op("v", "tensor_copy", v_sb[:, kb*DK:(kb+1)*DK],
                                pvt[0:128, 0:64], deps=[tt])
            t_v = prev_ev
            for qb in range(NQ):
                c0 = qb * 512
                nkb = (qb + 1) * 4
                prev_probs = None
                for kb in range(nkb):
                    t_s = pr.op("p", "matmul", Sps[:], k_sb[:, kb*128:(kb+1)*128],
                                q_sb[:, c0:c0+512], start=True, stop=True,
                                deps=t_g + ([prev_probs] if prev_probs else []))
                    t_e = pr.op("s", "activation", probs[:], Sps[:], AF.Exp,
                                scale=0.125, deps=[t_s, pr.tok("v")])
                    tok_probs = t_e
                    off = kb - 4 * qb
                    if off >= 0:
                        tok_probs = pr.op("v", "tensor_mul", probs[:], probs[:],
                                          mks[off][:], deps=[t_e])
                    t_av = pr.op("p", "matmul", acc[:],
                                 v_sb[:, kb*DK:(kb+1)*DK], probs[:],
                                 start=(kb == 0), stop=(kb == nkb - 1),
                                 deps=[tok_probs, t_v])
                    t_nm = pr.op("p", "matmul", nps[:], onesb[:], probs[:],
                                 start=(kb == 0), stop=(kb == nkb - 1),
                                 deps=[tok_probs])
                    prev_probs = t_av
                t_cp = pr.op("v", "tensor_copy", accsb[:], acc[:],
                             deps=[prev_probs, pr.tok("p")])
                t_n = pr.op("v", "tensor_copy", nrm[:], nps[:])
                t_r = pr.op("v", "reciprocal", nrm[:], nrm[:])
                t_bc = pr.op("p", "matmul", rbc[:], ones64[:], nrm[:],
                             start=True, stop=True, deps=[t_r, pr.tok("v")])
                t_o = pr.op("v", "tensor_mul", ob16[:], accsb[:], rbc[:],
                            deps=[t_bc, pr.tok("So")])
                pr.dma("So", o_att[pi*64:(pi+1)*64, c0:c0+512], ob16[:],
                       deps=[t_o])
        b.out_done(pr.tok("So"))


def build_attn_core():
    b = Builder("attn_core")
    io = {}
    io["ag_q"] = b.din("ag_q", [NC * DM, TLOC], BF16)
    io["ag_k"] = b.din("ag_k", [NC * DM, TLOC], BF16)
    io["ag_v"] = b.din("ag_v", [NC * DM, TLOC], BF16)
    io["qkidx"] = b.din("qkidx", [64, 16], I32)
    io["vidx"] = b.din("vidx", [64, 16], I32)
    io["ident_in"] = b.din("ident", [128, 128], BF16)
    io["o_att"] = b.dout("o_att", [8 * DK, TSEQ], BF16)
    emit_attn_core(b, io)
    return b.finish()


def emit_attn_close(b, io):
    pr = b.pr
    aT = io["aT"]
    woT = io["woT"]
    bo = io["bo"]
    xr = io["xr"]
    o_x = io["o_x"]
    wob_d = b.dint("wob2", [DM, DM], BF16)
    NT = TLOC // 128

    @body(b)
    def _main():
        prep_ctx = ExitStack()
        wp = emit_weight_prep(b, [(woT, wob_d, DM, DM)], ctx=prep_ctx)
        prep_ctx.close()
        aT_sb = b.sb("aT_sb", [128, 8 * TLOC], BF16)
        t_a = pr.dma("La", aT_sb[:].rearrange("p (k c) -> p k c", k=8),
                     aT.rearrange("(k p) c -> p k c", p=128))
        wo_sb = b.sb("wo_sb", [128, 8 * DM], BF16)
        t_wo = pr.dma("Lwo", wo_sb[:].rearrange("p (k c) -> p k c", k=8),
                      wob_d.rearrange("(k p) c -> p k c", p=128), deps=wp)
        bor = b.sb("bor", [1, DM], F32)
        t_b0 = pr.dma("Lbo", bor[:], bo)
        bob = b.sb("bob", [128, DM], F32)
        t_bb = emit_row_broadcast(b, bor, bob, DM, "bo", deps=[t_b0])
        xt = b.sb("xt", [128, DM], F32)
        outt = b.sb("outt", [128, DM], F32)
        prev_users = []
        with b.nc.psum_tensor(b.prefix + "cps", [128, 512], F32) as cps:
            last_ev = None
            for tt in range(NT):
                t_ld = pr.dma(f"Lx{tt%2}", xt[:], xr[tt*128:(tt+1)*128, :],
                              deps=prev_users)
                for dn in range(2):
                    mt = None
                    for k in range(8):
                        mt = pr.op("p", "matmul", cps[:],
                                   aT_sb[:, k*TLOC + tt*128: k*TLOC + (tt+1)*128],
                                   wo_sb[:, k*DM + dn*512: k*DM + dn*512 + 512],
                                   start=(k == 0), stop=(k == 7),
                                   deps=[t_a, t_wo] +
                                        ([last_ev] if last_ev else [])
                                        if k == 0 else ())
                    t_o = pr.op("v", "tensor_add", outt[:, dn*512:(dn+1)*512],
                                cps[:], xt[:, dn*512:(dn+1)*512],
                                deps=[mt, t_ld])
                    t_o = pr.op("v", "tensor_add", outt[:, dn*512:(dn+1)*512],
                                outt[:, dn*512:(dn+1)*512],
                                bob[:, dn*512:(dn+1)*512], deps=[t_bb])
                    last_ev = t_o
                t_st = pr.dma("Sx", o_x[tt*128:(tt+1)*128, :], outt[:],
                              deps=[pr.tok("v")])
                prev_users = [t_st, pr.tok("v")]
        b.out_done(pr.tok("Sx"))


def build_attn_close():
    b = Builder("attn_close")
    io = {}
    io["aT"] = b.din("aT", [DM, TLOC], BF16)
    io["woT"] = b.din("woT", [DM, DM], F32)
    io["bo"] = b.din("bo", [1, DM], F32)
    io["xr"] = b.din("xr", [TLOC, DM], F32)
    io["o_x"] = b.dout("o_x", [TLOC, DM], F32)
    emit_attn_close(b, io)
    return b.finish()

def _perm_cols():
    perm = []
    for i in range(DM // 128):
        h0, h1 = 2 * i, 2 * i + 1
        for (h, dr) in ((h0, range(0, 32)), (h1, range(0, 32)),
                        (h0, range(32, 64)), (h1, range(32, 64))):
            perm.extend(h * DK + d for d in dr)
    return np.array(perm)


def attn_idx_tables(core):
    pairs = [(bb, h) for bb in range(4) for h in range(NH)][core*8:(core+1)*8]
    qk = np.zeros((64, 16), np.int32)
    vv = np.zeros((64, 16), np.int32)
    for pi, (bb, h) in enumerate(pairs):
        i = h // 2
        if h % 2 == 0:
            rows = list(range(i*128, i*128+32)) + list(range(i*128+64, i*128+96))
        else:
            rows = list(range(i*128+32, i*128+64)) + list(range(i*128+96, i*128+128))
        for rh in range(2):
            rank = 2 * bb + rh
            qk[:, pi*2+rh] = rank * DM + np.array(rows)
            vv[:, pi*2+rh] = rank * DM + h * DK + np.arange(64)
    return qk, vv


def run_attn_layer(x_full, I):
    import ml_dtypes
    xf = x_full.reshape(NC, TLOC, DM)
    ident = np.eye(128, dtype=ml_dtypes.bfloat16)
    perm = _perm_cols()

    ncJ = build_attn_qkv()
    wqT = np.ascontiguousarray(_f32(I["L1_attn_WQ_w"]).T[:, perm])
    wkT = np.ascontiguousarray(_f32(I["L1_attn_WK_w"]).T[:, perm])
    wvT = np.ascontiguousarray(_f32(I["L1_attn_WV_w"]).T)
    bq_rs = np.ascontiguousarray(_f32(I["L1_attn_WQ_b"])[perm].reshape(DM // 128, 128).T)
    bk_rs = np.ascontiguousarray(_f32(I["L1_attn_WK_b"])[perm].reshape(DM // 128, 128).T)
    bv_rs = np.ascontiguousarray(_f32(I["L1_attn_WV_b"]).reshape(DM // 128, 128).T)
    common = dict(wqT=wqT, wkT=wkT, wvT=wvT, bq_rs=bq_rs, bk_rs=bk_rs,
                  bv=bv_rs, ident=ident)
    in_maps = []
    for i in range(NC):
        t0 = (i % 2) * TLOC
        posr = np.arange(t0, t0 + TLOC, dtype=np.float32).reshape(1, TLOC)
        in_maps.append(dict(xr=_f32(xf[i]), pos=posr, **common))
    resJ = run_bass_kernel_spmd(ncJ, in_maps, core_ids=list(range(NC)))
    ag_q = np.concatenate([np.asarray(resJ.results[i]["o_q"]) for i in range(NC)], 0)
    ag_k = np.concatenate([np.asarray(resJ.results[i]["o_k"]) for i in range(NC)], 0)
    ag_v = np.concatenate([np.asarray(resJ.results[i]["o_v"]) for i in range(NC)], 0)

    ncK = build_attn_core()
    in_maps = []
    for c in range(NC):
        qk, vv = attn_idx_tables(c)
        in_maps.append(dict(ag_q=ag_q, ag_k=ag_k, ag_v=ag_v,
                            qkidx=qk, vidx=vv, ident=ident))
    resK = run_bass_kernel_spmd(ncK, in_maps, core_ids=list(range(NC)))
    ag_att = np.concatenate([np.asarray(resK.results[c]["o_att"]) for c in range(NC)], 0)

    ncL = build_attn_close()
    woT = np.ascontiguousarray(_f32(I["L1_attn_Wo_w"]).T)
    bo = _f32(I["L1_attn_Wo_b"]).reshape(1, DM)
    in_maps = []
    for i in range(NC):
        bb, hf = i // 2, i % 2
        aT = np.ascontiguousarray(ag_att[bb*1024:(bb+1)*1024, hf*TLOC:(hf+1)*TLOC])
        in_maps.append(dict(aT=aT, woT=woT, bo=bo, xr=_f32(xf[i])))
    resL = run_bass_kernel_spmd(ncL, in_maps, core_ids=list(range(NC)))
    x5 = np.concatenate([resL.results[i]["o_x"] for i in range(NC)], 0)
    return x5.reshape(x_full.shape)



def kernel_unfused(**inputs):
    I = inputs
    x = _f32(I["L0_attn_x"]).reshape(-1, DM)
    x = run_moe_layer(x, I, "L0_ffn_").reshape(-1, DM)
    x = run_mamba_layer(x, I).reshape(-1, DM)
    x = run_moe_layer(x, I, "L0_ffn_").reshape(-1, DM)
    x = run_ffn1_layer(x, I).reshape(-1, DM)
    x = run_attn_layer(x, I).reshape(-1, DM)
    x = run_ffn1_layer(x, I).reshape(-1, DM)
    return np.ascontiguousarray(x.reshape(4, TSEQ, DM), dtype=np.float32)


def phase_sync(b):
    pr = b.pr
    names = ["v", "s", "p", "g"] + sorted(pr.touched - {"v", "s", "p", "g"})
    for ename in ("v", "s", "p", "g", "sync"):
        eng = pr.eng(ename)
        for sname in names:
            cnt = pr.count.get(sname, 0)
            if cnt > 0:
                eng.wait_ge(pr.sem[sname], cnt)
    pr.touched.clear()


class PhaseCtx:
    def __init__(self, b, prefix):
        self.b = b
        self.prefix = prefix

    def __enter__(self):
        phase_sync(self.b)
        self.b.pr.new_phase()
        self._sid, _ = self.b.nc.enter_named_scope(self.prefix, False)
        self.b.prefix = self.prefix
        self.b._scope = ExitStack()
        return self

    def __exit__(self, *a):
        phase_sync(self.b)
        self.b.nc.leave_named_scope(self.prefix, self._sid, False)
        self.b._scope.close()
        self.b._scope = None
        self.b.prefix = ""
        return False


def emit_coll(b, kind, op, in_ap, out_ap, tag):
    pr = b.pr
    phase_sync(b)
    return pr.coll("CC", kind, op, [in_ap.opt()], [out_ap.opt()])


def emit_dram_copy(b, dst, src, R, C, dt, tag):
    pr = b.pr
    t = b.sb(f"{tag}_cp", [128, C], dt)
    prev = None
    toks = []
    for r0 in range(0, R, 128):
        rr = min(128, R - r0)
        ld = pr.dma("GCL", t[0:rr, :], src[r0:r0+rr, :], deps=[prev])
        prev = pr.dma("GCS", dst[r0:r0+rr, :], t[0:rr, :], deps=[ld])
        toks.append(prev)
    return toks[-1]


def emit_idx_gather_dram(b, dst_dram, src_dram, idx_in, ntiles, C, dt, bnd, tag,
                         zero_fill=False, rows=128):
    pr = b.pr
    t = b.sb(f"{tag}_g", [rows, C], dt)
    idxt = b.sb(f"{tag}_ix", [rows, ntiles], I32)
    t_ix = pr.dma("GIL", idxt[:], idx_in)
    prev = None
    for j in range(ntiles):
        dep = [t_ix] + ([prev] if prev else [])
        if zero_fill:
            z = pr.op("v", "memset", t[:], 0.0, deps=dep)
            dep = dep + [z]
        g = pr.idma("GIG", t[:], None, src_dram,
                    IndirectOffsetOnAxis(ap=idxt[:, j:j+1], axis=0),
                    bnd, deps=dep)
        prev = pr.dma("GIS", dst_dram[j*rows:(j+1)*rows, :], t[:], deps=[g])
    return prev


def emit_probsT_glue(b, probs_dram, out_dram, tag):
    pr = b.pr
    idf = b.sb(f"{tag}_idf", [128, 128], F32)
    iof = b.sb(f"{tag}_iof", [128, 128], F32)
    pr.op("g", "iota", iof[:], pattern=[[-1, 128]], base=0,
          channel_multiplier=1, allow_small_or_imprecise_dtypes=True)
    t_id = pr.op("v", "tensor_scalar", idf[:], iof[:], 0.0, None, ALU.is_equal,
                 deps=[pr.tok("g")])
    pt = b.sb(f"{tag}_pt", [128, NEXP], F32)
    ot = b.sb(f"{tag}_ot", [NEXP, TLOC], F32)
    pps = b.ps(f"{tag}_ps", [NEXP, 128], F32)
    prev = None
    for i in range(TLOC // 128):
        ld = pr.dma("GPL", pt[:], probs_dram[i*128:(i+1)*128, :],
                    deps=[prev])
        tt = pr.op("p", "transpose", pps[:], pt[:], idf[:],
                   deps=[ld, t_id, pr.tok("v")])
        prev = pr.op("v", "tensor_copy", ot[:, i*128:(i+1)*128], pps[:],
                     deps=[tt])
    return pr.dma("GPS", out_dram, ot[:], deps=[prev])


def build_fused(n_stages=6, debug=False):
    b = Builder("fused")
    dbg_bufs = {}
    RG = [list(range(NC))]

    xin = b.din("xin", [TLOC, DM], F32)
    e_gate = b.din("gate", [DM, NEXP], F32)
    e_ws1 = b.din("ws1", [DM, HD], F32)
    e_bs1 = b.din("bs1_rs", [128, HD // 128], F32)
    e_ws2 = b.din("ws2", [HD, DM], F32)
    e_bs2 = b.din("bs2", [1, DM], F32)
    e_w1 = b.din("w1", [DM, HD], F32)
    e_b1 = b.din("b1_rs", [128, HD // 128], F32)
    e_w2 = b.din("w2", [HD, DM], F32)
    e_b2 = b.din("b2", [1, DM], F32)
    ident_in = b.din("ident", [128, 128], BF16)
    m_winT = b.din("w_inT", [DM, 2 * DI], F32)
    m_nw = b.din("normw_rs", [128, DM // 128], F32)
    m_nb = b.din("normb_rs", [128, DM // 128], F32)
    m_cw = b.din("convw", [128, DI // 128 * 4], F32)
    m_cb = b.din("convb_rs", [128, DI // 128], F32)
    m_wdt = b.din("w_dtT", [DI, DI], F32)
    m_bdt = b.din("bdt_rs", [128, DI // 128], F32)
    m_wB = b.din("w_BT", [DI, NSTATE], F32)
    m_wC = b.din("w_CT", [DI, NSTATE], F32)
    m_alog = b.din("alog", [256, NSTATE], F32)
    m_D = b.din("D_rs", [128, DI // 128], F32)
    m_woT = b.din("woT", [DI, DM], F32)
    xprev_idx = b.din("xprev_idx", [128, 1], I32)
    m_wmask = b.din("wmask", [128, 1], F32)
    m_sel16 = b.din("sel16", [16, 16 * 128], F32)
    f_wT = b.din("wT", [DM, DM], F32)
    f_bias = b.din("bias", [1, DM], F32)
    a_wqT = b.din("wqT", [DM, DM], F32)
    a_wkT = b.din("wkT", [DM, DM], F32)
    a_wvT = b.din("wvT", [DM, DM], F32)
    a_bq = b.din("bq_rs", [128, DM // 128], F32)
    a_bk = b.din("bk_rs", [128, DM // 128], F32)
    a_bv = b.din("bv", [128, DM // 128], F32)
    a_pos = b.din("pos", [1, TLOC], F32)
    a_qkidx = b.din("qkidx", [64, 16], I32)
    a_vidx = b.din("vidx", [64, 16], I32)
    a_woT = b.din("a_woT", [DM, DM], F32)
    a_bo = b.din("bo", [1, DM], F32)
    peidx = b.din("peidx", [NEXP, 1], I32)
    slwidx = b.din("slwidx", [NEXP, 1], I32)
    aidx = b.din("aidx", [128, 8], I32)
    chidx = b.din("chidx", [128, 16], I32)

    o_x = b.dout("o_x", [TLOC, DM], F32)

    xb = [b.dint(f"xb{i}", [TLOC, DM], F32) for i in range(6)]
    pr = b.pr

    def moe_phase(tag, xr_ap, xout_ap):
        c_xn = b.dint(f"{tag}_cxn", [TLOC, DM], BF16)
        c_pr = b.dint(f"{tag}_cpr", [TLOC, NEXP], F32)
        c_sh = b.dint(f"{tag}_csh", [TLOC, DM], F32)
        c_pT = b.dint(f"{tag}_cpT", [NEXP, TLOC], F32)
        ag_xn = b.dint(f"{tag}_agxn", [NC * TLOC, DM], BF16)
        ag_pT = b.dint(f"{tag}_agpT", [NC * NEXP, TLOC], F32)
        c_pe = b.dint(f"{tag}_cpe", [NEXP, TLOC], F32)
        c_eo = b.dint(f"{tag}_ceo", [CAP, DM], F32)
        c_tau = b.dint(f"{tag}_ctau", [1, 2], F32)
        c_sl = b.dint(f"{tag}_csl", [NEXP, TLOC], F32)
        ag_eo = b.dint(f"{tag}_ageo", [NC * CAP, DM], F32)
        ag_sl = b.dint(f"{tag}_agsl", [NC * NEXP, TLOC], F32)
        c_slw = b.dint(f"{tag}_cslw", [NEXP, TLOC], F32)
        scr_idx = b.dint(f"{tag}_scri", [128, 16], I32)

        with PhaseCtx(b, f"{tag}a_"):
            emit_ffn0_pre(b, dict(xr=xr_ap, gate=e_gate, ws1=e_ws1,
                                  bs1_rs=e_bs1, ws2=e_ws2, bs2=e_bs2,
                                  ident_in=ident_in, o_xn=c_xn, o_probs=c_pr,
                                  o_shared=c_sh))
        with PhaseCtx(b, f"{tag}g1_"):
            emit_probsT_glue(b, c_pr, c_pT, f"{tag}g1")
        with PhaseCtx(b, f"{tag}c1_"):
            emit_coll(b, "AllGather", ALU.bypass, c_xn, ag_xn, "agxn")
            emit_coll(b, "AllGather", ALU.bypass, c_pT, ag_pT, "agpT")
        with PhaseCtx(b, f"{tag}g2_"):
            emit_idx_gather_dram(b, c_pe, ag_pT, peidx, 1, TLOC, F32,
                                 b.bound_reg(NC * NEXP - 1), f"{tag}g2",
                                 rows=NEXP)
        with PhaseCtx(b, f"{tag}b_"):
            emit_ffn0_expert(b, dict(pe_in=c_pe, xn_all=ag_xn, w1=e_w1,
                                     b1_rs=e_b1, w2=e_w2, b2=e_b2,
                                     ident_in=ident_in, o_eout=c_eo,
                                     o_tau=c_tau, o_idx=scr_idx,
                                     o_slots=c_sl))
        with PhaseCtx(b, f"{tag}c2_"):
            emit_coll(b, "AllGather", ALU.bypass, c_eo, ag_eo, "ageo")
            emit_coll(b, "AllGather", ALU.bypass, c_sl, ag_sl, "agsl")
        with PhaseCtx(b, f"{tag}g3_"):
            emit_idx_gather_dram(b, c_slw, ag_sl, slwidx, 1, TLOC, F32,
                                 b.bound_reg(NC * NEXP - 1), f"{tag}g3",
                                 rows=NEXP)
        with PhaseCtx(b, f"{tag}c_"):
            emit_ffn0_close(b, dict(xr=xr_ap, shared=c_sh, slots_win=c_slw,
                                    o_x=xout_ap, eout_all=ag_eo))
        if debug and tag == "m0":
            dbg_bufs.update(d_xn=(c_xn, BF16), d_pr=(c_pr, F32),
                            d_pe=(c_pe, F32), d_sl=(c_slw, F32),
                            d_eo=(c_eo, F32), d_sh=(c_sh, F32),
                            d_idx=(scr_idx, I32))

    def mamba_phase(tag, xr_ap, xout_ap):
        c_dt = b.dint(f"{tag}_cdt", [DI, TLOC], BF16)
        c_u = b.dint(f"{tag}_cu", [DI, TLOC], BF16)
        c_xc = b.dint(f"{tag}_cxc", [DI, TLOC], BF16)
        c_z = b.dint(f"{tag}_cz", [DI, TLOC], BF16)
        c_B = b.dint(f"{tag}_cB", [NSTATE, TLOC], F32)
        c_C = b.dint(f"{tag}_cC", [NSTATE, TLOC], F32)
        c_tail = b.dint(f"{tag}_ctail", [128, DM], F32)
        ag_tail = b.dint(f"{tag}_agtail", [NC * 128, DM], F32)
        c_xprev = b.dint(f"{tag}_cxprev", [128, DM], F32)
        ag_dt = b.dint(f"{tag}_agdt", [NC * DI, TLOC], BF16)
        ag_u = b.dint(f"{tag}_agu", [NC * DI, TLOC], BF16)
        g_dt = b.dint(f"{tag}_gdt", [DI, TLOC], BF16)
        g_u = b.dint(f"{tag}_gu", [DI, TLOC], BF16)
        ag_B = b.dint(f"{tag}_agB", [NC * NSTATE, TLOC], F32)
        ag_C = b.dint(f"{tag}_agC", [NC * NSTATE, TLOC], F32)
        yin = b.dint(f"{tag}_yin", [DI, TLOC], BF16)
        ag_y = b.dint(f"{tag}_agy", [NC * DI, TLOC], BF16)
        g_y = b.dint(f"{tag}_gy", [DI, TLOC], BF16)
        c_yg = b.dint(f"{tag}_cyg", [DI, TLOC], BF16)

        with PhaseCtx(b, f"{tag}t_"):
            emit_dram_copy(b, c_tail, xr_ap[TLOC-128:TLOC, :], 128, DM, F32,
                           f"{tag}t")
        with PhaseCtx(b, f"{tag}c0_"):
            emit_coll(b, "AllGather", ALU.bypass, c_tail, ag_tail, "agt")
        with PhaseCtx(b, f"{tag}g0_"):
            emit_idx_gather_dram(b, c_xprev, ag_tail, xprev_idx, 1, DM, F32,
                                 b.bound_reg(NC * 128 - 1), f"{tag}g0",
                                 zero_fill=True)
        with PhaseCtx(b, f"{tag}d_"):
            emit_mamba_pre(b, dict(xr=xr_ap, xprev=c_xprev, wmask=m_wmask,
                                   w_inT=m_winT,
                                   normw_rs=m_nw, normb_rs=m_nb, convw=m_cw,
                                   convb_rs=m_cb, w_dtT=m_wdt, bdt_rs=m_bdt,
                                   w_BT=m_wB, w_CT=m_wC, ident_in=ident_in,
                                   o_dt=c_dt, o_u=c_u, o_xc=c_xc, o_z=c_z,
                                   o_B=c_B, o_C=c_C))
        with PhaseCtx(b, f"{tag}c1_"):
            emit_coll(b, "AllGather", ALU.bypass, c_dt, ag_dt, "agdt")
            emit_coll(b, "AllGather", ALU.bypass, c_u, ag_u, "agu")
            emit_coll(b, "AllGather", ALU.bypass, c_B, ag_B, "agB")
            emit_coll(b, "AllGather", ALU.bypass, c_C, ag_C, "agC")
        with PhaseCtx(b, f"{tag}g1_"):
            emit_idx_gather_dram(b, g_dt, ag_dt, chidx, 16, TLOC, BF16,
                                 b.bound_reg(NC * DI - 1), f"{tag}gdt")
            emit_idx_gather_dram(b, g_u, ag_u, chidx, 16, TLOC, BF16,
                                 b.bound_reg(NC * DI - 1), f"{tag}gu")
        with PhaseCtx(b, f"{tag}e_"):
            emit_mamba_scan(b, dict(
                dts=g_dt.rearrange("(r c) t -> c r t", c=256),
                us=g_u.rearrange("(r c) t -> c r t", c=256),
                Bs=ag_B, Cs=ag_C, alog=m_alog, sel16=m_sel16,
                o_y=yin.rearrange("(w c) t -> c w t", c=256)))
        with PhaseCtx(b, f"{tag}c2_"):
            emit_coll(b, "AllGather", ALU.bypass, yin, ag_y, "agy")
        with PhaseCtx(b, f"{tag}g2_"):
            emit_idx_gather_dram(b, g_y, ag_y, chidx, 16, TLOC, BF16,
                                 b.bound_reg(NC * DI - 1), f"{tag}gy")
        with PhaseCtx(b, f"{tag}f_"):
            emit_mamba_close(b, dict(yT=g_y, xcT=c_xc, zT=c_z, D_rs=m_D,
                                     woT=m_woT, xr=xr_ap, o_x=xout_ap,
                                     o_yg=c_yg))
        if debug:
            dbg_bufs.update(d_dt=(c_dt, BF16), d_xc=(c_xc, BF16),
                            d_z=(c_z, BF16), d_B=(c_B, F32), d_C=(c_C, F32),
                            d_gy=(g_y, BF16), d_yg=(c_yg, BF16),
                            d_xprev=(c_xprev, F32))

    def ffn1_phase(tag, xr_ap, xout_ap):
        c_xn = b.dint(f"{tag}_cxn", [TLOC, DM], F32)
        c_am = b.dint(f"{tag}_cam", [1, 1], F32)
        ag_am = b.dint(f"{tag}_agam", [NC, 1], F32)
        with PhaseCtx(b, f"{tag}h_"):
            emit_ffn1_ln(b, dict(xr=xr_ap, o_xn=c_xn, o_amax=c_am))
        with PhaseCtx(b, f"{tag}c_"):
            emit_coll(b, "AllGather", ALU.bypass, c_am, ag_am, "agam")
        with PhaseCtx(b, f"{tag}i_"):
            emit_ffn1_close(b, dict(xn_in=c_xn,
                                    amax8=ag_am.rearrange("r u -> u r"),
                                    wT=f_wT, bias=f_bias, xr=xr_ap,
                                    ident_in=ident_in, o_x=xout_ap), 0.5)

    def attn_phase(tag, xr_ap, xout_ap):
        c_q = b.dint(f"{tag}_cq", [DM, TLOC], BF16)
        c_k = b.dint(f"{tag}_ck", [DM, TLOC], BF16)
        c_v = b.dint(f"{tag}_cv", [DM, TLOC], BF16)
        ag_q = b.dint(f"{tag}_agq", [NC * DM, TLOC], BF16)
        ag_k = b.dint(f"{tag}_agk", [NC * DM, TLOC], BF16)
        ag_v = b.dint(f"{tag}_agv", [NC * DM, TLOC], BF16)
        c_att = b.dint(f"{tag}_catt", [8 * DK, TSEQ], BF16)
        ag_att = b.dint(f"{tag}_agatt", [NC * 8 * DK, TSEQ], BF16)
        c_aT = b.dint(f"{tag}_caT", [DM, TLOC], BF16)
        with PhaseCtx(b, f"{tag}j_"):
            emit_attn_qkv(b, dict(xr=xr_ap, wqT=a_wqT, wkT=a_wkT, wvT=a_wvT,
                                  bq_rs=a_bq, bk_rs=a_bk, bv=a_bv, pos=a_pos,
                                  ident_in=ident_in, o_q=c_q, o_k=c_k,
                                  o_v=c_v))
        with PhaseCtx(b, f"{tag}c1_"):
            emit_coll(b, "AllGather", ALU.bypass, c_q, ag_q, "agq")
            emit_coll(b, "AllGather", ALU.bypass, c_k, ag_k, "agk")
            emit_coll(b, "AllGather", ALU.bypass, c_v, ag_v, "agv")
        with PhaseCtx(b, f"{tag}k_"):
            emit_attn_core(b, dict(ag_q=ag_q, ag_k=ag_k, ag_v=ag_v,
                                   qkidx=a_qkidx, vidx=a_vidx,
                                   ident_in=ident_in, o_att=c_att))
        with PhaseCtx(b, f"{tag}c2_"):
            emit_coll(b, "AllGather", ALU.bypass, c_att, ag_att, "agat")
        with PhaseCtx(b, f"{tag}g1_"):
            emit_idx_gather_dram(b, c_aT, ag_att, aidx, 8, TLOC, BF16,
                                 b.bound_reg(NC * 8 * DK - 1), f"{tag}g1")
        with PhaseCtx(b, f"{tag}l_"):
            emit_attn_close(b, dict(aT=c_aT, woT=a_woT, bo=a_bo, xr=xr_ap,
                                    o_x=xout_ap))

    stages = [
        lambda d: moe_phase("m0", xin, d),
        lambda d: mamba_phase("mb", xb[0], d),
        lambda d: moe_phase("m1", xb[1], d),
        lambda d: ffn1_phase("f0", xb[2], d),
        lambda d: attn_phase("at", xb[3], d),
        lambda d: ffn1_phase("f1", xb[4], d),
    ]

    @body(b)
    def _():
        for si in range(n_stages):
            dst = o_x if si == n_stages - 1 else xb[si]
            stages[si](dst)
        if debug:
            with PhaseCtx(b, "dbg_"):
                for name, (buf, dt) in dbg_bufs.items():
                    o = b.dout(f"o_{name}", list(buf.tensor.shape), dt)
                    R, C = buf.tensor.shape
                    b.out_done(emit_dram_copy(b, o, buf, R, C, dt, f"dbg{name}"))
    return b.finish()


_FUSED_CACHE = {}


def _chidx(core):
    idx = np.zeros((128, 16), np.int32)
    for r in range(NC):
        for ch in range(2):
            idx[:, r*2 + ch] = r * DI + core * 256 + ch * 128 + np.arange(128)
    return idx


def fused_in_maps(inputs):
    import ml_dtypes
    I = inputs
    xf = _f32(I["L0_attn_x"]).reshape(NC, TLOC, DM)
    ident = np.eye(128, dtype=ml_dtypes.bfloat16)
    perm = _perm_cols()
    cw = _f32(I["L0_attn_conv_w"])
    common = dict(
        gate=_f32(I["L0_ffn_gate"]), ws1=_f32(I["L0_ffn_Ws1"]),
        bs1_rs=np.ascontiguousarray(_f32(I["L0_ffn_bs1"]).reshape(HD // 128, 128).T),
        ws2=_f32(I["L0_ffn_Ws2"]), bs2=_f32(I["L0_ffn_bs2"]).reshape(1, DM),
        ident=ident,
        sel16=np.repeat(np.eye(16, dtype=np.float32), 128, axis=1),
        w_inT=np.ascontiguousarray(_f32(I["L0_attn_W_in"]).T),
        normw_rs=np.ascontiguousarray(_f32(I["L0_attn_norm_w"]).reshape(DM // 128, 128).T),
        normb_rs=np.ascontiguousarray(_f32(I["L0_attn_norm_b"]).reshape(DM // 128, 128).T),
        convw=np.ascontiguousarray(cw.reshape(DI // 128, 128, 4).transpose(1, 0, 2).reshape(128, -1)),
        convb_rs=np.ascontiguousarray(_f32(I["L0_attn_conv_b"]).reshape(DI // 128, 128).T),
        w_dtT=np.ascontiguousarray(_f32(I["L0_attn_W_dt"]).T),
        bdt_rs=np.ascontiguousarray(_f32(I["L0_attn_b_dt"]).reshape(DI // 128, 128).T),
        w_BT=np.ascontiguousarray(_f32(I["L0_attn_W_B"]).T),
        w_CT=np.ascontiguousarray(_f32(I["L0_attn_W_C"]).T),
        D_rs=np.ascontiguousarray(_f32(I["L0_attn_D"]).reshape(DI // 128, 128).T),
        woT=np.ascontiguousarray(_f32(I["L0_attn_W_out"]).T),
        wT=np.ascontiguousarray(_f32(I["L1_ffn_weight"]).T),
        bias=_f32(I["L1_ffn_bias"]).reshape(1, DM),
        wqT=np.ascontiguousarray(_f32(I["L1_attn_WQ_w"]).T[:, perm]),
        wkT=np.ascontiguousarray(_f32(I["L1_attn_WK_w"]).T[:, perm]),
        wvT=np.ascontiguousarray(_f32(I["L1_attn_WV_w"]).T),
        bq_rs=np.ascontiguousarray(_f32(I["L1_attn_WQ_b"])[perm].reshape(DM // 128, 128).T),
        bk_rs=np.ascontiguousarray(_f32(I["L1_attn_WK_b"])[perm].reshape(DM // 128, 128).T),
        bv=np.ascontiguousarray(_f32(I["L1_attn_WV_b"]).reshape(DM // 128, 128).T),
        a_woT=np.ascontiguousarray(_f32(I["L1_attn_Wo_w"]).T),
        bo=_f32(I["L1_attn_Wo_b"]).reshape(1, DM),
    )
    in_maps = []
    for i in range(NC):
        t0 = (i % 2) * TLOC
        qk, vv = attn_idx_tables(i)
        if i % 2 == 1:
            xpidx = ((i - 1) * 128 + np.arange(128, dtype=np.int32)).reshape(128, 1)
        else:
            xpidx = np.full((128, 1), 1 << 28, np.int32)
        bb, hf = i // 2, i % 2
        aidx = np.zeros((128, 8), np.int32)
        for j in range(8):
            aidx[:, j] = bb * 1024 + j * 128 + np.arange(128)
        in_maps.append(dict(
            xin=np.ascontiguousarray(xf[i]),
            w1=np.ascontiguousarray(_f32(I["L0_ffn_W1"])[i]),
            b1_rs=np.ascontiguousarray(_f32(I["L0_ffn_b1"])[i].reshape(HD // 128, 128).T),
            w2=np.ascontiguousarray(_f32(I["L0_ffn_W2"])[i]),
            b2=_f32(I["L0_ffn_b2"])[i].reshape(1, DM),
            alog=np.ascontiguousarray(_f32(I["L0_attn_A_log"])[i*256:(i+1)*256]),
            xprev_idx=xpidx,
            wmask=np.full((128, 1), 1.0 if i % 2 else 0.0, np.float32),
            pos=np.arange(t0, t0 + TLOC, dtype=np.float32).reshape(1, TLOC),
            qkidx=qk, vidx=vv,
            peidx=(np.arange(NC, dtype=np.int32) * NEXP + i).reshape(NEXP, 1),
            slwidx=(np.arange(NEXP, dtype=np.int32) * NEXP + i).reshape(NEXP, 1),
            aidx=aidx,
            chidx=np.ascontiguousarray(
                (np.arange(NC, dtype=np.int32)[None, :, None] * DI + i * 256
                 + np.arange(2, dtype=np.int32)[None, None, :] * 128
                 + np.arange(128, dtype=np.int32)[:, None, None])
                .transpose(0, 1, 2).reshape(128, 16, order="F").astype(np.int32)
                if False else _chidx(i)),
            **common))
    return in_maps


def kernel(**inputs):
    if "nc" not in _FUSED_CACHE:
        _FUSED_CACHE["nc"] = build_fused()
    nc = _FUSED_CACHE["nc"]
    in_maps = fused_in_maps(inputs)
    res = run_bass_kernel_spmd(nc, in_maps, core_ids=list(range(NC)))
    x = np.concatenate([res.results[i]["o_x"] for i in range(NC)], 0)
    return np.ascontiguousarray(x.reshape(4, TSEQ, DM), dtype=np.float32)


# version 09 (final)
# speedup vs baseline: 1.0482x; 1.0025x over previous
import numpy as np
from contextlib import ExitStack
import concourse.bass as bass
import concourse.mybir as mybir
from concourse.bass_core_rust import IndirectOffsetOnAxis
from concourse.bass_utils import run_bass_kernel_spmd

F32 = mybir.dt.float32
BF16 = mybir.dt.bfloat16
I32 = mybir.dt.int32
AF = mybir.ActivationFunctionType
ALU = mybir.AluOpType
AX = mybir.AxisListType

NC = 8
TLOC = 2048
DM = 1024
DI = 2048
NSTATE = 16
NEXP = 8
CAP = 2048
HD = 4096
NH = 16
DK = 64
TSEQ = 4096
LN2 = 6.93147180559945286e-01
GC2 = 2.0 * 0.7978845608028654
GA2 = GC2 * 0.044715

_LAUNCHES = []



class Prog:

    def __init__(self, nc, ctx):
        self.nc = nc
        self.ctx = ctx
        self.sem = {}
        self.count = {}
        self.swdge = set()
        self.touched = set()
        self.phase_names = {}
        for name in ("v", "s", "p", "g"):
            self._mksem(name)

    def _mksem(self, name):
        self._serial = getattr(self, "_serial", 0) + 1
        h = self.nc.alloc_semaphore(name=f"{name}_s{self._serial}")
        self.sem[name] = h
        self.count[name] = 0

    def new_phase(self):
        self.phase_names = {}

    def _canon(self, name):
        if name in ("v", "s", "p", "g") or name in self.swdge or name == "CC":
            return name
        if name not in self.phase_names:
            canon = f"H{len(self.phase_names)}"
            self.phase_names[name] = canon
            if canon not in self.sem:
                self._mksem(canon)
        return self.phase_names[name]

    def stream(self, name):
        if name not in self.sem:
            self._mksem(name)
        return name

    def eng(self, name):
        nc = self.nc
        return {"v": nc.vector, "s": nc.scalar, "p": nc.tensor, "g": nc.gpsimd,
                "sync": nc.sync}[name]

    def _wait(self, eng_obj, deps):
        need = {}
        for d in deps:
            if d is None:
                continue
            (s, v) = d
            if v > need.get(s, -1):
                need[s] = v
        for s, v in need.items():
            if v > 0:
                eng_obj.wait_ge(self.sem[s], v)

    def op(self, ename, method, *args, deps=(), **kw):
        eng_obj = self.eng(ename)
        deps = list(deps)
        if ename in ("v", "s", "g") and self.count[ename] > 0:
            deps.append((ename, self.count[ename]))
        self._wait(eng_obj, deps)
        inst = getattr(eng_obj, method)(*args, **kw)
        self.touched.add(ename)
        self.count[ename] += 1
        inst.then_inc(self.sem[ename], 1)
        return (ename, self.count[ename])

    def dma(self, stream, out, in_, deps=(), queue="sync", **kw):
        if queue == "g":
            self.stream(stream)
            self.swdge.add(stream)
        else:
            stream = self._canon(stream)
            self.stream(stream)
        eng_obj = self.eng(queue)
        self._wait(eng_obj, deps)
        inst = eng_obj.dma_start(out=out, in_=in_, **kw)
        self.touched.add(stream)
        self.count[stream] += 16
        inst.then_inc(self.sem[stream], 16)
        return (stream, self.count[stream])

    def idma(self, stream, out, out_offset, in_, in_offset, bounds, deps=(),
             compute_op=ALU.bypass, rot=8):
        if not hasattr(self, "_rot"):
            self._rot = {}
        i = self._rot.get(stream, 0)
        self._rot[stream] = i + 1
        sub = f"{stream}~{i % rot}"
        self.stream(sub)
        self.swdge.add(sub)
        g = self.eng("g")
        deps = list(deps)
        if self.count[sub] > 0:
            deps.append((sub, self.count[sub]))
        self._wait(g, deps)
        inst = g.indirect_dma_start(out=out, out_offset=out_offset, in_=in_,
                                    in_offset=in_offset, bounds_check=bounds,
                                    oob_is_err=False, compute_op=compute_op)
        self.touched.add(sub)
        self.count[sub] += 16
        inst.then_inc(self.sem[sub], 16)
        return (sub, self.count[sub])

    def rot_toks(self, stream, rot=8):
        return [(f"{stream}~{i}", self.count.get(f"{stream}~{i}", 0))
                for i in range(rot)]

    def coll(self, stream, kind, op, ins, outs, deps=()):
        self.stream(stream)
        self.swdge.add(stream)
        g = self.eng("g")
        self._wait(g, deps)
        inst = g.collective_compute(kind, op, replica_groups=[list(range(NC))],
                                    ins=ins, outs=outs)
        self.touched.add(stream)
        self.count[stream] += 1
        inst.then_inc(self.sem[stream])
        return (stream, self.count[stream])

    def tok(self, name):
        if name not in self.sem or name in self.phase_names:
            name = self._canon(name)
        return (name, self.count.get(name, 0))


class Builder:
    def __init__(self, name):
        self.nc = bass.Bass("TRN2", target_bir_lowering=False, name=name)
        self.ctx = ExitStack()
        self.pr = Prog(self.nc, self.ctx)
        self.block = self.ctx.enter_context(self.nc.Block())
        self._final_deps = []
        self.prefix = ""
        self._scope = None

    def pname(self, name):
        return self.prefix + name

    def sub_side(self):
        return "left" if self._scope is not None else "right"

    def sb(self, name, shape, dt=F32, ctx=None, side=None):
        if ctx is None and self._scope is not None:
            ctx = self._scope
            side = "right"
        return (ctx or self.ctx).enter_context(
            self.nc.sbuf_tensor(self.pname(name), shape, dt, side=side))

    def ps(self, name, shape, dt=F32):
        ctx = self._scope if self._scope is not None else self.ctx
        return ctx.enter_context(self.nc.psum_tensor(self.pname(name), shape, dt))

    def psn(self, name, shape, dt=F32):
        return self.nc.psum_tensor(self.pname(name), shape, dt)

    def din(self, name, shape, dt=F32):
        return self.nc.dram_tensor(name, shape, dt, kind="ExternalInput").ap()

    def dout(self, name, shape, dt=F32):
        return self.nc.dram_tensor(name, shape, dt, kind="ExternalOutput").ap()

    def dint(self, name, shape, dt=F32, **kw):
        return self.nc.dram_tensor(self.pname(name), shape, dt,
                                   kind="Internal", **kw).ap()

    def out_done(self, tok):
        self._final_deps.append(tok)

    def bound_reg(self, value):
        if not hasattr(self, "_bregs"):
            self._bregs = {}
        if value not in self._bregs:
            reg = self.nc.gpsimd.alloc_register(f"bnd{value}")
            self.nc.gpsimd.reg_mov(reg, value)
            self._bregs[value] = reg
        return self._bregs[value]

    def finish(self):
        pr = self.pr

        def _waits():
            need = {}
            for (s, v) in self._final_deps:
                if s in pr.sem:
                    need[s] = max(need.get(s, 0), v)
            for s, v in need.items():
                self.nc.sync.wait_ge(pr.sem[s], v)
            for sname, cnt in list(pr.count.items()):
                if cnt > 0 and sname not in need:
                    self.nc.sync.wait_ge(pr.sem[sname], cnt)

        self.block.sync(lambda _s: _waits())
        self.ctx.close()
        return self.nc


def body(b):
    def deco(f):
        if getattr(b, "_in_body", False):
            f()
            return f
        b._in_body = True
        try:
            b.block.sync(lambda _s: f())
        finally:
            b._in_body = False
        return f
    return deco



def emit_weight_prep(b, pairs, ctx=None):
    pr = b.pr
    side = b.sub_side() if ctx is not None else None
    tf = [b.sb(f"wpf{i}", [128, 4096], F32, ctx=ctx, side=side) for i in range(2)]
    tb = [b.sb(f"wpb{i}", [128, 4096], BF16, ctx=ctx, side=side) for i in range(2)]
    seq = []
    for (src, dst, R, C) in pairs:
        for r0 in range(0, R, 128):
            rr = min(128, R - r0)
            for c0 in range(0, C, 4096):
                cc = min(4096, C - c0)
                seq.append((src, dst, r0, rr, c0, cc))
    toks = []
    cvt_by_buf = [None, None]
    st_by_buf = [None, None]
    for j, (src, dst, r0, rr, c0, cc) in enumerate(seq):
        i = j % 2
        ld = pr.dma(f"wp{i}", tf[i][0:rr, 0:cc], src[r0:r0+rr, c0:c0+cc],
                    deps=[cvt_by_buf[i]])
        cvt_by_buf[i] = pr.op("v", "tensor_copy", tb[i][0:rr, 0:cc],
                              tf[i][0:rr, 0:cc], deps=[ld, st_by_buf[i]])
        st_by_buf[i] = pr.dma(f"wps{i}", dst[r0:r0+rr, c0:c0+cc],
                              tb[i][0:rr, 0:cc], deps=[cvt_by_buf[i]])
        toks.append(st_by_buf[i])
    return [pr.tok("wps0"), pr.tok("wps1")]


def make_ln_stats(b, tag):
    st = {
        "bnout": b.sb(f"{tag}_bnout", [128, 12], F32),
        "bnag": b.sb(f"{tag}_bnag", [128, 2], F32),
        "sd": b.sb(f"{tag}_sd", [128, 1], F32),
        "rstd": b.sb(f"{tag}_rstd", [128, 1], F32),
        "negmur": b.sb(f"{tag}_negmur", [128, 1], F32),
        "eps": b.sb(f"{tag}_eps", [128, 1], F32),
    }
    t = b.pr.op("v", "memset", st["eps"][:], 1e-5)
    st["_init"] = t
    return st


def emit_ln_tile(b, st, x_tile, out_tile, deps=(), out_deps=()):
    pr = b.pr
    pr.op("v", "bn_stats", st["bnout"][:, 0:6], x_tile[:, 0:512],
          deps=list(deps) + [st["_init"]])
    pr.op("v", "bn_stats", st["bnout"][:, 6:12], x_tile[:, 512:1024], deps=deps)
    pr.op("v", "bn_aggr", st["bnag"][:], st["bnout"][:])
    t3 = pr.op("s", "activation", st["sd"][:], st["bnag"][:, 1:2], AF.Sqrt,
               bias=st["eps"][:], deps=[b.pr.tok("v")])
    t4 = pr.op("v", "reciprocal", st["rstd"][:], st["sd"][:], deps=[t3])
    t5 = pr.op("v", "scalar_tensor_tensor", st["negmur"][:],
               st["bnag"][:, 0:1], -1.0, st["rstd"][:], ALU.mult, ALU.mult)
    t6 = pr.op("s", "activation", out_tile, x_tile, AF.Identity,
               bias=st["negmur"][:], scale=st["rstd"][:],
               deps=[t5] + list(out_deps))
    return t6


def pe_fence(b):
    b.pr.eng("p").wait_ge(b.pr.sem["v"], b.pr.count["v"])


def emit_row_broadcast(b, src_row, dst_sb, C, tag, deps=()):
    pr = b.pr
    ones = b.sb(f"{tag}_ones", [1, 128], F32)
    t0 = pr.op("v", "memset", ones[:], 1.0)
    toks = []
    with b.nc.psum_tensor(b.prefix + f"{tag}_ps", [128, 512], F32) as pst:
        for c0 in range(0, C, 512):
            cc = min(512, C - c0)
            tm = pr.op("p", "matmul", pst[:, 0:cc], ones[:], src_row[0:1, c0:c0+cc],
                       start=True, stop=True, deps=list(deps) + [t0, pr.tok("v")])
            toks.append(pr.op("v", "tensor_copy", dst_sb[:, c0:c0+cc],
                              pst[:, 0:cc], deps=[tm]))
    pe_fence(b)
    return toks[-1]


def emit_transposes(b, src_row_bf, dst_col_bf, colbase, ident, psum_bf, n_dchunks,
                    dstride, deps=()):
    pr = b.pr
    last = None
    out_toks = []
    for dc in range(n_dchunks):
        tt = pr.op("p", "transpose", psum_bf[:],
                   src_row_bf[:, dc*128:(dc+1)*128], ident[:],
                   deps=list(deps) + ([last] if last else []))
        last = pr.op("v", "tensor_copy",
                     dst_col_bf[:, dc*dstride + colbase: dc*dstride + colbase + 128],
                     psum_bf[:], deps=[tt])
        out_toks.append(last)
    return out_toks


def emit_gelu_tanh_psum(b, ps_tile, bias_col, w, tmp, out_bf, deps=()):
    pr = b.pr
    t1 = pr.op("s", "activation", tmp["ub"][:, 0:w], ps_tile, AF.Identity,
               bias=bias_col, deps=deps)
    t2 = pr.op("s", "activation", tmp["sq"][:, 0:w], tmp["ub"][:, 0:w], AF.Square)
    t3 = pr.op("v", "tensor_scalar", tmp["p1"][:, 0:w], tmp["sq"][:, 0:w],
               GA2, GC2, ALU.mult, ALU.add, deps=[t2])
    t4 = pr.op("v", "tensor_mul", tmp["warg"][:, 0:w], tmp["p1"][:, 0:w],
               tmp["ub"][:, 0:w], deps=[t1])
    t5 = pr.op("s", "activation", tmp["sq"][:, 0:w], tmp["warg"][:, 0:w],
               AF.Sigmoid, deps=[t4])
    t6 = pr.op("v", "tensor_mul", out_bf, tmp["sq"][:, 0:w], tmp["ub"][:, 0:w],
               deps=[t5])
    return t6, t1



def emit_ffn0_pre(b, io):
    pr = b.pr
    xr = io["xr"]
    gate = io["gate"]
    ws1 = io["ws1"]
    bs1_rs = io["bs1_rs"]
    ws2 = io["ws2"]
    bs2 = io["bs2"]
    ident_in = io["ident_in"]

    o_xn = io["o_xn"]
    o_probs = io["o_probs"]
    o_shared = io["o_shared"]

    ws1b_d = b.dint("ws1b", [DM, HD], BF16)
    ws2b_d = b.dint("ws2b", [HD, DM], BF16)

    NT = TLOC // 128

    @body(b)
    def _main():
        wp = emit_weight_prep(b, [(ws1, ws1b_d, DM, HD), (ws2, ws2b_d, HD, DM)])
        wp_done_all = wp

        ident = b.sb("identt", [128, 128], BF16)
        t_id = pr.dma("Lid", ident[:], ident_in)
        gatetf = b.sb("gatetf", [128, 8 * NEXP], F32)
        gatet = b.sb("gatet", [128, 8 * NEXP], BF16)
        t_gate0 = pr.dma("Lg", gatetf[:].rearrange("p (k e) -> p k e", k=8),
                         gate.rearrange("(k p) e -> p k e", p=128))
        t_gate = pr.op("v", "tensor_copy", gatet[:], gatetf[:], deps=[t_gate0])
        bs1t = b.sb("bs1t", [128, HD // 128], F32)
        t_bs1 = pr.dma("Lb1", bs1t[:], bs1_rs)
        bs2row = b.sb("bs2row", [1, DM], F32)
        t_bs2 = pr.dma("Lb2", bs2row[:], bs2)
        bs2bc = b.sb("bs2bc", [128, DM], F32)
        t_bs2bc = emit_row_broadcast(b, bs2row, bs2bc, DM, "b2b", deps=[t_bs2])

        st = make_ln_stats(b, "ln")
        xt = [b.sb(f"xt{i}", [128, DM], F32) for i in range(2)]
        xnb = [b.sb(f"xnb{i}", [128, DM], BF16) for i in range(2)]
        xnT = b.sb("xnT", [128, 8 * TLOC], BF16)
        pbf = b.ps("pbf", [128, 128], BF16)

        prev_ln = {0: None, 1: None}
        prev_users = {0: (), 1: ()}
        prev_trs = []
        for i in range(NT):
            bi = i % 2
            ld = pr.dma(f"Lx{bi}", xt[bi][:], xr[i*128:(i+1)*128, :],
                        deps=[prev_ln[bi]])
            t_ln = emit_ln_tile(b, st, xt[bi][:], xnb[bi][:], deps=[ld],
                                out_deps=prev_users[bi])
            prev_ln[bi] = t_ln
            t_st = pr.dma(f"Sxn{bi}", o_xn[i*128:(i+1)*128, :], xnb[bi][:],
                          deps=[t_ln])
            trs = emit_transposes(b, xnb[bi], xnT, i*128, ident, pbf, 8, TLOC,
                                  deps=[t_ln, t_id] + prev_trs[-1:])
            prev_trs = trs
            prev_users[bi] = (t_st, trs[-1], pr.tok("p"))
        b.out_done(pr.tok("Sxn0"))
        b.out_done(pr.tok("Sxn1"))
        t_xnT = pr.tok("v")

        pgate = b.ps("pgate", [128, NEXP], F32)
        probs_t = b.sb("probs_t", [128, NEXP], F32)
        negmax = b.sb("negmax", [128, 1], F32)
        ssum = b.sb("ssum", [128, 1], F32)
        sinv = b.sb("sinv", [128, 1], F32)
        prev_gate_rd = None
        prev_sp = None
        for i in range(NT):
            tms = []
            for k in range(8):
                dep0 = [t_xnT, t_gate] + ([prev_gate_rd] if prev_gate_rd else [])
                tms.append(pr.op("p", "matmul", pgate[:],
                                 xnT[:, k*TLOC + i*128: k*TLOC + (i+1)*128],
                                 gatet[:, k*NEXP:(k+1)*NEXP],
                                 start=(k == 0), stop=(k == 7),
                                 deps=dep0 if k == 0 else ()))
            t_mx = pr.op("v", "reduce_max", negmax[:], pgate[:], axis=AX.X,
                         negate=True, deps=[tms[-1]])
            t_e = pr.op("s", "activation", probs_t[:], pgate[:], AF.Exp,
                        bias=negmax[:], accum_out=ssum[:],
                        deps=[t_mx] + ([prev_sp] if prev_sp else []))
            t_r = pr.op("v", "reciprocal", sinv[:], ssum[:], deps=[t_e])
            t_p = pr.op("v", "tensor_scalar", probs_t[:], probs_t[:], sinv[:],
                        None, ALU.mult)
            prev_gate_rd = t_e
            prev_sp = pr.dma("Sp", o_probs[i*128:(i+1)*128, :], probs_t[:], deps=[t_p])
        b.out_done(pr.tok("Sp"))

        w1t = [b.sb(f"w1t{i}", [128, 8 * 128], BF16) for i in range(2)]
        w2t = [b.sb(f"w2t{i}", [128, 512], BF16) for i in range(2)]
        hT = b.sb("hT", [128, 32 * 512], BF16)
        gtmp = {"ub": b.sb("g_ub", [128, 512], F32),
                "sq": b.sb("g_sq", [128, 512], F32),
                "p1": b.sb("g_p1", [128, 512], F32),
                "warg": b.sb("g_wa", [128, 512], F32)}
        outrow = b.sb("outrow", [128, DM], F32)
        prev_ssh = None

        for nch in range(4):
            tn0 = nch * 512
            with b.nc.psum_tensor(b.prefix + f"hps{nch}", [128, 512], F32) as hps:
                t_ld_prev = {0: (pr.tok("v"), pr.tok("s")), 1: ()}
                prev_hps_rd = pr.tok("s")
                prev_ge = None
                for m in range(32):
                    bi = m % 2
                    ldw = pr.dma(f"Lw1{bi}",
                                 w1t[bi][:].rearrange("p (k c) -> p k c", k=8),
                                 ws1b_d[:, m*128:(m+1)*128].rearrange(
                                     "(k p) c -> p k c", p=128),
                                 deps=wp_done_all + list(t_ld_prev.get(bi) or ()))
                    mt = None
                    for k in range(8):
                        mt = pr.op("p", "matmul", hps[:],
                                   w1t[bi][:, k*128:(k+1)*128],
                                   xnT[:, k*TLOC + tn0: k*TLOC + tn0 + 512],
                                   start=(k == 0), stop=(k == 7),
                                   deps=[ldw, t_xnT, prev_hps_rd] if k == 0 else ())
                    ge, prev_hps_rd = emit_gelu_tanh_psum(
                        b, hps[:], bs1t[:, m:m+1], 512,
                        gtmp, hT[:, m*512:(m+1)*512],
                        deps=[mt, t_bs1] + ([prev_ge] if prev_ge else []))
                    prev_ge = ge
                    t_ld_prev[bi] = (ge,)
            t_hT = pr.tok("v")
            with (b.nc.psum_tensor(b.prefix + f"psA0_{nch}", [128, 512], F32) as psA0,
                  b.nc.psum_tensor(b.prefix + f"psA1_{nch}", [128, 512], F32) as psA1,
                  b.nc.psum_tensor(b.prefix + f"psA2_{nch}", [128, 512], F32) as psA2,
                  b.nc.psum_tensor(b.prefix + f"psA3_{nch}", [128, 512], F32) as psA3):
                psA = [psA0, psA1, psA2, psA3]
                for dn in range(2):
                    mts = [None] * 4
                    prev_ld = {0: (pr.tok("v"), pr.tok("s")), 1: ()}
                    for k in range(32):
                        bi = k % 2
                        ldw = pr.dma(f"Lw2{bi}", w2t[bi][:],
                                     ws2b_d[k*128:(k+1)*128, dn*512:(dn+1)*512],
                                     deps=wp_done_all + list(prev_ld.get(bi) or ()))
                        for tt in range(4):
                            dps = [ldw] if tt == 0 else []
                            if k == 0:
                                dps += [t_hT]
                            mts[tt] = pr.op("p", "matmul", psA[tt][:],
                                            hT[:, k*512 + tt*128: k*512 + (tt+1)*128],
                                            w2t[bi][:],
                                            start=(k == 0), stop=(k == 31),
                                            deps=dps)
                        prev_ld[bi] = (mts[3],)
                    for tt in range(4):
                        t_o = pr.op("v", "tensor_add", outrow[:, dn*512:(dn+1)*512],
                                    psA[tt][:], bs2bc[:, dn*512:(dn+1)*512],
                                    deps=[mts[tt], t_bs2bc] +
                                         ([prev_ssh] if prev_ssh else []))
                        trow = tn0 + tt*128
                        prev_ssh = pr.dma("Ssh", o_shared[trow:trow+128, dn*512:(dn+1)*512],
                                          outrow[:, dn*512:(dn+1)*512], deps=[t_o])
        b.out_done(pr.tok("Ssh"))


def build_ffn0_pre():
    b = Builder("ffn0_pre")
    io = {}
    io["xr"] = b.din("xr", [TLOC, DM], F32)
    io["gate"] = b.din("gate", [DM, NEXP], F32)
    io["ws1"] = b.din("ws1", [DM, HD], F32)
    io["bs1_rs"] = b.din("bs1_rs", [128, HD // 128], F32)
    io["ws2"] = b.din("ws2", [HD, DM], F32)
    io["bs2"] = b.din("bs2", [1, DM], F32)
    io["ident_in"] = b.din("ident", [128, 128], BF16)
    io["o_xn"] = b.dout("o_xn", [TLOC, DM], BF16)
    io["o_probs"] = b.dout("o_probs", [TLOC, NEXP], F32)
    io["o_shared"] = b.dout("o_shared", [TLOC, DM], F32)
    emit_ffn0_pre(b, io)
    return b.finish()



def _f32(a):
    return np.ascontiguousarray(a, dtype=np.float32)


def run_ffn0_pre(x_full, I):
    import ml_dtypes
    nc = build_ffn0_pre()
    xf = x_full.reshape(NC, TLOC, DM)
    ident = np.eye(128, dtype=ml_dtypes.bfloat16)
    bs1_rs = _f32(I["L0_ffn_bs1"]).reshape(HD // 128, 128).T.copy()
    common = dict(gate=_f32(I["L0_ffn_gate"]), ws1=_f32(I["L0_ffn_Ws1"]),
                  bs1_rs=bs1_rs, ws2=_f32(I["L0_ffn_Ws2"]),
                  bs2=_f32(I["L0_ffn_bs2"]).reshape(1, DM), ident=ident)
    in_maps = [dict(xr=_f32(xf[i]), **common) for i in range(NC)]
    res = run_bass_kernel_spmd(nc, in_maps, core_ids=list(range(NC)))
    xn = np.concatenate([res.results[i]["o_xn"] for i in range(NC)], 0)
    probs = np.concatenate([res.results[i]["o_probs"] for i in range(NC)], 0)
    shared = np.concatenate([res.results[i]["o_shared"] for i in range(NC)], 0)
    return xn, probs, shared


def kernel(**inputs):
    raise NotImplementedError("under construction")



def emit_ffn0_expert(b, io):
    pr = b.pr
    pe_in = io["pe_in"]
    xn_all = io["xn_all"]
    w1 = io["w1"]
    b1_rs = io["b1_rs"]
    w2 = io["w2"]
    b2 = io["b2"]
    ident_in = io["ident_in"]

    o_eout = io["o_eout"]
    o_tau = io["o_tau"]
    o_idx = io["o_idx"]
    o_slots = io["o_slots"]

    w1b_d = b.dint("w1b", [DM, HD], BF16)
    w2b_d = b.dint("w2b", [HD, DM], BF16)
    idx_d = b.dint("idx_d", [CAP, 1], I32)
    scr_d = b.dint("scr_d", [1, 16], F32)

    NITER = 34
    BIG = float(1 << 20)
    R_NEXT = 7

    @body(b)
    def _main():
        pe_flat = pe_in.rearrange("r (t u) -> (r t) u", u=1)
        moe_ctx = ExitStack()
        wp = emit_weight_prep(b, [(w1, w1b_d, DM, HD), (w2, w2b_d, HD, DM)],
                              ctx=moe_ctx)

        ident = b.sb("identt", [128, 128], BF16)
        t_id = pr.dma("Lid", ident[:], ident_in)
        pe = b.sb("pe_t", [NEXP, TLOC], F32, ctx=moe_ctx, side=b.sub_side())
        t_pe = pr.dma("Lpe", pe[:], pe_in)

        ones88 = b.sb("ones88", [NEXP, NEXP], F32, ctx=moe_ctx, side=b.sub_side())
        onesrow = b.sb("onesrow", [NEXP, TLOC], F32, ctx=moe_ctx, side=b.sub_side())
        ident8 = b.sb("ident8", [NEXP, NEXP], F32, ctx=moe_ctx, side=b.sub_side())
        iota88 = b.sb("iota88", [NEXP, NEXP], F32, ctx=moe_ctx, side=b.sub_side())
        iota128 = b.sb("iota128", [128, 1], F32)
        pr.op("v", "memset", ones88[:], 1.0)
        pr.op("v", "memset", onesrow[:], 1.0)
        pr.op("g", "iota", iota88[:], pattern=[[-1, NEXP]], base=0,
              channel_multiplier=1, allow_small_or_imprecise_dtypes=True)
        pr.op("g", "iota", iota128[:], pattern=[[1, 1]], base=0,
              channel_multiplier=1, allow_small_or_imprecise_dtypes=True)
        t_i8 = pr.op("v", "tensor_scalar", ident8[:], iota88[:], 0.0, None,
                     ALU.is_equal, deps=[pr.tok("g")])

        lo8 = b.sb("lo8", [NEXP, 1], F32)
        hi8 = b.sb("hi8", [NEXP, 1], F32)
        mid8 = b.sb("mid8", [NEXP, 1], F32)
        sel8 = b.sb("sel8", [NEXP, 1], F32)
        nsel8 = b.sb("nsel8", [NEXP, 1], F32)
        tmp8 = b.sb("tmp8", [NEXP, 1], F32)
        cnt8 = b.sb("cnt8", [NEXP, 1], F32)
        part8 = b.sb("part8", [NEXP, 1], F32)
        cmp8 = b.sb("cmp8", [NEXP, TLOC], F32, ctx=moe_ctx, side=b.sub_side())
        pcnt8 = b.ps("pcnt8", [NEXP, 1], F32)
        pr.op("v", "memset", lo8[:], 0.0)
        pr.op("v", "memset", hi8[:], 1.0)
        pr.op("v", "memset", mid8[:], 0.5)

        def count_gt(thr8, extra=()):
            t_c = pr.op("v", "tensor_scalar", cmp8[:], pe[:], thr8[:], None,
                        ALU.is_gt, ALU.add, accum_out=part8[:],
                        deps=[t_pe] + list(extra))
            t_m = pr.op("p", "matmul", pcnt8[:], ones88[:], part8[:],
                        start=True, stop=True, deps=[t_c, pr.tok("v")])
            return pr.op("v", "tensor_copy", cnt8[:], pcnt8[:], deps=[t_m])

        for it in range(NITER):
            t_ct = count_gt(mid8)
            pr.op("v", "tensor_scalar", sel8[:], cnt8[:], float(CAP) - 0.5,
                  None, ALU.is_gt, deps=[t_ct])
            pr.op("v", "tensor_scalar", nsel8[:], sel8[:], -1.0, 1.0,
                  ALU.mult, ALU.add)
            pr.op("v", "tensor_mul", tmp8[:], sel8[:], mid8[:])
            pr.op("v", "tensor_mul", lo8[:], nsel8[:], lo8[:])
            pr.op("v", "tensor_add", lo8[:], lo8[:], tmp8[:])
            pr.op("v", "tensor_mul", tmp8[:], nsel8[:], mid8[:])
            pr.op("v", "tensor_mul", hi8[:], sel8[:], hi8[:])
            pr.op("v", "tensor_add", hi8[:], hi8[:], tmp8[:])
            pr.op("v", "tensor_add", mid8[:], lo8[:], hi8[:])
            pr.op("v", "tensor_scalar", mid8[:], mid8[:], 0.5, None, ALU.mult)

        t_clo = count_gt(lo8)
        c1 = b.sb("c1", [1, 1], F32)
        pr.op("v", "tensor_copy", c1[:], cnt8[0:1, :], deps=[t_clo])
        mv = b.sb("mv", [NEXP, TLOC], F32, ctx=moe_ctx, side=b.sub_side())
        rowmin = b.sb("rowmin", [NEXP, 1], F32, ctx=moe_ctx, side=b.sub_side())
        mrow = b.sb("mrow", [1, 16], F32)
        mj1 = b.sb("mj1", [1, 1], F32)
        mlist = b.sb("mlist", [1, R_NEXT], F32)
        thr8 = b.sb("thr8", [NEXP, 1], F32)
        pmid = b.ps("pmid", [NEXP, 1], F32)
        pr.op("v", "tensor_copy", thr8[:], lo8[:])
        for j in range(R_NEXT):
            t_c = pr.op("v", "tensor_scalar", cmp8[:], pe[:], thr8[:], None,
                        ALU.is_gt)
            pr.op("v", "tensor_mul", mv[:], cmp8[:], pe[:])
            pr.op("v", "tensor_scalar", cmp8[:], cmp8[:], -2.0, 2.0,
                  ALU.mult, ALU.add)
            t_mv = pr.op("v", "tensor_add", mv[:], mv[:], cmp8[:])
            t_rm = pr.op("v", "tensor_reduce", rowmin[:], mv[:], AX.X, ALU.min)
            t_st = pr.dma("EGs", scr_d[0:1, 0:NEXP].rearrange("u r -> r u"),
                          rowmin[:], deps=[t_rm])
            t_ld = pr.dma("EGl", mrow[0:1, 0:NEXP], scr_d[0:1, 0:NEXP], deps=[t_st])
            t_mn = pr.op("v", "tensor_reduce", mj1[:], mrow[0:1, 0:NEXP], AX.X,
                         ALU.min, deps=[t_ld])
            t_cp = pr.op("v", "tensor_copy", mlist[0:1, j:j+1], mj1[:])
            t_bb = pr.op("p", "matmul", pmid[:], ones88[0:1, :], mj1[:],
                         start=True, stop=True, deps=[t_cp, pr.tok("v")])
            pr.op("v", "tensor_copy", thr8[:], pmid[:], deps=[t_bb])
        vk = b.sb("vk", [1, 1], F32)
        ej = b.sb("ej", [1, 1], F32)
        tt_ = b.sb("tt_", [1, 1], F32)
        diff = b.sb("diff", [1, 1], F32)
        pr.op("v", "tensor_scalar", diff[:], c1[:], -float(CAP), None, ALU.add)
        pr.op("v", "memset", vk[:], 0.0)
        for j in range(R_NEXT):
            if j < R_NEXT - 1:
                pr.op("v", "tensor_scalar", ej[:], diff[:], float(j), None,
                      ALU.is_equal)
            else:
                pr.op("v", "tensor_scalar", ej[:], diff[:], float(j) - 0.5, None,
                      ALU.is_gt)
            pr.op("v", "tensor_mul", tt_[:], ej[:], mlist[0:1, j:j+1])
            pr.op("v", "tensor_add", vk[:], vk[:], tt_[:])
        vk8 = b.sb("vk8", [NEXP, 1], F32)
        t_vb = pr.op("p", "matmul", pmid[:], ones88[0:1, :], vk[:],
                     start=True, stop=True, deps=[pr.tok("v")])
        pr.op("v", "tensor_copy", vk8[:], pmid[:], deps=[t_vb])
        t_cgt = count_gt(vk8)
        cgt1 = b.sb("cgt1", [1, 1], F32)
        pr.op("v", "tensor_copy", cgt1[:], cnt8[0:1, :], deps=[t_cgt])
        tau_sb = b.sb("tau_sb", [1, 2], F32)
        pr.op("v", "tensor_copy", tau_sb[0:1, 0:1], vk[:])
        t_tau = pr.op("v", "tensor_copy", tau_sb[0:1, 1:2], cgt1[:])
        b.out_done(pr.dma("Stau", o_tau, tau_sb[:], deps=[t_tau]))

        gt = b.sb("gt", [NEXP, TLOC], F32, ctx=moe_ctx, side=b.sub_side())
        eq = b.sb("eq", [NEXP, TLOC], F32, ctx=moe_ctx, side=b.sub_side())
        pfx = b.sb("pfx", [NEXP, TLOC], F32, ctx=moe_ctx, side=b.sub_side())
        base8 = b.sb("base8", [NEXP, 1], F32)
        rowtot = b.sb("rowtot", [NEXP, 1], F32)
        brow = b.sb("brow", [1, 16], F32)
        brow2 = b.sb("brow2", [1, 16], F32)
        kq8 = b.sb("kq8", [NEXP, 1], F32)
        mask = b.sb("mask", [NEXP, TLOC], F32, ctx=moe_ctx, side=b.sub_side())
        slotp = b.sb("slotp", [NEXP, TLOC], F32, ctx=moe_ctx, side=b.sub_side())

        eg_prev = [None]
        def row_excl_bases(src_col, dst_col, extra=()):
            t1 = pr.dma("EGs", scr_d[0:1, 0:NEXP].rearrange("u r -> r u"),
                        src_col,
                        deps=list(extra) + ([eg_prev[0]] if eg_prev[0] else []))
            t2 = pr.dma("EGl", brow[0:1, 0:NEXP], scr_d[0:1, 0:NEXP], deps=[t1])
            t3 = pr.op("v", "tensor_tensor_scan", brow2[0:1, 0:NEXP],
                       onesrow[0:1, 0:NEXP], brow[0:1, 0:NEXP], 0.0,
                       ALU.mult, ALU.add, deps=[t2])
            t4 = pr.op("v", "tensor_sub", brow2[0:1, 0:NEXP], brow2[0:1, 0:NEXP],
                       brow[0:1, 0:NEXP])
            t5 = pr.dma("EGs", scr_d[0:1, 0:NEXP], brow2[0:1, 0:NEXP], deps=[t4])
            t6 = pr.dma("EGl", dst_col,
                        scr_d[0:1, 0:NEXP].rearrange("u r -> r u"), deps=[t5])
            eg_prev[0] = t6
            return t6

        t_gt = pr.op("v", "tensor_scalar", gt[:], pe[:], vk8[:], None, ALU.is_gt)
        t_eq = pr.op("v", "tensor_scalar", eq[:], pe[:], vk8[:], None, ALU.is_equal)
        pr.op("v", "tensor_tensor_scan", pfx[:], onesrow[:], eq[:], 0.0,
              ALU.mult, ALU.add)
        pr.op("v", "tensor_copy", rowtot[:], pfx[:, TLOC-1:TLOC])
        t_b = row_excl_bases(rowtot[:], base8[:], extra=[pr.tok("v")])
        pr.op("v", "tensor_scalar", pfx[:], pfx[:], base8[:], None, ALU.add,
              deps=[t_b])
        t_kq = pr.op("p", "matmul", pmid[:], ones88[0:1, :], cgt1[:],
                     start=True, stop=True, deps=[pr.tok("v")])
        pr.op("v", "tensor_copy", kq8[:], pmid[:], deps=[t_kq])
        pr.op("v", "tensor_scalar", kq8[:], kq8[:], -1.0, float(CAP),
              ALU.mult, ALU.add)
        seleq = b.sb("seleq", [NEXP, TLOC], F32, ctx=moe_ctx, side=b.sub_side())
        pr.op("v", "tensor_scalar", seleq[:], pfx[:], kq8[:], None, ALU.is_le)
        pr.op("v", "tensor_mul", seleq[:], seleq[:], eq[:])
        pr.op("v", "tensor_add", mask[:], gt[:], seleq[:])
        pr.op("v", "tensor_tensor_scan", slotp[:], onesrow[:], mask[:], 0.0,
              ALU.mult, ALU.add)
        pr.op("v", "tensor_copy", rowtot[:], slotp[:, TLOC-1:TLOC])
        t_b2 = row_excl_bases(rowtot[:], base8[:], extra=[pr.tok("v")])
        pr.op("v", "tensor_scalar", slotp[:], slotp[:], base8[:], None, ALU.add,
              deps=[t_b2])
        slotm = b.sb("slotm", [NEXP, TLOC], F32, ctx=moe_ctx, side=b.sub_side())
        pr.op("v", "tensor_scalar", slotm[:], slotp[:], -BIG, None, ALU.add)
        pr.op("v", "tensor_mul", slotm[:], slotm[:], mask[:])
        t_sm = pr.op("v", "tensor_scalar", slotm[:], slotm[:], BIG, None, ALU.add)

        b.out_done(pr.dma("Ssl", o_slots, slotm[:], deps=[t_sm]))

        ptr8 = b.ps("ptr8", [128, NEXP], F32)
        ptcol = b.sb("ptcol", [128, NEXP], F32, ctx=moe_ctx, side=b.sub_side())
        slotcol = b.sb("slotcol", [128, 1], F32)
        slot_i = b.sb("slot_i", [128, 1], I32)
        gidf = b.sb("gidf", [128, 1], F32)
        gid_i = b.sb("gid_i", [128, 1], I32)
        prev_sc = None
        for s in range(16):
            t_tr = pr.op("p", "transpose", ptr8[:], slotm[:, s*128:(s+1)*128],
                         ident8[:], deps=[t_sm, t_i8, pr.tok("v")])
            t_pc = pr.op("v", "tensor_copy", ptcol[:], ptr8[:], deps=[t_tr])
            for r in range(NEXP):
                pr.op("v", "tensor_scalar", slotcol[:], ptcol[:, r:r+1], -1.0,
                      None, ALU.add,
                      deps=[prev_sc] if prev_sc else [])
                t_si = pr.op("v", "tensor_copy", slot_i[:], slotcol[:])
                pr.op("v", "tensor_scalar", gidf[:], iota128[:],
                      float(r * TLOC + s * 128), None, ALU.add)
                t_gi = pr.op("v", "tensor_copy", gid_i[:], gidf[:])
                prev_sc = pr.idma("SCAT", idx_d,
                                  IndirectOffsetOnAxis(ap=slot_i[:], axis=0),
                                  gid_i[:], None, b.bound_reg(CAP - 1),
                                  deps=[t_si, t_gi])
        t_scat_l = pr.rot_toks("SCAT")
        moe_ctx.close()
        drain = [pr.tok("Ssl")]
        drain_t = b.sb("drain_t", [1, 1], F32)
        t_dv = pr.op("v", "memset", drain_t[:], 0.0, deps=drain)
        pr.op("s", "activation", drain_t[:], drain_t[:], AF.Identity,
              deps=[t_dv])

        idx_sb = b.sb("idx_sb", [128, 16], I32)
        t_idx = pr.dma("Lidx", idx_sb[:],
                       idx_d.rearrange("(s p) one -> p (s one)", p=128),
                       deps=t_scat_l + drain, queue="g",
                       allow_slow_non_contiguous=True)
        b.out_done(pr.dma("Sidx", o_idx, idx_sb[:], deps=[t_idx] + drain))
        tok_sb = b.sb("tok_sb", [128, 16 * DM], BF16)
        scores = b.sb("scores", [128, 16], F32)
        gt_toks = []
        for s in range(16):
            gt_toks.append(pr.idma("GATH", tok_sb[:, s*DM:(s+1)*DM], None,
                                   xn_all, IndirectOffsetOnAxis(ap=idx_sb[:, s:s+1], axis=0),
                                   b.bound_reg(NC * TLOC - 1), deps=[t_idx]))
            gt_toks.append(pr.idma("GATH", scores[:, s:s+1], None,
                                   pe_flat,
                                   IndirectOffsetOnAxis(ap=idx_sb[:, s:s+1], axis=0),
                                   b.bound_reg(NC * TLOC - 1), deps=[t_idx]))
        t_gath_l = pr.rot_toks("GATH")

        tokT = b.sb("tokT", [128, 8 * CAP], BF16)
        pbf2 = b.ps("pbf2", [128, 128], BF16)
        prev_tr = []
        for s in range(16):
            prev_tr = emit_transposes(b, tok_sb[:, s*DM:(s+1)*DM], tokT, s*128,
                                      ident, pbf2, 8, CAP,
                                      deps=t_gath_l + [t_id] + prev_tr[-1:])
        t_tokT = pr.tok("v")

        b1t = b.sb("b1t", [128, HD // 128], F32)
        t_b1 = pr.dma("Lb1", b1t[:], b1_rs)
        b2row = b.sb("b2row", [1, DM], F32)
        t_b2r = pr.dma("Lb2", b2row[:], b2)
        b2bc = b.sb("b2bc", [128, DM], F32)
        t_b2bc = emit_row_broadcast(b, b2row, b2bc, DM, "b2be", deps=[t_b2r])

        w1t = [b.sb(f"w1t{i}", [128, 8 * 128], BF16) for i in range(2)]
        w2t = [b.sb(f"w2t{i}", [128, 512], BF16) for i in range(2)]
        hT = b.sb("hT", [128, 32 * 512], BF16)
        gtmp = {"ub": b.sb("g_ub", [128, 512], F32),
                "sq": b.sb("g_sq", [128, 512], F32),
                "p1": b.sb("g_p1", [128, 512], F32),
                "warg": b.sb("g_wa", [128, 512], F32)}
        outrow = b.sb("outrow", [128, DM], F32)
        outsc = b.sb("outsc", [128, DM], F32)
        prev_ssh = None

        for nch in range(4):
            tn0 = nch * 512
            with b.nc.psum_tensor(b.prefix + f"ehps{nch}", [128, 512], F32) as hps:
                t_ld_prev = {0: (pr.tok("v"), pr.tok("s")), 1: ()}
                prev_hps_rd = pr.tok("s")
                prev_ge = None
                for m in range(32):
                    bi = m % 2
                    ldw = pr.dma(f"Lw1{bi}",
                                 w1t[bi][:].rearrange("p (k c) -> p k c", k=8),
                                 w1b_d[:, m*128:(m+1)*128].rearrange(
                                     "(k p) c -> p k c", p=128),
                                 deps=wp + list(t_ld_prev.get(bi) or ()))
                    mt = None
                    for k in range(8):
                        mt = pr.op("p", "matmul", hps[:],
                                   w1t[bi][:, k*128:(k+1)*128],
                                   tokT[:, k*CAP + tn0: k*CAP + tn0 + 512],
                                   start=(k == 0), stop=(k == 7),
                                   deps=[ldw, t_tokT, prev_hps_rd] if k == 0 else ())
                    ge, prev_hps_rd = emit_gelu_tanh_psum(
                        b, hps[:], b1t[:, m:m+1], 512,
                        gtmp, hT[:, m*512:(m+1)*512],
                        deps=[mt, t_b1] + ([prev_ge] if prev_ge else []))
                    prev_ge = ge
                    t_ld_prev[bi] = (ge,)
            t_hT = pr.tok("v")
            with (b.nc.psum_tensor(b.prefix + f"epsA0_{nch}", [128, 512], F32) as psA0,
                  b.nc.psum_tensor(b.prefix + f"epsA1_{nch}", [128, 512], F32) as psA1,
                  b.nc.psum_tensor(b.prefix + f"epsA2_{nch}", [128, 512], F32) as psA2,
                  b.nc.psum_tensor(b.prefix + f"epsA3_{nch}", [128, 512], F32) as psA3):
                psA = [psA0, psA1, psA2, psA3]
                for dn in range(2):
                    mts = [None] * 4
                    prev_ld = {0: (pr.tok("v"), pr.tok("s")), 1: ()}
                    for k in range(32):
                        bi = k % 2
                        ldw = pr.dma(f"Lw2{bi}", w2t[bi][:],
                                     w2b_d[k*128:(k+1)*128, dn*512:(dn+1)*512],
                                     deps=wp + list(prev_ld.get(bi) or ()))
                        for tt in range(4):
                            dps = [ldw] if tt == 0 else []
                            if k == 0:
                                dps += [t_hT]
                            mts[tt] = pr.op("p", "matmul", psA[tt][:],
                                            hT[:, k*512 + tt*128: k*512 + (tt+1)*128],
                                            w2t[bi][:],
                                            start=(k == 0), stop=(k == 31),
                                            deps=dps)
                        prev_ld[bi] = (mts[3],)
                    for tt in range(4):
                        stile = nch * 4 + tt
                        t_o = pr.op("v", "tensor_add", outrow[:, dn*512:(dn+1)*512],
                                    psA[tt][:], b2bc[:, dn*512:(dn+1)*512],
                                    deps=[mts[tt], t_b2bc])
                        t_s = pr.op("v", "tensor_scalar", outsc[:, dn*512:(dn+1)*512],
                                    outrow[:, dn*512:(dn+1)*512],
                                    scores[:, stile:stile+1], None, ALU.mult,
                                    deps=t_gath_l + ([prev_ssh] if prev_ssh else []))
                        srow = stile * 128
                        prev_ssh = pr.dma("Seo", o_eout[srow:srow+128, dn*512:(dn+1)*512],
                                          outsc[:, dn*512:(dn+1)*512], deps=[t_s])
        b.out_done(pr.tok("Seo"))


def build_ffn0_expert():
    b = Builder("ffn0_expert")
    io = {}
    io["pe_in"] = b.din("pe", [NEXP, TLOC], F32)
    io["xn_all"] = b.din("xn_all", [NC * TLOC, DM], BF16)
    io["w1"] = b.din("w1", [DM, HD], F32)
    io["b1_rs"] = b.din("b1_rs", [128, HD // 128], F32)
    io["w2"] = b.din("w2", [HD, DM], F32)
    io["b2"] = b.din("b2", [1, DM], F32)
    io["ident_in"] = b.din("ident", [128, 128], BF16)
    io["o_eout"] = b.dout("o_eout", [CAP, DM], F32)
    io["o_tau"] = b.dout("o_tau", [1, 2], F32)
    io["o_idx"] = b.dout("o_idx", [128, 16], I32)
    io["o_slots"] = b.dout("o_slots", [NEXP, TLOC], F32)
    emit_ffn0_expert(b, io)
    return b.finish()


def emit_ffn0_close(b, io):
    pr = b.pr
    xr = io["xr"]
    shared = io["shared"]
    slots_win = io["slots_win"]
    eout_all = io.get("eout_all")
    eouts = None if eout_all is not None else [io[f"eout{e}"] for e in range(NEXP)]

    o_x = io["o_x"]

    @body(b)
    def _main():
        sw = b.sb("sw", [NEXP, TLOC], F32)
        t_sw = pr.dma("Lsw", sw[:], slots_win)
        ident8 = b.sb("ident8", [NEXP, NEXP], F32)
        iota88 = b.sb("iota88", [NEXP, NEXP], F32)
        pr.op("g", "iota", iota88[:], pattern=[[-1, NEXP]], base=0,
              channel_multiplier=1, allow_small_or_imprecise_dtypes=True)
        t_i8 = pr.op("v", "tensor_scalar", ident8[:], iota88[:], 0.0, None,
                     ALU.is_equal, deps=[pr.tok("g")])
        ptr8 = b.ps("ptr8", [128, NEXP], F32)
        ptcol = b.sb("ptcol", [128, NEXP], F32)
        slotcol = b.sb("slotcol", [128, 1], F32)
        slot_i = b.sb("slot_i", [128, 1], I32)
        xt = b.sb("xt", [128, DM], F32)
        acc = b.sb("acc", [128, DM], F32)
        outt = b.sb("outt", [128, DM], F32)

        prev_users = []
        for i in range(TLOC // 128):
            t_tr = pr.op("p", "transpose", ptr8[:], sw[:, i*128:(i+1)*128],
                         ident8[:], deps=[t_sw, t_i8] + prev_users)
            t_pc = pr.op("v", "tensor_copy", ptcol[:], ptr8[:], deps=[t_tr])
            t_ld = pr.dma(f"Lx{i%2}", xt[:], xr[i*128:(i+1)*128, :],
                          deps=prev_users)
            t_sh = pr.dma(f"Lsh{i%2}", acc[:], shared[i*128:(i+1)*128, :],
                          deps=prev_users)
            gath_toks = []
            for e in range(NEXP):
                if eout_all is not None:
                    t_sc = pr.op("v", "tensor_scalar", slotcol[:], ptcol[:, e:e+1],
                                 float(e * CAP - 1), None, ALU.add,
                                 deps=gath_toks[-1:])
                    t_si = pr.op("v", "tensor_copy", slot_i[:], slotcol[:])
                    gath_toks.append(pr.idma("GA", acc[:], None, eout_all,
                                             IndirectOffsetOnAxis(ap=slot_i[:], axis=0),
                                             b.bound_reg((e + 1) * CAP - 1),
                                             deps=[t_si, t_sh],
                                             compute_op=ALU.add))
                    continue
                t_sc = pr.op("v", "tensor_scalar", slotcol[:], ptcol[:, e:e+1],
                             -1.0, None, ALU.add,
                             deps=gath_toks[-1:])
                t_si = pr.op("v", "tensor_copy", slot_i[:], slotcol[:])
                gath_toks.append(pr.idma("GA", acc[:], None, eouts[e],
                                         IndirectOffsetOnAxis(ap=slot_i[:], axis=0),
                                         b.bound_reg(CAP - 1),
                                         deps=[t_si, t_sh],
                                         compute_op=ALU.add))
            t_o = pr.op("v", "scalar_tensor_tensor", outt[:], acc[:], 0.5,
                        xt[:], ALU.mult, ALU.add,
                        deps=[t_ld] + gath_toks)
            t_st = pr.dma("Sx", o_x[i*128:(i+1)*128, :], outt[:], deps=[t_o])
            prev_users = [t_st, t_o]
        b.out_done(pr.tok("Sx"))


def build_ffn0_close():
    b = Builder("ffn0_close")
    io = {}
    io["xr"] = b.din("xr", [TLOC, DM], F32)
    io["shared"] = b.din("shared", [TLOC, DM], F32)
    io["slots_win"] = b.din("slots_win", [NEXP, TLOC], F32)
    io["o_x"] = b.dout("o_x", [TLOC, DM], F32)
    for e in range(NEXP):
        io[f"eout{e}"] = b.din(f"eout{e}", [CAP, DM], F32)
    emit_ffn0_close(b, io)
    return b.finish()

def run_moe_layer(x_full, I, pref):
    import ml_dtypes
    xf = x_full.reshape(NC, TLOC, DM)
    ident = np.eye(128, dtype=ml_dtypes.bfloat16)

    ncA = build_ffn0_pre()
    bs1_rs = _f32(I[pref + "bs1"]).reshape(HD // 128, 128).T.copy()
    common = dict(gate=_f32(I[pref + "gate"]), ws1=_f32(I[pref + "Ws1"]),
                  bs1_rs=bs1_rs, ws2=_f32(I[pref + "Ws2"]),
                  bs2=_f32(I[pref + "bs2"]).reshape(1, DM), ident=ident)
    in_maps = [dict(xr=_f32(xf[i]), **common) for i in range(NC)]
    resA = run_bass_kernel_spmd(ncA, in_maps, core_ids=list(range(NC)))
    xn = np.concatenate([resA.results[i]["o_xn"] for i in range(NC)], 0)
    probs = np.concatenate([resA.results[i]["o_probs"] for i in range(NC)], 0)
    sharedf = np.concatenate([resA.results[i]["o_shared"] for i in range(NC)], 0)

    ncB = build_ffn0_expert()
    in_maps = []
    for e in range(NC):
        b1_rs = np.ascontiguousarray(_f32(I[pref + "b1"])[e].reshape(HD // 128, 128).T)
        in_maps.append(dict(
            pe=np.ascontiguousarray(probs[:, e].reshape(NC, TLOC)),
            xn_all=np.ascontiguousarray(xn),
            w1=np.ascontiguousarray(_f32(I[pref + "W1"])[e]), b1_rs=b1_rs,
            w2=np.ascontiguousarray(_f32(I[pref + "W2"])[e]),
            b2=_f32(I[pref + "b2"])[e].reshape(1, DM), ident=ident))
    resB = run_bass_kernel_spmd(ncB, in_maps, core_ids=list(range(NC)))
    eouts = [np.ascontiguousarray(resB.results[e]["o_eout"]) for e in range(NC)]
    slots = np.stack([resB.results[e]["o_slots"].reshape(-1) for e in range(NC)])

    ncC = build_ffn0_close()
    in_maps = []
    for i in range(NC):
        im = dict(xr=_f32(xf[i]),
                  shared=np.ascontiguousarray(sharedf[i*TLOC:(i+1)*TLOC]),
                  slots_win=np.ascontiguousarray(slots[:, i*TLOC:(i+1)*TLOC]))
        for e in range(NC):
            im[f"eout{e}"] = eouts[e]
        in_maps.append(im)
    resC = run_bass_kernel_spmd(ncC, in_maps, core_ids=list(range(NC)))
    x1 = np.concatenate([resC.results[i]["o_x"] for i in range(NC)], 0)
    return x1.reshape(x_full.shape)


def emit_mamba_pre(b, io):
    pr = b.pr
    xr = io["xr"]
    xprev = io["xprev"]
    w_inT = io["w_inT"]
    normw_rs = io["normw_rs"]
    normb_rs = io["normb_rs"]
    convw = io["convw"]
    convb_rs = io["convb_rs"]
    w_dtT = io["w_dtT"]
    bdt_rs = io["bdt_rs"]
    w_BT = io["w_BT"]
    w_CT = io["w_CT"]
    ident_in = io["ident_in"]

    o_dt = io["o_dt"]
    o_u = io["o_u"]
    o_xc = io["o_xc"]
    o_z = io["o_z"]
    o_B = io["o_B"]
    o_C = io["o_C"]

    w_inb_d = b.dint("w_inb", [DM, 2 * DI], BF16)
    w_dtb_d = b.dint("w_dtb", [DI, DI], BF16)
    NT = TLOC // 128

    @body(b)
    def _main():
        xnT_ctx = ExitStack()
        W = 128 * (NT + 1)
        xnT = b.sb("xnT", [128, 8 * W], BF16, ctx=xnT_ctx, side=b.sub_side())
        prep_ctx = ExitStack()
        normw = b.sb("normw", [128, DM // 128], F32)
        t_nw = pr.dma("Lnw", normw[:], normw_rs)
        wpf = b.sb("wpf", [128, 4096], F32, ctx=prep_ctx, side=b.sub_side())
        wpb = b.sb("wpb", [128, 4096], BF16, ctx=prep_ctx, side=b.sub_side())
        wp_toks = []
        prev = None
        for r in range(DM // 128):
            ld = pr.dma("WP0", wpf[:], w_inT[r*128:(r+1)*128, 0:4096],
                        deps=[prev])
            t_s = pr.op("v", "tensor_scalar", wpb[:], wpf[:],
                        normw[:, r:r+1], None, ALU.mult, deps=[ld, t_nw])
            prev = pr.dma("WP1", w_inb_d[r*128:(r+1)*128, 0:4096], wpb[:],
                          deps=[t_s])
            wp_toks.append(prev)
        wp1 = [pr.tok("WP0"), pr.tok("WP1")]
        wp2 = emit_weight_prep(b, [(w_dtT, w_dtb_d, DI, DI)], ctx=prep_ctx)
        wp_all = wp1 + wp2
        prep_ctx.close()

        ident = b.sb("identt", [128, 128], BF16)
        t_id = pr.dma("Lid", ident[:], ident_in)

        normb = b.sb("normb", [128, DM // 128], F32)
        t_nb = pr.dma("Lnb", normb[:], normb_rs)
        normb_bf = b.sb("normb_bf", [128, DM // 128], BF16)
        nwinv = b.sb("nwinv", [128, DM // 128], F32)
        pr.op("v", "reciprocal", nwinv[:], normw[:], deps=[t_nw])
        pr.op("v", "tensor_mul", nwinv[:], nwinv[:], normb[:], deps=[t_nb])
        t_nbb = pr.op("v", "tensor_copy", normb_bf[:], nwinv[:])
        bias_e = b.sb("bias_e", [128, 2 * DI // 128], F32)
        wtile = [b.sb(f"wtile{i}", [128, 128], BF16) for i in range(2)]
        pbias = b.ps("pbias", [128, 1], F32)
        prev_ev = None
        mts = []
        for ec in range(2 * DI // 128):
            for k in range(DM // 128):
                bi = k % 2
                ldw = pr.dma(f"LB{bi}", wtile[bi][:],
                             w_inb_d[k*128:(k+1)*128, ec*128:(ec+1)*128],
                             deps=wp_all + mts[-2:-1] +
                                  ([prev_ev] if prev_ev else []))
                mts.append(pr.op("p", "matmul", pbias[:], wtile[bi][:],
                                 normb_bf[:, k:k+1],
                                 start=(k == 0), stop=(k == DM // 128 - 1),
                                 deps=[ldw, t_nbb] + ([prev_ev] if prev_ev and k == 0 else [])))
            prev_ev = pr.op("v", "tensor_copy", bias_e[:, ec:ec+1], pbias[:],
                            deps=[mts[-1]])
        t_bias = prev_ev

        st = make_ln_stats(b, "ln")
        xt = [b.sb(f"xt{i}", [128, DM], F32) for i in range(2)]
        xnb = [b.sb(f"xnb{i}", [128, DM], BF16) for i in range(2)]
        pbf = b.ps("pbf", [128, 128], BF16)
        prev_ln = {0: None, 1: None}
        prev_users = {0: (), 1: ()}
        prev_trs = []
        for i in range(NT + 1):
            bi = i % 2
            src = xprev if i == 0 else xr[(i-1)*128:i*128, :]
            ld = pr.dma(f"Lx{bi}", xt[bi][:], src, deps=[prev_ln[bi]])
            t_ln = emit_ln_tile(b, st, xt[bi][:], xnb[bi][:], deps=[ld],
                                out_deps=prev_users[bi])
            prev_ln[bi] = t_ln
            trs = emit_transposes(b, xnb[bi], xnT, i*128, ident, pbf, 8, W,
                                  deps=[t_ln, t_id] + prev_trs[-1:])
            prev_trs = trs
            prev_users[bi] = (trs[-1], pr.tok("p"))
        t_xnT = pr.tok("v")

        convw_t = b.sb("convw_t", [128, DI // 128 * 4], F32)
        t_cw = pr.dma("Lcw", convw_t[:], convw)
        wm = b.sb("wm", [128, 1], F32)
        t_wm = pr.dma("Lwm", wm[:], io["wmask"])
        convb_t = b.sb("convb_t", [128, DI // 128], F32)
        t_cb = pr.dma("Lcb", convb_t[:], convb_rs)
        w1t = [b.sb(f"w1t{i}", [128, 8 * 128], BF16) for i in range(2)]
        xpc = b.sb("xpc", [128, W], F32)
        cacc = b.sb("cacc", [128, TLOC], F32)
        xcT_bf = b.sb("xcT_bf", [128, 16 * TLOC], BF16)
        zbf = b.sb("zbf", [128, TLOC], BF16)

        t_ld_prev = {0: (), 1: ()}
        prev_xpc = None
        prev_z = None
        prev_t_z = None
        for m in range(2 * DI // 128):
            bi = m % 2
            ldw = pr.dma(f"Lw1{bi}",
                         w1t[bi][:].rearrange("p (k c) -> p k c", k=8),
                         w_inb_d[:, m*128:(m+1)*128].rearrange(
                             "(k p) c -> p k c", p=128),
                         deps=wp_all + list(t_ld_prev.get(bi) or ()))
            is_x = m < 16
            with b.nc.psum_tensor(b.prefix + f"dps{m}", [128, 512], F32) as dps:
                for nch in range(W // 512 if not is_x else 0):
                    pass
                n_chunks = W // 512 if is_x else TLOC // 512
                for nch in range(n_chunks + (1 if is_x else 0)):
                    if is_x:
                        c0 = nch * 512
                        cw = min(512, W - c0)
                    else:
                        c0 = 128 + nch * 512
                        cw = 512
                    if cw <= 0:
                        continue
                    mt = None
                    for k in range(8):
                        mt = pr.op("p", "matmul", dps[:, 0:cw],
                                   w1t[bi][:, k*128:(k+1)*128],
                                   xnT[:, k*W + c0: k*W + c0 + cw],
                                   start=(k == 0), stop=(k == 7),
                                   deps=([ldw, t_xnT, t_bias] +
                                         ([prev_xpc] if prev_xpc else []) +
                                         ([prev_z] if prev_z else []))
                                        if k == 0 else ())
                    if is_x:
                        prev_xpc = pr.op("s", "activation", xpc[:, c0:c0+cw],
                                         dps[:, 0:cw], AF.Identity,
                                         bias=bias_e[:, m:m+1], deps=[mt])
                    else:
                        prev_z = pr.op("s", "activation", zbf[:, c0-128:c0-128+cw],
                                       dps[:, 0:cw], AF.Identity,
                                       bias=bias_e[:, m:m+1],
                                       deps=[mt] + ([prev_t_z] if prev_t_z else []))
            if is_x:
                t_msk = pr.op("v", "tensor_scalar", xpc[:, 125:128],
                              xpc[:, 125:128], wm[:], None, ALU.mult,
                              deps=[prev_xpc, t_wm])
                taps = []
                for k in range(4):
                    o0 = 128 - 3 + k
                    wk = convw_t[:, m*4 + k: m*4 + k + 1]
                    if k == 0:
                        taps.append(pr.op("v", "tensor_scalar", cacc[:],
                                          xpc[:, o0:o0+TLOC], wk, None,
                                          ALU.mult, deps=[t_msk, t_cw]))
                    else:
                        taps.append(pr.op("v", "scalar_tensor_tensor", cacc[:],
                                          xpc[:, o0:o0+TLOC], wk, cacc[:],
                                          ALU.mult, ALU.add))
                t_sl = pr.op("s", "activation", xcT_bf[:, m*TLOC:(m+1)*TLOC],
                             cacc[:], AF.Silu, bias=convb_t[:, m:m+1],
                             deps=[taps[-1], t_cb])
                t_xc = pr.dma("Sxc", o_xc[m*128:(m+1)*128, :],
                              xcT_bf[:, m*TLOC:(m+1)*TLOC], deps=[t_sl])
                t_ld_prev[bi] = (t_sl,)
            else:
                t_z = pr.dma("Sz", o_z[(m-16)*128:(m-15)*128, :], zbf[:],
                             deps=[prev_z])
                prev_t_z = t_z
                t_ld_prev[bi] = (prev_z, t_z)
        b.out_done(pr.tok("Sxc"))
        b.out_done(pr.tok("Sz"))
        t_xc_all = pr.tok("s")
        xnT_ctx.close()

        bdt = b.sb("bdt", [128, DI // 128], F32)
        t_bdt = pr.dma("Lbdt", bdt[:], bdt_rs)
        onec = b.sb("onec", [128, 1], F32)
        pr.op("v", "memset", onec[:], 1.0)
        w2t = [b.sb(f"w2t{i}", [128, 16 * 128], BF16) for i in range(2)]
        ub = b.sb("ub", [128, 512], F32)
        ab = b.sb("ab", [128, 512], F32)
        dtv = b.sb("dtv", [128, 512], F32)
        dtb16 = b.sb("dtb16", [128, TLOC], BF16)
        ub16 = b.sb("ub16", [128, TLOC], BF16)
        xcf = b.sb("xcf", [128, 512], F32)
        t_ld_prev = {0: (), 1: ()}
        for m in range(DI // 128):
            bi = m % 2
            ldw = pr.dma(f"Lw2{bi}",
                         w2t[bi][:].rearrange("p (k c) -> p k c", k=16),
                         w_dtb_d[:, m*128:(m+1)*128].rearrange(
                             "(k p) c -> p k c", p=128),
                         deps=wp2 + list(t_ld_prev.get(bi) or ()))
            with b.nc.psum_tensor(b.prefix + f"tps{m}", [128, 512], F32) as tps:
                last_u = None
                for nch in range(TLOC // 512):
                    c0 = nch * 512
                    mt = None
                    for k in range(16):
                        mt = pr.op("p", "matmul", tps[:],
                                   w2t[bi][:, k*128:(k+1)*128],
                                   xcT_bf[:, k*TLOC + c0: k*TLOC + c0 + 512],
                                   start=(k == 0), stop=(k == 15),
                                   deps=[ldw, t_xc_all] +
                                        ([last_u] if last_u else [])
                                        if k == 0 else ())
                    t1 = pr.op("s", "activation", ub[:], tps[:], AF.Identity,
                               bias=bdt[:, m:m+1], deps=[mt, t_bdt])
                    t2 = pr.op("s", "activation", ab[:], ub[:], AF.Abs)
                    t3 = pr.op("s", "activation", ab[:], ab[:], AF.Exp, scale=-1.0)
                    t4 = pr.op("s", "activation", ab[:], ab[:], AF.Ln,
                               bias=onec[:], deps=[pr.tok("v")])
                    t5 = pr.op("v", "scalar_tensor_tensor", dtv[:], ub[:], 0.0,
                               ab[:], ALU.max, ALU.add, deps=[t4])
                    t6 = pr.op("v", "tensor_copy", dtb16[:, c0:c0+512], dtv[:],
                               deps=list(t_ld_prev.get("st") or ()))
                    t7 = pr.op("v", "tensor_copy", xcf[:],
                               xcT_bf[:, m*TLOC + c0: m*TLOC + c0 + 512])
                    t8 = pr.op("v", "tensor_mul", ub16[:, c0:c0+512], dtv[:],
                               xcf[:])
                    last_u = t8
                t_sd = pr.dma("Sdt", o_dt[m*128:(m+1)*128, :], dtb16[:],
                              deps=[pr.tok("v")])
                t_su = pr.dma("Su", o_u[m*128:(m+1)*128, :], ub16[:],
                              deps=[pr.tok("v")])
            pe_fence(b)
            t_ld_prev[bi] = (t_sd, t_su)
            t_ld_prev["st"] = (t_sd, t_su)
        b.out_done(pr.tok("Sdt"))
        b.out_done(pr.tok("Su"))

        wBt = b.sb("wBt", [128, 16 * NSTATE], BF16)
        wCt = b.sb("wCt", [128, 16 * NSTATE], BF16)
        wBf = b.sb("wBf", [128, 16 * NSTATE], F32)
        t_wb0 = pr.dma("LwB", wBf[:].rearrange("p (k c) -> p k c", k=16),
                       w_BT.rearrange("(k p) c -> p k c", p=128))
        t_wb = pr.op("v", "tensor_copy", wBt[:], wBf[:], deps=[t_wb0])
        t_wc0 = pr.dma("LwC", wBf[:].rearrange("p (k c) -> p k c", k=16),
                       w_CT.rearrange("(k p) c -> p k c", p=128), deps=[t_wb])
        t_wc = pr.op("v", "tensor_copy", wCt[:], wBf[:], deps=[t_wc0])
        bct = b.sb("bct", [NSTATE, TLOC], F32)
        with b.nc.psum_tensor(b.prefix + "bps", [NSTATE, 512], F32) as bps:
            prev_e = None
            prev_store = None
            for (wt, wtok, odst, sname) in ((wBt, t_wb, o_B, "SB"), (wCt, t_wc, o_C, "SC")):
                for nch in range(TLOC // 512):
                    c0 = nch * 512
                    mt = None
                    for k in range(16):
                        mt = pr.op("p", "matmul", bps[:],
                                   wt[:, k*NSTATE:(k+1)*NSTATE],
                                   xcT_bf[:, k*TLOC + c0: k*TLOC + c0 + 512],
                                   start=(k == 0), stop=(k == 15),
                                   deps=[wtok, t_xc_all] +
                                        ([prev_e] if prev_e else [])
                                        if k == 0 else ())
                    prev_e = pr.op("v", "tensor_copy", bct[:, c0:c0+512], bps[:],
                                   deps=[mt] + ([prev_store] if prev_store and nch == 0 else []))
                prev_store = pr.dma(sname, odst, bct[:], deps=[prev_e])
        b.out_done(pr.tok("SB"))
        b.out_done(pr.tok("SC"))


def build_mamba_pre():
    b = Builder("mamba_pre")
    io = {}
    io["xr"] = b.din("xr", [TLOC, DM], F32)
    io["xprev"] = b.din("xprev", [128, DM], F32)
    io["wmask"] = b.din("wmask", [128, 1], F32)
    io["w_inT"] = b.din("w_inT", [DM, 2 * DI], F32)
    io["normw_rs"] = b.din("normw_rs", [128, DM // 128], F32)
    io["normb_rs"] = b.din("normb_rs", [128, DM // 128], F32)
    io["convw"] = b.din("convw", [128, DI // 128 * 4], F32)
    io["convb_rs"] = b.din("convb_rs", [128, DI // 128], F32)
    io["w_dtT"] = b.din("w_dtT", [DI, DI], F32)
    io["bdt_rs"] = b.din("bdt_rs", [128, DI // 128], F32)
    io["w_BT"] = b.din("w_BT", [DI, NSTATE], F32)
    io["w_CT"] = b.din("w_CT", [DI, NSTATE], F32)
    io["ident_in"] = b.din("ident", [128, 128], BF16)
    io["o_dt"] = b.dout("o_dt", [DI, TLOC], BF16)
    io["o_u"] = b.dout("o_u", [DI, TLOC], BF16)
    io["o_xc"] = b.dout("o_xc", [DI, TLOC], BF16)
    io["o_z"] = b.dout("o_z", [DI, TLOC], BF16)
    io["o_B"] = b.dout("o_B", [NSTATE, TLOC], F32)
    io["o_C"] = b.dout("o_C", [NSTATE, TLOC], F32)
    emit_mamba_pre(b, io)
    return b.finish()


def emit_mamba_scan(b, io):
    pr = b.pr
    TG = 4 * TSEQ
    dts = io["dts"]
    us = io["us"]
    Bs = io["Bs"]
    Cs = io["Cs"]
    alog = io["alog"]
    o_y = io["o_y"]
    HALF = TSEQ // 2

    @body(b)
    def _main():
        sel = b.sb("sel", [NSTATE, NSTATE * 128], F32)
        t_sel = pr.dma("Lsel", sel[:], io["sel16"])
        Ball = [b.sb(f"Ball{i}", [NSTATE, HALF], F32) for i in range(2)]
        Call = [b.sb(f"Call{i}", [NSTATE, HALF], F32) for i in range(2)]
        alog_sb = b.sb("alog_sb", [128, NSTATE], F32)
        Aneg = b.sb("Aneg", [128, NSTATE], F32)
        dt_sb = b.sb("dt_sb", [128, TG], BF16)
        u_sb = b.sb("u_sb", [128, TG], BF16)
        ufh = b.sb("ufh", [128, HALF], F32)
        af = [b.sb(f"af{i}", [128, HALF], F32) for i in range(2)]
        un = b.sb("un", [128, HALF], F32)
        hh = b.sb("hh", [128, HALF], F32)
        hct = b.sb("hct", [128, HALF], F32)
        yacc = b.sb("yacc", [128, HALF], F32)
        ybf = b.sb("ybf", [128, HALF], BF16)
        hcarr = b.sb("hcarr", [128, NSTATE], F32)
        pB = b.ps("pB", [128, HALF], F32)
        pC = b.ps("pC", [128, HALF], F32)

        for di in range(2):
            t_dt = pr.dma("Ldt", dt_sb[:].rearrange("p (r t) -> p r t", r=NC),
                          dts[di*128:(di+1)*128],
                          deps=[pr.tok("s"), pr.tok("v")])
            t_u = pr.dma("Lu", u_sb[:].rearrange("p (r t) -> p r t", r=NC),
                         us[di*128:(di+1)*128],
                         deps=[pr.tok("v")])
            t_al = pr.dma("Lal", alog_sb[:], alog[di*128:(di+1)*128, :],
                          deps=[pr.tok("v")])
            t_a1 = pr.op("s", "activation", Aneg[:], alog_sb[:], AF.Exp,
                         deps=[t_al])
            t_a2 = pr.op("v", "tensor_scalar", Aneg[:], Aneg[:], -1.0, None,
                         ALU.mult, deps=[t_a1])
            lastBC = {0: None, 1: None}
            for bb in range(4):
                for half in range(2):
                    c0 = bb * TSEQ + half * HALF
                    blk = 2 * bb + half
                    bi = blk % 2
                    t_lB = pr.dma(f"LBb{bi}", Ball[bi][:],
                                  Bs[blk*NSTATE:(blk+1)*NSTATE, 0:HALF],
                                  deps=[lastBC[bi]])
                    t_lC = pr.dma(f"LCb{bi}", Call[bi][:],
                                  Cs[blk*NSTATE:(blk+1)*NSTATE, 0:HALF],
                                  deps=[lastBC[bi]])
                    t_uf = pr.op("v", "tensor_copy", ufh[:],
                                 u_sb[:, c0:c0+HALF], deps=[t_u])
                    prev_un = None
                    prev_hc = None
                    prev_scan = {0: None, 1: None}
                    for n in range(NSTATE):
                        abi = n % 2
                        btsB, btsC = [], []
                        for c in range(HALF // 512):
                            btsB.append(pr.op(
                                "p", "matmul", pB[:, c*512:(c+1)*512],
                                sel[:, n*128:(n+1)*128],
                                Ball[bi][:, c*512:(c+1)*512],
                                start=True, stop=True,
                                deps=[t_lB, t_sel] +
                                     ([prev_un] if prev_un
                                      else [pr.tok("v")])))
                            btsC.append(pr.op(
                                "p", "matmul", pC[:, c*512:(c+1)*512],
                                sel[:, n*128:(n+1)*128],
                                Call[bi][:, c*512:(c+1)*512],
                                start=True, stop=True,
                                deps=[t_lC] + ([prev_hc] if prev_hc
                                               else [pr.tok("v")])))
                        t_ae = pr.op("s", "activation", af[abi][:],
                                     dt_sb[:, c0:c0+HALF], AF.Exp,
                                     scale=Aneg[:, n:n+1],
                                     deps=[t_dt, t_a2] +
                                          ([prev_scan[abi]] if prev_scan[abi]
                                           else [pr.tok("v")]))
                        prev_un = pr.op("v", "tensor_mul", un[:], ufh[:],
                                        pB[:], deps=[t_uf, btsB[-1]])
                        init = 0.0 if half == 0 else hcarr[:, n:n+1]
                        t_sc = pr.op("v", "tensor_tensor_scan", hh[:],
                                     af[abi][:], un[:], init, ALU.mult,
                                     ALU.add, deps=[t_ae])
                        prev_scan[abi] = t_sc
                        if half == 0:
                            pr.op("v", "tensor_copy", hcarr[:, n:n+1],
                                  hh[:, HALF-1:HALF])
                        prev_hc = pr.op("v", "tensor_mul", hct[:], hh[:],
                                        pC[:], deps=[btsC[-1]])
                        if n == 0:
                            pr.op("v", "tensor_copy", yacc[:], hct[:])
                        else:
                            pr.op("v", "tensor_add", yacc[:], yacc[:], hct[:])
                    lastBC[bi] = prev_hc
                    t_yb = pr.op("v", "tensor_copy", ybf[:], yacc[:],
                                 deps=[pr.tok("Sy")])
                    w = 2 * bb + half
                    pr.dma("Sy", o_y[di*128:(di+1)*128, w:w+1, :],
                           ybf[:].rearrange("p (u t) -> p u t", u=1),
                           deps=[t_yb])
        b.out_done(pr.tok("Sy"))


def build_mamba_scan():
    b = Builder("mamba_scan")
    TG = 4 * TSEQ
    io = {}
    io["dts"] = b.din("dts", [256, TG], BF16)
    io["us"] = b.din("us", [256, TG], BF16)
    io["Bs"] = b.din("Bs", [128, TLOC], F32)
    io["Cs"] = b.din("Cs", [128, TLOC], F32)
    io["alog"] = b.din("alog", [256, NSTATE], F32)
    io["sel16"] = b.din("sel16", [NSTATE, NSTATE * 128], F32)
    io["o_y"] = b.dout("o_y", [256, TG], BF16).rearrange("c (w t) -> c w t", w=NC)
    io["dts"] = io["dts"].rearrange("c (r t) -> c r t", r=NC)
    io["us"] = io["us"].rearrange("c (r t) -> c r t", r=NC)
    emit_mamba_scan(b, io)
    return b.finish()



def emit_mamba_close(b, io):
    pr = b.pr
    yT = io["yT"]
    xcT = io["xcT"]
    zT = io["zT"]
    D_rs = io["D_rs"]
    woT = io["woT"]
    xr = io["xr"]
    o_x = io["o_x"]
    o_yg = io["o_yg"]
    wob_d = b.dint("wob", [DI, DM], BF16)
    NK = DI // 128

    @body(b)
    def _main():
        prep_ctx = ExitStack()
        wp = emit_weight_prep(b, [(woT, wob_d, DI, DM)], ctx=prep_ctx)
        prep_ctx.close()
        Dv = b.sb("Dv", [128, DI // 128], F32)
        t_D = pr.dma("LD", Dv[:], D_rs)
        ygT = b.sb("ygT", [128, NK * TLOC], BF16)
        ych = b.sb("ych", [128, TLOC], BF16)
        xch = b.sb("xch", [128, TLOC], BF16)
        zch = b.sb("zch", [128, TLOC], BF16)
        yff = b.sb("yff", [128, TLOC], F32)
        szf = b.sb("szf", [128, TLOC], F32)
        for m in range(NK):
            t_y = pr.dma(f"Ly{m%2}", ych[:], yT[m*128:(m+1)*128, :],
                         deps=[pr.tok("v")])
            t_xc = pr.dma(f"Lxc{m%2}", xch[:], xcT[m*128:(m+1)*128, :],
                          deps=[pr.tok("v")])
            t_z = pr.dma(f"Lz{m%2}", zch[:], zT[m*128:(m+1)*128, :],
                         deps=[pr.tok("s")])
            t_yf = pr.op("v", "scalar_tensor_tensor", yff[:], xch[:],
                         Dv[:, m:m+1], ych[:], ALU.mult, ALU.add,
                         deps=[t_y, t_xc, t_D])
            t_sz = pr.op("s", "activation", szf[:], zch[:], AF.Silu,
                         deps=[t_z, pr.tok("v")])
            t_yg = pr.op("v", "tensor_mul", ygT[:, m*TLOC:(m+1)*TLOC],
                         yff[:], szf[:], deps=[t_sz])
        t_yg_all = pr.tok("v")
        for m in range(NK):
            pr.dma("Syg", o_yg[m*128:(m+1)*128, :], ygT[:, m*TLOC:(m+1)*TLOC],
                   deps=[t_yg_all])
        b.out_done(pr.tok("Syg"))

        wo_sb = b.sb("wo_sb", [128, NK * DM], BF16)
        t_wo = pr.dma("Lwo", wo_sb[:].rearrange("p (k c) -> p k c", k=NK),
                      wob_d.rearrange("(k p) c -> p k c", p=128), deps=wp)
        xt = b.sb("xt", [128, DM], F32)
        xnf = b.sb("xnf", [128, DM], F32)
        outt = b.sb("outt", [128, DM], F32)
        st = make_ln_stats(b, "lnf")
        prev_users = []
        with b.nc.psum_tensor(b.prefix + "fps", [128, 512], F32) as fps:
            last_ev = None
            for tt in range(TLOC // 128):
                t_ld = pr.dma(f"Lx{tt%2}", xt[:], xr[tt*128:(tt+1)*128, :],
                              deps=prev_users)
                t_ln = emit_ln_tile(b, st, xt[:], xnf[:], deps=[t_ld],
                                    out_deps=prev_users)
                for dn in range(2):
                    mt = None
                    for k in range(NK):
                        mt = pr.op("p", "matmul", fps[:],
                                   ygT[:, k*TLOC + tt*128: k*TLOC + (tt+1)*128],
                                   wo_sb[:, k*DM + dn*512: k*DM + dn*512 + 512],
                                   start=(k == 0), stop=(k == NK - 1),
                                   deps=[t_yg_all, t_wo] + prev_users +
                                        ([last_ev] if last_ev else [])
                                        if k == 0 else ())
                    t_o = pr.op("v", "tensor_add", outt[:, dn*512:(dn+1)*512],
                                fps[:], xt[:, dn*512:(dn+1)*512],
                                deps=[mt, t_ld])
                    t_o = pr.op("v", "tensor_add", outt[:, dn*512:(dn+1)*512],
                                outt[:, dn*512:(dn+1)*512],
                                xnf[:, dn*512:(dn+1)*512], deps=[t_ln])
                    last_ev = t_o
                t_st = pr.dma("Sx", o_x[tt*128:(tt+1)*128, :], outt[:],
                              deps=[pr.tok("v")])
                prev_users = [t_st, pr.tok("v"), pr.tok("s")]
        b.out_done(pr.tok("Sx"))


def build_mamba_close():
    b = Builder("mamba_close")
    io = {}
    io["yT"] = b.din("yT", [DI, TLOC], BF16)
    io["xcT"] = b.din("xcT", [DI, TLOC], BF16)
    io["zT"] = b.din("zT", [DI, TLOC], BF16)
    io["D_rs"] = b.din("D_rs", [128, DI // 128], F32)
    io["woT"] = b.din("woT", [DI, DM], F32)
    io["xr"] = b.din("xr", [TLOC, DM], F32)
    io["o_x"] = b.dout("o_x", [TLOC, DM], F32)
    io["o_yg"] = b.dout("o_yg", [DI, TLOC], BF16)
    emit_mamba_close(b, io)
    return b.finish()


def run_mamba_layer(x_full, I):
    import ml_dtypes
    xf = x_full.reshape(NC, TLOC, DM)
    ident = np.eye(128, dtype=ml_dtypes.bfloat16)

    ncD = build_mamba_pre()
    w_inT = np.ascontiguousarray(_f32(I["L0_attn_W_in"]).T)
    normw_rs = np.ascontiguousarray(_f32(I["L0_attn_norm_w"]).reshape(DM // 128, 128).T)
    normb_rs = np.ascontiguousarray(_f32(I["L0_attn_norm_b"]).reshape(DM // 128, 128).T)
    cw = _f32(I["L0_attn_conv_w"])
    convw = np.ascontiguousarray(cw.reshape(DI // 128, 128, 4).transpose(1, 0, 2).reshape(128, -1))
    convb_rs = np.ascontiguousarray(_f32(I["L0_attn_conv_b"]).reshape(DI // 128, 128).T)
    w_dtT = np.ascontiguousarray(_f32(I["L0_attn_W_dt"]).T)
    bdt_rs = np.ascontiguousarray(_f32(I["L0_attn_b_dt"]).reshape(DI // 128, 128).T)
    w_BT = np.ascontiguousarray(_f32(I["L0_attn_W_B"]).T)
    w_CT = np.ascontiguousarray(_f32(I["L0_attn_W_C"]).T)
    common = dict(w_inT=w_inT, normw_rs=normw_rs, normb_rs=normb_rs,
                  convw=convw, convb_rs=convb_rs, w_dtT=w_dtT, bdt_rs=bdt_rs,
                  w_BT=w_BT, w_CT=w_CT, ident=ident)
    in_maps = []
    for i in range(NC):
        half = i % 2
        if half == 0:
            xprev = np.zeros((128, DM), np.float32)
        else:
            xprev = _f32(xf[i - 1][TLOC - 128:])
        in_maps.append(dict(xr=_f32(xf[i]), xprev=xprev,
                            wmask=np.full((128, 1), 1.0 if i % 2 else 0.0,
                                          np.float32), **common))
    resD = run_bass_kernel_spmd(ncD, in_maps, core_ids=list(range(NC)))

    def glue(key, dtype):
        return np.concatenate([np.asarray(resD.results[i][key]) for i in range(NC)], 1)
    dtT = glue("o_dt", None)
    uT = glue("o_u", None)
    BT = np.concatenate([np.asarray(resD.results[i]["o_B"]) for i in range(NC)], 1)
    CT = np.concatenate([np.asarray(resD.results[i]["o_C"]) for i in range(NC)], 1)

    ncE = build_mamba_scan()
    alog = _f32(I["L0_attn_A_log"])
    Bs = np.ascontiguousarray(BT.T.reshape(NC, TLOC, NSTATE).transpose(0, 2, 1)
                              .reshape(NC * NSTATE, TLOC))
    Cs = np.ascontiguousarray(CT.T.reshape(NC, TLOC, NSTATE).transpose(0, 2, 1)
                              .reshape(NC * NSTATE, TLOC))
    in_maps = []
    for e in range(NC):
        rows = slice(e * 256, (e + 1) * 256)
        in_maps.append(dict(dts=np.ascontiguousarray(dtT[rows]),
                            us=np.ascontiguousarray(uT[rows]),
                            Bs=Bs, Cs=Cs,
                            alog=np.ascontiguousarray(alog[rows])))
    resE = run_bass_kernel_spmd(ncE, in_maps, core_ids=list(range(NC)))
    yT = np.concatenate([np.asarray(resE.results[e]["o_y"]) for e in range(NC)], 0)

    ncF = build_mamba_close()
    D_rs = np.ascontiguousarray(_f32(I["L0_attn_D"]).reshape(DI // 128, 128).T)
    woT = np.ascontiguousarray(_f32(I["L0_attn_W_out"]).T)
    in_maps = []
    for i in range(NC):
        in_maps.append(dict(
            yT=np.ascontiguousarray(yT[:, i*TLOC:(i+1)*TLOC]),
            xcT=np.asarray(resD.results[i]["o_xc"]),
            zT=np.asarray(resD.results[i]["o_z"]),
            D_rs=D_rs, woT=woT, xr=_f32(xf[i])))
    resF = run_bass_kernel_spmd(ncF, in_maps, core_ids=list(range(NC)))
    x2 = np.concatenate([resF.results[i]["o_x"] for i in range(NC)], 0)
    return x2.reshape(x_full.shape)


def emit_round_grid(b, v_tile, out_tile, w, tmp, mb, emin=None, deps=()):
    pr = b.pr
    t1 = pr.op("s", "activation", tmp["a"][:, 0:w], v_tile, AF.Abs, deps=deps)
    t2 = pr.op("v", "tensor_scalar", tmp["a"][:, 0:w], tmp["a"][:, 0:w],
               float(2.0 ** -30), None, ALU.max, deps=[t1])
    t3 = pr.op("s", "activation", tmp["l"][:, 0:w], tmp["a"][:, 0:w], AF.Ln,
               deps=[t2])
    t4 = pr.op("v", "tensor_scalar", tmp["l"][:, 0:w], tmp["l"][:, 0:w],
               1.4426950408889634, 64.0, ALU.mult, ALU.add, deps=[t3])
    t5 = pr.op("v", "tensor_copy", tmp["ri"][:, 0:w], tmp["l"][:, 0:w])
    t6 = pr.op("v", "tensor_copy", tmp["l"][:, 0:w], tmp["ri"][:, 0:w])
    lo = (emin + 64.0) if emin is not None else -1e9
    t7 = pr.op("v", "tensor_scalar", tmp["l"][:, 0:w], tmp["l"][:, 0:w],
               float(lo), float(-64.0 - mb), ALU.max, ALU.add)
    t8 = pr.op("s", "activation", tmp["d"][:, 0:w], tmp["l"][:, 0:w], AF.Exp,
               scale=-0.6931471805599453, deps=[t7])
    t9 = pr.op("s", "activation", tmp["l"][:, 0:w], tmp["l"][:, 0:w], AF.Exp,
               scale=0.6931471805599453)
    ta = pr.op("v", "tensor_mul", tmp["r0"][:, 0:w], v_tile, tmp["d"][:, 0:w],
               deps=[t8])
    tb = pr.op("v", "tensor_copy", tmp["ri"][:, 0:w], tmp["r0"][:, 0:w])
    tc = pr.op("v", "tensor_copy", tmp["d"][:, 0:w], tmp["ri"][:, 0:w])
    td = pr.op("v", "tensor_sub", tmp["a"][:, 0:w], tmp["r0"][:, 0:w],
               tmp["d"][:, 0:w])
    te = pr.op("v", "tensor_scalar", tmp["aj"][:, 0:w], tmp["a"][:, 0:w],
               0.5, None, ALU.is_gt)
    tf = pr.op("v", "tensor_add", tmp["d"][:, 0:w], tmp["d"][:, 0:w],
               tmp["aj"][:, 0:w])
    tg = pr.op("v", "tensor_scalar", tmp["aj"][:, 0:w], tmp["a"][:, 0:w],
               -0.5, None, ALU.is_lt)
    th = pr.op("v", "tensor_sub", tmp["d"][:, 0:w], tmp["d"][:, 0:w],
               tmp["aj"][:, 0:w])
    ti = pr.op("v", "tensor_tensor", out_tile, tmp["d"][:, 0:w],
               tmp["l"][:, 0:w], ALU.mult, deps=[t9])
    return ti


def mk_grid_tmp(b, tag, w):
    return {"a": b.sb(f"{tag}_a", [128, w], F32),
            "l": b.sb(f"{tag}_l", [128, w], F32),
            "r0": b.sb(f"{tag}_r0", [128, w], F32),
            "d": b.sb(f"{tag}_d", [128, w], F32),
            "aj": b.sb(f"{tag}_aj", [128, w], F32),
            "ri": b.sb(f"{tag}_ri", [128, w], I32)}


def emit_part_reduce(b, col, out11, scr_d, op, tag, deps=()):
    pr = b.pr
    row = b.sb(f"{tag}_row", [1, 128], F32)
    t1 = pr.dma(f"{tag}S", scr_d[0:1, 0:128].rearrange("u r -> r u"), col,
                deps=list(deps))
    t2 = pr.dma(f"{tag}L", row[:], scr_d[0:1, 0:128], deps=[t1])
    t3 = pr.op("v", "tensor_reduce", out11, row[:], AX.X, op, deps=[t2])
    return t3


def emit_ffn1_ln(b, io):
    pr = b.pr
    xr = io["xr"]
    o_xn = io["o_xn"]
    o_amax = io["o_amax"]
    scr_d = b.dint("scr", [1, 128], F32)

    @body(b)
    def _main():
        st = make_ln_stats(b, "ln")
        xt = [b.sb(f"xt{i}", [128, DM], F32) for i in range(2)]
        xnb = [b.sb(f"xnb{i}", [128, DM], F32) for i in range(2)]
        mcol = b.sb("mcol", [128, 1], F32)
        macc = b.sb("macc", [128, 1], F32)
        am = b.sb("am", [1, 1], F32)
        pr.op("v", "memset", macc[:], 0.0)
        prev_ln = {0: None, 1: None}
        prev_st = {0: (), 1: ()}
        for i in range(TLOC // 128):
            bi = i % 2
            ld = pr.dma(f"Lx{bi}", xt[bi][:], xr[i*128:(i+1)*128, :],
                        deps=[prev_ln[bi]])
            t_ln = emit_ln_tile(b, st, xt[bi][:], xnb[bi][:], deps=[ld],
                                out_deps=prev_st[bi])
            prev_ln[bi] = t_ln
            t_st = pr.dma(f"Sxn{bi}", o_xn[i*128:(i+1)*128, :], xnb[bi][:],
                          deps=[t_ln])
            t_m = pr.op("v", "tensor_reduce", mcol[:], xnb[bi][:], AX.X,
                        ALU.max, apply_absolute_value=True, deps=[t_ln])
            t_m2 = pr.op("v", "tensor_max", macc[:], macc[:], mcol[:])
            prev_st[bi] = (t_st, t_m)
        t_am = emit_part_reduce(b, macc[:], am[:], scr_d, ALU.max, "am",
                                deps=[pr.tok("v")])
        b.out_done(pr.dma("Sam", o_amax, am[:], deps=[t_am]))
        b.out_done(pr.tok("Sxn0"))
        b.out_done(pr.tok("Sxn1"))


def build_ffn1_ln():
    b = Builder("ffn1_ln")
    io = {}
    io["xr"] = b.din("xr", [TLOC, DM], F32)
    io["o_xn"] = b.dout("o_xn", [TLOC, DM], F32)
    io["o_amax"] = b.dout("o_amax", [1, 1], F32)
    emit_ffn1_ln(b, io)
    return b.finish()


def emit_ffn1_close(b, io, scale_half):
    pr = b.pr
    xn_in = io["xn_in"]
    amax8 = io["amax8"]
    wT = io["wT"]
    bias = io["bias"]
    xr = io["xr"]
    ident_in = io["ident_in"]
    o_x = io["o_x"]
    scr_d = b.dint("scr", [1, 128], F32)
    NT = TLOC // 128

    @body(b)
    def _main():
        ident = b.sb("identt", [128, 128], BF16)
        t_id = pr.dma("Lid", ident[:], ident_in)
        am8 = b.sb("am8", [1, NC], F32)
        t_a8 = pr.dma("La8", am8[:], amax8)
        amax = b.sb("amax", [1, 1], F32)
        t_am = pr.op("v", "tensor_reduce", amax[:], am8[:], AX.X, ALU.max,
                     deps=[t_a8])
        sx = b.sb("sx", [1, 1], F32)
        pr.op("v", "tensor_scalar", sx[:], amax[:], 1e-12, None, ALU.max)
        pr.op("v", "reciprocal", sx[:], sx[:])
        pr.op("v", "tensor_scalar", sx[:], sx[:], 448.0, 448.0, ALU.mult, ALU.min)
        wtl = b.sb("wtl", [128, DM], F32)
        wcol = b.sb("wcol", [128, 1], F32)
        wacc = b.sb("wacc", [128, 1], F32)
        pr.op("v", "memset", wacc[:], 0.0)
        prev = None
        for i in range(DM // 128):
            ld = pr.dma(f"Lw{i%2}", wtl[:], wT[i*128:(i+1)*128, :], deps=[prev])
            t_m = pr.op("v", "tensor_reduce", wcol[:], wtl[:], AX.X, ALU.max,
                        apply_absolute_value=True, deps=[ld])
            prev = pr.op("v", "tensor_max", wacc[:], wacc[:], wcol[:])
        wam = b.sb("wam", [1, 1], F32)
        t_wam = emit_part_reduce(b, wacc[:], wam[:], scr_d, ALU.max, "wm",
                                 deps=[prev])
        sw = b.sb("sw", [1, 1], F32)
        pr.op("v", "tensor_scalar", sw[:], wam[:], 1e-12, None, ALU.max,
              deps=[t_wam])
        pr.op("v", "reciprocal", sw[:], sw[:])
        pr.op("v", "tensor_scalar", sw[:], sw[:], 448.0, 448.0, ALU.mult, ALU.min)
        dq = b.sb("dq", [1, 1], F32)
        pr.op("v", "tensor_mul", dq[:], sx[:], sw[:])
        t_dq = pr.op("v", "reciprocal", dq[:], dq[:])
        onesf = b.sb("onesf", [1, 128], F32)
        pr.op("v", "memset", onesf[:], 1.0)
        sxc = b.sb("sxc", [128, 1], F32)
        swc = b.sb("swc", [128, 1], F32)
        dqc = b.sb("dqc", [128, 1], F32)
        with b.nc.psum_tensor(b.prefix + "pbc1", [128, 1], F32) as pbc:
            for (src, dst) in ((sx, sxc), (sw, swc), (dq, dqc)):
                t_b = pr.op("p", "matmul", pbc[:], onesf[:], src[:],
                            start=True, stop=True, deps=[t_dq, pr.tok("v")])
                pr.op("v", "tensor_copy", dst[:], pbc[:], deps=[t_b])
        pe_fence(b)
        t_bc = pr.tok("v")

        gtmp = mk_grid_tmp(b, "gq", DM)
        wq = b.sb("wq", [128, (DM // 128) * DM], BF16)
        vsc = b.sb("vsc", [128, DM], F32)
        prev = None
        for i in range(DM // 128):
            ld = pr.dma(f"Lw{i%2}", wtl[:], wT[i*128:(i+1)*128, :],
                        deps=[prev] if prev else [pr.tok("v")])
            t_s = pr.op("v", "tensor_scalar", vsc[:], wtl[:], swc[:], None,
                        ALU.mult, deps=[ld, t_bc])
            t_cl = pr.op("v", "tensor_scalar", vsc[:], vsc[:], 448.0, -448.0,
                         ALU.min, ALU.max)
            prev = emit_round_grid(b, vsc[:], wq[:, i*DM:(i+1)*DM], DM, gtmp,
                                   3.0, emin=-6.0, deps=[t_cl])
        t_wq = prev

        xnt = b.sb("xnt", [128, DM], F32)
        xq_row = b.sb("xq_row", [128, DM], BF16)
        xqT = b.sb("xqT", [128, DM], BF16)
        pbf = b.ps("pbf", [128, 128], BF16)
        xt2 = b.sb("xt2", [128, DM], F32)
        outt = b.sb("outt", [128, DM], F32)
        biasr = b.sb("biasr", [1, DM], F32)
        t_bi = pr.dma("Lbi", biasr[:], bias)
        biasb = b.sb("biasb", [128, DM], F32)
        t_bb = emit_row_broadcast(b, biasr, biasb, DM, "bb", deps=[t_bi])
        gtmp2 = mk_grid_tmp(b, "gx", DM)
        prev_users = []
        with b.nc.psum_tensor(b.prefix + "qps", [128, 512], F32) as qps:
            last_ev = None
            for tt in range(NT):
                t_ld = pr.dma(f"LxT{tt%2}", xnt[:], xn_in[tt*128:(tt+1)*128, :],
                              deps=prev_users)
                t_s = pr.op("v", "tensor_scalar", xt2[:], xnt[:], sxc[:], None,
                            ALU.mult, deps=[t_ld, t_bc] + prev_users)
                t_cl = pr.op("v", "tensor_scalar", xt2[:], xt2[:], 448.0, -448.0,
                             ALU.min, ALU.max)
                t_q = emit_round_grid(b, xt2[:], xq_row[:], DM, gtmp2, 3.0,
                                      emin=-6.0, deps=[t_cl])
                trs = emit_transposes(b, xq_row, xqT, 0, ident, pbf, 8, 128,
                                      deps=[t_q, t_id] + prev_users)
                t_ldr = pr.dma(f"LxR{tt%2}", xt2[:], xr[tt*128:(tt+1)*128, :],
                               deps=[t_q, pr.tok("s")] + prev_users)
                for dn in range(2):
                    mt = None
                    for k in range(DM // 128):
                        mt = pr.op("p", "matmul", qps[:],
                                   xqT[:, k*128:(k+1)*128],
                                   wq[:, k*DM + dn*512: k*DM + dn*512 + 512],
                                   start=(k == 0), stop=(k == DM // 128 - 1),
                                   deps=[trs[-1], t_wq] +
                                        ([last_ev] if last_ev else [])
                                        if k == 0 else ())
                    t_e = pr.op("v", "scalar_tensor_tensor",
                                outt[:, dn*512:(dn+1)*512], qps[:], dqc[:],
                                biasb[:, dn*512:(dn+1)*512],
                                ALU.mult, ALU.add, deps=[mt, t_bb])
                    last_ev = t_e
                t_r = emit_round_grid(b, outt[:], outt[:], DM, gtmp2, 7.0,
                                      deps=[last_ev])
                t_f = pr.op("v", "scalar_tensor_tensor", outt[:], outt[:],
                            scale_half, xt2[:], ALU.mult, ALU.add,
                            deps=[t_r, t_ldr])
                t_st = pr.dma("Sx", o_x[tt*128:(tt+1)*128, :], outt[:],
                              deps=[t_f])
                prev_users = [t_st, t_f]
        b.out_done(pr.tok("Sx"))


def build_ffn1_close(scale_half):
    b = Builder("ffn1_close")
    io = {}
    io["xn_in"] = b.din("xn", [TLOC, DM], F32)
    io["amax8"] = b.din("amax8", [1, NC], F32)
    io["wT"] = b.din("wT", [DM, DM], F32)
    io["bias"] = b.din("bias", [1, DM], F32)
    io["xr"] = b.din("xr", [TLOC, DM], F32)
    io["ident_in"] = b.din("ident", [128, 128], BF16)
    io["o_x"] = b.dout("o_x", [TLOC, DM], F32)
    emit_ffn1_close(b, io, scale_half)
    return b.finish()


def run_ffn1_layer(x_full, I):
    xf = x_full.reshape(NC, TLOC, DM)
    import ml_dtypes
    ncH = build_ffn1_ln()
    in_maps = [dict(xr=_f32(xf[i])) for i in range(NC)]
    resH = run_bass_kernel_spmd(ncH, in_maps, core_ids=list(range(NC)))
    xn = [np.asarray(resH.results[i]["o_xn"]) for i in range(NC)]
    amax8 = np.concatenate([np.asarray(resH.results[i]["o_amax"]) for i in range(NC)],
                           1).reshape(1, NC)
    ncI = build_ffn1_close(0.5)
    ident = np.eye(128, dtype=ml_dtypes.bfloat16)
    wT = np.ascontiguousarray(_f32(I["L1_ffn_weight"]).T)
    bias = _f32(I["L1_ffn_bias"]).reshape(1, DM)
    in_maps = [dict(xn=xn[i], amax8=amax8, wT=wT, bias=bias, xr=_f32(xf[i]),
                    ident=ident) for i in range(NC)]
    resI = run_bass_kernel_spmd(ncI, in_maps, core_ids=list(range(NC)))
    xo = np.concatenate([resI.results[i]["o_x"] for i in range(NC)], 0)
    return xo.reshape(x_full.shape)


def emit_attn_qkv(b, io):
    pr = b.pr
    xr = io["xr"]
    wqT = io["wqT"]
    wkT = io["wkT"]
    wvT = io["wvT"]
    bq_rs = io["bq_rs"]
    bk_rs = io["bk_rs"]
    bv = io["bv"]
    pos = io["pos"]
    ident_in = io["ident_in"]
    o_q = io["o_q"]
    o_k = io["o_k"]
    o_v = io["o_v"]
    wqb_d = b.dint("wqb", [DM, DM], BF16)
    wkb_d = b.dint("wkb", [DM, DM], BF16)
    wvb_d = b.dint("wvb", [DM, DM], BF16)
    NT = TLOC // 128
    TWO_PI = 6.283185307179586
    INV_2PI = 0.15915494309189535

    @body(b)
    def _main():
        prep_ctx = ExitStack()
        wp = emit_weight_prep(b, [(wqT, wqb_d, DM, DM), (wkT, wkb_d, DM, DM),
                                  (wvT, wvb_d, DM, DM)], ctx=prep_ctx)
        prep_ctx.close()
        ident = b.sb("identt", [128, 128], BF16)
        t_id = pr.dma("Lid", ident[:], ident_in)

        st = make_ln_stats(b, "ln")
        xt = [b.sb(f"xt{i}", [128, DM], F32) for i in range(2)]
        xnb = [b.sb(f"xnb{i}", [128, DM], BF16) for i in range(2)]
        xnT = b.sb("xnT", [128, 8 * TLOC], BF16)
        pbf = b.ps("pbf", [128, 128], BF16)
        prev_ln = {0: None, 1: None}
        prev_users = {0: (), 1: ()}
        prev_trs = []
        for i in range(NT):
            bi = i % 2
            ld = pr.dma(f"Lx{bi}", xt[bi][:], xr[i*128:(i+1)*128, :],
                        deps=[prev_ln[bi]])
            t_ln = emit_ln_tile(b, st, xt[bi][:], xnb[bi][:], deps=[ld],
                                out_deps=prev_users[bi])
            prev_ln[bi] = t_ln
            trs = emit_transposes(b, xnb[bi], xnT, i*128, ident, pbf, 8, TLOC,
                                  deps=[t_ln, t_id] + prev_trs[-1:])
            prev_trs = trs
            prev_users[bi] = (trs[-1], pr.tok("p"))
        t_xnT = pr.tok("v")

        iot = b.sb("iot", [128, 1], F32)
        pr.op("g", "iota", iot[:], pattern=[[1, 1]], base=0,
              channel_multiplier=1, allow_small_or_imprecise_dtypes=True)
        pm = b.sb("pm", [128, 1], F32)
        pmi = b.sb("pmi", [128, 1], I32)
        freqc = b.sb("freqc", [128, 1], F32)
        t0 = pr.op("v", "tensor_scalar", pm[:], iot[:], 1.0 / 32.0, None,
                   ALU.mult, deps=[pr.tok("g")])
        pr.op("v", "tensor_copy", pmi[:], pm[:])
        pr.op("v", "tensor_copy", pm[:], pmi[:])
        pr.op("v", "tensor_scalar", pm[:], pm[:], -32.0, None, ALU.mult)
        pr.op("v", "tensor_add", pm[:], pm[:], iot[:])
        t_fr = pr.op("s", "activation", freqc[:], pm[:], AF.Exp,
                     scale=-0.2878231366242557, deps=[pr.tok("v")])
        posr = b.sb("posr", [1, TLOC], F32)
        t_pos = pr.dma("Lpos", posr[:], pos)
        posb = b.sb("posb", [128, TLOC], F32)
        t_pb = emit_row_broadcast(b, posr, posb, TLOC, "pb", deps=[t_pos])
        ang = b.sb("ang", [128, TLOC], F32)
        red = b.sb("red", [128, TLOC], F32)
        kf = b.sb("kf", [128, TLOC], F32)
        kfi = b.sb("kfi", [128, TLOC], I32)
        msk = b.sb("msk", [128, TLOC], F32)
        cosT = b.sb("cosT", [128, TLOC], F32)
        sinT = b.sb("sinT", [128, TLOC], F32)

        def build_trig(dst, shift, dep):
            t1 = pr.op("v", "tensor_scalar", ang[:], posb[:], freqc[:], None,
                       ALU.mult, deps=[t_pb, t_fr, dep])
            if shift:
                t1 = pr.op("v", "tensor_scalar", ang[:], ang[:], shift, None,
                           ALU.add)
            pr.op("v", "tensor_scalar", red[:], ang[:], INV_2PI, None, ALU.mult)
            pr.op("v", "tensor_copy", kfi[:], red[:])
            pr.op("v", "tensor_copy", kf[:], kfi[:])
            pr.op("v", "tensor_sub", msk[:], red[:], kf[:])
            pr.op("v", "tensor_scalar", msk[:], msk[:], 0.0, None, ALU.is_lt)
            pr.op("v", "tensor_sub", kf[:], kf[:], msk[:])
            pr.op("v", "scalar_tensor_tensor", red[:], kf[:], -TWO_PI, ang[:],
                  ALU.mult, ALU.add)
            pr.op("v", "tensor_scalar", msk[:], red[:], 3.14159265, None,
                  ALU.is_gt)
            t2 = pr.op("v", "scalar_tensor_tensor", red[:], msk[:], -TWO_PI,
                       red[:], ALU.mult, ALU.add)
            return pr.op("s", "activation", dst[:], red[:], AF.Sin, deps=[t2])

        t_cos = build_trig(cosT, 1.5707963267948966, None)
        t_sin = build_trig(sinT, None, t_cos)

        bqc = b.sb("bqc", [128, DM // 128], F32)
        t_bq = pr.dma("Lbq", bqc[:], bq_rs)
        bkc = b.sb("bkc", [128, DM // 128], F32)
        t_bk = pr.dma("Lbk", bkc[:], bk_rs)
        bvc = b.sb("bvc", [128, DM // 128], F32)
        t_bvc = pr.dma("Lbv", bvc[:], bv)

        wtl = [b.sb(f"wtl{i}", [128, 8 * 128], BF16) for i in range(2)]
        qcol = b.sb("qcol", [128, TLOC], F32)
        qrot = b.sb("qrot", [128, TLOC], F32)
        tA = b.sb("tA", [64, TLOC], F32)
        tB = b.sb("tB", [64, TLOC], F32)
        qb16 = b.sb("qb16", [128, TLOC], BF16)

        prev_stq = ()
        for (wsrc, bcol, btok, odst, sname) in (
                (wqb_d, bqc, t_bq, o_q, "Sq"), (wkb_d, bkc, t_bk, o_k, "Sk")):
            t_ld_prev = {0: (), 1: ()}
            for m in range(DM // 128):
                bi = m % 2
                ldw = pr.dma(f"LW{sname}{bi}",
                             wtl[bi][:].rearrange("p (k c) -> p k c", k=8),
                             wsrc[:, m*128:(m+1)*128].rearrange(
                                 "(k p) c -> p k c", p=128),
                             deps=wp + list(t_ld_prev.get(bi) or ()))
                with b.nc.psum_tensor(b.prefix + f"qp{sname}{m}", [128, 512], F32) as qp:
                    prev_ev = None
                    for nch in range(TLOC // 512):
                        c0 = nch * 512
                        mt = None
                        for k in range(8):
                            mt = pr.op("p", "matmul", qp[:],
                                       wtl[bi][:, k*128:(k+1)*128],
                                       xnT[:, k*TLOC + c0: k*TLOC + c0 + 512],
                                       start=(k == 0), stop=(k == 7),
                                       deps=[ldw, t_xnT] +
                                            ([prev_ev] if prev_ev else [])
                                            if k == 0 else ())
                        prev_ev = pr.op("s", "activation", qcol[:, c0:c0+512],
                                        qp[:], AF.Identity, bias=bcol[:, m:m+1],
                                        deps=[mt, btok, pr.tok("v")])
                pr.op("v", "tensor_mul", tA[:], qcol[0:64, :], cosT[0:64, :],
                      deps=[prev_ev, t_cos, t_sin])
                pr.op("v", "tensor_mul", tB[:], qcol[64:128, :], sinT[64:128, :])
                pr.op("v", "tensor_sub", qrot[0:64, :], tA[:], tB[:])
                pr.op("v", "tensor_mul", tA[:], qcol[0:64, :], sinT[0:64, :])
                pr.op("v", "tensor_mul", tB[:], qcol[64:128, :], cosT[64:128, :])
                pr.op("v", "tensor_add", qrot[64:128, :], tA[:], tB[:])
                t_qb = pr.op("v", "tensor_copy", qb16[:], qrot[:],
                             deps=list(prev_stq))
                t_st = pr.dma(sname, odst[m*128:(m+1)*128, :], qb16[:],
                              deps=[t_qb])
                prev_stq = (t_st,)
                t_ld_prev[bi] = (t_st, t_qb)
            b.out_done(pr.tok(sname))

        vb16 = b.sb("vb16", [128, TLOC], BF16)
        for m in range(DM // 128):
            bi = m % 2
            ldw = pr.dma(f"LWv{bi}",
                         wtl[bi][:].rearrange("p (k c) -> p k c", k=8),
                         wvb_d[:, m*128:(m+1)*128].rearrange(
                             "(k p) c -> p k c", p=128),
                         deps=wp + list(t_ld_prev.get(bi) or ()))
            with b.nc.psum_tensor(b.prefix + f"vqp{m}", [128, 512], F32) as qp:
                prev_ev = None
                for nch in range(TLOC // 512):
                    c0 = nch * 512
                    mt = None
                    for k in range(8):
                        mt = pr.op("p", "matmul", qp[:],
                                   wtl[bi][:, k*128:(k+1)*128],
                                   xnT[:, k*TLOC + c0: k*TLOC + c0 + 512],
                                   start=(k == 0), stop=(k == 7),
                                   deps=[ldw, t_xnT] +
                                        ([prev_ev] if prev_ev else [])
                                        if k == 0 else ())
                    prev_ev = pr.op("s", "activation", vb16[:, c0:c0+512],
                                    qp[:], AF.Identity, bias=bvc[:, m:m+1],
                                    deps=[mt, t_bvc, pr.tok("v"),
                                          pr.tok("Sv")])
            t_st = pr.dma("Sv", o_v[m*128:(m+1)*128, :], vb16[:],
                          deps=[prev_ev])
            t_ld_prev[bi] = (t_st,)
        b.out_done(pr.tok("Sv"))


def build_attn_qkv():
    b = Builder("attn_qkv")
    io = {}
    io["xr"] = b.din("xr", [TLOC, DM], F32)
    io["wqT"] = b.din("wqT", [DM, DM], F32)
    io["wkT"] = b.din("wkT", [DM, DM], F32)
    io["wvT"] = b.din("wvT", [DM, DM], F32)
    io["bq_rs"] = b.din("bq_rs", [128, DM // 128], F32)
    io["bk_rs"] = b.din("bk_rs", [128, DM // 128], F32)
    io["bv"] = b.din("bv", [128, DM // 128], F32)
    io["pos"] = b.din("pos", [1, TLOC], F32)
    io["ident_in"] = b.din("ident", [128, 128], BF16)
    io["o_q"] = b.dout("o_q", [DM, TLOC], BF16)
    io["o_k"] = b.dout("o_k", [DM, TLOC], BF16)
    io["o_v"] = b.dout("o_v", [DM, TLOC], BF16)
    emit_attn_qkv(b, io)
    return b.finish()


def emit_attn_core(b, io):
    pr = b.pr
    ag_q = io["ag_q"]
    ag_k = io["ag_k"]
    ag_v = io["ag_v"]
    qkidx_in = io["qkidx"]
    vidx_in = io["vidx"]
    ident_in = io["ident_in"]
    o_att = io["o_att"]
    NQ = TSEQ // 512
    NROW = NC * DM

    @body(b)
    def _main():
        onesb = b.sb("onesb", [128, 1], BF16)
        pr.op("v", "memset", onesb[:], 1.0)
        ones64 = b.sb("ones64", [1, 64], F32)
        pr.op("v", "memset", ones64[:], 1.0)
        ident = b.sb("identt", [128, 128], BF16)
        t_id = pr.dma("Lid", ident[:], ident_in)
        qkidx = b.sb("qkidx_t", [64, 16], I32)
        t_qi = pr.dma("Lqi", qkidx[:], qkidx_in)
        vidx = b.sb("vidx_t", [64, 16], I32)
        t_vi = pr.dma("Lvi", vidx[:], vidx_in)
        mks = []
        mkf = b.sb("mkf", [128, 512], F32)
        for off in range(4):
            mk = b.sb(f"mk{off}", [128, 512], BF16)
            pr.op("g", "iota", mkf[:], pattern=[[1, 512]], base=-off * 128,
                  channel_multiplier=-1, allow_small_or_imprecise_dtypes=True,
                  deps=[pr.tok("v")])
            pr.op("v", "tensor_scalar", mkf[:], mkf[:], -0.5, None, ALU.is_gt,
                  deps=[pr.tok("g")])
            pr.op("v", "tensor_copy", mk[:], mkf[:])
            mks.append(mk)

        q_sb = b.sb("q_sb", [64, TSEQ], BF16)
        k_sb = b.sb("k_sb", [64, TSEQ], BF16)
        vc_sb = b.sb("vc_sb", [64, TSEQ], BF16)
        v_sb = b.sb("v_sb", [128, (TSEQ // 128) * DK], BF16)
        probs = b.sb("probs", [128, 512], BF16)
        accsb = b.sb("accsb", [64, 512], F32)
        nrm = b.sb("nrm", [1, 512], F32)
        ob16 = b.sb("ob16", [64, 512], BF16)
        Sps = b.ps("Sps", [128, 512], F32)
        acc = b.ps("acc", [64, 512], F32)
        nps = b.ps("nps", [1, 512], F32)
        rbc = b.ps("rbc", [64, 512], F32)
        pvt = b.ps("pvt", [128, 64], BF16)

        bnd = b.bound_reg(NROW - 1)
        for pi in range(8):
            pdeps = [pr.tok("p"), pr.tok("v"), pr.tok("s")]
            g_toks = []
            for (dst, srcag, idxt, it) in ((q_sb, ag_q, qkidx, t_qi),
                                           (k_sb, ag_k, qkidx, t_qi),
                                           (vc_sb, ag_v, vidx, t_vi)):
                for rh in range(2):
                    g_toks.append(pr.idma(
                        "GQK", dst[:, rh*TLOC:(rh+1)*TLOC], None, srcag,
                        IndirectOffsetOnAxis(ap=idxt[:, pi*2+rh:pi*2+rh+1], axis=0),
                        bnd, deps=[it] + pdeps))
            t_g = pr.rot_toks("GQK")
            prev_ev = None
            for kb in range(TSEQ // 128):
                tt = pr.op("p", "transpose", pvt[0:128, 0:64],
                           vc_sb[:, kb*128:(kb+1)*128], ident[0:64, 0:64],
                           deps=t_g + [t_id] + ([prev_ev] if prev_ev else []))
                prev_ev = pr.op("v", "tensor_copy", v_sb[:, kb*DK:(kb+1)*DK],
                                pvt[0:128, 0:64], deps=[tt])
            t_v = prev_ev
            for qb in range(NQ):
                c0 = qb * 512
                nkb = (qb + 1) * 4
                prev_probs = None
                for kb in range(nkb):
                    t_s = pr.op("p", "matmul", Sps[:], k_sb[:, kb*128:(kb+1)*128],
                                q_sb[:, c0:c0+512], start=True, stop=True,
                                deps=t_g + ([prev_probs] if prev_probs else []))
                    t_e = pr.op("s", "activation", probs[:], Sps[:], AF.Exp,
                                scale=0.125, deps=[t_s, pr.tok("v")])
                    tok_probs = t_e
                    off = kb - 4 * qb
                    if off >= 0:
                        tok_probs = pr.op("v", "tensor_mul", probs[:], probs[:],
                                          mks[off][:], deps=[t_e])
                    t_av = pr.op("p", "matmul", acc[:],
                                 v_sb[:, kb*DK:(kb+1)*DK], probs[:],
                                 start=(kb == 0), stop=(kb == nkb - 1),
                                 deps=[tok_probs, t_v])
                    t_nm = pr.op("p", "matmul", nps[:], onesb[:], probs[:],
                                 start=(kb == 0), stop=(kb == nkb - 1),
                                 deps=[tok_probs])
                    prev_probs = t_av
                t_cp = pr.op("v", "tensor_copy", accsb[:], acc[:],
                             deps=[prev_probs, pr.tok("p")])
                t_n = pr.op("v", "tensor_copy", nrm[:], nps[:])
                t_r = pr.op("v", "reciprocal", nrm[:], nrm[:])
                t_bc = pr.op("p", "matmul", rbc[:], ones64[:], nrm[:],
                             start=True, stop=True, deps=[t_r, pr.tok("v")])
                t_o = pr.op("v", "tensor_mul", ob16[:], accsb[:], rbc[:],
                            deps=[t_bc, pr.tok("So")])
                pr.dma("So", o_att[pi*64:(pi+1)*64, c0:c0+512], ob16[:],
                       deps=[t_o])
        b.out_done(pr.tok("So"))


def build_attn_core():
    b = Builder("attn_core")
    io = {}
    io["ag_q"] = b.din("ag_q", [NC * DM, TLOC], BF16)
    io["ag_k"] = b.din("ag_k", [NC * DM, TLOC], BF16)
    io["ag_v"] = b.din("ag_v", [NC * DM, TLOC], BF16)
    io["qkidx"] = b.din("qkidx", [64, 16], I32)
    io["vidx"] = b.din("vidx", [64, 16], I32)
    io["ident_in"] = b.din("ident", [128, 128], BF16)
    io["o_att"] = b.dout("o_att", [8 * DK, TSEQ], BF16)
    emit_attn_core(b, io)
    return b.finish()


def emit_attn_close(b, io):
    pr = b.pr
    aT = io["aT"]
    woT = io["woT"]
    bo = io["bo"]
    xr = io["xr"]
    o_x = io["o_x"]
    wob_d = b.dint("wob2", [DM, DM], BF16)
    NT = TLOC // 128

    @body(b)
    def _main():
        prep_ctx = ExitStack()
        wp = emit_weight_prep(b, [(woT, wob_d, DM, DM)], ctx=prep_ctx)
        prep_ctx.close()
        aT_sb = b.sb("aT_sb", [128, 8 * TLOC], BF16)
        t_a = pr.dma("La", aT_sb[:].rearrange("p (k c) -> p k c", k=8),
                     aT.rearrange("(k p) c -> p k c", p=128))
        wo_sb = b.sb("wo_sb", [128, 8 * DM], BF16)
        t_wo = pr.dma("Lwo", wo_sb[:].rearrange("p (k c) -> p k c", k=8),
                      wob_d.rearrange("(k p) c -> p k c", p=128), deps=wp)
        bor = b.sb("bor", [1, DM], F32)
        t_b0 = pr.dma("Lbo", bor[:], bo)
        bob = b.sb("bob", [128, DM], F32)
        t_bb = emit_row_broadcast(b, bor, bob, DM, "bo", deps=[t_b0])
        xt = b.sb("xt", [128, DM], F32)
        outt = b.sb("outt", [128, DM], F32)
        prev_users = []
        with b.nc.psum_tensor(b.prefix + "cps", [128, 512], F32) as cps:
            last_ev = None
            for tt in range(NT):
                t_ld = pr.dma(f"Lx{tt%2}", xt[:], xr[tt*128:(tt+1)*128, :],
                              deps=prev_users)
                for dn in range(2):
                    mt = None
                    for k in range(8):
                        mt = pr.op("p", "matmul", cps[:],
                                   aT_sb[:, k*TLOC + tt*128: k*TLOC + (tt+1)*128],
                                   wo_sb[:, k*DM + dn*512: k*DM + dn*512 + 512],
                                   start=(k == 0), stop=(k == 7),
                                   deps=[t_a, t_wo] +
                                        ([last_ev] if last_ev else [])
                                        if k == 0 else ())
                    t_o = pr.op("v", "tensor_add", outt[:, dn*512:(dn+1)*512],
                                cps[:], xt[:, dn*512:(dn+1)*512],
                                deps=[mt, t_ld])
                    t_o = pr.op("v", "tensor_add", outt[:, dn*512:(dn+1)*512],
                                outt[:, dn*512:(dn+1)*512],
                                bob[:, dn*512:(dn+1)*512], deps=[t_bb])
                    last_ev = t_o
                t_st = pr.dma("Sx", o_x[tt*128:(tt+1)*128, :], outt[:],
                              deps=[pr.tok("v")])
                prev_users = [t_st, pr.tok("v")]
        b.out_done(pr.tok("Sx"))


def build_attn_close():
    b = Builder("attn_close")
    io = {}
    io["aT"] = b.din("aT", [DM, TLOC], BF16)
    io["woT"] = b.din("woT", [DM, DM], F32)
    io["bo"] = b.din("bo", [1, DM], F32)
    io["xr"] = b.din("xr", [TLOC, DM], F32)
    io["o_x"] = b.dout("o_x", [TLOC, DM], F32)
    emit_attn_close(b, io)
    return b.finish()

def _perm_cols():
    perm = []
    for i in range(DM // 128):
        h0, h1 = 2 * i, 2 * i + 1
        for (h, dr) in ((h0, range(0, 32)), (h1, range(0, 32)),
                        (h0, range(32, 64)), (h1, range(32, 64))):
            perm.extend(h * DK + d for d in dr)
    return np.array(perm)


def attn_idx_tables(core):
    pairs = [(bb, h) for bb in range(4) for h in range(NH)][core*8:(core+1)*8]
    qk = np.zeros((64, 16), np.int32)
    vv = np.zeros((64, 16), np.int32)
    for pi, (bb, h) in enumerate(pairs):
        i = h // 2
        if h % 2 == 0:
            rows = list(range(i*128, i*128+32)) + list(range(i*128+64, i*128+96))
        else:
            rows = list(range(i*128+32, i*128+64)) + list(range(i*128+96, i*128+128))
        for rh in range(2):
            rank = 2 * bb + rh
            qk[:, pi*2+rh] = rank * DM + np.array(rows)
            vv[:, pi*2+rh] = rank * DM + h * DK + np.arange(64)
    return qk, vv


def run_attn_layer(x_full, I):
    import ml_dtypes
    xf = x_full.reshape(NC, TLOC, DM)
    ident = np.eye(128, dtype=ml_dtypes.bfloat16)
    perm = _perm_cols()

    ncJ = build_attn_qkv()
    wqT = np.ascontiguousarray(_f32(I["L1_attn_WQ_w"]).T[:, perm])
    wkT = np.ascontiguousarray(_f32(I["L1_attn_WK_w"]).T[:, perm])
    wvT = np.ascontiguousarray(_f32(I["L1_attn_WV_w"]).T)
    bq_rs = np.ascontiguousarray(_f32(I["L1_attn_WQ_b"])[perm].reshape(DM // 128, 128).T)
    bk_rs = np.ascontiguousarray(_f32(I["L1_attn_WK_b"])[perm].reshape(DM // 128, 128).T)
    bv_rs = np.ascontiguousarray(_f32(I["L1_attn_WV_b"]).reshape(DM // 128, 128).T)
    common = dict(wqT=wqT, wkT=wkT, wvT=wvT, bq_rs=bq_rs, bk_rs=bk_rs,
                  bv=bv_rs, ident=ident)
    in_maps = []
    for i in range(NC):
        t0 = (i % 2) * TLOC
        posr = np.arange(t0, t0 + TLOC, dtype=np.float32).reshape(1, TLOC)
        in_maps.append(dict(xr=_f32(xf[i]), pos=posr, **common))
    resJ = run_bass_kernel_spmd(ncJ, in_maps, core_ids=list(range(NC)))
    ag_q = np.concatenate([np.asarray(resJ.results[i]["o_q"]) for i in range(NC)], 0)
    ag_k = np.concatenate([np.asarray(resJ.results[i]["o_k"]) for i in range(NC)], 0)
    ag_v = np.concatenate([np.asarray(resJ.results[i]["o_v"]) for i in range(NC)], 0)

    ncK = build_attn_core()
    in_maps = []
    for c in range(NC):
        qk, vv = attn_idx_tables(c)
        in_maps.append(dict(ag_q=ag_q, ag_k=ag_k, ag_v=ag_v,
                            qkidx=qk, vidx=vv, ident=ident))
    resK = run_bass_kernel_spmd(ncK, in_maps, core_ids=list(range(NC)))
    ag_att = np.concatenate([np.asarray(resK.results[c]["o_att"]) for c in range(NC)], 0)

    ncL = build_attn_close()
    woT = np.ascontiguousarray(_f32(I["L1_attn_Wo_w"]).T)
    bo = _f32(I["L1_attn_Wo_b"]).reshape(1, DM)
    in_maps = []
    for i in range(NC):
        bb, hf = i // 2, i % 2
        aT = np.ascontiguousarray(ag_att[bb*1024:(bb+1)*1024, hf*TLOC:(hf+1)*TLOC])
        in_maps.append(dict(aT=aT, woT=woT, bo=bo, xr=_f32(xf[i])))
    resL = run_bass_kernel_spmd(ncL, in_maps, core_ids=list(range(NC)))
    x5 = np.concatenate([resL.results[i]["o_x"] for i in range(NC)], 0)
    return x5.reshape(x_full.shape)



def kernel_unfused(**inputs):
    I = inputs
    x = _f32(I["L0_attn_x"]).reshape(-1, DM)
    x = run_moe_layer(x, I, "L0_ffn_").reshape(-1, DM)
    x = run_mamba_layer(x, I).reshape(-1, DM)
    x = run_moe_layer(x, I, "L0_ffn_").reshape(-1, DM)
    x = run_ffn1_layer(x, I).reshape(-1, DM)
    x = run_attn_layer(x, I).reshape(-1, DM)
    x = run_ffn1_layer(x, I).reshape(-1, DM)
    return np.ascontiguousarray(x.reshape(4, TSEQ, DM), dtype=np.float32)


def phase_sync(b):
    pr = b.pr
    names = ["v", "s", "p", "g"] + sorted(pr.touched - {"v", "s", "p", "g"})
    for ename in ("v", "s", "p", "g", "sync"):
        eng = pr.eng(ename)
        for sname in names:
            cnt = pr.count.get(sname, 0)
            if cnt > 0:
                eng.wait_ge(pr.sem[sname], cnt)
    pr.touched.clear()


class PhaseCtx:
    def __init__(self, b, prefix):
        self.b = b
        self.prefix = prefix

    def __enter__(self):
        phase_sync(self.b)
        self.b.pr.new_phase()
        self._sid, _ = self.b.nc.enter_named_scope(self.prefix, False)
        self.b.prefix = self.prefix
        self.b._scope = ExitStack()
        return self

    def __exit__(self, *a):
        phase_sync(self.b)
        self.b.nc.leave_named_scope(self.prefix, self._sid, False)
        self.b._scope.close()
        self.b._scope = None
        self.b.prefix = ""
        return False


def emit_coll(b, kind, op, in_ap, out_ap, tag):
    pr = b.pr
    if pr.touched - {"CC"}:
        phase_sync(b)
    return pr.coll("CC", kind, op, [in_ap.opt()], [out_ap.opt()])


def emit_dram_copy(b, dst, src, R, C, dt, tag):
    pr = b.pr
    t = b.sb(f"{tag}_cp", [128, C], dt)
    prev = None
    toks = []
    for r0 in range(0, R, 128):
        rr = min(128, R - r0)
        ld = pr.dma("GCL", t[0:rr, :], src[r0:r0+rr, :], deps=[prev])
        prev = pr.dma("GCS", dst[r0:r0+rr, :], t[0:rr, :], deps=[ld])
        toks.append(prev)
    return toks[-1]


def emit_idx_gather_dram(b, dst_dram, src_dram, idx_in, ntiles, C, dt, bnd, tag,
                         zero_fill=False, rows=128):
    pr = b.pr
    t = b.sb(f"{tag}_g", [rows, C], dt)
    idxt = b.sb(f"{tag}_ix", [rows, ntiles], I32)
    t_ix = pr.dma("GIL", idxt[:], idx_in)
    prev = None
    for j in range(ntiles):
        dep = [t_ix] + ([prev] if prev else [])
        if zero_fill:
            z = pr.op("v", "memset", t[:], 0.0, deps=dep)
            dep = dep + [z]
        g = pr.idma("GIG", t[:], None, src_dram,
                    IndirectOffsetOnAxis(ap=idxt[:, j:j+1], axis=0),
                    bnd, deps=dep)
        prev = pr.dma("GIS", dst_dram[j*rows:(j+1)*rows, :], t[:], deps=[g])
    return prev


def emit_probsT_glue(b, probs_dram, out_dram, tag):
    pr = b.pr
    idf = b.sb(f"{tag}_idf", [128, 128], F32)
    iof = b.sb(f"{tag}_iof", [128, 128], F32)
    pr.op("g", "iota", iof[:], pattern=[[-1, 128]], base=0,
          channel_multiplier=1, allow_small_or_imprecise_dtypes=True)
    t_id = pr.op("v", "tensor_scalar", idf[:], iof[:], 0.0, None, ALU.is_equal,
                 deps=[pr.tok("g")])
    pt = b.sb(f"{tag}_pt", [128, NEXP], F32)
    ot = b.sb(f"{tag}_ot", [NEXP, TLOC], F32)
    pps = b.ps(f"{tag}_ps", [NEXP, 128], F32)
    prev = None
    for i in range(TLOC // 128):
        ld = pr.dma("GPL", pt[:], probs_dram[i*128:(i+1)*128, :],
                    deps=[prev])
        tt = pr.op("p", "transpose", pps[:], pt[:], idf[:],
                   deps=[ld, t_id, pr.tok("v")])
        prev = pr.op("v", "tensor_copy", ot[:, i*128:(i+1)*128], pps[:],
                     deps=[tt])
    return pr.dma("GPS", out_dram, ot[:], deps=[prev])


def build_fused(n_stages=6, debug=False):
    b = Builder("fused")
    dbg_bufs = {}
    RG = [list(range(NC))]

    xin = b.din("xin", [TLOC, DM], F32)
    e_gate = b.din("gate", [DM, NEXP], F32)
    e_ws1 = b.din("ws1", [DM, HD], F32)
    e_bs1 = b.din("bs1_rs", [128, HD // 128], F32)
    e_ws2 = b.din("ws2", [HD, DM], F32)
    e_bs2 = b.din("bs2", [1, DM], F32)
    e_w1 = b.din("w1", [DM, HD], F32)
    e_b1 = b.din("b1_rs", [128, HD // 128], F32)
    e_w2 = b.din("w2", [HD, DM], F32)
    e_b2 = b.din("b2", [1, DM], F32)
    ident_in = b.din("ident", [128, 128], BF16)
    m_winT = b.din("w_inT", [DM, 2 * DI], F32)
    m_nw = b.din("normw_rs", [128, DM // 128], F32)
    m_nb = b.din("normb_rs", [128, DM // 128], F32)
    m_cw = b.din("convw", [128, DI // 128 * 4], F32)
    m_cb = b.din("convb_rs", [128, DI // 128], F32)
    m_wdt = b.din("w_dtT", [DI, DI], F32)
    m_bdt = b.din("bdt_rs", [128, DI // 128], F32)
    m_wB = b.din("w_BT", [DI, NSTATE], F32)
    m_wC = b.din("w_CT", [DI, NSTATE], F32)
    m_alog = b.din("alog", [256, NSTATE], F32)
    m_D = b.din("D_rs", [128, DI // 128], F32)
    m_woT = b.din("woT", [DI, DM], F32)
    xprev_idx = b.din("xprev_idx", [128, 1], I32)
    m_wmask = b.din("wmask", [128, 1], F32)
    m_sel16 = b.din("sel16", [16, 16 * 128], F32)
    f_wT = b.din("wT", [DM, DM], F32)
    f_bias = b.din("bias", [1, DM], F32)
    a_wqT = b.din("wqT", [DM, DM], F32)
    a_wkT = b.din("wkT", [DM, DM], F32)
    a_wvT = b.din("wvT", [DM, DM], F32)
    a_bq = b.din("bq_rs", [128, DM // 128], F32)
    a_bk = b.din("bk_rs", [128, DM // 128], F32)
    a_bv = b.din("bv", [128, DM // 128], F32)
    a_pos = b.din("pos", [1, TLOC], F32)
    a_qkidx = b.din("qkidx", [64, 16], I32)
    a_vidx = b.din("vidx", [64, 16], I32)
    a_woT = b.din("a_woT", [DM, DM], F32)
    a_bo = b.din("bo", [1, DM], F32)
    peidx = b.din("peidx", [NEXP, 1], I32)
    slwidx = b.din("slwidx", [NEXP, 1], I32)
    aidx = b.din("aidx", [128, 8], I32)
    chidx = b.din("chidx", [128, 16], I32)

    o_x = b.dout("o_x", [TLOC, DM], F32)

    xb = [b.dint(f"xb{i}", [TLOC, DM], F32) for i in range(6)]
    pr = b.pr

    def moe_phase(tag, xr_ap, xout_ap):
        c_xn = b.dint(f"{tag}_cxn", [TLOC, DM], BF16)
        c_pr = b.dint(f"{tag}_cpr", [TLOC, NEXP], F32)
        c_sh = b.dint(f"{tag}_csh", [TLOC, DM], F32)
        c_pT = b.dint(f"{tag}_cpT", [NEXP, TLOC], F32)
        ag_xn = b.dint(f"{tag}_agxn", [NC * TLOC, DM], BF16)
        ag_pT = b.dint(f"{tag}_agpT", [NC * NEXP, TLOC], F32)
        c_pe = b.dint(f"{tag}_cpe", [NEXP, TLOC], F32)
        c_eo = b.dint(f"{tag}_ceo", [CAP, DM], F32)
        c_tau = b.dint(f"{tag}_ctau", [1, 2], F32)
        c_sl = b.dint(f"{tag}_csl", [NEXP, TLOC], F32)
        ag_eo = b.dint(f"{tag}_ageo", [NC * CAP, DM], F32)
        ag_sl = b.dint(f"{tag}_agsl", [NC * NEXP, TLOC], F32)
        c_slw = b.dint(f"{tag}_cslw", [NEXP, TLOC], F32)
        scr_idx = b.dint(f"{tag}_scri", [128, 16], I32)

        with PhaseCtx(b, f"{tag}a_"):
            emit_ffn0_pre(b, dict(xr=xr_ap, gate=e_gate, ws1=e_ws1,
                                  bs1_rs=e_bs1, ws2=e_ws2, bs2=e_bs2,
                                  ident_in=ident_in, o_xn=c_xn, o_probs=c_pr,
                                  o_shared=c_sh))
        with PhaseCtx(b, f"{tag}g1_"):
            emit_probsT_glue(b, c_pr, c_pT, f"{tag}g1")
        with PhaseCtx(b, f"{tag}c1_"):
            emit_coll(b, "AllGather", ALU.bypass, c_xn, ag_xn, "agxn")
            emit_coll(b, "AllGather", ALU.bypass, c_pT, ag_pT, "agpT")
        with PhaseCtx(b, f"{tag}g2_"):
            emit_idx_gather_dram(b, c_pe, ag_pT, peidx, 1, TLOC, F32,
                                 b.bound_reg(NC * NEXP - 1), f"{tag}g2",
                                 rows=NEXP)
        with PhaseCtx(b, f"{tag}b_"):
            emit_ffn0_expert(b, dict(pe_in=c_pe, xn_all=ag_xn, w1=e_w1,
                                     b1_rs=e_b1, w2=e_w2, b2=e_b2,
                                     ident_in=ident_in, o_eout=c_eo,
                                     o_tau=c_tau, o_idx=scr_idx,
                                     o_slots=c_sl))
        with PhaseCtx(b, f"{tag}c2_"):
            emit_coll(b, "AllGather", ALU.bypass, c_eo, ag_eo, "ageo")
            emit_coll(b, "AllGather", ALU.bypass, c_sl, ag_sl, "agsl")
        with PhaseCtx(b, f"{tag}g3_"):
            emit_idx_gather_dram(b, c_slw, ag_sl, slwidx, 1, TLOC, F32,
                                 b.bound_reg(NC * NEXP - 1), f"{tag}g3",
                                 rows=NEXP)
        with PhaseCtx(b, f"{tag}c_"):
            emit_ffn0_close(b, dict(xr=xr_ap, shared=c_sh, slots_win=c_slw,
                                    o_x=xout_ap, eout_all=ag_eo))
        if debug and tag == "m0":
            dbg_bufs.update(d_xn=(c_xn, BF16), d_pr=(c_pr, F32),
                            d_pe=(c_pe, F32), d_sl=(c_slw, F32),
                            d_eo=(c_eo, F32), d_sh=(c_sh, F32),
                            d_idx=(scr_idx, I32))

    def mamba_phase(tag, xr_ap, xout_ap):
        c_dt = b.dint(f"{tag}_cdt", [DI, TLOC], BF16)
        c_u = b.dint(f"{tag}_cu", [DI, TLOC], BF16)
        c_xc = b.dint(f"{tag}_cxc", [DI, TLOC], BF16)
        c_z = b.dint(f"{tag}_cz", [DI, TLOC], BF16)
        c_B = b.dint(f"{tag}_cB", [NSTATE, TLOC], F32)
        c_C = b.dint(f"{tag}_cC", [NSTATE, TLOC], F32)
        c_tail = b.dint(f"{tag}_ctail", [128, DM], F32)
        ag_tail = b.dint(f"{tag}_agtail", [NC * 128, DM], F32)
        c_xprev = b.dint(f"{tag}_cxprev", [128, DM], F32)
        ag_dt = b.dint(f"{tag}_agdt", [NC * DI, TLOC], BF16)
        ag_u = b.dint(f"{tag}_agu", [NC * DI, TLOC], BF16)
        g_dt = b.dint(f"{tag}_gdt", [DI, TLOC], BF16)
        g_u = b.dint(f"{tag}_gu", [DI, TLOC], BF16)
        ag_B = b.dint(f"{tag}_agB", [NC * NSTATE, TLOC], F32)
        ag_C = b.dint(f"{tag}_agC", [NC * NSTATE, TLOC], F32)
        yin = b.dint(f"{tag}_yin", [DI, TLOC], BF16)
        ag_y = b.dint(f"{tag}_agy", [NC * DI, TLOC], BF16)
        g_y = b.dint(f"{tag}_gy", [DI, TLOC], BF16)
        c_yg = b.dint(f"{tag}_cyg", [DI, TLOC], BF16)

        with PhaseCtx(b, f"{tag}t_"):
            emit_dram_copy(b, c_tail, xr_ap[TLOC-128:TLOC, :], 128, DM, F32,
                           f"{tag}t")
        with PhaseCtx(b, f"{tag}c0_"):
            emit_coll(b, "AllGather", ALU.bypass, c_tail, ag_tail, "agt")
        with PhaseCtx(b, f"{tag}g0_"):
            emit_idx_gather_dram(b, c_xprev, ag_tail, xprev_idx, 1, DM, F32,
                                 b.bound_reg(NC * 128 - 1), f"{tag}g0",
                                 zero_fill=True)
        with PhaseCtx(b, f"{tag}d_"):
            emit_mamba_pre(b, dict(xr=xr_ap, xprev=c_xprev, wmask=m_wmask,
                                   w_inT=m_winT,
                                   normw_rs=m_nw, normb_rs=m_nb, convw=m_cw,
                                   convb_rs=m_cb, w_dtT=m_wdt, bdt_rs=m_bdt,
                                   w_BT=m_wB, w_CT=m_wC, ident_in=ident_in,
                                   o_dt=c_dt, o_u=c_u, o_xc=c_xc, o_z=c_z,
                                   o_B=c_B, o_C=c_C))
        with PhaseCtx(b, f"{tag}c1_"):
            emit_coll(b, "AllGather", ALU.bypass, c_dt, ag_dt, "agdt")
            emit_coll(b, "AllGather", ALU.bypass, c_u, ag_u, "agu")
            emit_coll(b, "AllGather", ALU.bypass, c_B, ag_B, "agB")
            emit_coll(b, "AllGather", ALU.bypass, c_C, ag_C, "agC")
        with PhaseCtx(b, f"{tag}g1_"):
            emit_idx_gather_dram(b, g_dt, ag_dt, chidx, 16, TLOC, BF16,
                                 b.bound_reg(NC * DI - 1), f"{tag}gdt")
            emit_idx_gather_dram(b, g_u, ag_u, chidx, 16, TLOC, BF16,
                                 b.bound_reg(NC * DI - 1), f"{tag}gu")
        with PhaseCtx(b, f"{tag}e_"):
            emit_mamba_scan(b, dict(
                dts=g_dt.rearrange("(r c) t -> c r t", c=256),
                us=g_u.rearrange("(r c) t -> c r t", c=256),
                Bs=ag_B, Cs=ag_C, alog=m_alog, sel16=m_sel16,
                o_y=yin.rearrange("(w c) t -> c w t", c=256)))
        with PhaseCtx(b, f"{tag}c2_"):
            emit_coll(b, "AllGather", ALU.bypass, yin, ag_y, "agy")
        with PhaseCtx(b, f"{tag}g2_"):
            emit_idx_gather_dram(b, g_y, ag_y, chidx, 16, TLOC, BF16,
                                 b.bound_reg(NC * DI - 1), f"{tag}gy")
        with PhaseCtx(b, f"{tag}f_"):
            emit_mamba_close(b, dict(yT=g_y, xcT=c_xc, zT=c_z, D_rs=m_D,
                                     woT=m_woT, xr=xr_ap, o_x=xout_ap,
                                     o_yg=c_yg))
        if debug:
            dbg_bufs.update(d_dt=(c_dt, BF16), d_xc=(c_xc, BF16),
                            d_z=(c_z, BF16), d_B=(c_B, F32), d_C=(c_C, F32),
                            d_gy=(g_y, BF16), d_yg=(c_yg, BF16),
                            d_xprev=(c_xprev, F32))

    def ffn1_phase(tag, xr_ap, xout_ap):
        c_xn = b.dint(f"{tag}_cxn", [TLOC, DM], F32)
        c_am = b.dint(f"{tag}_cam", [1, 1], F32)
        ag_am = b.dint(f"{tag}_agam", [NC, 1], F32)
        with PhaseCtx(b, f"{tag}h_"):
            emit_ffn1_ln(b, dict(xr=xr_ap, o_xn=c_xn, o_amax=c_am))
        with PhaseCtx(b, f"{tag}c_"):
            emit_coll(b, "AllGather", ALU.bypass, c_am, ag_am, "agam")
        with PhaseCtx(b, f"{tag}i_"):
            emit_ffn1_close(b, dict(xn_in=c_xn,
                                    amax8=ag_am.rearrange("r u -> u r"),
                                    wT=f_wT, bias=f_bias, xr=xr_ap,
                                    ident_in=ident_in, o_x=xout_ap), 0.5)

    def attn_phase(tag, xr_ap, xout_ap):
        c_q = b.dint(f"{tag}_cq", [DM, TLOC], BF16)
        c_k = b.dint(f"{tag}_ck", [DM, TLOC], BF16)
        c_v = b.dint(f"{tag}_cv", [DM, TLOC], BF16)
        ag_q = b.dint(f"{tag}_agq", [NC * DM, TLOC], BF16)
        ag_k = b.dint(f"{tag}_agk", [NC * DM, TLOC], BF16)
        ag_v = b.dint(f"{tag}_agv", [NC * DM, TLOC], BF16)
        c_att = b.dint(f"{tag}_catt", [8 * DK, TSEQ], BF16)
        ag_att = b.dint(f"{tag}_agatt", [NC * 8 * DK, TSEQ], BF16)
        c_aT = b.dint(f"{tag}_caT", [DM, TLOC], BF16)
        with PhaseCtx(b, f"{tag}j_"):
            emit_attn_qkv(b, dict(xr=xr_ap, wqT=a_wqT, wkT=a_wkT, wvT=a_wvT,
                                  bq_rs=a_bq, bk_rs=a_bk, bv=a_bv, pos=a_pos,
                                  ident_in=ident_in, o_q=c_q, o_k=c_k,
                                  o_v=c_v))
        with PhaseCtx(b, f"{tag}c1_"):
            emit_coll(b, "AllGather", ALU.bypass, c_q, ag_q, "agq")
            emit_coll(b, "AllGather", ALU.bypass, c_k, ag_k, "agk")
            emit_coll(b, "AllGather", ALU.bypass, c_v, ag_v, "agv")
        with PhaseCtx(b, f"{tag}k_"):
            emit_attn_core(b, dict(ag_q=ag_q, ag_k=ag_k, ag_v=ag_v,
                                   qkidx=a_qkidx, vidx=a_vidx,
                                   ident_in=ident_in, o_att=c_att))
        with PhaseCtx(b, f"{tag}c2_"):
            emit_coll(b, "AllGather", ALU.bypass, c_att, ag_att, "agat")
        with PhaseCtx(b, f"{tag}g1_"):
            emit_idx_gather_dram(b, c_aT, ag_att, aidx, 8, TLOC, BF16,
                                 b.bound_reg(NC * 8 * DK - 1), f"{tag}g1")
        with PhaseCtx(b, f"{tag}l_"):
            emit_attn_close(b, dict(aT=c_aT, woT=a_woT, bo=a_bo, xr=xr_ap,
                                    o_x=xout_ap))

    stages = [
        lambda d: moe_phase("m0", xin, d),
        lambda d: mamba_phase("mb", xb[0], d),
        lambda d: moe_phase("m1", xb[1], d),
        lambda d: ffn1_phase("f0", xb[2], d),
        lambda d: attn_phase("at", xb[3], d),
        lambda d: ffn1_phase("f1", xb[4], d),
    ]

    @body(b)
    def _():
        for si in range(n_stages):
            dst = o_x if si == n_stages - 1 else xb[si]
            stages[si](dst)
        if debug:
            with PhaseCtx(b, "dbg_"):
                for name, (buf, dt) in dbg_bufs.items():
                    o = b.dout(f"o_{name}", list(buf.tensor.shape), dt)
                    R, C = buf.tensor.shape
                    b.out_done(emit_dram_copy(b, o, buf, R, C, dt, f"dbg{name}"))
    return b.finish()


_FUSED_CACHE = {}


def _chidx(core):
    idx = np.zeros((128, 16), np.int32)
    for r in range(NC):
        for ch in range(2):
            idx[:, r*2 + ch] = r * DI + core * 256 + ch * 128 + np.arange(128)
    return idx


def fused_in_maps(inputs):
    import ml_dtypes
    I = inputs
    xf = _f32(I["L0_attn_x"]).reshape(NC, TLOC, DM)
    ident = np.eye(128, dtype=ml_dtypes.bfloat16)
    perm = _perm_cols()
    cw = _f32(I["L0_attn_conv_w"])
    common = dict(
        gate=_f32(I["L0_ffn_gate"]), ws1=_f32(I["L0_ffn_Ws1"]),
        bs1_rs=np.ascontiguousarray(_f32(I["L0_ffn_bs1"]).reshape(HD // 128, 128).T),
        ws2=_f32(I["L0_ffn_Ws2"]), bs2=_f32(I["L0_ffn_bs2"]).reshape(1, DM),
        ident=ident,
        sel16=np.repeat(np.eye(16, dtype=np.float32), 128, axis=1),
        w_inT=np.ascontiguousarray(_f32(I["L0_attn_W_in"]).T),
        normw_rs=np.ascontiguousarray(_f32(I["L0_attn_norm_w"]).reshape(DM // 128, 128).T),
        normb_rs=np.ascontiguousarray(_f32(I["L0_attn_norm_b"]).reshape(DM // 128, 128).T),
        convw=np.ascontiguousarray(cw.reshape(DI // 128, 128, 4).transpose(1, 0, 2).reshape(128, -1)),
        convb_rs=np.ascontiguousarray(_f32(I["L0_attn_conv_b"]).reshape(DI // 128, 128).T),
        w_dtT=np.ascontiguousarray(_f32(I["L0_attn_W_dt"]).T),
        bdt_rs=np.ascontiguousarray(_f32(I["L0_attn_b_dt"]).reshape(DI // 128, 128).T),
        w_BT=np.ascontiguousarray(_f32(I["L0_attn_W_B"]).T),
        w_CT=np.ascontiguousarray(_f32(I["L0_attn_W_C"]).T),
        D_rs=np.ascontiguousarray(_f32(I["L0_attn_D"]).reshape(DI // 128, 128).T),
        woT=np.ascontiguousarray(_f32(I["L0_attn_W_out"]).T),
        wT=np.ascontiguousarray(_f32(I["L1_ffn_weight"]).T),
        bias=_f32(I["L1_ffn_bias"]).reshape(1, DM),
        wqT=np.ascontiguousarray(_f32(I["L1_attn_WQ_w"]).T[:, perm]),
        wkT=np.ascontiguousarray(_f32(I["L1_attn_WK_w"]).T[:, perm]),
        wvT=np.ascontiguousarray(_f32(I["L1_attn_WV_w"]).T),
        bq_rs=np.ascontiguousarray(_f32(I["L1_attn_WQ_b"])[perm].reshape(DM // 128, 128).T),
        bk_rs=np.ascontiguousarray(_f32(I["L1_attn_WK_b"])[perm].reshape(DM // 128, 128).T),
        bv=np.ascontiguousarray(_f32(I["L1_attn_WV_b"]).reshape(DM // 128, 128).T),
        a_woT=np.ascontiguousarray(_f32(I["L1_attn_Wo_w"]).T),
        bo=_f32(I["L1_attn_Wo_b"]).reshape(1, DM),
    )
    in_maps = []
    for i in range(NC):
        t0 = (i % 2) * TLOC
        qk, vv = attn_idx_tables(i)
        if i % 2 == 1:
            xpidx = ((i - 1) * 128 + np.arange(128, dtype=np.int32)).reshape(128, 1)
        else:
            xpidx = np.full((128, 1), 1 << 28, np.int32)
        bb, hf = i // 2, i % 2
        aidx = np.zeros((128, 8), np.int32)
        for j in range(8):
            aidx[:, j] = bb * 1024 + j * 128 + np.arange(128)
        in_maps.append(dict(
            xin=np.ascontiguousarray(xf[i]),
            w1=np.ascontiguousarray(_f32(I["L0_ffn_W1"])[i]),
            b1_rs=np.ascontiguousarray(_f32(I["L0_ffn_b1"])[i].reshape(HD // 128, 128).T),
            w2=np.ascontiguousarray(_f32(I["L0_ffn_W2"])[i]),
            b2=_f32(I["L0_ffn_b2"])[i].reshape(1, DM),
            alog=np.ascontiguousarray(_f32(I["L0_attn_A_log"])[i*256:(i+1)*256]),
            xprev_idx=xpidx,
            wmask=np.full((128, 1), 1.0 if i % 2 else 0.0, np.float32),
            pos=np.arange(t0, t0 + TLOC, dtype=np.float32).reshape(1, TLOC),
            qkidx=qk, vidx=vv,
            peidx=(np.arange(NC, dtype=np.int32) * NEXP + i).reshape(NEXP, 1),
            slwidx=(np.arange(NEXP, dtype=np.int32) * NEXP + i).reshape(NEXP, 1),
            aidx=aidx,
            chidx=np.ascontiguousarray(
                (np.arange(NC, dtype=np.int32)[None, :, None] * DI + i * 256
                 + np.arange(2, dtype=np.int32)[None, None, :] * 128
                 + np.arange(128, dtype=np.int32)[:, None, None])
                .transpose(0, 1, 2).reshape(128, 16, order="F").astype(np.int32)
                if False else _chidx(i)),
            **common))
    return in_maps


def kernel(**inputs):
    if "nc" not in _FUSED_CACHE:
        _FUSED_CACHE["nc"] = build_fused()
    nc = _FUSED_CACHE["nc"]
    in_maps = fused_in_maps(inputs)
    res = run_bass_kernel_spmd(nc, in_maps, core_ids=list(range(NC)))
    x = np.concatenate([res.results[i]["o_x"] for i in range(NC)], 0)
    return np.ascontiguousarray(x.reshape(4, TSEQ, DM), dtype=np.float32)
